# version 09
# baseline (speedup 1.0000x reference)
import numpy as np
import concourse.bass as bass
import concourse.mybir as mybir
from concourse.bass_utils import run_bass_kernel_spmd
from concourse import library_config
from contextlib import ExitStack

F32 = mybir.dt.float32
F16 = mybir.dt.float16
I32 = mybir.dt.int32
U16 = mybir.dt.uint16
ALU = mybir.AluOpType
AF = mybir.ActivationFunctionType
AX = mybir.AxisListType

B, S, DM = 2, 8192, 2048
HPC, TPC = 4, 2048
NT = 64
NT5 = 16
NJL = 32
IFF = 5632
NOSET = 48
RS128 = float(1.0 / np.sqrt(128.0))
NEG = -1.0e38
ENGS = ("sync", "tensor", "vector", "scalar", "gpsimd")
PHASES = None
SEMMAP = {}


def build_nc():
    nc = bass.Bass("TRN2", target_bir_lowering=False, debug=False)

    xb = nc.dram_tensor("xb", [S, DM], F32, kind="ExternalInput").ap()
    xq = nc.dram_tensor("xq", [TPC, DM], F32, kind="ExternalInput").ap()
    wqT = nc.dram_tensor("wqT", [DM, 512], F32, kind="ExternalInput").ap()
    wkT = nc.dram_tensor("wkT", [DM, 512], F32, kind="ExternalInput").ap()
    wvT = nc.dram_tensor("wvT", [DM, 512], F32, kind="ExternalInput").ap()
    woT_d = nc.dram_tensor("woT", [512, DM], F32, kind="ExternalInput").ap()
    gain_d = nc.dram_tensor("gain_tm", [128, HPC], F32, kind="ExternalInput").ap()
    cos_d = nc.dram_tensor("cos_d", [S, 64], F32, kind="ExternalInput").ap()
    sin_d = nc.dram_tensor("sin_d", [S, 64], F32, kind="ExternalInput").ap()
    proj_d = nc.dram_tensor("proj_d", [128, NJL], F32, kind="ExternalInput").ap()
    iden_d = nc.dram_tensor("iden_d", [128, 128], F32, kind="ExternalInput").ap()
    blksel_d = nc.dram_tensor("blksel_d", [128, 2], F32, kind="ExternalInput").ap()
    par_d = nc.dram_tensor("par_d", [128, 1], F32, kind="ExternalInput").ap()
    ones_d = nc.dram_tensor("ones_d", [128, 1], F32, kind="ExternalInput").ap()
    onesr_d = nc.dram_tensor("onesr_d", [1, 128], F32, kind="ExternalInput").ap()
    gateT_d = nc.dram_tensor("gateT_d", [256, IFF], I32, kind="ExternalInput").ap()
    upT_d = nc.dram_tensor("upT_d", [256, IFF], I32, kind="ExternalInput").ap()
    downT_d = nc.dram_tensor("downT_d", [768, DM], I32, kind="ExternalInput").ap()
    gsc_d = nc.dram_tensor("gsc_d", [256, IFF], F16, kind="ExternalInput").ap()
    usc_d = nc.dram_tensor("usc_d", [256, IFF], F16, kind="ExternalInput").ap()
    dsc_d = nc.dram_tensor("dsc_d", [768, DM], F16, kind="ExternalInput").ap()
    cvals_d = nc.dram_tensor("cvals_d", [128, 4], F32, kind="ExternalInput").ap()
    ltm_d = nc.dram_tensor("ltm_d", [128, 128], F32, kind="ExternalInput").ap()
    iotab_d = nc.dram_tensor("iotab_d", [128, 128], F32, kind="ExternalInput").ap()
    blkid_d = nc.dram_tensor("blkid_d", [128, 1], F32, kind="ExternalInput").ap()
    poff_d = nc.dram_tensor("poff_d", [128, 1], F32, kind="ExternalInput").ap()
    out_d = nc.dram_tensor("out", [TPC, DM], F32, kind="ExternalOutput").ap()

    qfm_d = nc.dram_tensor("qfm_d", [HPC * 128, S], F32).ap()
    ktm_d = nc.dram_tensor("ktm_d", [HPC, S, 128], F32).ap()
    vtm_d = nc.dram_tensor("vtm_d", [HPC, S, 128], F32).ap()
    yfm_d = nc.dram_tensor("yfm_d", [HPC * 128, S], F32).ap()
    kpn_d = nc.dram_tensor("kpn_d", [128, HPC, NT, NJL], F32).ap()
    rs_in_h = nc.dram_tensor("rs_in", [S, DM], F32)
    rs_out_h = nc.dram_tensor("rs_out", [TPC, DM], F32)
    x2_d = nc.dram_tensor("x2_d", [TPC, DM], F32).ap()
    h_d = nc.dram_tensor("h_d", [TPC, IFF], F32).ap()
    wde_d = nc.dram_tensor("wde_d", [NOSET, 128, DM], F32).ap()
    ln2_d = nc.dram_tensor("ln2_d", [TPC, DM], F32).ap()
    centd = nc.dram_tensor("centd", [2, NT * 128], F32).ap()
    dotsd = nc.dram_tensor("dotsd", [2, NT * 128], F32).ap()

    outer = ExitStack()

    def osb(name, shape, dtype=F32):
        return outer.enter_context(nc.sbuf_tensor(name, shape, dtype))

    def mksem(name):
        h = outer.enter_context(nc.semaphore(name))
        SEMMAP[name] = h
        return h

    pb = [outer.enter_context(nc.psum_tensor(f"pb{k}", [128, 512], F32)) for k in range(8)]

    iden = osb("iden", [128, 128])
    proj = osb("proj", [128, NJL])
    blksel = osb("blksel", [128, 2])
    parm = osb("parm", [128, 1])
    onescol = osb("onescol", [128, 1])
    onesrow = osb("onesrow", [1, 128])
    gain_tm = osb("gain_s", [128, HPC])
    cbank = osb("cbank", [128, 4])
    nc.const_aps.register(F32, 1e-5, cbank[:, 0:1])
    nc.const_aps.register(F32, 1e-6, cbank[:, 1:2])
    nc.const_aps.register(F32, 0.0, cbank[:, 2:3])
    nc.const_aps.register(F32, 1.0, cbank[:, 3:4])
    idx_all = osb("idx_all", [128, HPC * 33], mybir.dt.uint32)
    ltm = osb("ltm", [128, 128])
    iotab = osb("iotab", [128, 128])
    blkid = osb("blkid", [128, 1])
    poff = osb("poff", [128, 1])
    qsum = osb("qsum", [128, HPC])

    s = {}
    for nm in ["c0", "x0", "x1", "cs0", "cs1", "odone", "kv2a", "kv2b",
               "q2a", "q2b", "yoa", "yob", "y3a", "y3b",
               "rsAa", "rsAb", "rsBa", "rsBb", "ccA", "ccB",
               "gdpe", "gst", "l6", "l5s", "lsa", "lsb", "lra", "lrb",
               "x5a", "x5b", "x2wa", "x2wb",
               "g6a", "g6b", "hw6a", "hw6b", "w7a", "w7b", "o7a", "o7b",
               "xf", "csf", "ln1", "ln2", "lnd", "tp", "tpe", "qkv",
               "qe", "rm1", "rm2", "rope", "kfme", "kp", "kpe", "kpt",
               "kptev", "kq1", "kq2", "kpn", "cent", "centev", "o1", "ph",
               "sq", "g15", "hfree2", "qf2", "lt", "exq", "esum", "av",
               "sum2", "rcp2", "bc2", "rbe2", "ye", "kt2", "kt2e", "wo3", "w3", "e3",
               "wde5", "l51", "l52", "t5", "t5e", "e6", "mmg6",
               "mmu6", "sil6", "h6", "h7", "t7", "t7e", "mm7", "b71",
               "b72", "xl7"]:
        s[nm] = mksem(nm)

    PH = {"n": 0}

    with outer, nc.Block() as block:

        def make_section(engname, body):
            sec = getattr(block, engname)

            def wrapper(eng, body=body):
                if PH["n"] > 0:
                    eng.wait_ge(s["ph"], 5 * PH["n"])
                body(eng)
                eng.drain().then_inc(s["ph"], 1)
            sec(wrapper)

        def phase(emitters):
            if PHASES is not None and PH["n"] not in PHASES:
                emitters = {}
            for engname in ENGS:
                make_section(engname, emitters.get(engname, lambda eng: None))
            PH["n"] += 1

        class Seq:

            def __init__(self):
                self.steps = []

            def step(self, engname, fn):
                self.steps.append((engname, fn))

            def emitters(self, base):
                def mk(engname):
                    def body(eng):
                        for pos, (en, fn) in enumerate(self.steps):
                            if en == engname:
                                eng.wait_ge(s["sq"], base + pos)
                                fn(eng)
                                eng.drain().then_inc(s["sq"], 1)
                    return body
                return {en: mk(en) for en in set(e for e, _ in self.steps)}

        ph1 = ExitStack()

        def sb1(name, shape, dtype=F32):
            return ph1.enter_context(nc.sbuf_tensor(name, shape, dtype))

        wsb = sb1("wsb", [128, 16 * 1536])
        cent_acc = sb1("cent_acc", [2, NT * 128])
        xbuf = [sb1(f"xbuf{j}", [128, DM]) for j in range(2)]
        csb = [sb1(f"csb{j}", [128, 128]) for j in range(2)]
        ln1fm = [sb1(f"ln1fm{j}", [128, DM]) for j in range(2)]
        qkvtm = [sb1(f"qkvtm{j}", [128, 1024]) for j in range(2)]
        vsb = [sb1(f"vsb{j}", [128, 512]) for j in range(2)]
        qro = [sb1(f"qro{j}", [128, 512]) for j in range(2)]
        kro = [sb1(f"kro{j}", [128, 512]) for j in range(2)]
        qfmsb = [sb1(f"qfmsb{j}", [128, 512]) for j in range(2)]
        kfmsb = [sb1(f"kfmsb{j}", [128, 512]) for j in range(2)]
        bnst = sb1("bnst", [128, 24])
        mv = sb1("mv", [128, 2])
        sd1 = sb1("sd1", [128, 1])
        rstd = sb1("rstd", [128, 1])
        sqb = sb1("sqb", [128, 1024])
        ssq = sb1("ssq", [128, 8])
        srt8 = sb1("srt8", [128, 8])
        rr8 = sb1("rr8", [128, 8])
        rrq = sb1("rrq", [128, 4])
        qn = sb1("qn", [128, 512])
        kn = sb1("kn", [128, 512])
        rop1 = sb1("rop1", [128, 256])
        rop2 = sb1("rop2", [128, 256])
        kpTsb = sb1("kpTsb", [32, 512])
        kpnsb = sb1("kpnsb", [128, 128])
        ss4 = sb1("ss4", [128, 4])
        srt4 = sb1("srt4", [128, 4])
        rn4 = sb1("rn4", [128, 4])
        red4 = sb1("red4", [128, 4])
        NG = 6

        def p1_sp(sp):
            for t, d in ((iden, iden_d), (proj, proj_d), (blksel, blksel_d),
                         (parm, par_d), (onescol, ones_d), (onesrow, onesr_d),
                         (gain_tm, gain_d), (cbank, cvals_d), (ltm, ltm_d),
                         (iotab, iotab_d), (blkid, blkid_d), (poff, poff_d)):
                sp.dma_start(out=t[:], in_=d[:]).then_inc(s["c0"], 16)
            wv = wsb[:].rearrange("p (c o) -> p c o", o=1536)
            sp.dma_start(out=wv[:, :, 0:512],
                         in_=wqT.rearrange("(c p) o -> p c o", p=128)).then_inc(s["c0"], 16)
            sp.dma_start(out=wv[:, :, 512:1024],
                         in_=wkT.rearrange("(c p) o -> p c o", p=128)).then_inc(s["c0"], 16)
            sp.dma_start(out=wv[:, :, 1024:1536],
                         in_=wvT.rearrange("(c p) o -> p c o", p=128)).then_inc(s["c0"], 16)
            def ld1(i):
                j = i % 2
                if i >= 2:
                    sp.wait_ge(s["xf"], i - 1)
                sp.dma_start(out=xbuf[j][:], in_=xb[128 * i:128 * (i + 1), :]).then_inc(s["x0" if j == 0 else "x1"], 16)
                if i >= 2:
                    sp.wait_ge(s["csf"], i - 1)
                sp.dma_start(out=csb[j][:, 0:64], in_=cos_d[128 * i:128 * (i + 1), :]).then_inc(s["cs0" if j == 0 else "cs1"], 16)
                sp.dma_start(out=csb[j][:, 64:128], in_=sin_d[128 * i:128 * (i + 1), :]).then_inc(s["cs0" if j == 0 else "cs1"], 16)

            ld1(0)
            ld1(1)
            for i in range(NT):
                j = i % 2
                if i + 2 < NT:
                    ld1(i + 2)
                if i >= 1:
                    sp.wait_ge(s["o1"], 16 * 4 * i)
                    sp.sem_inc(s["odone"], 1)
                sp.wait_ge(s["tpe"], NG * i + 5)
                sp.dma_start(out=qfm_d.rearrange("(h p) t -> p h t", p=128)[:, :, 128 * i:128 * (i + 1)],
                             in_=qfmsb[j][:].rearrange("p (h t) -> p h t", t=128)).then_inc(s["o1"], 16)
                sp.wait_ge(s["rope"], i + 1)
                sp.dma_start(out=ktm_d.rearrange("h (c p) d -> c p h d", p=128)[i],
                             in_=kro[j][:].rearrange("p (h d) -> p h d", d=128)).then_inc(s["o1"], 16)
                sp.wait_ge(s["qe"], 3 * i + 3)
                sp.dma_start(out=vtm_d.rearrange("h (c p) d -> c p h d", p=128)[i],
                             in_=vsb[j][:].rearrange("p (h d) -> p h d", d=128)).then_inc(s["o1"], 16)
                sp.wait_ge(s["kpn"], i + 1)
                sp.dma_start(out=kpn_d[:, :, i, :],
                             in_=kpnsb[:].rearrange("p (h j) -> p h j", j=NJL)).then_inc(s["o1"], 16)
            sp.wait_ge(s["centev"], NT)
            sp.dma_start(out=centd[:], in_=cent_acc[:]).then_inc(s["o1"], 16)
            sp.wait_ge(s["o1"], 16 * (4 * NT + 1))

        def p1_pe(pe):
            pe.wait_ge(s["c0"], 240)
            for i in range(NT):
                j = i % 2
                g0 = NG * i
                pe.wait_ge(s["lnd"], i + 1)
                for g in range(4):
                    gi = g0 + g
                    if gi >= 2:
                        pe.wait_ge(s["tpe"], gi - 1)
                    for c in range(4):
                        cc = 4 * g + c
                        pe.transpose(pb[gi % 2][:, 128 * c:128 * (c + 1)],
                                     xbuf[j][:, 128 * cc:128 * (cc + 1)], iden[:])
                    pe.drain().then_inc(s["tp"], 1)
                pe.drain().then_inc(s["xf"], 1)
                pe.wait_ge(s["tpe"], g0 + 4)
                if i >= 1:
                    pe.wait_ge(s["qe"], 3 * i)
                for oc in range(3):
                    for ci in range(16):
                        pe.matmul(pb[2 + oc][:],
                                  ln1fm[j][:, 128 * ci:128 * (ci + 1)],
                                  wsb[:, 1536 * ci + 512 * oc:1536 * ci + 512 * (oc + 1)],
                                  start=(ci == 0), stop=(ci == 15))
                    pe.drain().then_inc(s["qkv"], 1)
                pe.wait_ge(s["rope"], i + 1)
                for g, src in ((4, qro[j]), (5, kro[j])):
                    gi = g0 + g
                    pe.wait_ge(s["tpe"], gi - 1)
                    for c in range(4):
                        pe.transpose(pb[gi % 2][:, 128 * c:128 * (c + 1)],
                                     src[:, 128 * c:128 * (c + 1)], iden[:])
                    pe.drain().then_inc(s["tp"], 1)
                pe.wait_ge(s["kfme"], i + 1)
                if i >= 1:
                    pe.wait_ge(s["kpe"], i)
                for hl in range(HPC):
                    pe.matmul(pb[5][0:NJL, 128 * hl:128 * (hl + 1)], proj[:],
                              kfmsb[j][:, 128 * hl:128 * (hl + 1)], start=True, stop=True)
                pe.drain().then_inc(s["kp"], 1)
                pe.wait_ge(s["kpe"], i + 1)
                if i >= 1:
                    pe.wait_ge(s["kptev"], i)
                for hl in range(HPC):
                    pe.transpose(pb[6][:, NJL * hl:NJL * (hl + 1)],
                                 kpTsb[0:NJL, 128 * hl:128 * (hl + 1)], iden[0:NJL, 0:NJL])
                pe.drain().then_inc(s["kpt"], 1)
                pe.wait_ge(s["kpn"], i + 1)
                if i >= 1:
                    pe.wait_ge(s["centev"], i)
                for hl in range(HPC):
                    pe.matmul(pb[7][0:2, NJL * hl:NJL * (hl + 1)], blksel[:],
                              kpnsb[:, NJL * hl:NJL * (hl + 1)], start=True, stop=True)
                pe.drain().then_inc(s["cent"], 1)

        def p1_dve(v):
            def head(i):
                j = i % 2
                v.wait_ge(s["x0" if j == 0 else "x1"], 16 * (i // 2 + 1))
                for c in range(4):
                    v.bn_stats(out=bnst[:, 6 * c:6 * (c + 1)], in_=xbuf[j][:, 512 * c:512 * (c + 1)])
                v.drain()
                v.bn_aggr(out=mv[:], in_=bnst[:]).then_inc(s["ln1"], 1)
                v.wait_ge(s["ln2"], i + 1)
                v.reciprocal(out=rstd[:], in_=sd1[:])
                v.drain()
                v.tensor_scalar(out=xbuf[j][:], in0=xbuf[j][:], scalar1=mv[:, 0:1],
                                scalar2=rstd[:], op0=ALU.subtract,
                                op1=ALU.mult).then_inc(s["lnd"], 1)
                v.drain()

            def rest(i):
                j = i % 2
                g0 = NG * i
                if i >= 1:
                    v.wait_ge(s["qkv"], 3 * i)
                for g in range(4):
                    v.wait_ge(s["tp"], g0 + g + 1)
                    v.tensor_copy(ln1fm[j][:, 512 * g:512 * (g + 1)], pb[(g0 + g) % 2][:]).then_inc(s["tpe"], 1)
                v.drain()
                for oc in range(3):
                    v.wait_ge(s["qkv"], 3 * i + oc + 1)
                    if oc < 2:
                        v.tensor_copy(qkvtm[j][:, 512 * oc:512 * (oc + 1)], pb[2 + oc][:]).then_inc(s["qe"], 1)
                    else:
                        if i >= 2:
                            v.wait_ge(s["odone"], i - 1)
                        v.tensor_copy(vsb[j][:], pb[4][:]).then_inc(s["qe"], 1)
                v.drain()
                v.tensor_tensor(out=sqb[:], in0=qkvtm[j][:], in1=qkvtm[j][:], op=ALU.mult)
                v.drain()
                v.tensor_reduce(out=ssq[:].unsqueeze(2),
                                in_=sqb[:].rearrange("p (h d) -> p h d", d=128),
                                axis=AX.X, op=ALU.add).then_inc(s["rm1"], 1)
                v.wait_ge(s["rm2"], i + 1)
                v.reciprocal(out=rr8[:], in_=srt8[:])
                v.drain()
                v.tensor_tensor(out=rrq[:], in0=rr8[:, 0:4], in1=gain_tm[:], op=ALU.mult)
                v.drain()
                v.tensor_tensor(out=qn[:].rearrange("p (h d) -> p h d", d=128),
                                in0=qkvtm[j][:, 0:512].rearrange("p (h d) -> p h d", d=128),
                                in1=rrq[:].unsqueeze(2).broadcast_to([128, 4, 128]), op=ALU.mult)
                v.tensor_tensor(out=kn[:].rearrange("p (h d) -> p h d", d=128),
                                in0=qkvtm[j][:, 512:1024].rearrange("p (h d) -> p h d", d=128),
                                in1=rr8[:, 4:8].unsqueeze(2).broadcast_to([128, 4, 128]), op=ALU.mult)
                v.drain()
                v.wait_ge(s["cs0" if j == 0 else "cs1"], 32 * (i // 2 + 1))
                if i >= 2:
                    v.wait_ge(s["odone"], i - 1)
                for src, dst in ((qn, qro[j]), (kn, kro[j])):
                    sv = src[:].rearrange("p (h d) -> p h d", d=128)
                    dv = dst[:].rearrange("p (h d) -> p h d", d=128)
                    r1 = rop1[:].rearrange("p (h d) -> p h d", d=64)
                    r2 = rop2[:].rearrange("p (h d) -> p h d", d=64)
                    cosb = csb[j][:, 0:64].unsqueeze(1).broadcast_to([128, 4, 64])
                    sinb = csb[j][:, 64:128].unsqueeze(1).broadcast_to([128, 4, 64])
                    v.tensor_tensor(out=r1, in0=sv[:, :, 0:64], in1=cosb, op=ALU.mult)
                    v.tensor_tensor(out=r2, in0=sv[:, :, 64:128], in1=sinb, op=ALU.mult)
                    v.drain()
                    v.tensor_tensor(out=dv[:, :, 0:64], in0=r1, in1=r2, op=ALU.add)
                    v.drain()
                    v.tensor_tensor(out=r1, in0=sv[:, :, 0:64], in1=sinb, op=ALU.mult)
                    v.tensor_tensor(out=r2, in0=sv[:, :, 64:128], in1=cosb, op=ALU.mult)
                    v.drain()
                    v.tensor_tensor(out=dv[:, :, 64:128], in0=r2, in1=r1, op=ALU.subtract)
                    v.drain()
                v.drain().then_inc(s["rope"], 1)
                v.drain().then_inc(s["csf"], 1)
                v.wait_ge(s["tp"], g0 + 5)
                if i >= 2:
                    v.wait_ge(s["odone"], i - 1)
                v.tensor_copy(qfmsb[j][:], pb[(g0 + 4) % 2][:]).then_inc(s["tpe"], 1)
                v.wait_ge(s["tp"], g0 + 6)
                v.tensor_copy(kfmsb[j][:], pb[(g0 + 5) % 2][:]).then_inc(s["tpe"], 1)
                v.drain()
                v.drain().then_inc(s["kfme"], 1)
                v.tensor_reduce(out=red4[:].unsqueeze(2),
                                in_=qfmsb[j][:].rearrange("p (h t) -> p h t", t=128),
                                axis=AX.X, op=ALU.add)
                v.drain()
                v.tensor_tensor(out=qsum[:], in0=qsum[:], in1=red4[:], op=ALU.add)
                v.drain()
                v.wait_ge(s["kp"], i + 1)
                v.tensor_copy(kpTsb[:], pb[5][0:NJL, :]).then_inc(s["kpe"], 1)
                v.drain()
                v.wait_ge(s["kpt"], i + 1)
                if i >= 1:
                    v.wait_ge(s["odone"], i)
                v.tensor_copy(kpnsb[:], pb[6][:, 0:128]).then_inc(s["kptev"], 1)
                v.drain()
                v.tensor_tensor(out=sqb[:, 0:128], in0=kpnsb[:], in1=kpnsb[:], op=ALU.mult)
                v.drain()
                v.tensor_reduce(out=ss4[:].unsqueeze(2),
                                in_=sqb[:, 0:128].rearrange("p (h j) -> p h j", j=NJL),
                                axis=AX.X, op=ALU.add).then_inc(s["kq1"], 1)
                v.wait_ge(s["kq2"], i + 1)
                v.tensor_scalar(out=rn4[:], in0=srt4[:], scalar1=1e-12, scalar2=0.0,
                                op0=ALU.max, op1=ALU.add)
                v.drain()
                v.reciprocal(out=rn4[:], in_=rn4[:])
                v.drain()
                v.tensor_tensor(out=kpnsb[:].rearrange("p (h j) -> p h j", j=NJL),
                                in0=kpnsb[:].rearrange("p (h j) -> p h j", j=NJL),
                                in1=rn4[:].unsqueeze(2).broadcast_to([128, 4, NJL]),
                                op=ALU.mult).then_inc(s["kpn"], 1)
                v.drain()
                v.wait_ge(s["cent"], i + 1)
                v.tensor_copy(cent_acc[0:2, 128 * i:128 * (i + 1)], pb[7][0:2, 0:128]).then_inc(s["centev"], 1)
                v.drain()
            head(0)
            for i in range(NT):
                if i + 1 < NT:
                    head(i + 1)
                rest(i)

        def p1_act(a):
            a.wait_ge(s["c0"], 240)

            def ahead(i):
                a.wait_ge(s["ln1"], i + 1)
                a.activation(out=sd1[:], in_=mv[:, 1:2], func=AF.Sqrt,
                             bias=1e-5, scale=1.0).then_inc(s["ln2"], 1)

            ahead(0)
            for i in range(NT):
                if i + 1 < NT:
                    ahead(i + 1)
                a.wait_ge(s["rm1"], i + 1)
                a.activation(out=srt8[:], in_=ssq[:], func=AF.Sqrt,
                             bias=1e-6, scale=1.0 / 128.0).then_inc(s["rm2"], 1)
                a.wait_ge(s["kq1"], i + 1)
                a.activation(out=srt4[:], in_=ss4[:], func=AF.Sqrt,
                             bias=0.0, scale=1.0).then_inc(s["kq2"], 1)

        def p1_gp(gp):
            pass

        def p1_dve_full(v):
            v.memset(qsum[:], 0.0)
            v.drain()
            p1_dve(v)

        phase({"sync": p1_sp, "tensor": p1_pe, "vector": p1_dve_full,
               "scalar": p1_act, "gpsimd": p1_gp})
        ph1.close()

        ph15 = ExitStack()

        def sb15(name, shape, dtype=F32):
            return ph15.enter_context(nc.sbuf_tensor(name, shape, dtype))

        kpnh = sb15("kpnh", [128, NT * NJL])
        kfmt4 = sb15("kfmt4", [32, 512])
        centm = sb15("centm", [128, NJL])
        css1 = sb15("css1", [128, 1])
        csr1 = sb15("csr1", [128, 1])
        cfm = sb15("cfm", [32, 128])
        qm = sb15("qm", [1, NJL])
        qss = sb15("qss", [1, 1])
        qsr = sb15("qsr", [1, 1])
        qpn_row = sb15("qpn_row", [1, NJL])
        qpn_col = sb15("qpn_col", [32, 1])
        dots2 = sb15("dots2", [2, NT * 128])
        odrow = sb15("odrow", [1, NT * 128])
        dmin2 = sb15("dmin2", [1, 128])
        radr = sb15("radr", [1, 128])
        srow1 = sb15("srow1", [1, 128])
        scT = sb15("scT", [128, 1])
        sbc = sb15("sbc", [128, 128])
        junk15 = sb15("junk15", [128, 128])
        rank = sb15("rank", [128, 1])
        sel01 = sb15("sel01", [128, 1])
        frc = sb15("frc", [128, 1])
        mcol = sb15("mcol", [128, 1])
        possb = sb15("possb", [128, 1])
        selrow = sb15("selrow", [1, 128])
        selrep = sb15("selrep", [128, 33])
        O1F = 16 * (4 * NT + 1)

        seq = Seq()
        g15c = {"n": 0}

        def gdma(gp, out, in_, **kw):
            g15c["n"] += 16
            gp.dma_start(out=out, in_=in_, **kw).then_inc(s["g15"], 16)
            gp.wait_ge(s["g15"], g15c["n"])

        def frc_init(v):
            v.memset(frc[:], 0.0)
            v.drain()
            v.tensor_scalar(out=junk15[:, 0:1], in0=parm[:], scalar1=0.0, scalar2=0.0,
                            op0=ALU.mult, op1=ALU.add)
        seq.step("vector", frc_init)
        seq.step("gpsimd", lambda gp: gdma(gp, frc[126:128, :], ones_d[0:2, :]))

        for hl in range(HPC):
            def ld(sp, hl=hl):
                sp.dma_start(out=kpnh[:].rearrange("p (i j) -> p i j", j=NJL),
                             in_=kpn_d[:, hl, :, :]).then_inc(s["o1"], 16)
                sp.dma_start(out=centm[:],
                             in_=centd.rearrange("par (i h j) -> i par h j", h=HPC, j=NJL)[:, :, hl, :]).then_inc(s["o1"], 16)
                sp.wait_ge(s["o1"], O1F + 32 * (hl + 1))
            seq.step("sync", ld)

            def c1(v, hl=hl):
                v.tensor_scalar(out=centm[:], in0=centm[:],
                                scalar1=1.0 / 64.0, scalar2=0.0, op0=ALU.mult, op1=ALU.add)
                v.drain()
                v.tensor_tensor(out=junk15[:, 0:NJL], in0=centm[:], in1=centm[:], op=ALU.mult)
                v.drain()
                v.tensor_reduce(out=css1[:], in_=junk15[:, 0:NJL], axis=AX.X, op=ALU.add)
            seq.step("vector", c1)
            seq.step("scalar", lambda a: a.activation(out=csr1[:], in_=css1[:],
                                                      func=AF.Sqrt, bias=0.0, scale=1.0))

            def c2(v):
                v.tensor_scalar(out=csr1[:], in0=csr1[:], scalar1=1e-12, scalar2=0.0,
                                op0=ALU.max, op1=ALU.add)
                v.drain()
                v.reciprocal(out=csr1[:], in_=csr1[:])
                v.drain()
                v.tensor_scalar(out=centm[:], in0=centm[:], scalar1=csr1[:], scalar2=0.0,
                                op0=ALU.mult, op1=ALU.add)
            seq.step("vector", c2)
            seq.step("tensor", lambda pe: pe.transpose(pb[0][0:NJL, 0:128], centm[:], iden[:]))
            seq.step("vector", lambda v: v.tensor_copy(cfm[:], pb[0][0:NJL, 0:128]))
            seq.step("tensor", lambda pe, hl=hl: pe.matmul(pb[2][0:1, 0:NJL],
                                                           qsum[:, hl:hl + 1], proj[:],
                                                           start=True, stop=True))

            def q1(v):
                v.tensor_scalar(out=qm[:], in0=pb[2][0:1, 0:NJL], scalar1=1.0 / 8192.0,
                                scalar2=0.0, op0=ALU.mult, op1=ALU.add)
                v.drain()
                v.tensor_tensor(out=junk15[0:1, 0:NJL], in0=qm[:], in1=qm[:], op=ALU.mult)
                v.drain()
                v.tensor_reduce(out=qss[:], in_=junk15[0:1, 0:NJL], axis=AX.X, op=ALU.add)
            seq.step("vector", q1)
            seq.step("scalar", lambda a: a.activation(out=qsr[:], in_=qss[:],
                                                      func=AF.Sqrt, bias=0.0, scale=1.0))

            def q2(v):
                v.tensor_scalar(out=qsr[:], in0=qsr[:], scalar1=1e-12, scalar2=0.0,
                                op0=ALU.max, op1=ALU.add)
                v.drain()
                v.reciprocal(out=qsr[:], in_=qsr[:])
                v.drain()
                v.tensor_scalar(out=qpn_row[:], in0=qm[:], scalar1=qsr[:], scalar2=0.0,
                                op0=ALU.mult, op1=ALU.add)
            seq.step("vector", q2)
            seq.step("tensor", lambda pe: pe.transpose(pb[2][0:NJL, 128:129], qpn_row[:], iden[0:1, 0:1]))
            seq.step("vector", lambda v: v.tensor_copy(qpn_col[:], pb[2][0:NJL, 128:129]))
            for g4 in range(NT // 4):
                def t1(pe, g4=g4):
                    for t in range(4):
                        i = 4 * g4 + t
                        pe.transpose(pb[0][0:NJL, 128 * t:128 * (t + 1)],
                                     kpnh[:, NJL * i:NJL * (i + 1)], iden[:])
                seq.step("tensor", t1)
                seq.step("vector", lambda v: v.tensor_copy(kfmt4[:], pb[0][0:NJL, :]))

                def t2(pe, g4=g4):
                    for t in range(4):
                        i = 4 * g4 + t
                        pe.matmul(pb[1][0:2, 128 * t:128 * (t + 1)],
                                  cfm[:, 2 * i:2 * i + 2],
                                  kfmt4[:, 128 * t:128 * (t + 1)], start=True, stop=True)
                seq.step("tensor", t2)

                def t3(v, g4=g4):
                    v.tensor_copy(dots2[0:2, 512 * g4:512 * (g4 + 1)], pb[1][0:2, 0:512])
                seq.step("vector", t3)

            seq.step("gpsimd", lambda gp: gdma(gp, dotsd[:], dots2[:]))
            seq.step("gpsimd", lambda gp: gdma(gp, odrow[:], dotsd[1:2, :]))

            def rad(v):
                v.tensor_reduce(out=dmin2[:].rearrange("p (a b) -> p a b", b=1)[:, 0::2, :],
                                in_=dots2[0:1, :].rearrange("p (i c) -> p i c", c=128)[:, :, 0:64],
                                axis=AX.X, op=ALU.min)
                v.tensor_reduce(out=dmin2[:].rearrange("p (a b) -> p a b", b=1)[:, 1::2, :],
                                in_=odrow[0:1, :].rearrange("p (i c) -> p i c", c=128)[:, :, 64:128],
                                axis=AX.X, op=ALU.min)
                v.drain()
                v.tensor_scalar(out=radr[:], in0=dmin2[:], scalar1=1.0, scalar2=-1.0,
                                op0=ALU.subtract, op1=ALU.mult)
                v.drain()
                v.tensor_scalar(out=radr[:], in0=radr[:], scalar1=0.0, scalar2=1.0,
                                op0=ALU.max, op1=ALU.min)
            seq.step("vector", rad)
            seq.step("tensor", lambda pe: pe.matmul(pb[2][0:1, 256:384], qpn_col[:], cfm[:],
                                                    start=True, stop=True))
            seq.step("vector", lambda v: v.tensor_tensor(out=srow1[:], in0=pb[2][0:1, 256:384],
                                                         in1=radr[:], op=ALU.add))
            seq.step("tensor", lambda pe: pe.transpose(pb[2][0:128, 384:385], srow1[:], iden[0:1, 0:1]))
            seq.step("vector", lambda v: v.tensor_copy(scT[:], pb[2][0:128, 384:385]))
            seq.step("tensor", lambda pe: pe.matmul(pb[3][:, 0:128], onesrow[:], srow1[:],
                                                    start=True, stop=True))
            seq.step("vector", lambda v: v.tensor_copy(sbc[:], pb[3][:, 0:128]))

            def rk(v):
                v.memset(sbc[:, 126:128], NEG)
                v.drain()
                v.tensor_scalar(out=junk15[:], in0=sbc[:], scalar1=scT[:], scalar2=0.0,
                                op0=ALU.is_gt, op1=ALU.add, accum_out=rank[:])
                v.drain()
                v.tensor_scalar(out=sel01[:], in0=rank[:], scalar1=64.0, scalar2=0.0,
                                op0=ALU.is_lt, op1=ALU.add)
                v.drain()
                v.tensor_tensor(out=sel01[:], in0=sel01[:], in1=frc[:], op=ALU.max)
            seq.step("vector", rk)

            seq.step("tensor", lambda pe: pe.matmul(pb[3][:, 0:1], ltm[:], sel01[:],
                                                    start=True, stop=True))
            seq.step("vector", lambda v: v.tensor_copy(possb[:], pb[3][:, 0:1]))

            def mkonehot(v):
                v.tensor_scalar(out=junk15[:], in0=iotab[:], scalar1=possb[:], scalar2=sel01[:],
                                op0=ALU.is_equal, op1=ALU.mult)
                v.drain()
                v.tensor_scalar(out=junk15[:], in0=junk15[:], scalar1=blkid[:], scalar2=0.0,
                                op0=ALU.mult, op1=ALU.add)
            seq.step("vector", mkonehot)
            seq.step("tensor", lambda pe: pe.matmul(pb[3][0:1, 128:256], onescol[:], junk15[:],
                                                    start=True, stop=True))
            seq.step("vector", lambda v: v.tensor_copy(selrow[:], pb[3][0:1, 128:256]))

            def selrep_mm(pe):
                me = selrow[:].rearrange("p (k t) -> p t k", t=2)
                pe.matmul(pb[3][0:64, 256:289], onesrow[0:1, 0:64], me[:, 0, 0:33],
                          start=True, stop=True)
                pe.matmul(pb[3][0:64, 289:322], onesrow[0:1, 0:64], me[:, 1, 0:33],
                          start=True, stop=True)
            seq.step("tensor", selrep_mm)

            def mkidx(v, hl=hl):
                v.tensor_copy(selrep[0:64, :], pb[3][0:64, 256:289])
                v.tensor_copy(selrep[64:128, :], pb[3][0:64, 289:322])
                v.drain()
                v.tensor_scalar(out=selrep[:], in0=selrep[:], scalar1=64.0, scalar2=poff[:],
                                op0=ALU.mult, op1=ALU.add)
                v.drain()
                v.tensor_scalar(out=selrep[:], in0=selrep[:], scalar1=float(8192 * hl),
                                scalar2=0.0, op0=ALU.add, op1=ALU.add)
                v.drain()
                v.tensor_copy(idx_all[:, 33 * hl:33 * (hl + 1)], selrep[:])
                v.drain()
            seq.step("vector", mkidx)


        phase(seq.emitters(0))
        ph15.close()

        ph2 = ExitStack()

        def sb2(name, shape, dtype=F32):
            return ph2.enter_context(nc.sbuf_tensor(name, shape, dtype))

        NKC = 33
        KS = NKC * 128
        ksg = [sb2(f"ksg{j}", [128, KS]) for j in range(2)]
        kfs = [sb2(f"kfs{j}", [128, KS]) for j in range(2)]
        vsg = [sb2(f"vsg{j}", [128, KS]) for j in range(2)]
        qb2 = [sb2(f"qb2_{j}", [128, 512]) for j in range(2)]
        expT = [sb2(f"expT{j}", [128, 512]) for j in range(2)]
        expsum = sb2("expsum", [128, 512])
        rrow = sb2("rrow", [1, 512])
        rb_sb = sb2("rb_sb", [128, 512])
        ysb = [sb2(f"ysb{j}", [128, 512]) for j in range(2)]
        NQC = 16

        def p2_gp(gp):
            def gathers(hl):
                jh = hl % 2
                kvs = s["kv2a" if jh == 0 else "kv2b"]
                if hl >= 2:
                    gp.wait_ge(s["hfree2"], hl - 1)
                ktm_flat = ktm_d.rearrange("h s d -> (h s) d")
                vtm_flat = vtm_d.rearrange("h s d -> (h s) d")
                for c in range(NKC):
                    gp.indirect_dma_start(
                        out=ksg[jh][:, 128 * c:128 * (c + 1)], out_offset=None,
                        in_=ktm_flat,
                        in_offset=bass.IndirectOffsetOnAxis(
                            ap=idx_all[:, 33 * hl + c:33 * hl + c + 1], axis=0)).then_inc(kvs, 16)
                for c in range(NKC):
                    gp.indirect_dma_start(
                        out=vsg[jh][:, 128 * c:128 * (c + 1)], out_offset=None,
                        in_=vtm_flat,
                        in_offset=bass.IndirectOffsetOnAxis(
                            ap=idx_all[:, 33 * hl + c:33 * hl + c + 1], axis=0)).then_inc(kvs, 16)

            gathers(0)
            gathers(1)
            for hl in range(HPC):
                for qc in range(NQC):
                    t2 = NQC * hl + qc
                    gp.wait_ge(s["ye"], t2 + 1)
                    gp.dma_start(out=yfm_d[128 * hl:128 * (hl + 1), 512 * qc:512 * (qc + 1)],
                                 in_=ysb[qc % 2][:]).then_inc(s["yoa" if qc % 2 == 0 else "yob"], 16)
                if hl + 2 < HPC:
                    gathers(hl + 2)
            gp.wait_ge(s["yoa"], 16 * NQC * HPC // 2)
            gp.wait_ge(s["yob"], 16 * NQC * HPC // 2)

        def p2_sp(sp):
            for hl in range(HPC):
                for qc in range(NQC):
                    t2 = NQC * hl + qc
                    if t2 >= 2:
                        sp.wait_ge(s["qf2"], t2 - 1)
                    sp.dma_start(out=qb2[qc % 2][:],
                                 in_=qfm_d[128 * hl:128 * (hl + 1), 512 * qc:512 * (qc + 1)]).then_inc(s["q2a" if qc % 2 == 0 else "q2b"], 16)
            sp.wait_ge(s["yoa"], 16 * NQC * HPC // 2)
            sp.wait_ge(s["yob"], 16 * NQC * HPC // 2)

        def p2_pe(pe):
            pend = []

            def flush_pend():
                while pend:
                    fn = pend.pop(0)
                    fn()
            for hl in range(HPC):
                jh = hl % 2
                pe.wait_ge(s["kv2a" if jh == 0 else "kv2b"], 16 * 2 * NKC * (hl // 2 + 1))
                flush_pend()
                if hl >= 1:
                    pe.wait_ge(s["exq"], NKC * NQC * hl)
                for c in range(NKC):
                    kk2 = NKC * hl + c
                    if kk2 >= 2:
                        pe.wait_ge(s["kt2e"], kk2 - 1)
                    pe.transpose(pb[c % 2][:, 0:128], ksg[jh][:, 128 * c:128 * (c + 1)],
                                 iden[:]).then_inc(s["kt2"], 1)
                pe.wait_ge(s["kt2e"], NKC * (hl + 1))
                for qc in range(NQC):
                    t2 = NQC * hl + qc
                    pe.wait_ge(s["q2a" if qc % 2 == 0 else "q2b"], 16 * ((t2 // 2) + 1))
                    if t2 >= 2:
                        pe.wait_ge(s["ye"], t2 - 1)
                    for kc in range(NKC):
                        kk = NKC * t2 + kc
                        if kk >= 2:
                            pe.wait_ge(s["exq"], kk - 1)
                        pe.matmul(pb[kc % 2][:], kfs[jh][:, 128 * kc:128 * (kc + 1)],
                                  qb2[qc % 2][:], start=True, stop=True).then_inc(s["lt"], 1)
                        if kc == 1:
                            flush_pend()
                        if kc >= 1:
                            kprev = kc - 1
                            pe.wait_ge(s["exq"], NKC * t2 + kprev + 1)
                            pe.matmul(pb[2 + (qc % 2)][:],
                                      vsg[jh][:, 128 * kprev:128 * (kprev + 1)],
                                      expT[kprev % 2][:],
                                      start=(kprev == 0), stop=False).then_inc(s["av"], 1)
                    pe.wait_ge(s["exq"], NKC * (t2 + 1))
                    pe.matmul(pb[2 + (qc % 2)][:], vsg[jh][:, 128 * (NKC - 1):128 * NKC],
                              expT[(NKC - 1) % 2][:], start=False, stop=True).then_inc(s["av"], 1)
                    pe.wait_ge(s["esum"], NKC * (t2 + 1))
                    if t2 >= 1:
                        pe.wait_ge(s["rcp2"], t2)
                    pe.matmul(pb[4][0:1, :], onescol[:], expsum[:],
                              start=True, stop=True).then_inc(s["sum2"], 1)

                    def mk_bcast(t2=t2):
                        def f():
                            pe.wait_ge(s["rcp2"], t2 + 1)
                            if t2 >= 1:
                                pe.wait_ge(s["rbe2"], t2)
                            pe.matmul(pb[5][:], onesrow[:], rrow[:],
                                      start=True, stop=True).then_inc(s["bc2"], 1)
                        return f
                    pend.append(mk_bcast())
                    pe.drain().then_inc(s["qf2"], 1)
                pe.drain().then_inc(s["hfree2"], 1)
            flush_pend()

        def p2_act(a):
            for hl in range(HPC):
                for qc in range(NQC):
                    t2 = NQC * hl + qc
                    for kc in range(NKC):
                        kk = NKC * t2 + kc
                        a.wait_ge(s["lt"], kk + 1)
                        if kk >= 2:
                            a.wait_ge(s["av"], kk - 1)
                            a.wait_ge(s["esum"], kk - 1)
                        a.activation(out=expT[kc % 2][:], in_=pb[kc % 2][:], func=AF.Exp,
                                     bias=0.0, scale=RS128).then_inc(s["exq"], 1)
                    a.wait_ge(s["bc2"], t2 + 1)
                    if t2 >= 1:
                        a.wait_ge(s["ye"], t2)
                    a.copy(out=rb_sb[:], in_=pb[5][:]).then_inc(s["rbe2"], 1)

        def p2_dve(v):
            for hl in range(HPC):
                jh = hl % 2
                for c in range(NKC):
                    kk2 = NKC * hl + c
                    v.wait_ge(s["kt2"], kk2 + 1)
                    v.tensor_copy(kfs[jh][:, 128 * c:128 * (c + 1)],
                                  pb[c % 2][:, 0:128]).then_inc(s["kt2e"], 1)
                v.drain()
                for qc in range(NQC):
                    t2 = NQC * hl + qc
                    for kc in range(NKC):
                        kk = NKC * t2 + kc
                        v.wait_ge(s["exq"], kk + 1)
                        if kc == 0:
                            if t2 >= 1:
                                v.wait_ge(s["sum2"], t2)
                            v.tensor_copy(expsum[:], expT[0][:]).then_inc(s["esum"], 1)
                        else:
                            v.tensor_tensor(out=expsum[:], in0=expsum[:], in1=expT[kc % 2][:],
                                            op=ALU.add).then_inc(s["esum"], 1)
                        v.drain()
                    v.wait_ge(s["sum2"], t2 + 1)
                    if t2 >= 1:
                        v.wait_ge(s["bc2"], t2)
                    v.reciprocal(out=rrow[:], in_=pb[4][0:1, :]).then_inc(s["rcp2"], 1)
                    v.drain()
                    v.wait_ge(s["av"], NKC * (t2 + 1))
                    v.wait_ge(s["rbe2"], t2 + 1)
                    if t2 >= 2:
                        v.wait_ge(s["yoa" if qc % 2 == 0 else "yob"], 16 * ((t2 - 2) // 2 + 1))
                    v.tensor_tensor(out=ysb[qc % 2][:], in0=pb[2 + (qc % 2)][:],
                                    in1=rb_sb[:], op=ALU.mult).then_inc(s["ye"], 1)
                    v.drain()

        phase({"sync": p2_sp, "tensor": p2_pe, "scalar": p2_act, "vector": p2_dve,
               "gpsimd": p2_gp})
        ph2.close()

        ph3 = ExitStack()

        def sb3(name, shape, dtype=F32):
            return ph3.enter_context(nc.sbuf_tensor(name, shape, dtype))

        wo_sb = sb3("wo_sb", [128, HPC * DM])
        yb3 = [sb3(f"yb3_{j}", [128, 512]) for j in range(2)]
        rsb = [sb3(f"rsb{j}", [128, DM]) for j in range(2)]
        dpk = [sb3(f"dpk{j}", [128, DM], I32) for j in range(2)]
        dscs = [sb3(f"dscs{j}", [128, DM], F16) for j in range(2)]
        e_k = sb3("e_k", [128, 1024], U16)
        e_sb = sb3("e_sb", [128, 1024], U16)
        e_u = sb3("e_u", [128, 1024], U16)
        e_a = sb3("e_a", [128, 1024], U16)
        e_i = sb3("e_i", [128, 1024], U16)
        wout = [sb3(f"wout{j}", [128, DM]) for j in range(2)]

        def fp4_q(v, src_i32_ap, jj, q, outs):
            p16 = src_i32_ap.bitcast(U16)[:, 1024 * q:1024 * (q + 1)]
            v.tensor_scalar(out=e_k[:], in0=p16, scalar1=4 * jj, scalar2=7,
                            op0=ALU.logical_shift_right, op1=ALU.bitwise_and)
            v.tensor_scalar(out=e_sb[:], in0=p16, scalar1=12 - 4 * jj, scalar2=32768,
                            op0=ALU.logical_shift_left, op1=ALU.bitwise_and)
            v.drain()
            v.tensor_scalar(out=e_u[:], in0=e_k[:], scalar1=2, scalar2=0,
                            op0=ALU.min, op1=ALU.add)
            v.tensor_scalar(out=e_i[:], in0=e_k[:], scalar1=0, scalar2=0,
                            op0=ALU.is_gt, op1=ALU.add)
            v.drain()
            v.tensor_tensor(out=e_a[:], in0=e_k[:], in1=e_u[:], op=ALU.add)
            v.drain()
            v.tensor_scalar(out=e_a[:], in0=e_a[:], scalar1=26, scalar2=512,
                            op0=ALU.add, op1=ALU.mult)
            v.drain()
            v.tensor_tensor(out=e_a[:], in0=e_a[:], in1=e_i[:], op=ALU.mult)
            v.drain()
            v.tensor_tensor(out=e_a[:], in0=e_a[:], in1=e_sb[:], op=ALU.bitwise_or)
            v.drain()
            for hh in range(2):
                outs(hh, e_a[:].rearrange("p (o t) -> p o t", t=2)[:, :, hh].bitcast(F16))
            v.drain()

        WUNITS = [(P, jj, q) for P in range(6) for jj in range(4) for q in range(4)]

        TORDER = ([16 * g + r for g in range(4) for r in range(8)]
                  + [16 * g + 8 + r for g in range(4) for r in range(8)])

        def rs_sem(ii, j):
            if ii < 32:
                return s["rsAa" if j == 0 else "rsAb"]
            return s["rsBa" if j == 0 else "rsBb"]

        def p3_sp(sp):
            sp.dma_start(out=wo_sb[:].rearrange("p (h o) -> p h o", o=DM),
                         in_=woT_d.rearrange("(h p) o -> p h o", p=128)).then_inc(s["wo3"], 16)
            for ii in range(NT):
                i = TORDER[ii]
                j = ii % 2
                if ii >= 2:
                    sp.wait_ge(s["w3"], 4 * (ii - 1))
                sp.dma_start(out=yb3[j][:].rearrange("p (h t) -> p h t", t=128),
                             in_=yfm_d.rearrange("(h p) t -> p h t", p=128)[:, :, 128 * i:128 * (i + 1)]).then_inc(s["y3a" if j == 0 else "y3b"], 16)
                sp.wait_ge(s["e3"], 4 * (ii + 1))
                sp.dma_start(out=rs_in_h.ap()[128 * ii:128 * (ii + 1), :], in_=rsb[j][:]).then_inc(rs_sem(ii, j), 16)
            sp.wait_ge(s["rsBa"], 16 * 16)
            sp.wait_ge(s["rsBb"], 16 * 16)

        def p3_pe(pe):
            pe.wait_ge(s["wo3"], 16)
            for ii in range(NT):
                j = ii % 2
                pe.wait_ge(s["y3a" if j == 0 else "y3b"], 16 * (ii // 2 + 1))
                if ii >= 2:
                    pe.wait_ge(s["e3"], 4 * (ii - 1))
                for dmc in range(4):
                    for hl in range(HPC):
                        pe.matmul(pb[4 * j + dmc][:],
                                  yb3[j][:, 128 * hl:128 * (hl + 1)],
                                  wo_sb[:, DM * hl + 512 * dmc:DM * hl + 512 * (dmc + 1)],
                                  start=(hl == 0), stop=(hl == 3))
                    pe.drain().then_inc(s["w3"], 1)

        def p3_dve(v):
            un = {"u": 0}

            def emit_unit():
                if un["u"] >= len(WUNITS):
                    return
                P, jj, q = WUNITS[un["u"]]
                jP = P % 2
                if q == 0 and jj == 0:
                    v.wait_ge(s["gdpe"], P + 1)
                if q == 0:
                    for hh in range(2):
                        prevu = 4 * P + jj - 1
                        if prevu >= 0:
                            v.wait_ge(s["gst"], 2 * prevu + hh + 1)

                def outs(hh, src_ap, P=P, jj=jj, q=q, jP=jP):
                    last = (q == 3)
                    ins = v.tensor_tensor(out=wout[hh][:, 512 * q:512 * (q + 1)],
                                          in0=src_ap, in1=dscs[jP][:, 512 * q:512 * (q + 1)],
                                          op=ALU.mult)
                    if last and hh == 1:
                        ins.then_inc(s["wde5"], 1)
                fp4_q(v, dpk[jP][:], jj, q, outs)
                un["u"] += 1

            for ii in range(NT):
                j = ii % 2
                for dmc in range(4):
                    v.wait_ge(s["w3"], 4 * ii + dmc + 1)
                    if ii >= 2:
                        if ii < 34:
                            v.wait_ge(rs_sem(ii - 2, j), 16 * (ii // 2))
                        else:
                            v.wait_ge(rs_sem(ii - 2, j), 16 * ((ii - 32) // 2))
                    v.tensor_copy(rsb[j][:, 512 * dmc:512 * (dmc + 1)],
                                  pb[4 * j + dmc][:]).then_inc(s["e3"], 1)
                v.drain()
                target = min(len(WUNITS), ((ii + 1) * len(WUNITS)) // 44)
                while un["u"] < target:
                    emit_unit()
            while un["u"] < len(WUNITS):
                emit_unit()

        G15F = g15c["n"]
        g3 = {"n": 0}

        def p3_gp(gp):
            def g3dma(out, in_):
                g3["n"] += 16
                gp.dma_start(out=out, in_=in_).then_inc(s["g15"], 16)
                gp.wait_ge(s["g15"], G15F + g3["n"])
            for P in range(6):
                jP = P % 2
                if P >= 2:
                    gp.wait_ge(s["wde5"], 4 * (P - 1))
                g3dma(dpk[jP][:], downT_d[128 * P:128 * (P + 1), :])
                g3dma(dscs[jP][:], dsc_d[128 * P:128 * (P + 1), :])
                gp.sem_inc(s["gdpe"], 1)
                for jj in range(4):
                    for hh in range(2):
                        sid = 8 * P + 4 * hh + jj
                        gp.wait_ge(s["wde5"], 4 * P + jj + 1)
                        g3dma(wde_d[sid], wout[hh][:])
                        gp.sem_inc(s["gst"], 1)
            gp.wait_ge(s["rsAa"], 16 * 16)
            gp.wait_ge(s["rsAb"], 16 * 16)
            gp.collective_compute(
                "ReduceScatter", ALU.add,
                replica_groups=[[0, 1, 2, 3], [4, 5, 6, 7]],
                ins=[rs_in_h.ap()[0:4096, :].opt()],
                outs=[rs_out_h.ap()[0:1024, :].opt()],
            ).then_inc(s["ccA"], 1)
            gp.wait_ge(s["rsBa"], 16 * 16)
            gp.wait_ge(s["rsBb"], 16 * 16)
            gp.collective_compute(
                "ReduceScatter", ALU.add,
                replica_groups=[[0, 1, 2, 3], [4, 5, 6, 7]],
                ins=[rs_in_h.ap()[4096:8192, :].opt()],
                outs=[rs_out_h.ap()[1024:2048, :].opt()],
            ).then_inc(s["ccB"], 1)

        phase({"sync": p3_sp, "tensor": p3_pe, "vector": p3_dve, "gpsimd": p3_gp})
        ph3.close()

        ph6 = ExitStack()

        def sb6(name, shape, dtype=F32):
            return ph6.enter_context(nc.sbuf_tensor(name, shape, dtype))

        NOC = 22
        NTH = 8
        LINS = 2 * NOC
        ln2fmh = sb6("ln2fmh", [128, 16 * 128 * NTH])
        ln2t6 = [sb6(f"ln2t6_{j}", [128, DM]) for j in range(2)]
        wgu2 = [sb6(f"wgu2_{j}", [128, 16 * 512]) for j in range(2)]
        gpk = [sb6(f"gpk{j}", [128, 512], I32) for j in range(2)]
        upk = [sb6(f"upk{j}", [128, 512], I32) for j in range(2)]
        gscs = [sb6(f"gscs{j}", [128, 512], F16) for j in range(2)]
        uscs = [sb6(f"uscs{j}", [128, 512], F16) for j in range(2)]
        f_k = sb6("f_k", [128, 512], U16)
        f_sb = sb6("f_sb", [128, 512], U16)
        f_u = sb6("f_u", [128, 512], U16)
        f_a = sb6("f_a", [128, 512], U16)
        f_i = sb6("f_i", [128, 512], U16)
        sg_sb = [sb6(f"sg_sb{j}", [128, 256]) for j in range(2)]
        htl = [sb6(f"htl{j}", [128, 256]) for j in range(2)]
        x5b = [sb6(f"x5b{j}", [128, DM]) for j in range(2)]
        r5b = [sb6(f"r5b{j}", [128, DM]) for j in range(2)]
        bnst5 = sb6("bnst5", [128, 24])
        mv5 = sb6("mv5", [128, 2])
        sd5 = sb6("sd5", [128, 1])
        rstd5 = sb6("rstd5", [128, 1])

        def pass_of(g):
            P, c = g // 8, g % 8
            return 4 * P + (c % 4)

        def p6_sp(sp):
            def x_tile(i):
                j = i % 2
                if i == 0:
                    sp.wait_ge(s["ccA"], 1)
                if i == 8:
                    sp.wait_ge(s["ccB"], 1)
                if i >= 2:
                    sp.wait_ge(s["l5s"], i - 1)
                    sp.wait_ge(s["x2wa" if j == 0 else "x2wb"], 16 * ((i - 2) // 2 + 1))
                x5s = s["x5a" if j == 0 else "x5b"]
                for c in range(2):
                    sp.dma_start(out=x5b[j][:, 1024 * c:1024 * (c + 1)],
                                 in_=xq[128 * i:128 * (i + 1), 1024 * c:1024 * (c + 1)]).then_inc(x5s, 16)
                    sp.dma_start(out=r5b[j][:, 1024 * c:1024 * (c + 1)],
                                 in_=rs_out_h.ap()[128 * i:128 * (i + 1), 1024 * c:1024 * (c + 1)]).then_inc(x5s, 16)
                sp.wait_ge(s["l51"], 2 * i + 1)
                sp.dma_start(out=x2_d[128 * i:128 * (i + 1), :], in_=r5b[j][:]).then_inc(s["x2wa" if j == 0 else "x2wb"], 16)

            for half in range(2):
                for i in range(NTH * half, NTH * (half + 1)):
                    x_tile(i)
                for oc in range(NOC):
                    lin = NOC * half + oc
                    for tl in range(NTH):
                        gl = NTH * lin + tl
                        sp.wait_ge(s["h6"], gl + 1)
                        sp.dma_start(out=h_d[128 * (NTH * half + tl):128 * (NTH * half + tl) + 128,
                                             256 * oc:256 * (oc + 1)],
                                     in_=htl[tl % 2][:]).then_inc(s["hw6a" if tl % 2 == 0 else "hw6b"], 16)
            sp.wait_ge(s["hw6a"], 16 * 2 * NOC * NTH // 2)
            sp.wait_ge(s["hw6b"], 16 * 2 * NOC * NTH // 2)
            sp.wait_ge(s["x2wa"], 16 * NT5 // 2)
            sp.wait_ge(s["x2wb"], 16 * NT5 // 2)

        def p6_gp(gp):

            def pkload(lin):
                oc = lin % NOC
                j = lin % 2
                g6s = s["g6a" if j == 0 else "g6b"]
                if lin >= 2:
                    gp.wait_ge(s["e6"], 8 * (lin - 1))
                gp.dma_start(out=gpk[j][:].rearrange("p (P o) -> p P o", o=256),
                             in_=gateT_d.rearrange("(P p) o -> p P o", p=128)[:, :, 256 * oc:256 * (oc + 1)]).then_inc(g6s, 16)
                gp.dma_start(out=upk[j][:].rearrange("p (P o) -> p P o", o=256),
                             in_=upT_d.rearrange("(P p) o -> p P o", p=128)[:, :, 256 * oc:256 * (oc + 1)]).then_inc(g6s, 16)
                gp.dma_start(out=gscs[j][:].rearrange("p (P o) -> p P o", o=256),
                             in_=gsc_d.rearrange("(P p) o -> p P o", p=128)[:, :, 256 * oc:256 * (oc + 1)]).then_inc(g6s, 16)
                gp.dma_start(out=uscs[j][:].rearrange("p (P o) -> p P o", o=256),
                             in_=usc_d.rearrange("(P p) o -> p P o", p=128)[:, :, 256 * oc:256 * (oc + 1)]).then_inc(g6s, 16)

            def ln_store(lt):
                gp.wait_ge(s["l52"], lt + 1)
                for c in range(4):
                    gp.dma_start(out=ln2_d[128 * lt:128 * (lt + 1), 512 * c:512 * (c + 1)],
                                 in_=x5b[lt % 2][:, 512 * c:512 * (c + 1)]).then_inc(
                        s["lsa" if lt % 2 == 0 else "lsb"], 16)

            def ln_reload(lt):
                p = "lsa" if lt % 2 == 0 else "lsb"
                gp.wait_ge(s[p], 64 * (lt // 2 + 1))
                gp.sem_inc(s["l5s"], 1)
                if lt >= 2:
                    gp.wait_ge(s["t5"], 16 * (lt - 1))
                for c in range(4):
                    gp.dma_start(out=ln2t6[lt % 2][:, 512 * c:512 * (c + 1)],
                                 in_=ln2_d[128 * lt:128 * (lt + 1), 512 * c:512 * (c + 1)]).then_inc(
                        s["lra" if lt % 2 == 0 else "lrb"], 16)
                gp.wait_ge(s["lra" if lt % 2 == 0 else "lrb"], 64 * (lt // 2 + 1))
                gp.sem_inc(s["l6"], 1)

            def ln_chain_block(lo, hi):
                ln_store(lo)
                for lt in range(lo + 1, hi):
                    ln_store(lt)
                    ln_reload(lt - 1)
                ln_reload(hi - 1)

            pkload(0)
            pkload(1)
            ln_chain_block(0, NTH)
            for lin in range(2, NOC + 2):
                pkload(lin)
            ln_chain_block(NTH, 2 * NTH)
            for lin in range(NOC + 2, LINS):
                pkload(lin)

        def p6_pe(pe):
            for half in range(2):
                if half >= 1:
                    pe.wait_ge(s["sil6"], NTH * NOC * half)
                    pe.wait_ge(s["h6"], NTH * NOC * half)
                for tl in range(NTH):
                    lt = NTH * half + tl
                    pe.wait_ge(s["l6"], lt + 1)
                    for bg in range(4):
                        gti = 4 * lt + bg
                        if gti >= 2:
                            pe.wait_ge(s["t5e"], gti - 1)
                        for gg in range(4):
                            g = 4 * bg + gg
                            P, c = g // 8, g % 8
                            pe.transpose(pb[gti % 2][:, 128 * gg:128 * (gg + 1)],
                                         ln2t6[tl % 2][:].rearrange("p (a b) -> p b a", b=8)[:, c, 128 * P:128 * (P + 1)],
                                         iden[:])
                        pe.drain().then_inc(s["t5"], 1)
                    pe.drain().then_inc(s["t5"], 12)
                for oc in range(NOC):
                    lin = NOC * half + oc
                    j = lin % 2
                    if oc == 0:
                        pe.wait_ge(s["t5e"], 32 * (half + 1))
                    for tl in range(NTH):
                        gl = NTH * lin + tl
                        if gl >= 2:
                            pe.wait_ge(s["sil6"], gl - 1)
                        for g in range(16):
                            pe.wait_ge(s["e6"], 8 * lin + pass_of(g) + 1)
                            pe.matmul(pb[tl % 2][:, 0:256],
                                      ln2fmh[:, 1024 * g + 128 * tl:1024 * g + 128 * (tl + 1)],
                                      wgu2[j][:, 512 * g:512 * g + 256],
                                      start=(g == 0), stop=(g == 15))
                        pe.drain().then_inc(s["mmg6"], 1)
                        if gl >= 2:
                            pe.wait_ge(s["h6"], gl - 1)
                        for g in range(16):
                            pe.matmul(pb[2 + (tl % 2)][:, 0:256],
                                      ln2fmh[:, 1024 * g + 128 * tl:1024 * g + 128 * (tl + 1)],
                                      wgu2[j][:, 512 * g + 256:512 * (g + 1)],
                                      start=(g == 0), stop=(g == 15))
                        pe.drain().then_inc(s["mmu6"], 1)

        def p6_dve(v):
            def p5b_tile(i):
                j = i % 2
                v.wait_ge(s["x5a" if j == 0 else "x5b"], 64 * (i // 2 + 1))
                if i >= 2:
                    v.wait_ge(s["x2wa" if j == 0 else "x2wb"], 16 * ((i - 2) // 2 + 1))
                v.tensor_tensor(out=r5b[j][:], in0=x5b[j][:], in1=r5b[j][:],
                                op=ALU.add).then_inc(s["l51"], 1)
                v.drain()
                v.drain().then_inc(s["l51"], 1)
                for c in range(4):
                    v.bn_stats(out=bnst5[:, 6 * c:6 * (c + 1)], in_=r5b[j][:, 512 * c:512 * (c + 1)])
                v.drain()
                v.bn_aggr(out=mv5[:], in_=bnst5[:]).then_inc(s["ln1"], 1)
                v.wait_ge(s["ln2"], NT + i + 1)
                v.reciprocal(out=rstd5[:], in_=sd5[:])
                v.drain()
                v.tensor_scalar(out=x5b[j][:], in0=r5b[j][:], scalar1=mv5[:, 0:1],
                                scalar2=rstd5[:], op0=ALU.subtract, op1=ALU.mult).then_inc(s["l52"], 1)
                v.drain()

            def extract_pass(lin, P, jj):
                j = lin % 2
                p16g = gpk[j][:, 256 * P:256 * (P + 1)].bitcast(U16)
                p16u = upk[j][:, 256 * P:256 * (P + 1)].bitcast(U16)
                for p16, sc, half_o in ((p16g, gscs[j], 0), (p16u, uscs[j], 256)):
                    v.tensor_scalar(out=f_k[:, 0:512], in0=p16, scalar1=4 * jj, scalar2=7,
                                    op0=ALU.logical_shift_right, op1=ALU.bitwise_and)
                    v.tensor_scalar(out=f_sb[:, 0:512], in0=p16, scalar1=12 - 4 * jj,
                                    scalar2=32768, op0=ALU.logical_shift_left, op1=ALU.bitwise_and)
                    v.drain()
                    v.tensor_scalar(out=f_u[:, 0:512], in0=f_k[:, 0:512], scalar1=2, scalar2=0,
                                    op0=ALU.min, op1=ALU.add)
                    v.tensor_scalar(out=f_i[:, 0:512], in0=f_k[:, 0:512], scalar1=0, scalar2=0,
                                    op0=ALU.is_gt, op1=ALU.add)
                    v.drain()
                    v.tensor_tensor(out=f_a[:, 0:512], in0=f_k[:, 0:512], in1=f_u[:, 0:512], op=ALU.add)
                    v.drain()
                    v.tensor_scalar(out=f_a[:, 0:512], in0=f_a[:, 0:512], scalar1=26, scalar2=512,
                                    op0=ALU.add, op1=ALU.mult)
                    v.drain()
                    v.tensor_tensor(out=f_a[:, 0:512], in0=f_a[:, 0:512], in1=f_i[:, 0:512], op=ALU.mult)
                    v.drain()
                    v.tensor_tensor(out=f_a[:, 0:512], in0=f_a[:, 0:512], in1=f_sb[:, 0:512], op=ALU.bitwise_or)
                    v.drain()
                    for hh in range(2):
                        g = 8 * P + 4 * hh + jj
                        v.tensor_tensor(
                            out=wgu2[j][:, 512 * g + half_o:512 * g + half_o + 256],
                            in0=f_a[:, 0:512].rearrange("p (o t) -> p o t", t=2)[:, :, hh].bitcast(F16),
                            in1=sc[:, 256 * P:256 * (P + 1)], op=ALU.mult)
                    v.drain()
                v.drain().then_inc(s["e6"], 1)

            EUNITS = [(lin, P, jj) for lin in range(LINS) for P in range(2) for jj in range(4)]
            eu = {"u": 0}

            def emit_units(upto_lin):
                while eu["u"] < len(EUNITS) and EUNITS[eu["u"]][0] <= upto_lin:
                    lin, P, jj = EUNITS[eu["u"]]
                    j = lin % 2
                    if P == 0 and jj == 0:
                        v.wait_ge(s["g6a" if j == 0 else "g6b"], 64 * (lin // 2 + 1))
                        if lin >= 2:
                            v.wait_ge(s["mmu6"], NTH * (lin - 1))
                    extract_pass(lin, P, jj)
                    eu["u"] += 1

            for half in range(2):
                base = NTH * half

                def evac_tile(tl):
                    for bg in range(4):
                        v.wait_ge(s["t5"], 16 * base + 16 * tl + bg + 1)
                        v.tensor_copy(
                            ln2fmh[:].rearrange("p (g t) -> p g t", t=128 * NTH)[:, 4 * bg:4 * bg + 4,
                                                                                 128 * tl:128 * (tl + 1)],
                            pb[(4 * tl + bg) % 2][:].rearrange("p (g t) -> p g t", t=128)).then_inc(s["t5e"], 1)
                    v.drain()

                p5b_tile(base + 0)
                p5b_tile(base + 1)
                for k in range(2, NTH):
                    evac_tile(k - 2)
                    p5b_tile(base + k)
                evac_tile(NTH - 2)
                evac_tile(NTH - 1)
                for oc in range(NOC):
                    lin = NOC * half + oc
                    emit_units(lin)
                    for tl in range(NTH):
                        gl = NTH * lin + tl
                        v.wait_ge(s["mmu6"], gl + 1)
                        v.wait_ge(s["sil6"], gl + 1)
                        if gl >= 2:
                            v.wait_ge(s["hw6a" if tl % 2 == 0 else "hw6b"], 16 * ((gl - 2) // 2 + 1))
                        v.tensor_tensor(out=htl[tl % 2][:], in0=sg_sb[tl % 2][:],
                                        in1=pb[2 + (tl % 2)][:, 0:256], op=ALU.mult).then_inc(s["h6"], 1)
                        v.drain()
                        if tl >= 1 and (half < 1 or oc < NOC - 1):
                            emit_units(lin + (1 if tl >= 1 else 0))

        def p6_act(a):
            for half in range(2):
                for i in range(NTH * half, NTH * (half + 1)):
                    a.wait_ge(s["ln1"], NT + i + 1)
                    a.activation(out=sd5[:], in_=mv5[:, 1:2], func=AF.Sqrt,
                                 bias=1e-5, scale=1.0).then_inc(s["ln2"], 1)
                for oc in range(NOC):
                    lin = NOC * half + oc
                    for tl in range(NTH):
                        gl = NTH * lin + tl
                        a.wait_ge(s["mmg6"], gl + 1)
                        if gl >= 2:
                            a.wait_ge(s["h6"], gl - 1)
                        a.activation(out=sg_sb[tl % 2][:], in_=pb[tl % 2][:, 0:256],
                                     func=AF.Silu, bias=0.0, scale=1.0).then_inc(s["sil6"], 1)

        phase({"sync": p6_sp, "vector": p6_dve, "tensor": p6_pe, "scalar": p6_act,
               "gpsimd": p6_gp})
        ph6.close()

        ph7 = ExitStack()

        def sb7(name, shape, dtype=F32):
            return ph7.enter_context(nc.sbuf_tensor(name, shape, dtype))

        hb7 = sb7("hb7", [128, 2 * IFF])
        hfm = sb7("hfm", [128, 2 * NOSET * 128])
        wdb = [sb7(f"wdb{j}", [128, DM]) for j in range(2)]
        x2t = [sb7(f"x2t{j}", [128, DM]) for j in range(2)]
        outb = [sb7(f"outb{j}", [128, DM]) for j in range(2)]
        bnst7 = sb7("bnst7", [128, 24])
        mv7 = sb7("mv7", [128, 2])
        sd7 = sb7("sd7", [128, 1])
        rr7 = sb7("rr7", [128, 1])

        def hfm_slice(tt, sid):
            o = (2 * sid + tt) * 128
            return hfm[:, o:o + 128]

        def p7_sp(sp):
            for tg in range(8):
                if tg >= 1:
                    sp.wait_ge(s["t7"], 96 * tg)
                sp.dma_start(out=hb7[:].rearrange("p (t o) -> p t o", o=IFF),
                             in_=h_d.rearrange("(g t p) o -> g p t o", t=2, p=128)[tg]).then_inc(s["h7"], 16)
                for tt in range(2):
                    if tg >= 1:
                        sp.wait_ge(s["b72"], 2 * tg)
                    sp.dma_start(out=x2t[tt][:],
                                 in_=x2_d[256 * tg + 128 * tt:256 * tg + 128 * (tt + 1), :]).then_inc(s["xl7"], 16)
                for sid in range(NOSET):
                    kk = NOSET * tg + sid
                    if kk >= 2:
                        sp.wait_ge(s["mm7"], kk - 1)
                    sp.dma_start(out=wdb[sid % 2][:], in_=wde_d[sid]).then_inc(s["w7a" if sid % 2 == 0 else "w7b"], 16)
                for tt in range(2):
                    sp.wait_ge(s["b72"], 2 * tg + tt + 1)
                    sp.dma_start(out=out_d[256 * tg + 128 * tt:256 * tg + 128 * (tt + 1), :],
                                 in_=outb[tt][:]).then_inc(s["o7a" if tt == 0 else "o7b"], 16)
            sp.wait_ge(s["o7a"], 16 * 8)
            sp.wait_ge(s["o7b"], 16 * 8)

        def p7_pe(pe):
            for tg in range(8):
                pe.wait_ge(s["h7"], 16 * (tg + 1))
                if tg >= 1:
                    pe.wait_ge(s["b72"], 2 * tg)
                tb = 0
                for tt in range(2):
                    for sid in range(NOSET):
                        P, c = sid // 8, sid % 8
                        ncnt = 64 if P == 5 else 128
                        gidx = 96 * tg + tb
                        if gidx >= 2:
                            pe.wait_ge(s["t7e"], gidx - 1)
                        pe.transpose(pb[tb % 2][0:ncnt, 0:128],
                                     hb7[:].rearrange("p (t a b) -> p t b a", t=2, b=8)[:, tt, c, 128 * P:128 * P + ncnt],
                                     iden[:])
                        pe.drain().then_inc(s["t7"], 1)
                        tb += 1
                pe.wait_ge(s["t7e"], 96 * (tg + 1))
                for sid in range(NOSET):
                    kk = NOSET * tg + sid
                    pe.wait_ge(s["w7a" if sid % 2 == 0 else "w7b"], 16 * (kk // 2 + 1))
                    for tt in range(2):
                        for dmc in range(4):
                            pe.matmul(pb[4 * tt + dmc][:],
                                      hfm_slice(tt, sid),
                                      wdb[sid % 2][:, 512 * dmc:512 * (dmc + 1)],
                                      start=(sid == 0), stop=(sid == NOSET - 1))
                    pe.drain().then_inc(s["mm7"], 1)

        def p7_dve(v):
            for tg in range(8):
                tb = 0
                for tt in range(2):
                    for sid in range(NOSET):
                        P = sid // 8
                        ncnt = 64 if P == 5 else 128
                        gidx = 96 * tg + tb
                        v.wait_ge(s["t7"], gidx + 1)
                        v.tensor_copy(hfm_slice(tt, sid)[0:ncnt, :], pb[tb % 2][0:ncnt, 0:128])
                        if ncnt == 64:
                            v.memset(hfm_slice(tt, sid)[64:128, :], 0.0)
                        v.drain()
                        v.drain().then_inc(s["t7e"], 1)
                        tb += 1
                v.wait_ge(s["mm7"], NOSET * (tg + 1))
                v.wait_ge(s["xl7"], 32 * (tg + 1))
                for tt in range(2):
                    for c in range(4):
                        v.bn_stats(out=bnst7[:, 6 * c:6 * (c + 1)], in_=pb[4 * tt + c][:])
                    v.drain()
                    v.bn_aggr(out=mv7[:], in_=bnst7[:]).then_inc(s["b71"], 1)
                    v.drain()
                    v.tensor_tensor(out=sd7[:], in0=mv7[:, 0:1], in1=mv7[:, 0:1], op=ALU.mult)
                    v.drain()
                    v.tensor_tensor(out=sd7[:], in0=sd7[:], in1=mv7[:, 1:2], op=ALU.add).then_inc(s["b71"], 1)
                    v.wait_ge(s["kq2"], NT + 2 * tg + tt + 1)
                    v.reciprocal(out=rr7[:], in_=sd7[:])
                    v.drain()
                    if tg >= 1:
                        v.wait_ge(s["o7a" if tt == 0 else "o7b"], 16 * tg)
                    for dmc in range(4):
                        v.scalar_tensor_tensor(out=outb[tt][:, 512 * dmc:512 * (dmc + 1)],
                                               in0=pb[4 * tt + dmc][:], scalar=rr7[:],
                                               in1=x2t[tt][:, 512 * dmc:512 * (dmc + 1)],
                                               op0=ALU.mult, op1=ALU.add)
                    v.drain()
                    v.drain().then_inc(s["b72"], 1)

        def p7_act(a):
            for tg in range(8):
                for tt in range(2):
                    a.wait_ge(s["b71"], 2 * (2 * tg + tt) + 2)
                    a.activation(out=sd7[:], in_=sd7[:], func=AF.Sqrt,
                                 bias=1e-6, scale=1.0).then_inc(s["kq2"], 1)

        phase({"sync": p7_sp, "tensor": p7_pe, "vector": p7_dve, "scalar": p7_act})
        ph7.close()

    return nc


def _rope_tables():
    t = np.arange(S, dtype=np.float64)
    inv = (10000.0 ** (-(np.arange(0, 128, 2, dtype=np.float64)) / 128.0))
    ang32 = np.float32(np.outer(t, inv).astype(np.float32))
    cos = np.cos(ang32.astype(np.float64)).astype(np.float32)
    sin = np.sin(ang32.astype(np.float64)).astype(np.float32)
    return np.ascontiguousarray(cos), np.ascontiguousarray(sin)


_PROJ_B64 = (
    "eJwVmndYz98bxpsaioZoaCgJRaSlz/t5KsmeSXYyMjIyS0Za0tSQdhSpKAop+ryfB4WKZMvW15aVTRo/v7/Puc4f5zn3fb/u6zr2"
    "Pw5Ji42OQG5fLzg976mwdP8p8ZSmITxcGcajbyujcEmEQdX9sXlELO30nIJ7HoUhHHXDV6pmYlBhf3xTWw9jcozgkH44D5mpipqv"
    "h9KhHTP5yI5SHvc3BB6nDqVBP+fh+4Tt7Jt1gTZ1uMFK31jM7fUX2tekihcfnqbCwT9g6O5cYXDDNnI7kQ9/fvcHuZJ62ttkzBPS"
    "9Mm820I4mKBEQsZuodv83SDxcebpFXmwZMg8tNXJgfQrnVSxvkvY0tyT1D4aUl3hCpz4cjeofZGhhNwhtNE7hazeRYCL8iw82y7h"
    "1rqenLbfGbP7hlPwIWNxce1Ztn20G3stjOCvySsFR38dHtp+VCycbcYX+zXAm17A2++vZ/cBKXhZ66dk6nQfDh+sRG2+T+B6tCOZ"
    "ya3HS0WIj8+MwKeOJznrItPtlpvcOnUMrj32AB9+UEDlPgp44d959q9q6d6WA6DS/wAtv7cQm161i/a7/gix6ydBN3s77sjyFvIC"
    "tbBlpx6dWVjJwRlXYcg+Q2zOy5MmrojkkA0ddNEBedzN/bB2cCfcexFPlqmu6FdzWYgwPERPFg6EOxODIe3vfUj3j6TQYf6UUKlK"
    "+j3viL2wFL7KBGKcWTXLhRyA25IXEDbJGYeabeEXckbCg+QtOPOxDeu8l6EhKk1o9jUSfvytohR/Leyxt1T8nTgZI4MDqKWoFzbd"
    "6wm7k36B5etNlPLmNNs/s4X/ordDsbolFsz1ETTcb4l/r1lT+UZPrOWhtP/wPdYNfo4/XW3w+oL9UH97C1reLeZuUZM4+ZgcZHXu"
    "ExteNMIvt3U0v5s/pDoYwO4D2jx1pjxbPVoqWVNkh993Pqakjbo8+usINFLbxl3WG4SKtnwwHhYhBBqMEb1ddXjvp12gLE2GgimT"
    "hB2ZHnxlpTVoyZby8OkS3nCpEeaOiqSfziS6dE7nlb2OwOVzneJBWTM+cKkHQMFXCmlbweP//h6542QoDgnYjK/ak7kiIZv7hYfx"
    "80WzadEeZ340TVtoP9Md/B6eImmpCuvfH8lLWk3wzDRzVL6vixrjL2Lynxiez7v4kZcfr4/KIvsaRTqw0xHXF/Xj9bvz4PEKbXRp"
    "OS2k1SmyzKB2eLrIDo1fHQLXsGZ2fe7Cs2SD8Le4CWWUjXljjhl+7JCniIzJ7KV/V/hjmS3MHd4bYv9sEDOOaGP/Pc7CnbKXojAr"
    "H0Me24v26sjfjTdjStY5qli7ncRpE7GiYCQu4m54/PI4qiicjwezevAQ/e40LjAPfhXuEsfWHcD2ZQm8Vc6Jj2inoVdNKj7v44mk"
    "kE+tSqqo/NQeldJ24z7lv4L+yqGiWbwMBP97UdYT/PnunlNVZSP/QkZJO53dexsqGi+yrfxujByrDh+3fBXG+c7DNPG85EnfDLa/"
    "30broodi7e0Tgnb1fozU3IXNE805/mw+Pv69lC3uHMSucwH865EGN7shHpt+AeWCHtHwrSM5cuB+5DG3xECrh/RE8SVc2KGM1Skj"
    "eWD9HpD5uh+CXjlSxMe9nLrUBO2G+ZHv9AZ4ZvOXJl9MpXxHE/AwmAFlqyZL9799AX8to2D1wBDq/nwN/Db9QatKbHhxmzqkxJyh"
    "ldunc/4dH2hc64E+3q/h6Kty2JnZJW6oyeORVhV0Q7WK3uRn0d70Ouo/2fSfzuPQ71iOsPWDHhct2UfXu7rjE59k3r3uE8h/9GOX"
    "cX1R32ghrRzuAUO/LuSihqE82deSSlqG8/KJx+h2bSK+7ou8f3w2fFbNx3XvFyBqqKGnkhkMPV9Gg8deIWuz8WSkP4BsbmhR4yol"
    "3lc4ixWGzcZ3oRL89n0p1Cx7Qv9N6Qff/6bxMQUl9HQfw7pd5XTMRI4nl7GQtWo+h06M4q0GlnRB5g2oOxrgG+olOnyaKpyd6cbO"
    "Y81gwvJXsHlcHBodbIHDO0/Tt5ll0pJziJFuRvgpyQIT9eX4m8Fr0h1UCztUO0mpaA5/tLHA060tdPVSBFs5n4ZWnUC2fPALtHVj"
    "Wbm+D0Tst2VcspczL1+kpkEZMKJ6nbRlTTr6Onpy2YoZNLAolVz6rCLdqLmgUK3BvrYlEGRSy5sfTcZk9zyKzkzmjfP1cJlpExR+"
    "/ihpWZfJV07ZcIyFK68aZ8jLnQpYxSyCFxyfw79WRVDl9dewtOURvDw3m3/JbGfJgjsQYFGIet7KOKS/NbmYHxN+uRxzOnfuoViy"
    "dL946WogtRxfwcbv/lBSRR++2LGRh6yXUu/tSjg06he8XTSfj1sl8tSyOIz3SuF8v038MVeWF5ztx4EGcyEmuQViIqbBhZHd8M2X"
    "ebglYY7QqHhaotFtOXbH8Xh7pSU8iNZAnFvCft0PwehHDVTb8QDf6XuD9281Tlebww8s0ngzdlDemwncVqCCagO/wsvJLngifTLb"
    "Gdayw+1QGvLtnvDjgRWeuP9YOHCkmEKu7OCUrD/0fXwmxQzTwlvN0dxoHwEfg3thrNdDivmRAtHXHsOOeUeFz9kyeM+/AXv6RUPu"
    "CCc20h3ECdnyqHyjDZ4lLmPrS4P45IEWyL7fG3W3R7LyHF2c1VhGdLc7db4LgcqRf8Qd/ifRcnqYdK7jUOh6GIVNAz+RSqgsvNPY"
    "B8ZtMrhcsxyPDVNge725+LphNuZq53Cj7wj48GUGJdhG4/bZdwV9r9nYZPhOsF3fjSM/WfHWgIHsZSDSY2cP1HuTBwLvqhq6fhjs"
    "dFfGPk+sWL/fanyiuxJkHAxxV6M2zhowj6s7D2DYhud0tU8SH7wWBEqD7lP/PmHY+bICClNO0Z7KJ3Ts7QUMTbcTLtXuQdUPNVA5"
    "+SDv81bkK57GeClMHk1Ee340OQetS20xfmkYJ//MwHH7LlFVUypmPjvAHyX3adMVOUi/9YCS9QcKCzVy4UlxuzC3VsAcK0MuKUrC"
    "pTmPqJ8aiLx4PO1dHidM2TIF6ZIyBHxMorGrwniylj0GqneTyBlbiDM7F7Bm21VB1+EZhmwp5p+fB3Jnl5Y4oTwJnmX547VrtcIY"
    "Ky8qqSLqOpCFUaGh/N3sOdWWREDnlLMUsc4UWy/HoFmkHFk+PsKqwdli3IEpeD5llkQu8hF4+4+i86PjcFGcLc5Y5IW+AeGoxKk8"
    "f10AR9oZovKJJjZ+3EeqtWIQ+hbvpBa1+fhmejh33Y3Bce+rsUB2FP4MXIdXrfuibJohFd3QZP+dDUJe4geaUnKSHky/j1T7Xtqs"
    "voJ/D7dCpRBN/NpaAut3ufEPl1Wik7MJuHMSvh+QwiE9CmCBz0lx7bcg1puai4PrLwn338vAd5NQsi5Ux+dX/Why8nY+mZ4pOi4b"
    "hU3tWvzQzxJePi0UJylGCHlyDTCYW8TQ7oESm72V4obrF2Hb8XYeKPVhnxma/PXuWkx98gJM/yzgDK/lqPnbgp0vldKfne/IZJ0X"
    "2s2NwPSdqzi6WZOtcRCaKXvB72ZFNLvxh/ppzMeZqUo4s+0Zd1urDp4XX9NJtWOwdU4BvQpzELp9yoeg7y/BcOMjMrWXpznL5nPB"
    "u1rhQAPRjJm2KI1koeDkHrbxrOefMUewJUgRQ0oPk3fdPtxVyFSaoIUhLvloNOApOKhcg6Z5hynx/k6e/qeSdG6Pxbg/LrwPI+nG"
    "0ydkHngOvyZ9gA/FV2Dw9Q6auVADNx4qxNh5ijTq9VFcHhHPOS9iaM/ZUvHU4UH0X4I9nis35LfZs4RTNp3UNeq8eExyTKidMA0L"
    "xaPC9aORsNB8D7Z63Ba+r0yhJW1pMGq0HKcqLSar8tVwZosG7z+HfGuiLsdevQ1TLozhntibpPfUUK/+lWCp30FHhgWChc5e2LvD"
    "Ea1MZ8GIcjeMWeyC8mcvg2paNWXLhcCv1Jd0xH4afJSZg4Gb3cB8ZodkbkQSPQvywZP2V3lzuwJ3vkyEE6MT6WpsJP5ZXCyMu34B"
    "r+a4w7fxxCah/qJLvTVv914tLK57AdfepHJbggx6zHCBtMFxkFr3FKK6WqX9vi7APns24LOR46ljiSm9LeuOmbERUo8b3uKp1TKU"
    "9tQCfZ2S2Dj2ong/Yitc9xhCCuIxuu+QQW2XEOpOdoLf0hzs4TCY5DbfEG7ILOWRB2JQ2SIOp220R/nZ2TR76WtQaxfwzHTARWeX"
    "YtCcGDR8vp0DzEIxmg2ccl50/tP4bOx11BbNCqeIf6+m4Gs5Y76r9R0ieyzEpccnUd4/reX/ekK3osKw9o4Fnss5SzccLqHlRxv+"
    "qj0Um/I34QbbAWQv/1M0+dtEf7N3wly1VOq+7564xy8Ezz9bjFYZNrxs+Fg0DgiEL7462DLksKBpZYpb3g+hKasI3oenwYawaei9"
    "aTQrt8rCiBMJ+Fm3L8robcZchWXs/uYs5j+WY+i8TvZJ3c4duauP7/NP0CJNBayTDER/jy1sW68CC+77wDvfdBziM5SGsj2DYl9U"
    "DM+jnQm/oPfEVFh/ax5UD8tkbctedGvFSvHpozq4tk2AXhIXuvtZF6e+awPhkpxYW9ULJhQ2Sf3XvIXXdRbCvJtB3BXUm0e63IND"
    "AxRZbc15mrdMCddHF/KanwoUY7gU1gop0LllINUkR+HbFQMFvRWPSKFsK6cOcsV7NkMweKEix9qMwGA7JxjQ0wT29JxFybtXcXdX"
    "U5TxTeR6vdW8r/so3N/rLJj4jILYw8pQWbKP/VaY8J6jn+HuEWN+ekUPJSW9+UfLF6gaq41uBpqMWlIuXD0Wj+xqgGXpzbTeYxB8"
    "fKjCjmkhlFecKoTeWogz7jljdMgiOGR/HUZNrOQNA4LIN2A7JEbtZqNiINttFdj92p5/2Wwl5kWrg0XrYTog6SBlxYOgXJsEOe/V"
    "RYefKhA7KQ0UNAn2P1TGfvnOwu2ZV9CgXA8TboZi3sAbMNNoJ7o+//fWRi+k1ZlGPFLw4rZZjeCUoSaET5iFPx/8xz06dKWRnkdo"
    "3/skrN8WSLmn/8DebofAUaEbrh+fQxGql6jHzEWgnvkUrRYnYuHWOP5rPZCWB/dH7RfPYNjFWp7Z5xwvHroIN+1wYQ+5Z5KFWgq4"
    "VvIdtg+UF0ZajGO9cQ9wnbkevpyzj9cEL+eSf71kgagNrTtbQfjpixV5JgAPRJK1yubQUx7kkxDO027kceBrL37VWgWXxnjx5vo1"
    "9NXiI/XMl0G5BjmuOnVEqJ9XK5YMu40RuYMo+lwP7HFYSu8L7lKKwwhM2FJBfnNrOa7GWVz3MJvnv3TGcv/uePR8KviPrWTorsFJ"
    "ZbPEMpV74OqiyvqWbbRItppMch5C+T9Nrg5g1FWfBjrS0dh5vYbXzUnkOTN6Yv+wDAicbMS75aaDo+Zb6jsln41y+sKiUXZYN34F"
    "nnR7SjEDj8NCjZ9kVvtRtMs6SqP146D6kY4w91kE7kj/BWHtl0UN8T9YURrMZV2baEBTX1a5353X3ZTw64v/2PHDSRLLBN5tac5O"
    "L6Lo/j4VPvOrwdH+sTnvHmOF3U/40NJtGtjv7Hw85xoIa8YlwvrxSjhbTo7LAzJEj5Qw/PFtBK/LtOKe8zxxYmQHjAhuhzVqf+DX"
    "tAUspO8RTvwN4w3Ke1jOLhAd3wzjyj3XxW8VObhUcwpuuqKDc263CvlXwhiEm7RS2I5fPPph6EpVdH3xhOy8wmHu6z28f9MEjJR/"
    "CsMKJkubZnhhdwdTPnf/Cf/dMoGjvVxhRfUN2ptkzgU1N7C7/3tqV/fES28eUtXJFNF9ZSc05S9Fj20zuNPHEPfpDEJbS2MqPz2O"
    "oNIAZ3+yY5fRObyflDERIrjn/S/w9EEMvZg5jfPrYoWAce8g+YoR7m2bzxltvwTLzlLQMvkBGqfn8qdUCbRtaIByaqbijgF0J3c8"
    "5sM0yarnTyFyeT03OCfhqpr0yh0y+nAofyArvTrM6sccqWzebvQbuYrirdNJ02ucxMBbg/uu+Ejx0VpQ/dcBQyf35CXNZ4Q/59Zg"
    "ZPolIWqarPPefgfEkS978YT5IqtMdMDa6Iwq5csd0uV/PTFw/Th27vIQvkjSeKvDabAz+Q/gZy8cMXg4ng2+jXpHk/jC/HrqcN4F"
    "KmcnCXGva+j7yyGsNCwE1765LDTOLefpmXHwePJJUl83CP3XvyKj6cvwwydfcF50HU/M28PSLTH8bX6hZGv3HaT7dAHnDvLmgudd"
    "YGl2De2nm+GnrXfEopLv8GfrKvH5tHDUaN4DE5t3kGEvFxz3Jg37nXYA8/pEtBt+jHd66OCOY4Z4JL0HXu3qz4cvKeEgg3DqtvYM"
    "lOpZ02OlV3B+s7KoM9EZKo/LcKpid1riIYeaGyOpwTAGfz9op6S2UtqlsJu70s8LkSMIjl6eQqo+Cqw/XxQSNErh0I6hwrqb8lR6"
    "dC9e1FkobGpyADflndxoHsFNCsTJeJyl/Zso20xV1Mv/1x9vh+KfWV5YOHWMeEDUHvl87hFRaYaIj02iWV7lHHwbmQHPvvWjjlV7"
    "xLzsg5S5eB9OVp6O/63SxXUGUTgkfxeM7HWZOuZeFB6WWwnqloYw0vMKTn72gPv3dsWao4ep/WqetFh9C50yjeL+xzPAU6aKt94B"
    "3BKWAPYFuzm97Aj+V66IDh6W1PNEHC3XFXmyUqfTt8LDYvHAGkEvfjo9eGuH62Id2a92I6wRDtMxeg5Jjj3xzJ0AuO7QHx+rvmfJ"
    "LA/+s/0AdSnKi0d1+nOivD0+zg7lpd+r6NbpSCyO/yBKLhlA3XM7lm2ph3u6vVl70C/KDGv7lz0RHGOZSPuP1gD6nRbVl2UIhVmR"
    "PH/LS+jmMgpyXRfiaL0w2JM4He2PpcN493LRonIjFUnT2bZvLmVVijRZ7jt5GDVCiViKfxXc8LJXOclpWuPHahVucpkK418+p9Y+"
    "i7Fft99wa/lMnuOlwm/+nqDwit68vZsND5/6id0XFYlP49zZWceVO4ND2GNdGznX/6FhZj+Fq53ZvPGTMSbH11HvVC3nPva78cD5"
    "J3Bj0RPR9Y6RVO5sCtbNK8ey+8DJLRFs9GIrpu8ay/bnZvH+3se5MTOVLC9EkEewEwa0pkgOJ2qys/sXXvJwsdSqtyvVLXkrJI4e"
    "xPukHyDshhHeXuPNeyxPsc+K9ZSY/hSOHzkkFL3RxX+CY8++Xuy9KY8Co3whLmsYTdHL4rIMWag1eC1smRaMQ4NOc232RZLhpbhr"
    "+ED21Qvlz27HIP9XGContognvhSIyxvTaZXtQErtflAcoW+KGRkfaZxPCUTsWM1eSk4cWbYYtvcJ4GyrrZhTaMVtKj6YFqyC0u46"
    "6FuWzwVZSnjTu4a22YwjDbLks/+4PY8NaMjaofB8ZwW1ztxFU8fE8tY+lyDSbz562E/CR46j2DxRC8cON2aDWxv4Y/t51ogeD+LO"
    "ZOo/lARdmSoh4iqwrN0YuBuixltmyFHnnU76fuAQv/F9KQr+smwRvwIhrifPLnWjDQ7p1KvBXbjmdgfKdPLEyj66uLd1hTh9y1zh"
    "V2IMcR9fKo/II2O3AD4/JZedpjqR5flGMDsxHktT9vIp3WGYXn5WWB2uiv32KePelUOlt0q+iS+/XiG/n79p3oYtHOV6lVLHjcYC"
    "W1NsNtiIu36+o56DhuLSdFX4/vI7YahIv9vN+JnDatz0XyO4pPXmhxkVTnqBCYKqnz9In4aT7a9mOtKhjJrNB8XNs/Zx5d3jaDV2"
    "On795sV3uh/B1TdP8kC33vj1S4Xo2+oAx2RMWcVRhfUyzlOzkirnTugBdpbs1KT8FEZtK+EzQyc5PY6SAVODUIn+5cOkl2yKvXZU"
    "CwvfZeKdjrto3XKFu1+Lpf0V/bG1JYnuqAyHAUO24OKMOFy4bRXLGyvxypuKdEXnj8Cqi7EyVYO3KFRiv0JN6vfDmMcsCoQDg6bi"
    "qIC10Ngm4KaWUbzg21VsH2DPac0/RNUF8tye149T1qSLvccr4p1tunig/BvcDUzEZRbJ/K5rhfj79H345N74r5On0A3rXXDYbTf8"
    "N2IXTvCR8IE8VXxWbg7WIx4KPZyn8+9dSqBQ1YPk/C/Cd25A5eTF+Ec3GbwWP4H29inCue2jBaW45fDN7QS5qR8U/3jVUPiFmTz2"
    "xjbS97haZXDOUHxZeJCt9sxH3enp+OJCJLmUDUZ7u8dQNTdPUmytx6GW62jYrhooCDDFWu9DeEr6m45umgl37tniy47x4FajzXqb"
    "GvDPxVJ8F6YOw+9r8fB4LyHl/nwxxzsO15hfkbwLfkb9osKgIfgocnkCfOHDZKnaib36FtGUhgK8b30Kluy+BVZhKyjmR3f4WV4G"
    "Fw74kWxyBg90CueAVjv2sQ6iG4XNtChMAZem3cObZ6/ATm0dHjdNQikqc/k/zxRJ0mBnLM4po5W13Vn624DYcz+F6UykQjlFUsl9"
    "BQP0fkPlkJ8ctN5SeNp9CV+uXiUOH3iCHoaNgx+rY3hJlIJEcmM91Bn4sOuSUfSr6SNm2ijyunyBm0+owVn3WJ677QrNirrGjZeN"
    "YNusUTS4tkTiYbORF+iw5OuWKtwYUA3T5Su5qpYpQcsVr18I5MajrpgSvxcX3enOCkc1uPdJDTK8p4vfBANUOHuc7WtmUdOH55Rv"
    "epHeFLjTLtkENLhbBCv8u6Hx2ETSO3JFCN09BJLXl4rL6N/8p1vxs5ffwTtZHuda9sUABU00+m8F13XfKio4DSE0mcbmDd1p+NWP"
    "8MX5AmnGuOM/14HZjw/xC9OJeHOQGbfMMudNyr78uzpZnP63gbIMv9KFT1UU3GcVPZ9VUjVf9roYMPkTLbmYxOkP4tDqtDtrrs+g"
    "x2mROHLFC/pkdZPUP6ZBYNYZOFpwEJSiZvEPaxOc9eQKPbgbjPJeXyhduzsGFkwCM4d1+Pnzan51xJu+/zgjmb/hOm7yy6Py5UEo"
    "87oLvw86LKaeSSFDixcgG1HMtpXZ4kTfRiEhYAcMMAyS3iZDPFETAg0zPvD7vvNReaAeVpRshoE6elgQ/1FMq3LBZc5GHBw0nEaM"
    "/SPGW8aKQxu3YYvDIj7cX5bNu+Ww3lN98Nmsh9V1DbDoy3pQDzmIBhEsHpC8FA9ULYLbnyPIe2oDffwwDg9EGuHy1hisTvohlJoa"
    "C6F3plKDTH9USJBUDTT6Drtfb6bEOE+4u+8qLK1biVst+mCqqhSKeh6l4BPtMFGB4HXXCV52vRsntfzgu8vuQfignjj68wNK1XaD"
    "SWNWS2+8XgGO0b7w5W4PCNhoy3tL4lGjTzpkWMrw4zM3cEeEKdb1LpMMXK3GC4LPkUKAHJr8HgnGbklclgz8qV4R9SwZfq26QSVs"
    "yrfdanAoL6HQZYtxYIIZHplST6cPucNp5T7isLtOgv7szZxSI8uDR1wW5lz6IE5SyoYLuw1E02AdbraUSrdF7uTpMbGCf6cjGo2Y"
    "yGr9u6HduxicJJ38r74qQVBnJBnVDGCF35pcVW2KW3zLaP/OFEo69RWyLy7Bag2kXod8xd37ntH37R3Q879FeGHRTfrED3HKD3l0"
    "ykolbb+ZfOOUJd+/kyG2/ZA793ZWK3nJOXB7aze8IZ2Bf55GC28i+nOW+2P4qzSIzQ0suHBwKfQ/1AFfnu9k6Z8GYeurIGnzwTno"
    "6fAeBp9LgnWNm87+3JrFn4MqcIe5FVdkpLPK3td0In8BZY3aDS4VmfRf4n4OkUTST0k3envZX+LcH/Fvxg4a/3kILUuWwfqsJCjd"
    "acujH/Rlm4fHeELsAvC30YH+0Xq8Y+1HofJQPATaJPBmy4/C5v3n6cSLLWQ14RfsyRjO2cMCwP+aK/sMysMJ9naY+lhAieUHiIf1"
    "KJ9zgsqmHYWRsxXZN/4mFdvp4zV3dVp9I4R8D72ApXe78Qk9TTSms1TYV5frHSvp+OD3OKa7DFPGGmxySwXL1km4qZnIo5ce9Nmd"
    "ze5Kj6HF6ClZ65Tzjy1buGVSNU1p649qFRJWsLgHF+b90+cVYBNnLxaCy1hWzZ2XTKihjcGh/P5zPZw3/wHHFHdBXukyUvSLhSKH"
    "BDTJXov5ed8oz/ICr30vIO8ZjPXD5pH1wak8wccJ9UqHsM4+E/wgmYj7d8niCxtZbF57QuitGgv7dZPo2nQ1fjn4OC1IOlnV9bgb"
    "r9r4me6PUaGylr7gfbOM4pwOQiM1om3Mhn9M6YlfD76Cvn8Q1bJL8O6k2ZCWsY5Dlv1rdLJ/oNhFBVqDptC0YU2c038nFDsLuHKi"
    "F3lbyPMz+1v8ZshfOhN0kt9WXmTlk8MheFsgao/VwxHbDoH/7Cn8UGk/l10U4OjNBD7ZflnSPsoTFybpw8JANy7l35KDZ5IRPfpi"
    "14gJ+HDweL7yIBSr1YIlihdduFHnH0vnx/GUtY7od14VR32eQPF3XSllhZyzeW6QuMLLgCfEvBeKMqPw6Y9Yav7yiwZf64/a5/L4"
    "dO482v5VGQ1ch+KkFSD9HZzNVx7O4aLUIP679B5Eq/sIFn0bYNbEkWh1Mo43KO7E1tkOQvyMw2iy+I7QnvQczBuH4l3dG6LKr/44"
    "cmgoD1wgQQ4fx9UrM7n3wGkUnj+Oaz7cI5m07bDjYwV0VT8BJ99QWl0wBEfqzeBhV0uoee1IHDe3CV5Yj+Sp3sXov1GVR6qqcRSo"
    "c1v+Vfjw3ZPkLr3CX2cmgfLUHP7s1wxh52zFG9N38TP5CD4xu446DFQEqy9d9MrJEBdWq517Y/xVHP8jmi5Tl1C10wYWjq6mmmZV"
    "3F+kSG1lzWS9zoi/eM2iQsv56Pt9Iyf0a5e+uCYVNn1bDeG7lkJe53hOO9ATQp3KyX74RT6zQ4ELKhK5XMFZGNajjBesfEuT7+4S"
    "Pu8ZxHO0d5GCUl9aYKBCvsonsWj5exrp1w9Tzj8RenoaUex+ZU5/+oYHK9RhjtMzMfTHU7A5vxQeHzGh/uDExZun45KWcEyviUVz"
    "T0PcabsXPi2xFM7s0oF2+1fiDNfdoPVrPwdb9BSnlE2DqzfCcLXNMKywUcAvUjVx+ccp+MX9AJzs6I267ae55aA1Dbp2peqxRgIv"
    "9lnLJ/ddwszZ+nw+wYA3/GqjPjmvYeAkTdyTIAunFJPg7IHVYHd2IOaLsqAz5l9/SdYX909JhXd3lDDEJw/0b7GYmNqbB6/qid1t"
    "FVDWS1FiOkUOVxbO5Afx4XD3ky43LxLo0ssjqPQoEM+tSxEytieLTlYlUovpSuDuv57/+0/KYWPa4SrfhhunNTk45BV5yNSTjdNs"
    "TJyrwyslA9h3oC+VrsmDmB/p4Bswn/5oxcKp5Sp4dKyEa1r0yDy3L2pNNuZs10h2fVsCHx274GTpdLFivT23VMkyeu9hjRkbUU7L"
    "GPseMuOqkCZ4ppyGYz+mcODsg3DisTn3DlFnm5+XODGyhcrq9+Dmeco8OuYJ2mSW4duc+Rj72IoGv+lGK4LCwTDNhVBpJKsuUeXh"
    "LiHoohqABf388JJHNmz/68X9emaJxTpTWFKymWWG27Ci/E5sCAvnmNmfyadoPgd9/EIfb8lD0cF16A0nhegriXja6orQI7q7cHCy"
    "9b+9czDkg7q44UtPaFbrz05JsVTXGiUu2ShwikJvnHR+r/jG8CN0jR+Etr3SuNaWKH7OJvLRyuFGFwXQuhhOk7NXYLPiBaB35aDy"
    "vRKt7r8Bk4yHdLAfkW/5GZb+2Ms93mwFzdoU+Hv1LJi3paPT8AChvTAN1jWnwmTLp3DXbQ+PMdwPYxz2CrKTd/Mip904v6YVVoc8"
    "gbxBF8WNgfJY2+SAvWYm4/mkNMHKxobX9s6G2JgGPpNaJb44G4HObMxDSWRvh1VkubYPjvyUTPvcp7Lx43Fcv+EJaV6PpJFfj/It"
    "75+84cYMfqUZ+K93xvK8nDR8cOQ4B8ywxcZ1MnR1zAOSLvoN1k9yaci55eISrTxsObkCS5Zf5tr2Hpi2Dzh4jzXNfuuBkvhKUldT"
    "/7dWLYZEFJMYORUcumWJAzLrpCqTF5LjvAQcce4gK/gr0IOz/3J5fhpur9PCBWPMKG9zBrRUh3Lofm2uGfCa9t7XI+u0QDb8+ICj"
    "Y8Jof7svZ0d9oe0bo/jM1O5iZIMtAKZKJB6vwFA7BhyuOXLzzbdk0r6Po//qs0/JZNyaKtA2o//o+OwY0rLx5IK+JXxSsZo2Hk/h"
    "7aPzoOctX15um0+7k7vxjqxw9nRy5qU1sbDO1QhdWy+T125rfPanj2hSPoDahywSZhwfxxdCJ1DA0tc0OMuGfn+7DnNXuuHt6nBM"
    "7N0DGirHwSK1f/f7Yzl0LF0hJg79Q0dJU8zdlwv5X0ZBvE05BNxbxdv6vqNuVW9Rd1EIL7xaLo5YH491X0miblEopCUa8kPF3XD3"
    "ciR0yBRT7tAPMLZ6PRdc9eFTxf64uSjhH7da4JMoZTZ7PE602RSMcR93UPOY7pJlYjQMOPkbPO+fRVhmwXk+vmLbvatQffknKa1V"
    "Ya7qLXFQURW7/NNY7spwuJe4lvyWrhJW7rYTgk5Z4R41f8hMWge2jqfZ3TkSbQOk7PUlDg2+vqXAE1H0qOMDsFIk5/+IBMMFSeKO"
    "oCZIg9VCbkcdPBmaIP1u9wnOp8Vg4qhZ1E14C6rBOVQ6JoqztPJ4/cfzdFulAoftKxDCMiLB+ucQ7OmlBP0CHsCRl3Y0clkKuf8O"
    "Zf2o4ZyweQVmaI6i6M5WetdwEVzrtoHb2HH4av0B9D9YDhK/GDScNAzln3RRW9Z2Ol2wBLdYLaNrGSG84IMqdY1MPNsjIvFfpoWB"
    "mvJ0rno1hjtNprDa0BAnwwne0JBwimbcCgHZbHeMHnQUJ/z4xhHdzuCjj734sUI0CqvzMXLlI2lugxNNmroHZKiIOyfM5TdWPzgz"
    "PxvT44t5xvRjwrGWhTx6WRKqZ3kIRuJKyaaeEgqNzWLjDVfEhHvp0KP7K3LXjRcrC7vD1hHzecyJgRTh/AH6nLdj/1l/IKv3GTC/"
    "7CrMsgwQ22SHiBt9GZe/3MK/++7APVVKfNzYQixxCMY1ywZxrZsNXLJTp5aeFXDEdgcs3elNmjtrQDIgh3p/6YF3DQP4m8dyHpBu"
    "jpH1atAlKsPcbQv46oyfpKZ7hb8tn4m2/tfg7Qkl9C08DMZvXVg+wkZINA8E18O9WGOcCe8dacRHG5fge9c+NFoVOFtVmyP+RLCD"
    "ezg9la6ihnnlUv/sIEi1CwfTqBYSFq1g5UFDcGPoCTCTOLD35IXcKptKG9PVxKXHzfCNRSHke/QSlA5rQeHVvnh4fYJwq0pCez37"
    "wIToRdQwOhSbs4exf8F1ajQawnfz9uLFrZfIr/c03PJ7lJg+X5Y6fYMwKCqQxpaoYfGpZVw4IUrkml58IVsbzdZoiBv0poHSWxn0"
    "XKnON/tMQhTC2Ov+I1GyKgy/JjrwOotcQdtkIq77OpFv7pxCf1Iu0F/3oVyhsQBtR2Vhap9eoPbXDgb73BXnlwfD0bkbaOmQNtJb"
    "8xir9p/Db/UvaUrwZTzrWM6y1w7BPXkpfTM1hmcrh0ntTmTj52HROOVgI0yaq8d6luXQ0b87hptv5SvHrTC3YwBOd4inAY0LeecA"
    "E2wdkCv6ruvPw3cOZrpryorTP4o69ccZx+twt2olTs9sh37aejDmjT/tHqXN1Q+rKUn/OGXpmnHEBXleNsWJ+5jn4WXHyZhIzqiZ"
    "eFnywfoYnTkaLF55Gkf3rk/D80eNRCX1/rjY2ZX2jDwP8no6/7j7jvAMXbCuTUtY4BnNsTqr8OHzKFj1tx/qzZBSZbQp75nqA3lb"
    "RtD34OEY8VSNkx9uw6HecbB43kZeP2U4Hz1qxKu3z6EeS2fynIKJKHbWihPP3xf95I252jURw/SD+K1fNprVa6FgbAoVy2tghedY"
    "vlL3ltZmE15ODMOmd6uJTG/xIefRGIJ53DH4AR5PkcHJg8+CrJYbWvbZDs6n5wjDS4aQncRSfDrsLMyM74WSfg2CoqkcB9SZIc3U"
    "wJFOM3Fmc3826aPIq3ftElztLEH9ozfOtjVBaXsf3lpYJJRuVsH3o6KE1DuvaDm9ofz209QxSh/PFN8ToqsfCbuG6vAD0xswrv0y"
    "5nVdJAXZh9LezhHij0m/QUP+O61/oQsDW0t4xaDd3F/pniQ5V2D3IR+oz3Jf6C2XTQm25ZJ5gxTOoYEUx45+LURNm4YLZGLoZtQu"
    "uMMXhRfjQnCf/yapSdYp/G9HJg3fMc3ptG1/1u52iZ12zqCAEQNw9fd5+PHuGB7qG4d7P9rwDV1nunl7LE7/3EErZsaCSv01br5T"
    "DN39wiDW3hyCLrwgce9y3rdFE57NVHQuqlFnxUf/fPDVdPGLZwHHl/iz/kl5UTp3Pu2vTKVKj0Y6IDVCuygVVhAO4bWAM/x+YI4A"
    "+8JxklY/ELRHsNF/9/+xRT/R6GYUKp5y4Mxb5VLzAXbcJ7cRcugy/D7RRbe0uvGiYjnUcl6E8q0r+G2PNhq+7iQ3L5yC4bVuuHXq"
    "a0ruuxfPXdnMwZ/6ctmdAdDvba1ky29fnJtlhF9PtlDo3QjYoPiQThXVMjQe4TXiGlQ5HygmPJnO+xZv4tkGQ7j3a284bSZ77tel"
    "oxA1UY7iYjTRus4Uyz9ViEpq+8H8SCQ6bnNE9Zk7WWOriThRmgFT9f/QhpM/JTvfNsJ9ZQt4a3kIXpv9hpJFTeIE/aE8TrUSXl3x"
    "xu2HDHjmy9n0Y3E2vclaTK490vnbY0daZb8L9Geerco+pU5L9Cx5c9l+Snf9wHYnkoWtB/rTE686UaNzBunWt0N49SgxPmdelWay"
    "O3OYPT1P+kzRdQ/oPgZTzaK1bNDynjsSRvGDY0boH1RBoz5ooslFkfOKPoP72ycc8KZLbF7lI2rLpcAClb4cersC2g7N5YO338K6"
    "61JyfL6f5lzy5fHf90JtfbEY4+XEjSsHC1pn9Wj9jWsw7XoR78BhtG1YCN50SYORww1Eick2OHnIC1Tfm7Jl6Hzc3XSKdLabsIuD"
    "GSnYrOUeCqlsVuIzUuPOU+CvT+nYK2VUHFaNZdlL8FZmiNQyME/42buafv2WQR8ZBdxn+YJ21BjhneInlOpZBpuuzcWYZncYpzUW"
    "1Q650aZGHXRrm0grZ08mz2VJVBguUtwjR7g9+L34y9KRq6/98/yO70LS9s+CTu9cnmfqhBbDPoiVRhN51s9nQlOrOuoZNpK2ngZV"
    "2BrDhlmDsG2LG+pWRGKd8Tj8k5oPr3omsGrXbG4Za/CPv4Zi1cJ97Nkkj4ojp+GebTEQ1BzGX2fKk1X6Gszpb4x3ti9BnZIAMfux"
    "FiqcSqIok6dC46RUOB7hLppc7M03zqvgxcXPYfUqNVFnhjYGeurghqla7FRszivL5chiZhw/7JgNWs+dIHOtjXTZgc18Ut6U1Vvc"
    "+Xt5PHuH2MO90jzJuOsbUfXtGGy48RqG3vbgouUTaNHXaZS18z/Iqawmna19OOJUMA5SFXF16TZY5NglHS2zmUrj8uHwjB+kdHcK"
    "2oReour+ZliVfRESPT2kDZEprK13FLp2x2CtYwV3Ly5n47t7eLmaDzn55WCOYgVHGrgQn0mHN9c9cZq+I3706aJJpf25rvQzzT7f"
    "Lq3I6M3T3OvwQf0ynjI7jAd3idIgRzcs3XmChndo8rDq3my+yZXfmU/BhF2ZXNF3BPZrjEc3i+G8bW49flu3EVd0BvPc73/AW2kl"
    "R/sr4/OPp8V1kiiel/gSKk0LoOuNAt5sKeOT/VbSrM0H+NGd0dT5ZQNXbW8SBBsfWrLnFm2+EMLn7BKhU3EtDP+iztd9F6HsVwV8"
    "t/YJjZk3FuX3FdDgP+3srfgJFOd9pk9mp1lebiArzNjEV8p1+XB1DH+ubKfwRANe/rsvLlr2Q1jyqobcspXI7ZsZf/OdC0P/3CL3"
    "Nfo4LWJ8lcP7s8LPvANkXvyMgvL3kkXQdojt0sVfN8Zw4o5Gyt4chRuzY6Wta3LosMc+aW/zX/R8RQGUNpfCxZFGaGS5FnXhmvT/"
    "Hcc4Qo0Hz9iD8uczUCbLAt3GtkBubSRm9cviSVcGcb2JkvOyhq9CdtpPqXd/Ha79CPhu8BTyMgG8mNZIP2wPsYnMS9jm4c5DIhTZ"
    "9uATYX5UreR2phYfb70k3fSPWQ8/GSjoFMXig6k9KG9ULm0Mz+Qsn3Nw88156bDyGuHy0ShUOJYNRRaNnGlqhDIlP8ReD3Pg3add"
    "vLPWCTaWjIEJLtWgvNQV3v5S4KO5MoJydQ/0i2qkUqc3MDbZmk3FNq4qviJ2ZMRgpcd3arCJwUVx5+jDcTtBWBRDe2rfwPct8fTk"
    "VRq9+ZdU3iN+sXXdVD5034Bv9LLABVar4LDvL/rpUIFbV+pxx/qFbDFoB0pfR2H9yVTqF6DJLxaYoGbPIdAZNBIDFiwQa7wRZk3Y"
    "Ta4AvGnIBFw3PgyzSuz/ddTt6HhBpM+6y3jfmAhusE/giA/9+XG0o6j9XRbtjTKh/EsJ+/qo8NtPstCSt4s3hwPLJ42GuGtXwKho"
    "E7afvONkMvWP8DpoEtztp8y6vvqcWTCCp500hBJVb/6zoF00PKlFBhYLaXbMd2FqUQ+oC6sWsr5dhcceByFndh7XbznMy9ZI2HDT"
    "U+oe81aYlHkZ7hybAvtuxYLvrHHSe9e28q+ex0nTczAOTp+NqX72oGJgyPL4jgpa3pNBTBybzvHlbTfiaO2dQ3xx9g1ufJ8mVmj4"
    "w6T9lXRqdiQ/mqyEygVnYNHowzzw1nkxeZYZn9l+kle0ToVjhivw0tMlNFExBPx2rcQ4+Oeh3x0w0n8Wlnnk8YC6WN46eS+mDVfG"
    "xUtthUIrLfby3ymx6HzLQWoKXCXbh+mjE8odTqZ7kxewzPlIdrsqlXgKc4U74hu6JdTiBI8ReHX7esHgpzcXTGynnE1pWOawirfm"
    "DGDT47fFhb8SeI/5BDQ/LQr9G7y44mY++yv2xYS4Q7B+WgsUdNuCCr/SaJKjPR7/mybEH6mnLOsg7qMhy44hcex1YiHW+BiQi80B"
    "Al0bMWObGt5MUkML8SV1bOvGJt2t8J7GIGpPsYXj35ZyW88rNCJ6D0hG1EqufqmiwCDgi8tdOCDcER9vCwHv8Bi4/9ecyXGncGeN"
    "E1zc5cS9Px8nvSUdFKFsxDVXdNF76lQ+sCQEl8pG8aoFz2DfpCJhyHALmLM+DDUk3cQd8YFgOame9s+/SYv/eaX9uiI+kOUq6m5T"
    "p6a/PaD4rgqjrD7arPxJ5WMyoMHwIm33/EsHouMAm34KD/a7Sb5N7cY32lPR7/wH0c41DBKKu0PinHTYnB4hDe5QgWGXjNjwQTFf"
    "KFzGL7oi+chPD84cHcc/3SRYtyIczBfsoC44C5V7y4XL/EDImBjH09uOo5p1BXntvUdHf+dB071p9MXSFGPUDDiqKwGtzyrSqKnj"
    "cI/5JD72tEmUCx+OH6ak4uKDrvgu01rydZmImvYO9KxBjp8mpdGS4jf06lG52LTR4F/6u/DpI+uFJ0adNHxBDTa/S0BtZwV076lI"
    "Y4a/gVwTXbSbPACejToCx3JlOXr0D3C19uMfhW8pPSaI5zztgdXDXvGpHp7YUnyZ8x2vks2nydilsFWYcVkb+4dEoW2ROarUjBYd"
    "VwWypfwyHn0nCg8HpkLKXz96MNGBFz59Dm+ejkGFgkY4biGPhmYBkH22Rqx2caFrqQ68/o8VTTXWh8CIdWRvX0XHflyn1hgr1NbM"
    "w7sqWmi+WED3mb0o83AT+Iw6xe4bg9Fk0kG+7mECWlGxtLZTExJ1j1CgSi7meDwUl5yowHrbU/D9nRoVjbTjzY6fxal1rZL3deNw"
    "6k1H1GuzAuFoC42/aoDFSpNQ5sBktvrejf+7byFQWi922+6P5h2qqLQ1UuL+Q8CiI2cgqzWSYwb5oe2y5ei0+DyONwigppRhuErb"
    "GgcYnqNlLhnC4dVjyE/WgbvffkIXT9+A4qktwqDnS8WQiF4sq6RLXUtPCh9ynsFQMRevLs/Cvbf287tZiSS8TaEb6+KE6Sn7QHGh"
    "Eqz8eRynOQ+Db0HJ8PtaDo1aNhzVbsdD5qExIk8fiSN3pbBYehee/prBWQeX8YriFNCaOIh2t/7/34MGZ/WNp5Bx8cQrkunDohX0"
    "X3M+l2ra/5tJFY+4tRna3rlwdfIEPNPNhnTCBb7UVQE93vox9jmEU/PP0NVFS4TUqZ70Y0YmTdoymZRbU/lLn3y+ahfCZHKMvm96"
    "KMw+2cUb5YrxS+B0cUPlRPiyaTYdnnBDbNY4Qr177sZtj7bhj7BCHlJnhXafgd/M/yOe3eTENw8+g8yyeky99gTCl0TDiboA6DIt"
    "wII4H155dBJdbBlURXFDsWzHevy9wAmsoxfy1ZTtxFnysOGQGvOxemHzp3ga+CweHLb+oKrl5bDr93tyXujjpIaWPLVcE1dm7MaO"
    "ZUVU2vyECkf34Jue6byxfxb+mtAAzcm/aE7sTcg/RnQy+g1o5Ryj2RPcSUXjPtV4vaPfd3dDbLkMz1uZJ5m/KQKyXfZD/bws8cXj"
    "eqpeXcBdrz5hk8MHWJ2xG3JCfqFfRydtWRUqhMZP5uda6jh35Ut43e+bcP63F31yPwI3P9yAR1t6sLbxZpax6U2uLn/pjq0hV+p9"
    "Eb/4y/PirmJW+1nEw8dO4NfjjMSiwyMpdE07vVyghpO+bYbgJWdZvnU+tWTv5CDlOLFQukPqOc6el7d4867bw9DNWR+c989DlXmT"
    "4Z2NPoBhBgSozUJld3NeVddAFp3I+hn6pJFvi81FS7Cq1fBftp7CxOlGnBnVRnPeWfMxcQMdfNUXPcpHsUzyKopru0/q3ra8QCWe"
    "Sj+8pSWmMugWe5BKf6mAzuEYrAoPgwFbX0FO/xq2GljttHVaBj8YcoUyiwkD85zBxrA/hmeN4r7xs5yi6vai6x15Wrt8PY3XqKds"
    "tQEQXzUTMn3zxUGvavjQ77mizrYDaPNuLIzQ74v/uZvSn762dPv0Gm56oIkfszuhd9JduDDpGsztTIfsnpux15RCCOr4TVO+DcWj"
    "0hxaKZ3HDnWHJB311rTn00Lsd14NH5kEUNqYyfRKeyrfV7oKw/v5ctekbfDSpVOa7z+RXx+voLUW0TDw1F56suyW9JBWOd/OD+PV"
    "DfJ06HculZ3sgXbbZ2LoCXeYcVoW/EpJ6PP+KezsGUszLsziUbNC8NeTyeickEXXcveDwo4luHoBQN74f/6ZNl8836+JXFb0ZYm6"
    "DCfN+3w2+6ia85qMTPrqt5Gtf13lugcX4WYzQc8elVKPqWEckB3F5/6mYJ8FV+mMaQyMP2iGk1PD6LRfBvYYJMvXGjSxr40ZNB94"
    "TxGve7HXgGSc12WMM29EOY3VRvRU15HaNM4km2Rjnti4i651mwQaETr4UilB0DE/jBNfmKA77uWqioPstEoCbXk++NBkB/+Os8Uq"
    "tTQ47CmLHm9XkYmGFzwc4Y5f4F/OrCmAfWtnifat42niCS/uWDmQS4r0eXxRHq1ZsVD66cIlUp2qBTVWF4ASl2JeH008MOcafZqT"
    "D49X9oFPrQV0se8cNo6PpM8658Ri4wG4Lby3cHZ6b7CYUEIqVh5cYa/NFxf040zln+JzMVJ8uT2OO+fniqFN32F471iUGz7eyWD1"
    "ZnFa2ixWWBQENSPqRQdXOzw10Bs33xrFDl/vUfrNrULqqVJBs0JCHsVOXDrQGp2q/Pi2uYlTXEYlXz3/V7jQo0l6+bOzJNJqBPRZ"
    "WM5urXslsjN9KfPWBzG29D6YRJ/gNFLgn1ozufFOCNymRCethw6YqngdTgyzYE3fZnhZM5cXnrDEVkdfnDDGnV/k+8L6VZOw3N2P"
    "oy2iuXOZO6Y9NuTFuQskC4+1UOXilXBBey9q4iA2nH8bJLOc8WhzHprY9gaFI3I8x7ODKv/TxNU7KijXbDRnz4gAjxs2OPOCLkZX"
    "DMbirQb4vo8WH5yvivdnpItd80dzXPNToWPLV/FvkQ0+DtZmw/wsakqU5U+96njqf5s5YN40zMvti/zxu8iqRhAw0J5z/Rx5nuVL"
    "aB86Db06Lvxj02rytvHgI85J0mxJh2A7vAwnFT2Abw7r6f08U0F/zGpQVfVBUN3Eo+TcOePPfBxWUUUa8XJcMC5OMqdmNY/6doBf"
    "SQzhz7HDokPqWup+d7c4Wt1eEuraQC/WpaPjQU/qlzORDdL2C8b/WH7V1VIy7VjLdhXjUd04HY4cUkDj9fPp0v04CF40hke+SMQj"
    "WM2hOywpqyKPxlxpAu0ES0w97UHta/cLtRel3NJnAiYFF3CWmEvPF0+ntyUr+cTUZHCZWUaaWt54WkFVenjUOnY3OwxnlH9Bm/J2"
    "XL55HzYlVcKZ4jaKf9xEDoODOaBXOPwPa40u3w=="
)


def _proj_const():
    import base64
    import zlib
    raw = zlib.decompress(base64.b64decode(_PROJ_B64))
    return np.frombuffer(raw, np.float32).reshape(128, NJL).copy()


_NC_CACHE = {}


def make_in_maps(L0_attn_x, L0_attn_Wq, L0_attn_Wk, L0_attn_Wv, L0_attn_Wo,
                 L0_attn_q_gain, L0_ffn_gate_up_packed, L0_ffn_gate_up_scales,
                 L0_ffn_down_packed, L0_ffn_down_scales):
    x = np.ascontiguousarray(np.asarray(L0_attn_x, np.float32))
    Wq = np.asarray(L0_attn_Wq, np.float32)
    Wk = np.asarray(L0_attn_Wk, np.float32)
    Wv = np.asarray(L0_attn_Wv, np.float32)
    Wo = np.asarray(L0_attn_Wo, np.float32)
    gain = np.asarray(L0_attn_q_gain, np.float32)
    gup = np.asarray(L0_ffn_gate_up_packed, np.int32)
    gus = np.asarray(L0_ffn_gate_up_scales, np.float16)
    dwp = np.asarray(L0_ffn_down_packed, np.int32)
    dws = np.asarray(L0_ffn_down_scales, np.float16)

    cos, sin = _rope_tables()
    projc = _proj_const()
    iden = np.eye(128, dtype=np.float32)
    blk = np.zeros((128, 2), np.float32)
    blk[0:64, 0] = 1.0
    blk[64:128, 1] = 1.0
    par = np.zeros((128, 1), np.float32)
    par[0::2, 0] = 1.0
    ones = np.ones((128, 1), np.float32)
    onesr = np.ones((1, 128), np.float32)
    cvals = np.zeros((128, 4), np.float32)
    cvals[:, 0] = 1e-5
    cvals[:, 1] = 1e-6
    cvals[:, 2] = 0.0
    cvals[:, 3] = 1.0
    ltm = np.triu(np.ones((128, 128), np.float32), 1)
    iotab = np.broadcast_to(np.arange(128, dtype=np.float32)[None, :], (128, 128)).copy()
    blkid = np.arange(128, dtype=np.float32)[:, None].copy()
    poff = (np.arange(128) % 64).astype(np.float32)[:, None].copy()

    gateT = np.ascontiguousarray(gup[:IFF].T)
    upT = np.ascontiguousarray(gup[IFF:].T)
    downT = np.zeros((768, DM), np.int32)
    downT[:704] = dwp.T
    gsc = np.ascontiguousarray(np.repeat(gus[:, :IFF], 16, axis=0))
    usc = np.ascontiguousarray(np.repeat(gus[:, IFF:], 16, axis=0))
    dsc = np.zeros((768, DM), np.float16)
    dsc[:704] = np.repeat(dws, 16, axis=0)

    in_maps = []
    for c in range(8):
        b, qr = c // 4, c % 4
        hs = slice(512 * qr, 512 * (qr + 1))
        gtm = np.broadcast_to(gain[4 * qr:4 * qr + 4][None, :], (128, HPC))
        in_maps.append({
            "xb": x[b],
            "xq": np.ascontiguousarray(x[b, TPC * qr:TPC * (qr + 1)]),
            "wqT": np.ascontiguousarray(Wq.T[:, hs]),
            "wkT": np.ascontiguousarray(Wk.T[:, hs]),
            "wvT": np.ascontiguousarray(Wv.T[:, hs]),
            "woT": np.ascontiguousarray(Wo.T[hs, :]),
            "gain_tm": np.ascontiguousarray(gtm),
            "cos_d": cos, "sin_d": sin, "proj_d": projc,
            "iden_d": iden, "blksel_d": blk, "par_d": par,
            "ones_d": ones, "onesr_d": onesr,
            "gateT_d": gateT, "upT_d": upT, "downT_d": downT,
            "gsc_d": gsc, "usc_d": usc, "dsc_d": dsc,
            "cvals_d": cvals, "ltm_d": ltm, "iotab_d": iotab,
            "blkid_d": blkid, "poff_d": poff,
        })
    return in_maps


def kernel(**inputs):
    in_maps = make_in_maps(**inputs)
    key = "nc"
    if key not in _NC_CACHE:
        _NC_CACHE[key] = build_nc()
    nc = _NC_CACHE[key]
    res = run_bass_kernel_spmd(nc, in_maps, core_ids=list(range(8)), trace_cores=[0])
    out = np.empty((B, S, DM), np.float32)
    for c in range(8):
        b, qr = c // 4, c % 4
        out[b, TPC * qr:TPC * (qr + 1)] = res.results[c]["out"]
    return out


# version 10
# speedup vs baseline: 1.0157x; 1.0157x over previous
import numpy as np
import concourse.bass as bass
import concourse.mybir as mybir
from concourse.bass_utils import run_bass_kernel_spmd
from concourse import library_config
from contextlib import ExitStack

F32 = mybir.dt.float32
F16 = mybir.dt.float16
I32 = mybir.dt.int32
U16 = mybir.dt.uint16
ALU = mybir.AluOpType
AF = mybir.ActivationFunctionType
AX = mybir.AxisListType

B, S, DM = 2, 8192, 2048
HPC, TPC = 4, 2048
NT = 64
NT5 = 16
NJL = 32
IFF = 5632
NOSET = 48
RS128 = float(1.0 / np.sqrt(128.0))
NEG = -1.0e38
ENGS = ("sync", "tensor", "vector", "scalar", "gpsimd")
PHASES = None
SEMMAP = {}


def build_nc():
    nc = bass.Bass("TRN2", target_bir_lowering=False, debug=False)

    xb = nc.dram_tensor("xb", [S, DM], F32, kind="ExternalInput").ap()
    xq = nc.dram_tensor("xq", [TPC, DM], F32, kind="ExternalInput").ap()
    wqT = nc.dram_tensor("wqT", [DM, 512], F32, kind="ExternalInput").ap()
    wkT = nc.dram_tensor("wkT", [DM, 512], F32, kind="ExternalInput").ap()
    wvT = nc.dram_tensor("wvT", [DM, 512], F32, kind="ExternalInput").ap()
    woT_d = nc.dram_tensor("woT", [512, DM], F32, kind="ExternalInput").ap()
    gain_d = nc.dram_tensor("gain_tm", [128, HPC], F32, kind="ExternalInput").ap()
    cos_d = nc.dram_tensor("cos_d", [S, 64], F32, kind="ExternalInput").ap()
    sin_d = nc.dram_tensor("sin_d", [S, 64], F32, kind="ExternalInput").ap()
    proj_d = nc.dram_tensor("proj_d", [128, NJL], F32, kind="ExternalInput").ap()
    iden_d = nc.dram_tensor("iden_d", [128, 128], F32, kind="ExternalInput").ap()
    blksel_d = nc.dram_tensor("blksel_d", [128, 2], F32, kind="ExternalInput").ap()
    par_d = nc.dram_tensor("par_d", [128, 1], F32, kind="ExternalInput").ap()
    ones_d = nc.dram_tensor("ones_d", [128, 1], F32, kind="ExternalInput").ap()
    onesr_d = nc.dram_tensor("onesr_d", [1, 128], F32, kind="ExternalInput").ap()
    gateT_d = nc.dram_tensor("gateT_d", [256, IFF], I32, kind="ExternalInput").ap()
    upT_d = nc.dram_tensor("upT_d", [256, IFF], I32, kind="ExternalInput").ap()
    downT_d = nc.dram_tensor("downT_d", [768, DM], I32, kind="ExternalInput").ap()
    gsc_d = nc.dram_tensor("gsc_d", [256, IFF], F16, kind="ExternalInput").ap()
    usc_d = nc.dram_tensor("usc_d", [256, IFF], F16, kind="ExternalInput").ap()
    dsc_d = nc.dram_tensor("dsc_d", [768, DM], F16, kind="ExternalInput").ap()
    cvals_d = nc.dram_tensor("cvals_d", [128, 4], F32, kind="ExternalInput").ap()
    ltm_d = nc.dram_tensor("ltm_d", [128, 128], F32, kind="ExternalInput").ap()
    iotab_d = nc.dram_tensor("iotab_d", [128, 128], F32, kind="ExternalInput").ap()
    blkid_d = nc.dram_tensor("blkid_d", [128, 1], F32, kind="ExternalInput").ap()
    poff_d = nc.dram_tensor("poff_d", [128, 1], F32, kind="ExternalInput").ap()
    out_d = nc.dram_tensor("out", [TPC, DM], F32, kind="ExternalOutput").ap()

    qfm_d = nc.dram_tensor("qfm_d", [HPC * 128, S], F32).ap()
    ktm_d = nc.dram_tensor("ktm_d", [HPC, S, 128], F32).ap()
    vtm_d = nc.dram_tensor("vtm_d", [HPC, S, 128], F32).ap()
    yfm_d = nc.dram_tensor("yfm_d", [HPC * 128, S], F32).ap()
    kpn_d = nc.dram_tensor("kpn_d", [128, HPC, NT, NJL], F32).ap()
    rs_in_h = nc.dram_tensor("rs_in", [S, DM], F32)
    rs_out_h = nc.dram_tensor("rs_out", [TPC, DM], F32)
    x2_d = nc.dram_tensor("x2_d", [TPC, DM], F32).ap()
    h_d = nc.dram_tensor("h_d", [TPC, IFF], F32).ap()
    wde_d = nc.dram_tensor("wde_d", [NOSET, 128, DM], F32).ap()
    ln2_d = nc.dram_tensor("ln2_d", [TPC, DM], F32).ap()
    centd = nc.dram_tensor("centd", [2, NT * 128], F32).ap()
    dotsd = nc.dram_tensor("dotsd", [2, NT * 128], F32).ap()

    outer = ExitStack()

    def osb(name, shape, dtype=F32):
        return outer.enter_context(nc.sbuf_tensor(name, shape, dtype))

    def mksem(name):
        h = outer.enter_context(nc.semaphore(name))
        SEMMAP[name] = h
        return h

    pb = [outer.enter_context(nc.psum_tensor(f"pb{k}", [128, 512], F32)) for k in range(8)]

    iden = osb("iden", [128, 128])
    proj = osb("proj", [128, NJL])
    blksel = osb("blksel", [128, 2])
    parm = osb("parm", [128, 1])
    onescol = osb("onescol", [128, 1])
    onesrow = osb("onesrow", [1, 128])
    gain_tm = osb("gain_s", [128, HPC])
    cbank = osb("cbank", [128, 4])
    nc.const_aps.register(F32, 1e-5, cbank[:, 0:1])
    nc.const_aps.register(F32, 1e-6, cbank[:, 1:2])
    nc.const_aps.register(F32, 0.0, cbank[:, 2:3])
    nc.const_aps.register(F32, 1.0, cbank[:, 3:4])
    idx_all = osb("idx_all", [128, HPC * 33], mybir.dt.uint32)
    ltm = osb("ltm", [128, 128])
    iotab = osb("iotab", [128, 128])
    blkid = osb("blkid", [128, 1])
    poff = osb("poff", [128, 1])
    qsum = osb("qsum", [128, HPC])

    s = {}
    for nm in ["c0", "x0", "x1", "cs0", "cs1", "odone", "kv2a", "kv2b",
               "q2a", "q2b", "yoa", "yob", "y3a", "y3b",
               "rsAa", "rsAb", "rsBa", "rsBb", "ccA", "ccB",
               "gdpe", "gst", "l6", "l5s", "lsa", "lsb", "lra", "lrb",
               "x5a", "x5b", "x2wa", "x2wb",
               "g6a", "g6b", "hw6a", "hw6b", "w7a", "w7b", "o7a", "o7b",
               "xf", "csf", "ln1", "ln2", "lnd", "tp", "tpe", "qkv",
               "qe", "rm1", "rm2", "rope", "kfme", "kp", "kpe", "kpt",
               "kptev", "kq1", "kq2", "kpn", "cent", "centev", "o1", "ph",
               "sq", "g15", "hfree2", "qf2", "lt", "exq", "esum", "av",
               "sum2", "rcp2", "bc2", "rbe2", "ye", "kt2", "kt2e", "wo3", "w3", "e3",
               "wde5", "l51", "l52", "t5", "t5e", "e6", "mmg6",
               "mmu6", "sil6", "h6", "h7", "t7", "t7e", "mm7", "b71",
               "b72", "xl7"]:
        s[nm] = mksem(nm)

    PH = {"n": 0}

    with outer, nc.Block() as block:

        def make_section(engname, body):
            sec = getattr(block, engname)

            def wrapper(eng, body=body):
                if PH["n"] > 0:
                    eng.wait_ge(s["ph"], 5 * PH["n"])
                body(eng)
                eng.drain().then_inc(s["ph"], 1)
            sec(wrapper)

        def phase(emitters):
            if PHASES is not None and PH["n"] not in PHASES:
                emitters = {}
            for engname in ENGS:
                make_section(engname, emitters.get(engname, lambda eng: None))
            PH["n"] += 1

        class Seq:

            def __init__(self):
                self.steps = []

            def step(self, engname, fn):
                self.steps.append((engname, fn))

            def emitters(self, base):
                def mk(engname):
                    def body(eng):
                        for pos, (en, fn) in enumerate(self.steps):
                            if en == engname:
                                eng.wait_ge(s["sq"], base + pos)
                                fn(eng)
                                eng.drain().then_inc(s["sq"], 1)
                    return body
                return {en: mk(en) for en in set(e for e, _ in self.steps)}

        ph1 = ExitStack()

        def sb1(name, shape, dtype=F32):
            return ph1.enter_context(nc.sbuf_tensor(name, shape, dtype))

        wsb = sb1("wsb", [128, 16 * 1536])
        cent_acc = sb1("cent_acc", [2, NT * 128])
        xbuf = [sb1(f"xbuf{j}", [128, DM]) for j in range(2)]
        csb = [sb1(f"csb{j}", [128, 128]) for j in range(2)]
        ln1fm = [sb1(f"ln1fm{j}", [128, DM]) for j in range(2)]
        qkvtm = [sb1(f"qkvtm{j}", [128, 1024]) for j in range(2)]
        vsb = [sb1(f"vsb{j}", [128, 512]) for j in range(2)]
        qro = [sb1(f"qro{j}", [128, 512]) for j in range(2)]
        kro = [sb1(f"kro{j}", [128, 512]) for j in range(2)]
        qfmsb = [sb1(f"qfmsb{j}", [128, 512]) for j in range(2)]
        kfmsb = [sb1(f"kfmsb{j}", [128, 512]) for j in range(2)]
        bnst = sb1("bnst", [128, 24])
        mv = sb1("mv", [128, 2])
        sd1 = sb1("sd1", [128, 1])
        rstd = sb1("rstd", [128, 1])
        sqb = sb1("sqb", [128, 1024])
        ssq = sb1("ssq", [128, 8])
        srt8 = sb1("srt8", [128, 8])
        rr8 = sb1("rr8", [128, 8])
        rrq = sb1("rrq", [128, 4])
        qn = sb1("qn", [128, 512])
        kn = sb1("kn", [128, 512])
        rop1 = sb1("rop1", [128, 256])
        rop2 = sb1("rop2", [128, 256])
        kpTsb = sb1("kpTsb", [32, 512])
        kpnsb = sb1("kpnsb", [128, 128])
        ss4 = sb1("ss4", [128, 4])
        srt4 = sb1("srt4", [128, 4])
        rn4 = sb1("rn4", [128, 4])
        red4 = sb1("red4", [128, 4])
        NG = 6

        def p1_sp(sp):
            for t, d in ((iden, iden_d), (proj, proj_d), (blksel, blksel_d),
                         (parm, par_d), (onescol, ones_d), (onesrow, onesr_d),
                         (gain_tm, gain_d), (cbank, cvals_d), (ltm, ltm_d),
                         (iotab, iotab_d), (blkid, blkid_d), (poff, poff_d)):
                sp.dma_start(out=t[:], in_=d[:]).then_inc(s["c0"], 16)
            wv = wsb[:].rearrange("p (c o) -> p c o", o=1536)
            sp.dma_start(out=wv[:, :, 0:512],
                         in_=wqT.rearrange("(c p) o -> p c o", p=128)).then_inc(s["c0"], 16)
            sp.dma_start(out=wv[:, :, 512:1024],
                         in_=wkT.rearrange("(c p) o -> p c o", p=128)).then_inc(s["c0"], 16)
            sp.dma_start(out=wv[:, :, 1024:1536],
                         in_=wvT.rearrange("(c p) o -> p c o", p=128)).then_inc(s["c0"], 16)
            def ld1(i):
                j = i % 2
                if i >= 2:
                    sp.wait_ge(s["xf"], i - 1)
                sp.dma_start(out=xbuf[j][:], in_=xb[128 * i:128 * (i + 1), :]).then_inc(s["x0" if j == 0 else "x1"], 16)
                if i >= 2:
                    sp.wait_ge(s["csf"], i - 1)
                sp.dma_start(out=csb[j][:, 0:64], in_=cos_d[128 * i:128 * (i + 1), :]).then_inc(s["cs0" if j == 0 else "cs1"], 16)
                sp.dma_start(out=csb[j][:, 64:128], in_=sin_d[128 * i:128 * (i + 1), :]).then_inc(s["cs0" if j == 0 else "cs1"], 16)

            ld1(0)
            ld1(1)
            for i in range(NT):
                j = i % 2
                if i + 2 < NT:
                    ld1(i + 2)
                if i >= 1:
                    sp.wait_ge(s["o1"], 16 * 4 * i)
                    sp.sem_inc(s["odone"], 1)
                sp.wait_ge(s["tpe"], NG * i + 5)
                sp.dma_start(out=qfm_d.rearrange("(h p) t -> p h t", p=128)[:, :, 128 * i:128 * (i + 1)],
                             in_=qfmsb[j][:].rearrange("p (h t) -> p h t", t=128)).then_inc(s["o1"], 16)
                sp.wait_ge(s["rope"], i + 1)
                sp.dma_start(out=ktm_d.rearrange("h (c p) d -> c p h d", p=128)[i],
                             in_=kro[j][:].rearrange("p (h d) -> p h d", d=128)).then_inc(s["o1"], 16)
                sp.wait_ge(s["qe"], 3 * i + 3)
                sp.dma_start(out=vtm_d.rearrange("h (c p) d -> c p h d", p=128)[i],
                             in_=vsb[j][:].rearrange("p (h d) -> p h d", d=128)).then_inc(s["o1"], 16)
                sp.wait_ge(s["kpn"], i + 1)
                sp.dma_start(out=kpn_d[:, :, i, :],
                             in_=kpnsb[:].rearrange("p (h j) -> p h j", j=NJL)).then_inc(s["o1"], 16)
            sp.wait_ge(s["centev"], NT)
            sp.dma_start(out=centd[:], in_=cent_acc[:]).then_inc(s["o1"], 16)
            sp.wait_ge(s["o1"], 16 * (4 * NT + 1))

        def p1_pe(pe):
            pe.wait_ge(s["c0"], 240)
            for i in range(NT):
                j = i % 2
                g0 = NG * i
                pe.wait_ge(s["lnd"], i + 1)
                for g in range(4):
                    gi = g0 + g
                    if gi >= 2:
                        pe.wait_ge(s["tpe"], gi - 1)
                    for c in range(4):
                        cc = 4 * g + c
                        ins = pe.transpose(pb[gi % 2][:, 128 * c:128 * (c + 1)],
                                           xbuf[j][:, 128 * cc:128 * (cc + 1)], iden[:])
                        if c == 3:
                            ins.then_inc(s["tp"], 1)
                pe.drain().then_inc(s["xf"], 1)
                pe.wait_ge(s["tpe"], g0 + 4)
                if i >= 1:
                    pe.wait_ge(s["qe"], 3 * i)
                for oc in range(3):
                    for ci in range(16):
                        ins = pe.matmul(pb[2 + oc][:],
                                        ln1fm[j][:, 128 * ci:128 * (ci + 1)],
                                        wsb[:, 1536 * ci + 512 * oc:1536 * ci + 512 * (oc + 1)],
                                        start=(ci == 0), stop=(ci == 15))
                        if ci == 15:
                            ins.then_inc(s["qkv"], 1)
                pe.wait_ge(s["rope"], i + 1)
                for g, src in ((4, qro[j]), (5, kro[j])):
                    gi = g0 + g
                    pe.wait_ge(s["tpe"], gi - 1)
                    for c in range(4):
                        ins = pe.transpose(pb[gi % 2][:, 128 * c:128 * (c + 1)],
                                           src[:, 128 * c:128 * (c + 1)], iden[:])
                        if c == 3:
                            ins.then_inc(s["tp"], 1)
                pe.wait_ge(s["kfme"], i + 1)
                if i >= 1:
                    pe.wait_ge(s["kpe"], i)
                for hl in range(HPC):
                    ins = pe.matmul(pb[5][0:NJL, 128 * hl:128 * (hl + 1)], proj[:],
                                    kfmsb[j][:, 128 * hl:128 * (hl + 1)], start=True, stop=True)
                    if hl == HPC - 1:
                        ins.then_inc(s["kp"], 1)
                pe.wait_ge(s["kpe"], i + 1)
                if i >= 1:
                    pe.wait_ge(s["kptev"], i)
                for hl in range(HPC):
                    ins = pe.transpose(pb[6][:, NJL * hl:NJL * (hl + 1)],
                                       kpTsb[0:NJL, 128 * hl:128 * (hl + 1)], iden[0:NJL, 0:NJL])
                    if hl == HPC - 1:
                        ins.then_inc(s["kpt"], 1)
                pe.wait_ge(s["kpn"], i + 1)
                if i >= 1:
                    pe.wait_ge(s["centev"], i)
                for hl in range(HPC):
                    ins = pe.matmul(pb[7][0:2, NJL * hl:NJL * (hl + 1)], blksel[:],
                                    kpnsb[:, NJL * hl:NJL * (hl + 1)], start=True, stop=True)
                    if hl == HPC - 1:
                        ins.then_inc(s["cent"], 1)

        def p1_dve(v):
            def head(i):
                j = i % 2
                v.wait_ge(s["x0" if j == 0 else "x1"], 16 * (i // 2 + 1))
                for c in range(4):
                    v.bn_stats(out=bnst[:, 6 * c:6 * (c + 1)], in_=xbuf[j][:, 512 * c:512 * (c + 1)])
                v.drain()
                v.bn_aggr(out=mv[:], in_=bnst[:]).then_inc(s["ln1"], 1)
                v.wait_ge(s["ln2"], i + 1)
                v.reciprocal(out=rstd[:], in_=sd1[:])
                v.drain()
                v.tensor_scalar(out=xbuf[j][:], in0=xbuf[j][:], scalar1=mv[:, 0:1],
                                scalar2=rstd[:], op0=ALU.subtract,
                                op1=ALU.mult).then_inc(s["lnd"], 1)
                v.drain()

            def rest(i):
                j = i % 2
                g0 = NG * i
                if i >= 1:
                    v.wait_ge(s["qkv"], 3 * i)
                for g in range(4):
                    v.wait_ge(s["tp"], g0 + g + 1)
                    v.tensor_copy(ln1fm[j][:, 512 * g:512 * (g + 1)], pb[(g0 + g) % 2][:]).then_inc(s["tpe"], 1)
                v.drain()
                for oc in range(3):
                    v.wait_ge(s["qkv"], 3 * i + oc + 1)
                    if oc < 2:
                        v.tensor_copy(qkvtm[j][:, 512 * oc:512 * (oc + 1)], pb[2 + oc][:]).then_inc(s["qe"], 1)
                    else:
                        if i >= 2:
                            v.wait_ge(s["odone"], i - 1)
                        v.tensor_copy(vsb[j][:], pb[4][:]).then_inc(s["qe"], 1)
                v.drain()
                v.tensor_tensor(out=sqb[:], in0=qkvtm[j][:], in1=qkvtm[j][:], op=ALU.mult)
                v.drain()
                v.tensor_reduce(out=ssq[:].unsqueeze(2),
                                in_=sqb[:].rearrange("p (h d) -> p h d", d=128),
                                axis=AX.X, op=ALU.add).then_inc(s["rm1"], 1)
                v.wait_ge(s["rm2"], i + 1)
                v.reciprocal(out=rr8[:], in_=srt8[:])
                v.drain()
                v.tensor_tensor(out=rrq[:], in0=rr8[:, 0:4], in1=gain_tm[:], op=ALU.mult)
                v.drain()
                v.tensor_tensor(out=qn[:].rearrange("p (h d) -> p h d", d=128),
                                in0=qkvtm[j][:, 0:512].rearrange("p (h d) -> p h d", d=128),
                                in1=rrq[:].unsqueeze(2).broadcast_to([128, 4, 128]), op=ALU.mult)
                v.tensor_tensor(out=kn[:].rearrange("p (h d) -> p h d", d=128),
                                in0=qkvtm[j][:, 512:1024].rearrange("p (h d) -> p h d", d=128),
                                in1=rr8[:, 4:8].unsqueeze(2).broadcast_to([128, 4, 128]), op=ALU.mult)
                v.drain()
                v.wait_ge(s["cs0" if j == 0 else "cs1"], 32 * (i // 2 + 1))
                if i >= 2:
                    v.wait_ge(s["odone"], i - 1)
                for src, dst in ((qn, qro[j]), (kn, kro[j])):
                    sv = src[:].rearrange("p (h d) -> p h d", d=128)
                    dv = dst[:].rearrange("p (h d) -> p h d", d=128)
                    r1 = rop1[:].rearrange("p (h d) -> p h d", d=64)
                    r2 = rop2[:].rearrange("p (h d) -> p h d", d=64)
                    cosb = csb[j][:, 0:64].unsqueeze(1).broadcast_to([128, 4, 64])
                    sinb = csb[j][:, 64:128].unsqueeze(1).broadcast_to([128, 4, 64])
                    v.tensor_tensor(out=r1, in0=sv[:, :, 0:64], in1=cosb, op=ALU.mult)
                    v.tensor_tensor(out=r2, in0=sv[:, :, 64:128], in1=sinb, op=ALU.mult)
                    v.drain()
                    v.tensor_tensor(out=dv[:, :, 0:64], in0=r1, in1=r2, op=ALU.add)
                    v.drain()
                    v.tensor_tensor(out=r1, in0=sv[:, :, 0:64], in1=sinb, op=ALU.mult)
                    v.tensor_tensor(out=r2, in0=sv[:, :, 64:128], in1=cosb, op=ALU.mult)
                    v.drain()
                    v.tensor_tensor(out=dv[:, :, 64:128], in0=r2, in1=r1, op=ALU.subtract)
                    v.drain()
                v.drain().then_inc(s["rope"], 1)
                v.drain().then_inc(s["csf"], 1)
                v.wait_ge(s["tp"], g0 + 5)
                if i >= 2:
                    v.wait_ge(s["odone"], i - 1)
                v.tensor_copy(qfmsb[j][:], pb[(g0 + 4) % 2][:]).then_inc(s["tpe"], 1)
                v.wait_ge(s["tp"], g0 + 6)
                v.tensor_copy(kfmsb[j][:], pb[(g0 + 5) % 2][:]).then_inc(s["tpe"], 1)
                v.drain()
                v.drain().then_inc(s["kfme"], 1)
                v.tensor_reduce(out=red4[:].unsqueeze(2),
                                in_=qfmsb[j][:].rearrange("p (h t) -> p h t", t=128),
                                axis=AX.X, op=ALU.add)
                v.drain()
                v.tensor_tensor(out=qsum[:], in0=qsum[:], in1=red4[:], op=ALU.add)
                v.drain()
                v.wait_ge(s["kp"], i + 1)
                v.tensor_copy(kpTsb[:], pb[5][0:NJL, :]).then_inc(s["kpe"], 1)
                v.drain()
                v.wait_ge(s["kpt"], i + 1)
                if i >= 1:
                    v.wait_ge(s["odone"], i)
                v.tensor_copy(kpnsb[:], pb[6][:, 0:128]).then_inc(s["kptev"], 1)
                v.drain()
                v.tensor_tensor(out=sqb[:, 0:128], in0=kpnsb[:], in1=kpnsb[:], op=ALU.mult)
                v.drain()
                v.tensor_reduce(out=ss4[:].unsqueeze(2),
                                in_=sqb[:, 0:128].rearrange("p (h j) -> p h j", j=NJL),
                                axis=AX.X, op=ALU.add).then_inc(s["kq1"], 1)
                v.wait_ge(s["kq2"], i + 1)
                v.tensor_scalar(out=rn4[:], in0=srt4[:], scalar1=1e-12, scalar2=0.0,
                                op0=ALU.max, op1=ALU.add)
                v.drain()
                v.reciprocal(out=rn4[:], in_=rn4[:])
                v.drain()
                v.tensor_tensor(out=kpnsb[:].rearrange("p (h j) -> p h j", j=NJL),
                                in0=kpnsb[:].rearrange("p (h j) -> p h j", j=NJL),
                                in1=rn4[:].unsqueeze(2).broadcast_to([128, 4, NJL]),
                                op=ALU.mult).then_inc(s["kpn"], 1)
                v.drain()
                v.wait_ge(s["cent"], i + 1)
                v.tensor_copy(cent_acc[0:2, 128 * i:128 * (i + 1)], pb[7][0:2, 0:128]).then_inc(s["centev"], 1)
                v.drain()
            head(0)
            for i in range(NT):
                if i + 1 < NT:
                    head(i + 1)
                rest(i)

        def p1_act(a):
            a.wait_ge(s["c0"], 240)

            def ahead(i):
                a.wait_ge(s["ln1"], i + 1)
                a.activation(out=sd1[:], in_=mv[:, 1:2], func=AF.Sqrt,
                             bias=1e-5, scale=1.0).then_inc(s["ln2"], 1)

            ahead(0)
            for i in range(NT):
                if i + 1 < NT:
                    ahead(i + 1)
                a.wait_ge(s["rm1"], i + 1)
                a.activation(out=srt8[:], in_=ssq[:], func=AF.Sqrt,
                             bias=1e-6, scale=1.0 / 128.0).then_inc(s["rm2"], 1)
                a.wait_ge(s["kq1"], i + 1)
                a.activation(out=srt4[:], in_=ss4[:], func=AF.Sqrt,
                             bias=0.0, scale=1.0).then_inc(s["kq2"], 1)

        def p1_gp(gp):
            pass

        def p1_dve_full(v):
            v.memset(qsum[:], 0.0)
            v.drain()
            p1_dve(v)

        phase({"sync": p1_sp, "tensor": p1_pe, "vector": p1_dve_full,
               "scalar": p1_act, "gpsimd": p1_gp})
        ph1.close()

        ph15 = ExitStack()

        def sb15(name, shape, dtype=F32):
            return ph15.enter_context(nc.sbuf_tensor(name, shape, dtype))

        kpnh = sb15("kpnh", [128, NT * NJL])
        kfmt4 = sb15("kfmt4", [32, 512])
        centm = sb15("centm", [128, NJL])
        css1 = sb15("css1", [128, 1])
        csr1 = sb15("csr1", [128, 1])
        cfm = sb15("cfm", [32, 128])
        qm = sb15("qm", [1, NJL])
        qss = sb15("qss", [1, 1])
        qsr = sb15("qsr", [1, 1])
        qpn_row = sb15("qpn_row", [1, NJL])
        qpn_col = sb15("qpn_col", [32, 1])
        dots2 = sb15("dots2", [2, NT * 128])
        odrow = sb15("odrow", [1, NT * 128])
        dmin2 = sb15("dmin2", [1, 128])
        radr = sb15("radr", [1, 128])
        srow1 = sb15("srow1", [1, 128])
        scT = sb15("scT", [128, 1])
        sbc = sb15("sbc", [128, 128])
        junk15 = sb15("junk15", [128, 128])
        rank = sb15("rank", [128, 1])
        sel01 = sb15("sel01", [128, 1])
        frc = sb15("frc", [128, 1])
        mcol = sb15("mcol", [128, 1])
        possb = sb15("possb", [128, 1])
        selrow = sb15("selrow", [1, 128])
        selrep = sb15("selrep", [128, 33])
        O1F = 16 * (4 * NT + 1)

        seq = Seq()
        g15c = {"n": 0}

        def gdma(gp, out, in_, **kw):
            g15c["n"] += 16
            gp.dma_start(out=out, in_=in_, **kw).then_inc(s["g15"], 16)
            gp.wait_ge(s["g15"], g15c["n"])

        def frc_init(v):
            v.memset(frc[:], 0.0)
            v.drain()
            v.tensor_scalar(out=junk15[:, 0:1], in0=parm[:], scalar1=0.0, scalar2=0.0,
                            op0=ALU.mult, op1=ALU.add)
        seq.step("vector", frc_init)
        seq.step("gpsimd", lambda gp: gdma(gp, frc[126:128, :], ones_d[0:2, :]))

        for hl in range(HPC):
            def ld(sp, hl=hl):
                sp.dma_start(out=kpnh[:].rearrange("p (i j) -> p i j", j=NJL),
                             in_=kpn_d[:, hl, :, :]).then_inc(s["o1"], 16)
                sp.dma_start(out=centm[:],
                             in_=centd.rearrange("par (i h j) -> i par h j", h=HPC, j=NJL)[:, :, hl, :]).then_inc(s["o1"], 16)
                sp.wait_ge(s["o1"], O1F + 32 * (hl + 1))
            seq.step("sync", ld)

            def c1(v, hl=hl):
                v.tensor_scalar(out=centm[:], in0=centm[:],
                                scalar1=1.0 / 64.0, scalar2=0.0, op0=ALU.mult, op1=ALU.add)
                v.drain()
                v.tensor_tensor(out=junk15[:, 0:NJL], in0=centm[:], in1=centm[:], op=ALU.mult)
                v.drain()
                v.tensor_reduce(out=css1[:], in_=junk15[:, 0:NJL], axis=AX.X, op=ALU.add)
            seq.step("vector", c1)
            seq.step("scalar", lambda a: a.activation(out=csr1[:], in_=css1[:],
                                                      func=AF.Sqrt, bias=0.0, scale=1.0))

            def c2(v):
                v.tensor_scalar(out=csr1[:], in0=csr1[:], scalar1=1e-12, scalar2=0.0,
                                op0=ALU.max, op1=ALU.add)
                v.drain()
                v.reciprocal(out=csr1[:], in_=csr1[:])
                v.drain()
                v.tensor_scalar(out=centm[:], in0=centm[:], scalar1=csr1[:], scalar2=0.0,
                                op0=ALU.mult, op1=ALU.add)
            seq.step("vector", c2)
            seq.step("tensor", lambda pe: pe.transpose(pb[0][0:NJL, 0:128], centm[:], iden[:]))
            seq.step("vector", lambda v: v.tensor_copy(cfm[:], pb[0][0:NJL, 0:128]))
            seq.step("tensor", lambda pe, hl=hl: pe.matmul(pb[2][0:1, 0:NJL],
                                                           qsum[:, hl:hl + 1], proj[:],
                                                           start=True, stop=True))

            def q1(v):
                v.tensor_scalar(out=qm[:], in0=pb[2][0:1, 0:NJL], scalar1=1.0 / 8192.0,
                                scalar2=0.0, op0=ALU.mult, op1=ALU.add)
                v.drain()
                v.tensor_tensor(out=junk15[0:1, 0:NJL], in0=qm[:], in1=qm[:], op=ALU.mult)
                v.drain()
                v.tensor_reduce(out=qss[:], in_=junk15[0:1, 0:NJL], axis=AX.X, op=ALU.add)
            seq.step("vector", q1)
            seq.step("scalar", lambda a: a.activation(out=qsr[:], in_=qss[:],
                                                      func=AF.Sqrt, bias=0.0, scale=1.0))

            def q2(v):
                v.tensor_scalar(out=qsr[:], in0=qsr[:], scalar1=1e-12, scalar2=0.0,
                                op0=ALU.max, op1=ALU.add)
                v.drain()
                v.reciprocal(out=qsr[:], in_=qsr[:])
                v.drain()
                v.tensor_scalar(out=qpn_row[:], in0=qm[:], scalar1=qsr[:], scalar2=0.0,
                                op0=ALU.mult, op1=ALU.add)
            seq.step("vector", q2)
            seq.step("tensor", lambda pe: pe.transpose(pb[2][0:NJL, 128:129], qpn_row[:], iden[0:1, 0:1]))
            seq.step("vector", lambda v: v.tensor_copy(qpn_col[:], pb[2][0:NJL, 128:129]))
            for g4 in range(NT // 4):
                def t1(pe, g4=g4):
                    for t in range(4):
                        i = 4 * g4 + t
                        pe.transpose(pb[0][0:NJL, 128 * t:128 * (t + 1)],
                                     kpnh[:, NJL * i:NJL * (i + 1)], iden[:])
                seq.step("tensor", t1)
                seq.step("vector", lambda v: v.tensor_copy(kfmt4[:], pb[0][0:NJL, :]))

                def t2(pe, g4=g4):
                    for t in range(4):
                        i = 4 * g4 + t
                        pe.matmul(pb[1][0:2, 128 * t:128 * (t + 1)],
                                  cfm[:, 2 * i:2 * i + 2],
                                  kfmt4[:, 128 * t:128 * (t + 1)], start=True, stop=True)
                seq.step("tensor", t2)

                def t3(v, g4=g4):
                    v.tensor_copy(dots2[0:2, 512 * g4:512 * (g4 + 1)], pb[1][0:2, 0:512])
                seq.step("vector", t3)

            seq.step("gpsimd", lambda gp: gdma(gp, dotsd[:], dots2[:]))
            seq.step("gpsimd", lambda gp: gdma(gp, odrow[:], dotsd[1:2, :]))

            def rad(v):
                v.tensor_reduce(out=dmin2[:].rearrange("p (a b) -> p a b", b=1)[:, 0::2, :],
                                in_=dots2[0:1, :].rearrange("p (i c) -> p i c", c=128)[:, :, 0:64],
                                axis=AX.X, op=ALU.min)
                v.tensor_reduce(out=dmin2[:].rearrange("p (a b) -> p a b", b=1)[:, 1::2, :],
                                in_=odrow[0:1, :].rearrange("p (i c) -> p i c", c=128)[:, :, 64:128],
                                axis=AX.X, op=ALU.min)
                v.drain()
                v.tensor_scalar(out=radr[:], in0=dmin2[:], scalar1=1.0, scalar2=-1.0,
                                op0=ALU.subtract, op1=ALU.mult)
                v.drain()
                v.tensor_scalar(out=radr[:], in0=radr[:], scalar1=0.0, scalar2=1.0,
                                op0=ALU.max, op1=ALU.min)
            seq.step("vector", rad)
            seq.step("tensor", lambda pe: pe.matmul(pb[2][0:1, 256:384], qpn_col[:], cfm[:],
                                                    start=True, stop=True))
            seq.step("vector", lambda v: v.tensor_tensor(out=srow1[:], in0=pb[2][0:1, 256:384],
                                                         in1=radr[:], op=ALU.add))
            seq.step("tensor", lambda pe: pe.transpose(pb[2][0:128, 384:385], srow1[:], iden[0:1, 0:1]))
            seq.step("vector", lambda v: v.tensor_copy(scT[:], pb[2][0:128, 384:385]))
            seq.step("tensor", lambda pe: pe.matmul(pb[3][:, 0:128], onesrow[:], srow1[:],
                                                    start=True, stop=True))
            seq.step("vector", lambda v: v.tensor_copy(sbc[:], pb[3][:, 0:128]))

            def rk(v):
                v.memset(sbc[:, 126:128], NEG)
                v.drain()
                v.tensor_scalar(out=junk15[:], in0=sbc[:], scalar1=scT[:], scalar2=0.0,
                                op0=ALU.is_gt, op1=ALU.add, accum_out=rank[:])
                v.drain()
                v.tensor_scalar(out=sel01[:], in0=rank[:], scalar1=64.0, scalar2=0.0,
                                op0=ALU.is_lt, op1=ALU.add)
                v.drain()
                v.tensor_tensor(out=sel01[:], in0=sel01[:], in1=frc[:], op=ALU.max)
            seq.step("vector", rk)

            seq.step("tensor", lambda pe: pe.matmul(pb[3][:, 0:1], ltm[:], sel01[:],
                                                    start=True, stop=True))
            seq.step("vector", lambda v: v.tensor_copy(possb[:], pb[3][:, 0:1]))

            def mkonehot(v):
                v.tensor_scalar(out=junk15[:], in0=iotab[:], scalar1=possb[:], scalar2=sel01[:],
                                op0=ALU.is_equal, op1=ALU.mult)
                v.drain()
                v.tensor_scalar(out=junk15[:], in0=junk15[:], scalar1=blkid[:], scalar2=0.0,
                                op0=ALU.mult, op1=ALU.add)
            seq.step("vector", mkonehot)
            seq.step("tensor", lambda pe: pe.matmul(pb[3][0:1, 128:256], onescol[:], junk15[:],
                                                    start=True, stop=True))
            seq.step("vector", lambda v: v.tensor_copy(selrow[:], pb[3][0:1, 128:256]))

            def selrep_mm(pe):
                me = selrow[:].rearrange("p (k t) -> p t k", t=2)
                pe.matmul(pb[3][0:64, 256:289], onesrow[0:1, 0:64], me[:, 0, 0:33],
                          start=True, stop=True)
                pe.matmul(pb[3][0:64, 289:322], onesrow[0:1, 0:64], me[:, 1, 0:33],
                          start=True, stop=True)
            seq.step("tensor", selrep_mm)

            def mkidx(v, hl=hl):
                v.tensor_copy(selrep[0:64, :], pb[3][0:64, 256:289])
                v.tensor_copy(selrep[64:128, :], pb[3][0:64, 289:322])
                v.drain()
                v.tensor_scalar(out=selrep[:], in0=selrep[:], scalar1=64.0, scalar2=poff[:],
                                op0=ALU.mult, op1=ALU.add)
                v.drain()
                v.tensor_scalar(out=selrep[:], in0=selrep[:], scalar1=float(8192 * hl),
                                scalar2=0.0, op0=ALU.add, op1=ALU.add)
                v.drain()
                v.tensor_copy(idx_all[:, 33 * hl:33 * (hl + 1)], selrep[:])
                v.drain()
            seq.step("vector", mkidx)


        phase(seq.emitters(0))
        ph15.close()

        ph2 = ExitStack()

        def sb2(name, shape, dtype=F32):
            return ph2.enter_context(nc.sbuf_tensor(name, shape, dtype))

        NKC = 33
        KS = NKC * 128
        ksg = [sb2(f"ksg{j}", [128, KS]) for j in range(2)]
        kfs = [sb2(f"kfs{j}", [128, KS]) for j in range(2)]
        vsg = [sb2(f"vsg{j}", [128, KS]) for j in range(2)]
        qb2 = [sb2(f"qb2_{j}", [128, 512]) for j in range(2)]
        expT = [sb2(f"expT{j}", [128, 512]) for j in range(2)]
        expsum = sb2("expsum", [128, 512])
        rrow = sb2("rrow", [1, 512])
        rb_sb = sb2("rb_sb", [128, 512])
        ysb = [sb2(f"ysb{j}", [128, 512]) for j in range(2)]
        NQC = 16

        def p2_gp(gp):
            def gathers(hl):
                jh = hl % 2
                kvs = s["kv2a" if jh == 0 else "kv2b"]
                if hl >= 2:
                    gp.wait_ge(s["hfree2"], hl - 1)
                ktm_flat = ktm_d.rearrange("h s d -> (h s) d")
                vtm_flat = vtm_d.rearrange("h s d -> (h s) d")
                for c in range(NKC):
                    gp.indirect_dma_start(
                        out=ksg[jh][:, 128 * c:128 * (c + 1)], out_offset=None,
                        in_=ktm_flat,
                        in_offset=bass.IndirectOffsetOnAxis(
                            ap=idx_all[:, 33 * hl + c:33 * hl + c + 1], axis=0)).then_inc(kvs, 16)
                for c in range(NKC):
                    gp.indirect_dma_start(
                        out=vsg[jh][:, 128 * c:128 * (c + 1)], out_offset=None,
                        in_=vtm_flat,
                        in_offset=bass.IndirectOffsetOnAxis(
                            ap=idx_all[:, 33 * hl + c:33 * hl + c + 1], axis=0)).then_inc(kvs, 16)

            gathers(0)
            gathers(1)
            for hl in range(HPC):
                for qc in range(NQC):
                    t2 = NQC * hl + qc
                    gp.wait_ge(s["ye"], t2 + 1)
                    gp.dma_start(out=yfm_d[128 * hl:128 * (hl + 1), 512 * qc:512 * (qc + 1)],
                                 in_=ysb[qc % 2][:]).then_inc(s["yoa" if qc % 2 == 0 else "yob"], 16)
                if hl + 2 < HPC:
                    gathers(hl + 2)
            gp.wait_ge(s["yoa"], 16 * NQC * HPC // 2)
            gp.wait_ge(s["yob"], 16 * NQC * HPC // 2)

        def p2_sp(sp):
            for hl in range(HPC):
                for qc in range(NQC):
                    t2 = NQC * hl + qc
                    if t2 >= 2:
                        sp.wait_ge(s["qf2"], t2 - 1)
                    sp.dma_start(out=qb2[qc % 2][:],
                                 in_=qfm_d[128 * hl:128 * (hl + 1), 512 * qc:512 * (qc + 1)]).then_inc(s["q2a" if qc % 2 == 0 else "q2b"], 16)
            sp.wait_ge(s["yoa"], 16 * NQC * HPC // 2)
            sp.wait_ge(s["yob"], 16 * NQC * HPC // 2)

        def p2_pe(pe):
            pend = []

            def flush_pend():
                while pend:
                    fn = pend.pop(0)
                    fn()
            for hl in range(HPC):
                jh = hl % 2
                pe.wait_ge(s["kv2a" if jh == 0 else "kv2b"], 16 * 2 * NKC * (hl // 2 + 1))
                flush_pend()
                if hl >= 1:
                    pe.wait_ge(s["exq"], NKC * NQC * hl)
                for c in range(NKC):
                    kk2 = NKC * hl + c
                    if kk2 >= 2:
                        pe.wait_ge(s["kt2e"], kk2 - 1)
                    pe.transpose(pb[c % 2][:, 0:128], ksg[jh][:, 128 * c:128 * (c + 1)],
                                 iden[:]).then_inc(s["kt2"], 1)
                pe.wait_ge(s["kt2e"], NKC * (hl + 1))
                for qc in range(NQC):
                    t2 = NQC * hl + qc
                    pe.wait_ge(s["q2a" if qc % 2 == 0 else "q2b"], 16 * ((t2 // 2) + 1))
                    if t2 >= 2:
                        pe.wait_ge(s["ye"], t2 - 1)
                    for kc in range(NKC):
                        kk = NKC * t2 + kc
                        if kk >= 2:
                            pe.wait_ge(s["exq"], kk - 1)
                        pe.matmul(pb[kc % 2][:], kfs[jh][:, 128 * kc:128 * (kc + 1)],
                                  qb2[qc % 2][:], start=True, stop=True).then_inc(s["lt"], 1)
                        if kc == 1:
                            flush_pend()
                        if kc >= 1:
                            kprev = kc - 1
                            pe.wait_ge(s["exq"], NKC * t2 + kprev + 1)
                            pe.matmul(pb[2 + (qc % 2)][:],
                                      vsg[jh][:, 128 * kprev:128 * (kprev + 1)],
                                      expT[kprev % 2][:],
                                      start=(kprev == 0), stop=False).then_inc(s["av"], 1)
                    pe.wait_ge(s["exq"], NKC * (t2 + 1))
                    pe.matmul(pb[2 + (qc % 2)][:], vsg[jh][:, 128 * (NKC - 1):128 * NKC],
                              expT[(NKC - 1) % 2][:], start=False, stop=True).then_inc(s["av"], 1)
                    pe.wait_ge(s["esum"], NKC * (t2 + 1))
                    if t2 >= 1:
                        pe.wait_ge(s["rcp2"], t2)
                    pe.matmul(pb[4][0:1, :], onescol[:], expsum[:],
                              start=True, stop=True).then_inc(s["sum2"], 1)

                    def mk_bcast(t2=t2):
                        def f():
                            pe.wait_ge(s["rcp2"], t2 + 1)
                            if t2 >= 1:
                                pe.wait_ge(s["rbe2"], t2)
                            pe.matmul(pb[5][:], onesrow[:], rrow[:],
                                      start=True, stop=True).then_inc(s["bc2"], 1)
                        return f
                    pend.append(mk_bcast())
                    pe.drain().then_inc(s["qf2"], 1)
                pe.drain().then_inc(s["hfree2"], 1)
            flush_pend()

        def p2_act(a):
            for hl in range(HPC):
                for qc in range(NQC):
                    t2 = NQC * hl + qc
                    for kc in range(NKC):
                        kk = NKC * t2 + kc
                        a.wait_ge(s["lt"], kk + 1)
                        if kk >= 2:
                            a.wait_ge(s["av"], kk - 1)
                            a.wait_ge(s["esum"], kk - 1)
                        a.activation(out=expT[kc % 2][:], in_=pb[kc % 2][:], func=AF.Exp,
                                     bias=0.0, scale=RS128).then_inc(s["exq"], 1)
                    a.wait_ge(s["bc2"], t2 + 1)
                    if t2 >= 1:
                        a.wait_ge(s["ye"], t2)
                    a.copy(out=rb_sb[:], in_=pb[5][:]).then_inc(s["rbe2"], 1)

        def p2_dve(v):
            for hl in range(HPC):
                jh = hl % 2
                for c in range(NKC):
                    kk2 = NKC * hl + c
                    v.wait_ge(s["kt2"], kk2 + 1)
                    v.tensor_copy(kfs[jh][:, 128 * c:128 * (c + 1)],
                                  pb[c % 2][:, 0:128]).then_inc(s["kt2e"], 1)
                v.drain()
                for qc in range(NQC):
                    t2 = NQC * hl + qc
                    for kc in range(NKC):
                        kk = NKC * t2 + kc
                        v.wait_ge(s["exq"], kk + 1)
                        if kc == 0:
                            if t2 >= 1:
                                v.wait_ge(s["sum2"], t2)
                            v.tensor_copy(expsum[:], expT[0][:]).then_inc(s["esum"], 1)
                        else:
                            v.tensor_tensor(out=expsum[:], in0=expsum[:], in1=expT[kc % 2][:],
                                            op=ALU.add).then_inc(s["esum"], 1)
                        v.drain()
                    v.wait_ge(s["sum2"], t2 + 1)
                    if t2 >= 1:
                        v.wait_ge(s["bc2"], t2)
                    v.reciprocal(out=rrow[:], in_=pb[4][0:1, :]).then_inc(s["rcp2"], 1)
                    v.drain()
                    v.wait_ge(s["av"], NKC * (t2 + 1))
                    v.wait_ge(s["rbe2"], t2 + 1)
                    if t2 >= 2:
                        v.wait_ge(s["yoa" if qc % 2 == 0 else "yob"], 16 * ((t2 - 2) // 2 + 1))
                    v.tensor_tensor(out=ysb[qc % 2][:], in0=pb[2 + (qc % 2)][:],
                                    in1=rb_sb[:], op=ALU.mult).then_inc(s["ye"], 1)
                    v.drain()

        phase({"sync": p2_sp, "tensor": p2_pe, "scalar": p2_act, "vector": p2_dve,
               "gpsimd": p2_gp})
        ph2.close()

        ph3 = ExitStack()

        def sb3(name, shape, dtype=F32):
            return ph3.enter_context(nc.sbuf_tensor(name, shape, dtype))

        wo_sb = sb3("wo_sb", [128, HPC * DM])
        yb3 = [sb3(f"yb3_{j}", [128, 512]) for j in range(2)]
        rsb = [sb3(f"rsb{j}", [128, DM]) for j in range(2)]
        dpk = [sb3(f"dpk{j}", [128, DM], I32) for j in range(2)]
        dscs = [sb3(f"dscs{j}", [128, DM], F16) for j in range(2)]
        e_k = sb3("e_k", [128, 1024], U16)
        e_sb = sb3("e_sb", [128, 1024], U16)
        e_u = sb3("e_u", [128, 1024], U16)
        e_a = sb3("e_a", [128, 1024], U16)
        e_i = sb3("e_i", [128, 1024], U16)
        wout = [sb3(f"wout{j}", [128, DM]) for j in range(2)]

        def fp4_q(v, src_i32_ap, jj, q, outs):
            p16 = src_i32_ap.bitcast(U16)[:, 1024 * q:1024 * (q + 1)]
            v.tensor_scalar(out=e_k[:], in0=p16, scalar1=4 * jj, scalar2=7,
                            op0=ALU.logical_shift_right, op1=ALU.bitwise_and)
            v.tensor_scalar(out=e_sb[:], in0=p16, scalar1=12 - 4 * jj, scalar2=32768,
                            op0=ALU.logical_shift_left, op1=ALU.bitwise_and)
            v.drain()
            v.tensor_scalar(out=e_u[:], in0=e_k[:], scalar1=2, scalar2=0,
                            op0=ALU.min, op1=ALU.add)
            v.tensor_scalar(out=e_i[:], in0=e_k[:], scalar1=0, scalar2=0,
                            op0=ALU.is_gt, op1=ALU.add)
            v.drain()
            v.tensor_tensor(out=e_a[:], in0=e_k[:], in1=e_u[:], op=ALU.add)
            v.drain()
            v.tensor_scalar(out=e_a[:], in0=e_a[:], scalar1=26, scalar2=512,
                            op0=ALU.add, op1=ALU.mult)
            v.drain()
            v.tensor_tensor(out=e_a[:], in0=e_a[:], in1=e_i[:], op=ALU.mult)
            v.drain()
            v.tensor_tensor(out=e_a[:], in0=e_a[:], in1=e_sb[:], op=ALU.bitwise_or)
            v.drain()
            for hh in range(2):
                outs(hh, e_a[:].rearrange("p (o t) -> p o t", t=2)[:, :, hh].bitcast(F16))
            v.drain()

        WUNITS = [(P, jj, q) for P in range(6) for jj in range(4) for q in range(4)]

        TORDER = ([16 * g + r for g in range(4) for r in range(8)]
                  + [16 * g + 8 + r for g in range(4) for r in range(8)])

        def rs_sem(ii, j):
            if ii < 32:
                return s["rsAa" if j == 0 else "rsAb"]
            return s["rsBa" if j == 0 else "rsBb"]

        def p3_sp(sp):
            sp.dma_start(out=wo_sb[:].rearrange("p (h o) -> p h o", o=DM),
                         in_=woT_d.rearrange("(h p) o -> p h o", p=128)).then_inc(s["wo3"], 16)
            for ii in range(NT):
                i = TORDER[ii]
                j = ii % 2
                if ii >= 2:
                    sp.wait_ge(s["w3"], 4 * (ii - 1))
                sp.dma_start(out=yb3[j][:].rearrange("p (h t) -> p h t", t=128),
                             in_=yfm_d.rearrange("(h p) t -> p h t", p=128)[:, :, 128 * i:128 * (i + 1)]).then_inc(s["y3a" if j == 0 else "y3b"], 16)
                sp.wait_ge(s["e3"], 4 * (ii + 1))
                sp.dma_start(out=rs_in_h.ap()[128 * ii:128 * (ii + 1), :], in_=rsb[j][:]).then_inc(rs_sem(ii, j), 16)
            sp.wait_ge(s["rsBa"], 16 * 16)
            sp.wait_ge(s["rsBb"], 16 * 16)

        def p3_pe(pe):
            pe.wait_ge(s["wo3"], 16)
            for ii in range(NT):
                j = ii % 2
                pe.wait_ge(s["y3a" if j == 0 else "y3b"], 16 * (ii // 2 + 1))
                if ii >= 2:
                    pe.wait_ge(s["e3"], 4 * (ii - 1))
                for dmc in range(4):
                    for hl in range(HPC):
                        ins = pe.matmul(pb[4 * j + dmc][:],
                                        yb3[j][:, 128 * hl:128 * (hl + 1)],
                                        wo_sb[:, DM * hl + 512 * dmc:DM * hl + 512 * (dmc + 1)],
                                        start=(hl == 0), stop=(hl == 3))
                        if hl == 3:
                            ins.then_inc(s["w3"], 1)

        def p3_dve(v):
            un = {"u": 0}

            def emit_unit():
                if un["u"] >= len(WUNITS):
                    return
                P, jj, q = WUNITS[un["u"]]
                jP = P % 2
                if q == 0 and jj == 0:
                    v.wait_ge(s["gdpe"], P + 1)
                if q == 0:
                    for hh in range(2):
                        prevu = 4 * P + jj - 1
                        if prevu >= 0:
                            v.wait_ge(s["gst"], 2 * prevu + hh + 1)

                def outs(hh, src_ap, P=P, jj=jj, q=q, jP=jP):
                    last = (q == 3)
                    ins = v.tensor_tensor(out=wout[hh][:, 512 * q:512 * (q + 1)],
                                          in0=src_ap, in1=dscs[jP][:, 512 * q:512 * (q + 1)],
                                          op=ALU.mult)
                    if last and hh == 1:
                        ins.then_inc(s["wde5"], 1)
                fp4_q(v, dpk[jP][:], jj, q, outs)
                un["u"] += 1

            for ii in range(NT):
                j = ii % 2
                for dmc in range(4):
                    v.wait_ge(s["w3"], 4 * ii + dmc + 1)
                    if ii >= 2:
                        if ii < 34:
                            v.wait_ge(rs_sem(ii - 2, j), 16 * (ii // 2))
                        else:
                            v.wait_ge(rs_sem(ii - 2, j), 16 * ((ii - 32) // 2))
                    v.tensor_copy(rsb[j][:, 512 * dmc:512 * (dmc + 1)],
                                  pb[4 * j + dmc][:]).then_inc(s["e3"], 1)
                v.drain()
                target = min(len(WUNITS), ((ii + 1) * len(WUNITS)) // 44)
                while un["u"] < target:
                    emit_unit()
            while un["u"] < len(WUNITS):
                emit_unit()

        G15F = g15c["n"]
        g3 = {"n": 0}

        def p3_gp(gp):
            def g3dma(out, in_):
                g3["n"] += 16
                gp.dma_start(out=out, in_=in_).then_inc(s["g15"], 16)
                gp.wait_ge(s["g15"], G15F + g3["n"])
            for P in range(6):
                jP = P % 2
                if P >= 2:
                    gp.wait_ge(s["wde5"], 4 * (P - 1))
                g3dma(dpk[jP][:], downT_d[128 * P:128 * (P + 1), :])
                g3dma(dscs[jP][:], dsc_d[128 * P:128 * (P + 1), :])
                gp.sem_inc(s["gdpe"], 1)
                for jj in range(4):
                    for hh in range(2):
                        sid = 8 * P + 4 * hh + jj
                        gp.wait_ge(s["wde5"], 4 * P + jj + 1)
                        g3dma(wde_d[sid], wout[hh][:])
                        gp.sem_inc(s["gst"], 1)
            gp.wait_ge(s["rsAa"], 16 * 16)
            gp.wait_ge(s["rsAb"], 16 * 16)
            gp.collective_compute(
                "ReduceScatter", ALU.add,
                replica_groups=[[0, 1, 2, 3], [4, 5, 6, 7]],
                ins=[rs_in_h.ap()[0:4096, :].opt()],
                outs=[rs_out_h.ap()[0:1024, :].opt()],
            ).then_inc(s["ccA"], 1)
            gp.wait_ge(s["rsBa"], 16 * 16)
            gp.wait_ge(s["rsBb"], 16 * 16)
            gp.collective_compute(
                "ReduceScatter", ALU.add,
                replica_groups=[[0, 1, 2, 3], [4, 5, 6, 7]],
                ins=[rs_in_h.ap()[4096:8192, :].opt()],
                outs=[rs_out_h.ap()[1024:2048, :].opt()],
            ).then_inc(s["ccB"], 1)

        phase({"sync": p3_sp, "tensor": p3_pe, "vector": p3_dve, "gpsimd": p3_gp})
        ph3.close()

        ph6 = ExitStack()

        def sb6(name, shape, dtype=F32):
            return ph6.enter_context(nc.sbuf_tensor(name, shape, dtype))

        NOC = 22
        NTH = 8
        LINS = 2 * NOC
        ln2fmh = sb6("ln2fmh", [128, 16 * 128 * NTH])
        ln2t6 = [sb6(f"ln2t6_{j}", [128, DM]) for j in range(2)]
        wgu2 = [sb6(f"wgu2_{j}", [128, 16 * 512]) for j in range(2)]
        gpk = [sb6(f"gpk{j}", [128, 512], I32) for j in range(2)]
        upk = [sb6(f"upk{j}", [128, 512], I32) for j in range(2)]
        gscs = [sb6(f"gscs{j}", [128, 512], F16) for j in range(2)]
        uscs = [sb6(f"uscs{j}", [128, 512], F16) for j in range(2)]
        f_k = sb6("f_k", [128, 512], U16)
        f_sb = sb6("f_sb", [128, 512], U16)
        f_u = sb6("f_u", [128, 512], U16)
        f_a = sb6("f_a", [128, 512], U16)
        f_i = sb6("f_i", [128, 512], U16)
        sg_sb = [sb6(f"sg_sb{j}", [128, 256]) for j in range(2)]
        htl = [sb6(f"htl{j}", [128, 256]) for j in range(2)]
        x5b = [sb6(f"x5b{j}", [128, DM]) for j in range(2)]
        r5b = [sb6(f"r5b{j}", [128, DM]) for j in range(2)]
        bnst5 = sb6("bnst5", [128, 24])
        mv5 = sb6("mv5", [128, 2])
        sd5 = sb6("sd5", [128, 1])
        rstd5 = sb6("rstd5", [128, 1])

        def pass_of(g):
            P, c = g // 8, g % 8
            return 4 * P + (c % 4)

        def p6_sp(sp):
            def x_tile(i):
                j = i % 2
                if i == 0:
                    sp.wait_ge(s["ccA"], 1)
                if i == 8:
                    sp.wait_ge(s["ccB"], 1)
                if i >= 2:
                    sp.wait_ge(s["l5s"], i - 1)
                    sp.wait_ge(s["x2wa" if j == 0 else "x2wb"], 16 * ((i - 2) // 2 + 1))
                x5s = s["x5a" if j == 0 else "x5b"]
                for c in range(2):
                    sp.dma_start(out=x5b[j][:, 1024 * c:1024 * (c + 1)],
                                 in_=xq[128 * i:128 * (i + 1), 1024 * c:1024 * (c + 1)]).then_inc(x5s, 16)
                    sp.dma_start(out=r5b[j][:, 1024 * c:1024 * (c + 1)],
                                 in_=rs_out_h.ap()[128 * i:128 * (i + 1), 1024 * c:1024 * (c + 1)]).then_inc(x5s, 16)
                sp.wait_ge(s["l51"], 2 * i + 1)
                sp.dma_start(out=x2_d[128 * i:128 * (i + 1), :], in_=r5b[j][:]).then_inc(s["x2wa" if j == 0 else "x2wb"], 16)

            for half in range(2):
                for i in range(NTH * half, NTH * (half + 1)):
                    x_tile(i)
                for oc in range(NOC):
                    lin = NOC * half + oc
                    for tl in range(NTH):
                        gl = NTH * lin + tl
                        sp.wait_ge(s["h6"], gl + 1)
                        sp.dma_start(out=h_d[128 * (NTH * half + tl):128 * (NTH * half + tl) + 128,
                                             256 * oc:256 * (oc + 1)],
                                     in_=htl[tl % 2][:]).then_inc(s["hw6a" if tl % 2 == 0 else "hw6b"], 16)
            sp.wait_ge(s["hw6a"], 16 * 2 * NOC * NTH // 2)
            sp.wait_ge(s["hw6b"], 16 * 2 * NOC * NTH // 2)
            sp.wait_ge(s["x2wa"], 16 * NT5 // 2)
            sp.wait_ge(s["x2wb"], 16 * NT5 // 2)

        def p6_gp(gp):

            def pkload(lin):
                oc = lin % NOC
                j = lin % 2
                g6s = s["g6a" if j == 0 else "g6b"]
                if lin >= 2:
                    gp.wait_ge(s["e6"], 8 * (lin - 1))
                gp.dma_start(out=gpk[j][:].rearrange("p (P o) -> p P o", o=256),
                             in_=gateT_d.rearrange("(P p) o -> p P o", p=128)[:, :, 256 * oc:256 * (oc + 1)]).then_inc(g6s, 16)
                gp.dma_start(out=upk[j][:].rearrange("p (P o) -> p P o", o=256),
                             in_=upT_d.rearrange("(P p) o -> p P o", p=128)[:, :, 256 * oc:256 * (oc + 1)]).then_inc(g6s, 16)
                gp.dma_start(out=gscs[j][:].rearrange("p (P o) -> p P o", o=256),
                             in_=gsc_d.rearrange("(P p) o -> p P o", p=128)[:, :, 256 * oc:256 * (oc + 1)]).then_inc(g6s, 16)
                gp.dma_start(out=uscs[j][:].rearrange("p (P o) -> p P o", o=256),
                             in_=usc_d.rearrange("(P p) o -> p P o", p=128)[:, :, 256 * oc:256 * (oc + 1)]).then_inc(g6s, 16)

            def ln_store(lt):
                gp.wait_ge(s["l52"], lt + 1)
                for c in range(4):
                    gp.dma_start(out=ln2_d[128 * lt:128 * (lt + 1), 512 * c:512 * (c + 1)],
                                 in_=x5b[lt % 2][:, 512 * c:512 * (c + 1)]).then_inc(
                        s["lsa" if lt % 2 == 0 else "lsb"], 16)

            def ln_reload(lt):
                p = "lsa" if lt % 2 == 0 else "lsb"
                gp.wait_ge(s[p], 64 * (lt // 2 + 1))
                gp.sem_inc(s["l5s"], 1)
                if lt >= 2:
                    gp.wait_ge(s["t5"], 16 * (lt - 1))
                for c in range(4):
                    gp.dma_start(out=ln2t6[lt % 2][:, 512 * c:512 * (c + 1)],
                                 in_=ln2_d[128 * lt:128 * (lt + 1), 512 * c:512 * (c + 1)]).then_inc(
                        s["lra" if lt % 2 == 0 else "lrb"], 16)
                gp.wait_ge(s["lra" if lt % 2 == 0 else "lrb"], 64 * (lt // 2 + 1))
                gp.sem_inc(s["l6"], 1)

            def ln_chain_block(lo, hi):
                ln_store(lo)
                for lt in range(lo + 1, hi):
                    ln_store(lt)
                    ln_reload(lt - 1)
                ln_reload(hi - 1)

            pkload(0)
            pkload(1)
            ln_chain_block(0, NTH)
            for lin in range(2, NOC + 2):
                pkload(lin)
            ln_chain_block(NTH, 2 * NTH)
            for lin in range(NOC + 2, LINS):
                pkload(lin)

        def p6_pe(pe):
            for half in range(2):
                if half >= 1:
                    pe.wait_ge(s["sil6"], NTH * NOC * half)
                    pe.wait_ge(s["h6"], NTH * NOC * half)
                for tl in range(NTH):
                    lt = NTH * half + tl
                    pe.wait_ge(s["l6"], lt + 1)
                    for bg in range(4):
                        gti = 4 * lt + bg
                        if gti >= 2:
                            pe.wait_ge(s["t5e"], gti - 1)
                        for gg in range(4):
                            g = 4 * bg + gg
                            P, c = g // 8, g % 8
                            ins = pe.transpose(pb[gti % 2][:, 128 * gg:128 * (gg + 1)],
                                               ln2t6[tl % 2][:].rearrange("p (a b) -> p b a", b=8)[:, c, 128 * P:128 * (P + 1)],
                                               iden[:])
                            if gg == 3:
                                ins.then_inc(s["t5"], 1)
                    pe.drain().then_inc(s["t5"], 12)
                for oc in range(NOC):
                    lin = NOC * half + oc
                    j = lin % 2
                    if oc == 0:
                        pe.wait_ge(s["t5e"], 32 * (half + 1))
                    for tl in range(NTH):
                        gl = NTH * lin + tl
                        if gl >= 2:
                            pe.wait_ge(s["sil6"], gl - 1)
                        for g in range(16):
                            pe.wait_ge(s["e6"], 8 * lin + pass_of(g) + 1)
                            ins = pe.matmul(pb[tl % 2][:, 0:256],
                                            ln2fmh[:, 1024 * g + 128 * tl:1024 * g + 128 * (tl + 1)],
                                            wgu2[j][:, 512 * g:512 * g + 256],
                                            start=(g == 0), stop=(g == 15))
                            if g == 15:
                                ins.then_inc(s["mmg6"], 1)
                        if gl >= 2:
                            pe.wait_ge(s["h6"], gl - 1)
                        for g in range(16):
                            ins = pe.matmul(pb[2 + (tl % 2)][:, 0:256],
                                            ln2fmh[:, 1024 * g + 128 * tl:1024 * g + 128 * (tl + 1)],
                                            wgu2[j][:, 512 * g + 256:512 * (g + 1)],
                                            start=(g == 0), stop=(g == 15))
                            if g == 15:
                                ins.then_inc(s["mmu6"], 1)

        def p6_dve(v):
            def p5b_tile(i):
                j = i % 2
                v.wait_ge(s["x5a" if j == 0 else "x5b"], 64 * (i // 2 + 1))
                if i >= 2:
                    v.wait_ge(s["x2wa" if j == 0 else "x2wb"], 16 * ((i - 2) // 2 + 1))
                v.tensor_tensor(out=r5b[j][:], in0=x5b[j][:], in1=r5b[j][:],
                                op=ALU.add).then_inc(s["l51"], 1)
                v.drain()
                v.drain().then_inc(s["l51"], 1)
                for c in range(4):
                    v.bn_stats(out=bnst5[:, 6 * c:6 * (c + 1)], in_=r5b[j][:, 512 * c:512 * (c + 1)])
                v.drain()
                v.bn_aggr(out=mv5[:], in_=bnst5[:]).then_inc(s["ln1"], 1)
                v.wait_ge(s["ln2"], NT + i + 1)
                v.reciprocal(out=rstd5[:], in_=sd5[:])
                v.drain()
                v.tensor_scalar(out=x5b[j][:], in0=r5b[j][:], scalar1=mv5[:, 0:1],
                                scalar2=rstd5[:], op0=ALU.subtract, op1=ALU.mult).then_inc(s["l52"], 1)
                v.drain()

            def extract_pass(lin, P, jj):
                j = lin % 2
                p16g = gpk[j][:, 256 * P:256 * (P + 1)].bitcast(U16)
                p16u = upk[j][:, 256 * P:256 * (P + 1)].bitcast(U16)
                for p16, sc, half_o in ((p16g, gscs[j], 0), (p16u, uscs[j], 256)):
                    v.tensor_scalar(out=f_k[:, 0:512], in0=p16, scalar1=4 * jj, scalar2=7,
                                    op0=ALU.logical_shift_right, op1=ALU.bitwise_and)
                    v.tensor_scalar(out=f_sb[:, 0:512], in0=p16, scalar1=12 - 4 * jj,
                                    scalar2=32768, op0=ALU.logical_shift_left, op1=ALU.bitwise_and)
                    v.drain()
                    v.tensor_scalar(out=f_u[:, 0:512], in0=f_k[:, 0:512], scalar1=2, scalar2=0,
                                    op0=ALU.min, op1=ALU.add)
                    v.tensor_scalar(out=f_i[:, 0:512], in0=f_k[:, 0:512], scalar1=0, scalar2=0,
                                    op0=ALU.is_gt, op1=ALU.add)
                    v.drain()
                    v.tensor_tensor(out=f_a[:, 0:512], in0=f_k[:, 0:512], in1=f_u[:, 0:512], op=ALU.add)
                    v.drain()
                    v.tensor_scalar(out=f_a[:, 0:512], in0=f_a[:, 0:512], scalar1=26, scalar2=512,
                                    op0=ALU.add, op1=ALU.mult)
                    v.drain()
                    v.tensor_tensor(out=f_a[:, 0:512], in0=f_a[:, 0:512], in1=f_i[:, 0:512], op=ALU.mult)
                    v.drain()
                    v.tensor_tensor(out=f_a[:, 0:512], in0=f_a[:, 0:512], in1=f_sb[:, 0:512], op=ALU.bitwise_or)
                    v.drain()
                    for hh in range(2):
                        g = 8 * P + 4 * hh + jj
                        v.tensor_tensor(
                            out=wgu2[j][:, 512 * g + half_o:512 * g + half_o + 256],
                            in0=f_a[:, 0:512].rearrange("p (o t) -> p o t", t=2)[:, :, hh].bitcast(F16),
                            in1=sc[:, 256 * P:256 * (P + 1)], op=ALU.mult)
                    v.drain()
                v.drain().then_inc(s["e6"], 1)

            EUNITS = [(lin, P, jj) for lin in range(LINS) for P in range(2) for jj in range(4)]
            eu = {"u": 0}

            def emit_units(upto_lin):
                while eu["u"] < len(EUNITS) and EUNITS[eu["u"]][0] <= upto_lin:
                    lin, P, jj = EUNITS[eu["u"]]
                    j = lin % 2
                    if P == 0 and jj == 0:
                        v.wait_ge(s["g6a" if j == 0 else "g6b"], 64 * (lin // 2 + 1))
                        if lin >= 2:
                            v.wait_ge(s["mmu6"], NTH * (lin - 1))
                    extract_pass(lin, P, jj)
                    eu["u"] += 1

            for half in range(2):
                base = NTH * half

                def evac_tile(tl):
                    for bg in range(4):
                        v.wait_ge(s["t5"], 16 * base + 16 * tl + bg + 1)
                        v.tensor_copy(
                            ln2fmh[:].rearrange("p (g t) -> p g t", t=128 * NTH)[:, 4 * bg:4 * bg + 4,
                                                                                 128 * tl:128 * (tl + 1)],
                            pb[(4 * tl + bg) % 2][:].rearrange("p (g t) -> p g t", t=128)).then_inc(s["t5e"], 1)
                    v.drain()

                p5b_tile(base + 0)
                p5b_tile(base + 1)
                for k in range(2, NTH):
                    evac_tile(k - 2)
                    p5b_tile(base + k)
                evac_tile(NTH - 2)
                evac_tile(NTH - 1)
                for oc in range(NOC):
                    lin = NOC * half + oc
                    emit_units(lin)
                    for tl in range(NTH):
                        gl = NTH * lin + tl
                        v.wait_ge(s["mmu6"], gl + 1)
                        v.wait_ge(s["sil6"], gl + 1)
                        if gl >= 2:
                            v.wait_ge(s["hw6a" if tl % 2 == 0 else "hw6b"], 16 * ((gl - 2) // 2 + 1))
                        v.tensor_tensor(out=htl[tl % 2][:], in0=sg_sb[tl % 2][:],
                                        in1=pb[2 + (tl % 2)][:, 0:256], op=ALU.mult).then_inc(s["h6"], 1)
                        v.drain()
                        if tl >= 1 and (half < 1 or oc < NOC - 1):
                            emit_units(lin + (1 if tl >= 1 else 0))

        def p6_act(a):
            for half in range(2):
                for i in range(NTH * half, NTH * (half + 1)):
                    a.wait_ge(s["ln1"], NT + i + 1)
                    a.activation(out=sd5[:], in_=mv5[:, 1:2], func=AF.Sqrt,
                                 bias=1e-5, scale=1.0).then_inc(s["ln2"], 1)
                for oc in range(NOC):
                    lin = NOC * half + oc
                    for tl in range(NTH):
                        gl = NTH * lin + tl
                        a.wait_ge(s["mmg6"], gl + 1)
                        if gl >= 2:
                            a.wait_ge(s["h6"], gl - 1)
                        a.activation(out=sg_sb[tl % 2][:], in_=pb[tl % 2][:, 0:256],
                                     func=AF.Silu, bias=0.0, scale=1.0).then_inc(s["sil6"], 1)

        phase({"sync": p6_sp, "vector": p6_dve, "tensor": p6_pe, "scalar": p6_act,
               "gpsimd": p6_gp})
        ph6.close()

        ph7 = ExitStack()

        def sb7(name, shape, dtype=F32):
            return ph7.enter_context(nc.sbuf_tensor(name, shape, dtype))

        hb7 = sb7("hb7", [128, 2 * IFF])
        hfm = sb7("hfm", [128, 2 * NOSET * 128])
        wdb = [sb7(f"wdb{j}", [128, DM]) for j in range(2)]
        x2t = [sb7(f"x2t{j}", [128, DM]) for j in range(2)]
        outb = [sb7(f"outb{j}", [128, DM]) for j in range(2)]
        bnst7 = sb7("bnst7", [128, 24])
        mv7 = sb7("mv7", [128, 2])
        sd7 = sb7("sd7", [128, 1])
        rr7 = sb7("rr7", [128, 1])

        def hfm_slice(tt, sid):
            o = (2 * sid + tt) * 128
            return hfm[:, o:o + 128]

        def p7_sp(sp):
            for tg in range(8):
                if tg >= 1:
                    sp.wait_ge(s["t7"], 96 * tg)
                sp.dma_start(out=hb7[:].rearrange("p (t o) -> p t o", o=IFF),
                             in_=h_d.rearrange("(g t p) o -> g p t o", t=2, p=128)[tg]).then_inc(s["h7"], 16)
                for tt in range(2):
                    if tg >= 1:
                        sp.wait_ge(s["b72"], 2 * tg)
                    sp.dma_start(out=x2t[tt][:],
                                 in_=x2_d[256 * tg + 128 * tt:256 * tg + 128 * (tt + 1), :]).then_inc(s["xl7"], 16)
                for sid in range(NOSET):
                    kk = NOSET * tg + sid
                    if kk >= 2:
                        sp.wait_ge(s["mm7"], kk - 1)
                    sp.dma_start(out=wdb[sid % 2][:], in_=wde_d[sid]).then_inc(s["w7a" if sid % 2 == 0 else "w7b"], 16)
                for tt in range(2):
                    sp.wait_ge(s["b72"], 2 * tg + tt + 1)
                    sp.dma_start(out=out_d[256 * tg + 128 * tt:256 * tg + 128 * (tt + 1), :],
                                 in_=outb[tt][:]).then_inc(s["o7a" if tt == 0 else "o7b"], 16)
            sp.wait_ge(s["o7a"], 16 * 8)
            sp.wait_ge(s["o7b"], 16 * 8)

        def p7_pe(pe):
            for tg in range(8):
                pe.wait_ge(s["h7"], 16 * (tg + 1))
                if tg >= 1:
                    pe.wait_ge(s["b72"], 2 * tg)
                tb = 0
                for tt in range(2):
                    for sid in range(NOSET):
                        P, c = sid // 8, sid % 8
                        ncnt = 64 if P == 5 else 128
                        gidx = 96 * tg + tb
                        if gidx >= 2:
                            pe.wait_ge(s["t7e"], gidx - 1)
                        pe.transpose(pb[tb % 2][0:ncnt, 0:128],
                                     hb7[:].rearrange("p (t a b) -> p t b a", t=2, b=8)[:, tt, c, 128 * P:128 * P + ncnt],
                                     iden[:]).then_inc(s["t7"], 1)
                        tb += 1
                pe.wait_ge(s["t7e"], 96 * (tg + 1))
                for sid in range(NOSET):
                    kk = NOSET * tg + sid
                    pe.wait_ge(s["w7a" if sid % 2 == 0 else "w7b"], 16 * (kk // 2 + 1))
                    for tt in range(2):
                        for dmc in range(4):
                            ins = pe.matmul(pb[4 * tt + dmc][:],
                                            hfm_slice(tt, sid),
                                            wdb[sid % 2][:, 512 * dmc:512 * (dmc + 1)],
                                            start=(sid == 0), stop=(sid == NOSET - 1))
                            if tt == 1 and dmc == 3:
                                ins.then_inc(s["mm7"], 1)

        def p7_dve(v):
            for tg in range(8):
                tb = 0
                for tt in range(2):
                    for sid in range(NOSET):
                        P = sid // 8
                        ncnt = 64 if P == 5 else 128
                        gidx = 96 * tg + tb
                        v.wait_ge(s["t7"], gidx + 1)
                        v.tensor_copy(hfm_slice(tt, sid)[0:ncnt, :], pb[tb % 2][0:ncnt, 0:128])
                        if ncnt == 64:
                            v.memset(hfm_slice(tt, sid)[64:128, :], 0.0)
                        v.drain()
                        v.drain().then_inc(s["t7e"], 1)
                        tb += 1
                v.wait_ge(s["mm7"], NOSET * (tg + 1))
                v.wait_ge(s["xl7"], 32 * (tg + 1))
                for tt in range(2):
                    for c in range(4):
                        v.bn_stats(out=bnst7[:, 6 * c:6 * (c + 1)], in_=pb[4 * tt + c][:])
                    v.drain()
                    v.bn_aggr(out=mv7[:], in_=bnst7[:]).then_inc(s["b71"], 1)
                    v.drain()
                    v.tensor_tensor(out=sd7[:], in0=mv7[:, 0:1], in1=mv7[:, 0:1], op=ALU.mult)
                    v.drain()
                    v.tensor_tensor(out=sd7[:], in0=sd7[:], in1=mv7[:, 1:2], op=ALU.add).then_inc(s["b71"], 1)
                    v.wait_ge(s["kq2"], NT + 2 * tg + tt + 1)
                    v.reciprocal(out=rr7[:], in_=sd7[:])
                    v.drain()
                    if tg >= 1:
                        v.wait_ge(s["o7a" if tt == 0 else "o7b"], 16 * tg)
                    for dmc in range(4):
                        v.scalar_tensor_tensor(out=outb[tt][:, 512 * dmc:512 * (dmc + 1)],
                                               in0=pb[4 * tt + dmc][:], scalar=rr7[:],
                                               in1=x2t[tt][:, 512 * dmc:512 * (dmc + 1)],
                                               op0=ALU.mult, op1=ALU.add)
                    v.drain()
                    v.drain().then_inc(s["b72"], 1)

        def p7_act(a):
            for tg in range(8):
                for tt in range(2):
                    a.wait_ge(s["b71"], 2 * (2 * tg + tt) + 2)
                    a.activation(out=sd7[:], in_=sd7[:], func=AF.Sqrt,
                                 bias=1e-6, scale=1.0).then_inc(s["kq2"], 1)

        phase({"sync": p7_sp, "tensor": p7_pe, "vector": p7_dve, "scalar": p7_act})
        ph7.close()

    return nc


def _rope_tables():
    t = np.arange(S, dtype=np.float64)
    inv = (10000.0 ** (-(np.arange(0, 128, 2, dtype=np.float64)) / 128.0))
    ang32 = np.float32(np.outer(t, inv).astype(np.float32))
    cos = np.cos(ang32.astype(np.float64)).astype(np.float32)
    sin = np.sin(ang32.astype(np.float64)).astype(np.float32)
    return np.ascontiguousarray(cos), np.ascontiguousarray(sin)


_PROJ_B64 = (
    "eJwVmndYz98bxpsaioZoaCgJRaSlz/t5KsmeSXYyMjIyS0Za0tSQdhSpKAop+ryfB4WKZMvW15aVTRo/v7/Puc4f5zn3fb/u6zr2"
    "Pw5Ji42OQG5fLzg976mwdP8p8ZSmITxcGcajbyujcEmEQdX9sXlELO30nIJ7HoUhHHXDV6pmYlBhf3xTWw9jcozgkH44D5mpipqv"
    "h9KhHTP5yI5SHvc3BB6nDqVBP+fh+4Tt7Jt1gTZ1uMFK31jM7fUX2tekihcfnqbCwT9g6O5cYXDDNnI7kQ9/fvcHuZJ62ttkzBPS"
    "9Mm820I4mKBEQsZuodv83SDxcebpFXmwZMg8tNXJgfQrnVSxvkvY0tyT1D4aUl3hCpz4cjeofZGhhNwhtNE7hazeRYCL8iw82y7h"
    "1rqenLbfGbP7hlPwIWNxce1Ztn20G3stjOCvySsFR38dHtp+VCycbcYX+zXAm17A2++vZ/cBKXhZ66dk6nQfDh+sRG2+T+B6tCOZ"
    "ya3HS0WIj8+MwKeOJznrItPtlpvcOnUMrj32AB9+UEDlPgp44d959q9q6d6WA6DS/wAtv7cQm161i/a7/gix6ydBN3s77sjyFvIC"
    "tbBlpx6dWVjJwRlXYcg+Q2zOy5MmrojkkA0ddNEBedzN/bB2cCfcexFPlqmu6FdzWYgwPERPFg6EOxODIe3vfUj3j6TQYf6UUKlK"
    "+j3viL2wFL7KBGKcWTXLhRyA25IXEDbJGYeabeEXckbCg+QtOPOxDeu8l6EhKk1o9jUSfvytohR/Leyxt1T8nTgZI4MDqKWoFzbd"
    "6wm7k36B5etNlPLmNNs/s4X/ordDsbolFsz1ETTcb4l/r1lT+UZPrOWhtP/wPdYNfo4/XW3w+oL9UH97C1reLeZuUZM4+ZgcZHXu"
    "ExteNMIvt3U0v5s/pDoYwO4D2jx1pjxbPVoqWVNkh993Pqakjbo8+usINFLbxl3WG4SKtnwwHhYhBBqMEb1ddXjvp12gLE2GgimT"
    "hB2ZHnxlpTVoyZby8OkS3nCpEeaOiqSfziS6dE7nlb2OwOVzneJBWTM+cKkHQMFXCmlbweP//h6542QoDgnYjK/ak7kiIZv7hYfx"
    "80WzadEeZ340TVtoP9Md/B6eImmpCuvfH8lLWk3wzDRzVL6vixrjL2Lynxiez7v4kZcfr4/KIvsaRTqw0xHXF/Xj9bvz4PEKbXRp"
    "OS2k1SmyzKB2eLrIDo1fHQLXsGZ2fe7Cs2SD8Le4CWWUjXljjhl+7JCniIzJ7KV/V/hjmS3MHd4bYv9sEDOOaGP/Pc7CnbKXojAr"
    "H0Me24v26sjfjTdjStY5qli7ncRpE7GiYCQu4m54/PI4qiicjwezevAQ/e40LjAPfhXuEsfWHcD2ZQm8Vc6Jj2inoVdNKj7v44mk"
    "kE+tSqqo/NQeldJ24z7lv4L+yqGiWbwMBP97UdYT/PnunlNVZSP/QkZJO53dexsqGi+yrfxujByrDh+3fBXG+c7DNPG85EnfDLa/"
    "30broodi7e0Tgnb1fozU3IXNE805/mw+Pv69lC3uHMSucwH865EGN7shHpt+AeWCHtHwrSM5cuB+5DG3xECrh/RE8SVc2KGM1Skj"
    "eWD9HpD5uh+CXjlSxMe9nLrUBO2G+ZHv9AZ4ZvOXJl9MpXxHE/AwmAFlqyZL9799AX8to2D1wBDq/nwN/Db9QatKbHhxmzqkxJyh"
    "ldunc/4dH2hc64E+3q/h6Kty2JnZJW6oyeORVhV0Q7WK3uRn0d70Ouo/2fSfzuPQ71iOsPWDHhct2UfXu7rjE59k3r3uE8h/9GOX"
    "cX1R32ghrRzuAUO/LuSihqE82deSSlqG8/KJx+h2bSK+7ou8f3w2fFbNx3XvFyBqqKGnkhkMPV9Gg8deIWuz8WSkP4BsbmhR4yol"
    "3lc4ixWGzcZ3oRL89n0p1Cx7Qv9N6Qff/6bxMQUl9HQfw7pd5XTMRI4nl7GQtWo+h06M4q0GlnRB5g2oOxrgG+olOnyaKpyd6cbO"
    "Y81gwvJXsHlcHBodbIHDO0/Tt5ll0pJziJFuRvgpyQIT9eX4m8Fr0h1UCztUO0mpaA5/tLHA060tdPVSBFs5n4ZWnUC2fPALtHVj"
    "Wbm+D0Tst2VcspczL1+kpkEZMKJ6nbRlTTr6Onpy2YoZNLAolVz6rCLdqLmgUK3BvrYlEGRSy5sfTcZk9zyKzkzmjfP1cJlpExR+"
    "/ihpWZfJV07ZcIyFK68aZ8jLnQpYxSyCFxyfw79WRVDl9dewtOURvDw3m3/JbGfJgjsQYFGIet7KOKS/NbmYHxN+uRxzOnfuoViy"
    "dL946WogtRxfwcbv/lBSRR++2LGRh6yXUu/tSjg06he8XTSfj1sl8tSyOIz3SuF8v038MVeWF5ztx4EGcyEmuQViIqbBhZHd8M2X"
    "ebglYY7QqHhaotFtOXbH8Xh7pSU8iNZAnFvCft0PwehHDVTb8QDf6XuD9281Tlebww8s0ngzdlDemwncVqCCagO/wsvJLngifTLb"
    "Gdayw+1QGvLtnvDjgRWeuP9YOHCkmEKu7OCUrD/0fXwmxQzTwlvN0dxoHwEfg3thrNdDivmRAtHXHsOOeUeFz9kyeM+/AXv6RUPu"
    "CCc20h3ECdnyqHyjDZ4lLmPrS4P45IEWyL7fG3W3R7LyHF2c1VhGdLc7db4LgcqRf8Qd/ifRcnqYdK7jUOh6GIVNAz+RSqgsvNPY"
    "B8ZtMrhcsxyPDVNge725+LphNuZq53Cj7wj48GUGJdhG4/bZdwV9r9nYZPhOsF3fjSM/WfHWgIHsZSDSY2cP1HuTBwLvqhq6fhjs"
    "dFfGPk+sWL/fanyiuxJkHAxxV6M2zhowj6s7D2DYhud0tU8SH7wWBEqD7lP/PmHY+bICClNO0Z7KJ3Ts7QUMTbcTLtXuQdUPNVA5"
    "+SDv81bkK57GeClMHk1Ee340OQetS20xfmkYJ//MwHH7LlFVUypmPjvAHyX3adMVOUi/9YCS9QcKCzVy4UlxuzC3VsAcK0MuKUrC"
    "pTmPqJ8aiLx4PO1dHidM2TIF6ZIyBHxMorGrwniylj0GqneTyBlbiDM7F7Bm21VB1+EZhmwp5p+fB3Jnl5Y4oTwJnmX547VrtcIY"
    "Ky8qqSLqOpCFUaGh/N3sOdWWREDnlLMUsc4UWy/HoFmkHFk+PsKqwdli3IEpeD5llkQu8hF4+4+i86PjcFGcLc5Y5IW+AeGoxKk8"
    "f10AR9oZovKJJjZ+3EeqtWIQ+hbvpBa1+fhmejh33Y3Bce+rsUB2FP4MXIdXrfuibJohFd3QZP+dDUJe4geaUnKSHky/j1T7Xtqs"
    "voJ/D7dCpRBN/NpaAut3ufEPl1Wik7MJuHMSvh+QwiE9CmCBz0lx7bcg1puai4PrLwn338vAd5NQsi5Ux+dX/Why8nY+mZ4pOi4b"
    "hU3tWvzQzxJePi0UJylGCHlyDTCYW8TQ7oESm72V4obrF2Hb8XYeKPVhnxma/PXuWkx98gJM/yzgDK/lqPnbgp0vldKfne/IZJ0X"
    "2s2NwPSdqzi6WZOtcRCaKXvB72ZFNLvxh/ppzMeZqUo4s+0Zd1urDp4XX9NJtWOwdU4BvQpzELp9yoeg7y/BcOMjMrWXpznL5nPB"
    "u1rhQAPRjJm2KI1koeDkHrbxrOefMUewJUgRQ0oPk3fdPtxVyFSaoIUhLvloNOApOKhcg6Z5hynx/k6e/qeSdG6Pxbg/LrwPI+nG"
    "0ydkHngOvyZ9gA/FV2Dw9Q6auVADNx4qxNh5ijTq9VFcHhHPOS9iaM/ZUvHU4UH0X4I9nis35LfZs4RTNp3UNeq8eExyTKidMA0L"
    "xaPC9aORsNB8D7Z63Ba+r0yhJW1pMGq0HKcqLSar8tVwZosG7z+HfGuiLsdevQ1TLozhntibpPfUUK/+lWCp30FHhgWChc5e2LvD"
    "Ea1MZ8GIcjeMWeyC8mcvg2paNWXLhcCv1Jd0xH4afJSZg4Gb3cB8ZodkbkQSPQvywZP2V3lzuwJ3vkyEE6MT6WpsJP5ZXCyMu34B"
    "r+a4w7fxxCah/qJLvTVv914tLK57AdfepHJbggx6zHCBtMFxkFr3FKK6WqX9vi7APns24LOR46ljiSm9LeuOmbERUo8b3uKp1TKU"
    "9tQCfZ2S2Dj2ong/Yitc9xhCCuIxuu+QQW2XEOpOdoLf0hzs4TCY5DbfEG7ILOWRB2JQ2SIOp220R/nZ2TR76WtQaxfwzHTARWeX"
    "YtCcGDR8vp0DzEIxmg2ccl50/tP4bOx11BbNCqeIf6+m4Gs5Y76r9R0ieyzEpccnUd4/reX/ekK3osKw9o4Fnss5SzccLqHlRxv+"
    "qj0Um/I34QbbAWQv/1M0+dtEf7N3wly1VOq+7564xy8Ezz9bjFYZNrxs+Fg0DgiEL7462DLksKBpZYpb3g+hKasI3oenwYawaei9"
    "aTQrt8rCiBMJ+Fm3L8robcZchWXs/uYs5j+WY+i8TvZJ3c4duauP7/NP0CJNBayTDER/jy1sW68CC+77wDvfdBziM5SGsj2DYl9U"
    "DM+jnQm/oPfEVFh/ax5UD8tkbctedGvFSvHpozq4tk2AXhIXuvtZF6e+awPhkpxYW9ULJhQ2Sf3XvIXXdRbCvJtB3BXUm0e63IND"
    "AxRZbc15mrdMCddHF/KanwoUY7gU1gop0LllINUkR+HbFQMFvRWPSKFsK6cOcsV7NkMweKEix9qMwGA7JxjQ0wT29JxFybtXcXdX"
    "U5TxTeR6vdW8r/so3N/rLJj4jILYw8pQWbKP/VaY8J6jn+HuEWN+ekUPJSW9+UfLF6gaq41uBpqMWlIuXD0Wj+xqgGXpzbTeYxB8"
    "fKjCjmkhlFecKoTeWogz7jljdMgiOGR/HUZNrOQNA4LIN2A7JEbtZqNiINttFdj92p5/2Wwl5kWrg0XrYTog6SBlxYOgXJsEOe/V"
    "RYefKhA7KQ0UNAn2P1TGfvnOwu2ZV9CgXA8TboZi3sAbMNNoJ7o+//fWRi+k1ZlGPFLw4rZZjeCUoSaET5iFPx/8xz06dKWRnkdo"
    "3/skrN8WSLmn/8DebofAUaEbrh+fQxGql6jHzEWgnvkUrRYnYuHWOP5rPZCWB/dH7RfPYNjFWp7Z5xwvHroIN+1wYQ+5Z5KFWgq4"
    "VvIdtg+UF0ZajGO9cQ9wnbkevpyzj9cEL+eSf71kgagNrTtbQfjpixV5JgAPRJK1yubQUx7kkxDO027kceBrL37VWgWXxnjx5vo1"
    "9NXiI/XMl0G5BjmuOnVEqJ9XK5YMu40RuYMo+lwP7HFYSu8L7lKKwwhM2FJBfnNrOa7GWVz3MJvnv3TGcv/uePR8KviPrWTorsFJ"
    "ZbPEMpV74OqiyvqWbbRItppMch5C+T9Nrg5g1FWfBjrS0dh5vYbXzUnkOTN6Yv+wDAicbMS75aaDo+Zb6jsln41y+sKiUXZYN34F"
    "nnR7SjEDj8NCjZ9kVvtRtMs6SqP146D6kY4w91kE7kj/BWHtl0UN8T9YURrMZV2baEBTX1a5353X3ZTw64v/2PHDSRLLBN5tac5O"
    "L6Lo/j4VPvOrwdH+sTnvHmOF3U/40NJtGtjv7Hw85xoIa8YlwvrxSjhbTo7LAzJEj5Qw/PFtBK/LtOKe8zxxYmQHjAhuhzVqf+DX"
    "tAUspO8RTvwN4w3Ke1jOLhAd3wzjyj3XxW8VObhUcwpuuqKDc263CvlXwhiEm7RS2I5fPPph6EpVdH3xhOy8wmHu6z28f9MEjJR/"
    "CsMKJkubZnhhdwdTPnf/Cf/dMoGjvVxhRfUN2ptkzgU1N7C7/3tqV/fES28eUtXJFNF9ZSc05S9Fj20zuNPHEPfpDEJbS2MqPz2O"
    "oNIAZ3+yY5fRObyflDERIrjn/S/w9EEMvZg5jfPrYoWAce8g+YoR7m2bzxltvwTLzlLQMvkBGqfn8qdUCbRtaIByaqbijgF0J3c8"
    "5sM0yarnTyFyeT03OCfhqpr0yh0y+nAofyArvTrM6sccqWzebvQbuYrirdNJ02ucxMBbg/uu+Ejx0VpQ/dcBQyf35CXNZ4Q/59Zg"
    "ZPolIWqarPPefgfEkS978YT5IqtMdMDa6Iwq5csd0uV/PTFw/Th27vIQvkjSeKvDabAz+Q/gZy8cMXg4ng2+jXpHk/jC/HrqcN4F"
    "KmcnCXGva+j7yyGsNCwE1765LDTOLefpmXHwePJJUl83CP3XvyKj6cvwwydfcF50HU/M28PSLTH8bX6hZGv3HaT7dAHnDvLmgudd"
    "YGl2De2nm+GnrXfEopLv8GfrKvH5tHDUaN4DE5t3kGEvFxz3Jg37nXYA8/pEtBt+jHd66OCOY4Z4JL0HXu3qz4cvKeEgg3DqtvYM"
    "lOpZ02OlV3B+s7KoM9EZKo/LcKpid1riIYeaGyOpwTAGfz9op6S2UtqlsJu70s8LkSMIjl6eQqo+Cqw/XxQSNErh0I6hwrqb8lR6"
    "dC9e1FkobGpyADflndxoHsFNCsTJeJyl/Zso20xV1Mv/1x9vh+KfWV5YOHWMeEDUHvl87hFRaYaIj02iWV7lHHwbmQHPvvWjjlV7"
    "xLzsg5S5eB9OVp6O/63SxXUGUTgkfxeM7HWZOuZeFB6WWwnqloYw0vMKTn72gPv3dsWao4ep/WqetFh9C50yjeL+xzPAU6aKt94B"
    "3BKWAPYFuzm97Aj+V66IDh6W1PNEHC3XFXmyUqfTt8LDYvHAGkEvfjo9eGuH62Id2a92I6wRDtMxeg5Jjj3xzJ0AuO7QHx+rvmfJ"
    "LA/+s/0AdSnKi0d1+nOivD0+zg7lpd+r6NbpSCyO/yBKLhlA3XM7lm2ph3u6vVl70C/KDGv7lz0RHGOZSPuP1gD6nRbVl2UIhVmR"
    "PH/LS+jmMgpyXRfiaL0w2JM4He2PpcN493LRonIjFUnT2bZvLmVVijRZ7jt5GDVCiViKfxXc8LJXOclpWuPHahVucpkK418+p9Y+"
    "i7Fft99wa/lMnuOlwm/+nqDwit68vZsND5/6id0XFYlP49zZWceVO4ND2GNdGznX/6FhZj+Fq53ZvPGTMSbH11HvVC3nPva78cD5"
    "J3Bj0RPR9Y6RVO5sCtbNK8ey+8DJLRFs9GIrpu8ay/bnZvH+3se5MTOVLC9EkEewEwa0pkgOJ2qys/sXXvJwsdSqtyvVLXkrJI4e"
    "xPukHyDshhHeXuPNeyxPsc+K9ZSY/hSOHzkkFL3RxX+CY8++Xuy9KY8Co3whLmsYTdHL4rIMWag1eC1smRaMQ4NOc232RZLhpbhr"
    "+ED21Qvlz27HIP9XGContognvhSIyxvTaZXtQErtflAcoW+KGRkfaZxPCUTsWM1eSk4cWbYYtvcJ4GyrrZhTaMVtKj6YFqyC0u46"
    "6FuWzwVZSnjTu4a22YwjDbLks/+4PY8NaMjaofB8ZwW1ztxFU8fE8tY+lyDSbz562E/CR46j2DxRC8cON2aDWxv4Y/t51ogeD+LO"
    "ZOo/lARdmSoh4iqwrN0YuBuixltmyFHnnU76fuAQv/F9KQr+smwRvwIhrifPLnWjDQ7p1KvBXbjmdgfKdPLEyj66uLd1hTh9y1zh"
    "V2IMcR9fKo/II2O3AD4/JZedpjqR5flGMDsxHktT9vIp3WGYXn5WWB2uiv32KePelUOlt0q+iS+/XiG/n79p3oYtHOV6lVLHjcYC"
    "W1NsNtiIu36+o56DhuLSdFX4/vI7YahIv9vN+JnDatz0XyO4pPXmhxkVTnqBCYKqnz9In4aT7a9mOtKhjJrNB8XNs/Zx5d3jaDV2"
    "On795sV3uh/B1TdP8kC33vj1S4Xo2+oAx2RMWcVRhfUyzlOzkirnTugBdpbs1KT8FEZtK+EzQyc5PY6SAVODUIn+5cOkl2yKvXZU"
    "CwvfZeKdjrto3XKFu1+Lpf0V/bG1JYnuqAyHAUO24OKMOFy4bRXLGyvxypuKdEXnj8Cqi7EyVYO3KFRiv0JN6vfDmMcsCoQDg6bi"
    "qIC10Ngm4KaWUbzg21VsH2DPac0/RNUF8tye149T1qSLvccr4p1tunig/BvcDUzEZRbJ/K5rhfj79H345N74r5On0A3rXXDYbTf8"
    "N2IXTvCR8IE8VXxWbg7WIx4KPZyn8+9dSqBQ1YPk/C/Cd25A5eTF+Ec3GbwWP4H29inCue2jBaW45fDN7QS5qR8U/3jVUPiFmTz2"
    "xjbS97haZXDOUHxZeJCt9sxH3enp+OJCJLmUDUZ7u8dQNTdPUmytx6GW62jYrhooCDDFWu9DeEr6m45umgl37tniy47x4FajzXqb"
    "GvDPxVJ8F6YOw+9r8fB4LyHl/nwxxzsO15hfkbwLfkb9osKgIfgocnkCfOHDZKnaib36FtGUhgK8b30Kluy+BVZhKyjmR3f4WV4G"
    "Fw74kWxyBg90CueAVjv2sQ6iG4XNtChMAZem3cObZ6/ATm0dHjdNQikqc/k/zxRJ0mBnLM4po5W13Vn624DYcz+F6UykQjlFUsl9"
    "BQP0fkPlkJ8ctN5SeNp9CV+uXiUOH3iCHoaNgx+rY3hJlIJEcmM91Bn4sOuSUfSr6SNm2ijyunyBm0+owVn3WJ677QrNirrGjZeN"
    "YNusUTS4tkTiYbORF+iw5OuWKtwYUA3T5Su5qpYpQcsVr18I5MajrpgSvxcX3enOCkc1uPdJDTK8p4vfBANUOHuc7WtmUdOH55Rv"
    "epHeFLjTLtkENLhbBCv8u6Hx2ETSO3JFCN09BJLXl4rL6N/8p1vxs5ffwTtZHuda9sUABU00+m8F13XfKio4DSE0mcbmDd1p+NWP"
    "8MX5AmnGuOM/14HZjw/xC9OJeHOQGbfMMudNyr78uzpZnP63gbIMv9KFT1UU3GcVPZ9VUjVf9roYMPkTLbmYxOkP4tDqtDtrrs+g"
    "x2mROHLFC/pkdZPUP6ZBYNYZOFpwEJSiZvEPaxOc9eQKPbgbjPJeXyhduzsGFkwCM4d1+Pnzan51xJu+/zgjmb/hOm7yy6Py5UEo"
    "87oLvw86LKaeSSFDixcgG1HMtpXZ4kTfRiEhYAcMMAyS3iZDPFETAg0zPvD7vvNReaAeVpRshoE6elgQ/1FMq3LBZc5GHBw0nEaM"
    "/SPGW8aKQxu3YYvDIj7cX5bNu+Ww3lN98Nmsh9V1DbDoy3pQDzmIBhEsHpC8FA9ULYLbnyPIe2oDffwwDg9EGuHy1hisTvohlJoa"
    "C6F3plKDTH9USJBUDTT6Drtfb6bEOE+4u+8qLK1biVst+mCqqhSKeh6l4BPtMFGB4HXXCV52vRsntfzgu8vuQfignjj68wNK1XaD"
    "SWNWS2+8XgGO0b7w5W4PCNhoy3tL4lGjTzpkWMrw4zM3cEeEKdb1LpMMXK3GC4LPkUKAHJr8HgnGbklclgz8qV4R9SwZfq26QSVs"
    "yrfdanAoL6HQZYtxYIIZHplST6cPucNp5T7isLtOgv7szZxSI8uDR1wW5lz6IE5SyoYLuw1E02AdbraUSrdF7uTpMbGCf6cjGo2Y"
    "yGr9u6HduxicJJ38r74qQVBnJBnVDGCF35pcVW2KW3zLaP/OFEo69RWyLy7Bag2kXod8xd37ntH37R3Q879FeGHRTfrED3HKD3l0"
    "ykolbb+ZfOOUJd+/kyG2/ZA793ZWK3nJOXB7aze8IZ2Bf55GC28i+nOW+2P4qzSIzQ0suHBwKfQ/1AFfnu9k6Z8GYeurIGnzwTno"
    "6fAeBp9LgnWNm87+3JrFn4MqcIe5FVdkpLPK3td0In8BZY3aDS4VmfRf4n4OkUTST0k3envZX+LcH/Fvxg4a/3kILUuWwfqsJCjd"
    "acujH/Rlm4fHeELsAvC30YH+0Xq8Y+1HofJQPATaJPBmy4/C5v3n6cSLLWQ14RfsyRjO2cMCwP+aK/sMysMJ9naY+lhAieUHiIf1"
    "KJ9zgsqmHYWRsxXZN/4mFdvp4zV3dVp9I4R8D72ApXe78Qk9TTSms1TYV5frHSvp+OD3OKa7DFPGGmxySwXL1km4qZnIo5ce9Nmd"
    "ze5Kj6HF6ClZ65Tzjy1buGVSNU1p649qFRJWsLgHF+b90+cVYBNnLxaCy1hWzZ2XTKihjcGh/P5zPZw3/wHHFHdBXukyUvSLhSKH"
    "BDTJXov5ed8oz/ICr30vIO8ZjPXD5pH1wak8wccJ9UqHsM4+E/wgmYj7d8niCxtZbF57QuitGgv7dZPo2nQ1fjn4OC1IOlnV9bgb"
    "r9r4me6PUaGylr7gfbOM4pwOQiM1om3Mhn9M6YlfD76Cvn8Q1bJL8O6k2ZCWsY5Dlv1rdLJ/oNhFBVqDptC0YU2c038nFDsLuHKi"
    "F3lbyPMz+1v8ZshfOhN0kt9WXmTlk8MheFsgao/VwxHbDoH/7Cn8UGk/l10U4OjNBD7ZflnSPsoTFybpw8JANy7l35KDZ5IRPfpi"
    "14gJ+HDweL7yIBSr1YIlihdduFHnH0vnx/GUtY7od14VR32eQPF3XSllhZyzeW6QuMLLgCfEvBeKMqPw6Y9Yav7yiwZf64/a5/L4"
    "dO482v5VGQ1ch+KkFSD9HZzNVx7O4aLUIP679B5Eq/sIFn0bYNbEkWh1Mo43KO7E1tkOQvyMw2iy+I7QnvQczBuH4l3dG6LKr/44"
    "cmgoD1wgQQ4fx9UrM7n3wGkUnj+Oaz7cI5m07bDjYwV0VT8BJ99QWl0wBEfqzeBhV0uoee1IHDe3CV5Yj+Sp3sXov1GVR6qqcRSo"
    "c1v+Vfjw3ZPkLr3CX2cmgfLUHP7s1wxh52zFG9N38TP5CD4xu446DFQEqy9d9MrJEBdWq517Y/xVHP8jmi5Tl1C10wYWjq6mmmZV"
    "3F+kSG1lzWS9zoi/eM2iQsv56Pt9Iyf0a5e+uCYVNn1bDeG7lkJe53hOO9ATQp3KyX74RT6zQ4ELKhK5XMFZGNajjBesfEuT7+4S"
    "Pu8ZxHO0d5GCUl9aYKBCvsonsWj5exrp1w9Tzj8RenoaUex+ZU5/+oYHK9RhjtMzMfTHU7A5vxQeHzGh/uDExZun45KWcEyviUVz"
    "T0PcabsXPi2xFM7s0oF2+1fiDNfdoPVrPwdb9BSnlE2DqzfCcLXNMKywUcAvUjVx+ccp+MX9AJzs6I267ae55aA1Dbp2peqxRgIv"
    "9lnLJ/ddwszZ+nw+wYA3/GqjPjmvYeAkTdyTIAunFJPg7IHVYHd2IOaLsqAz5l9/SdYX909JhXd3lDDEJw/0b7GYmNqbB6/qid1t"
    "FVDWS1FiOkUOVxbO5Afx4XD3ky43LxLo0ssjqPQoEM+tSxEytieLTlYlUovpSuDuv57/+0/KYWPa4SrfhhunNTk45BV5yNSTjdNs"
    "TJyrwyslA9h3oC+VrsmDmB/p4Bswn/5oxcKp5Sp4dKyEa1r0yDy3L2pNNuZs10h2fVsCHx274GTpdLFivT23VMkyeu9hjRkbUU7L"
    "GPseMuOqkCZ4ppyGYz+mcODsg3DisTn3DlFnm5+XODGyhcrq9+Dmeco8OuYJ2mSW4duc+Rj72IoGv+lGK4LCwTDNhVBpJKsuUeXh"
    "LiHoohqABf388JJHNmz/68X9emaJxTpTWFKymWWG27Ci/E5sCAvnmNmfyadoPgd9/EIfb8lD0cF16A0nhegriXja6orQI7q7cHCy"
    "9b+9czDkg7q44UtPaFbrz05JsVTXGiUu2ShwikJvnHR+r/jG8CN0jR+Etr3SuNaWKH7OJvLRyuFGFwXQuhhOk7NXYLPiBaB35aDy"
    "vRKt7r8Bk4yHdLAfkW/5GZb+2Ms93mwFzdoU+Hv1LJi3paPT8AChvTAN1jWnwmTLp3DXbQ+PMdwPYxz2CrKTd/Mip904v6YVVoc8"
    "gbxBF8WNgfJY2+SAvWYm4/mkNMHKxobX9s6G2JgGPpNaJb44G4HObMxDSWRvh1VkubYPjvyUTPvcp7Lx43Fcv+EJaV6PpJFfj/It"
    "75+84cYMfqUZ+K93xvK8nDR8cOQ4B8ywxcZ1MnR1zAOSLvoN1k9yaci55eISrTxsObkCS5Zf5tr2Hpi2Dzh4jzXNfuuBkvhKUldT"
    "/7dWLYZEFJMYORUcumWJAzLrpCqTF5LjvAQcce4gK/gr0IOz/3J5fhpur9PCBWPMKG9zBrRUh3Lofm2uGfCa9t7XI+u0QDb8+ICj"
    "Y8Jof7svZ0d9oe0bo/jM1O5iZIMtAKZKJB6vwFA7BhyuOXLzzbdk0r6Po//qs0/JZNyaKtA2o//o+OwY0rLx5IK+JXxSsZo2Hk/h"
    "7aPzoOctX15um0+7k7vxjqxw9nRy5qU1sbDO1QhdWy+T125rfPanj2hSPoDahywSZhwfxxdCJ1DA0tc0OMuGfn+7DnNXuuHt6nBM"
    "7N0DGirHwSK1f/f7Yzl0LF0hJg79Q0dJU8zdlwv5X0ZBvE05BNxbxdv6vqNuVW9Rd1EIL7xaLo5YH491X0miblEopCUa8kPF3XD3"
    "ciR0yBRT7tAPMLZ6PRdc9eFTxf64uSjhH7da4JMoZTZ7PE602RSMcR93UPOY7pJlYjQMOPkbPO+fRVhmwXk+vmLbvatQffknKa1V"
    "Ya7qLXFQURW7/NNY7spwuJe4lvyWrhJW7rYTgk5Z4R41f8hMWge2jqfZ3TkSbQOk7PUlDg2+vqXAE1H0qOMDsFIk5/+IBMMFSeKO"
    "oCZIg9VCbkcdPBmaIP1u9wnOp8Vg4qhZ1E14C6rBOVQ6JoqztPJ4/cfzdFulAoftKxDCMiLB+ucQ7OmlBP0CHsCRl3Y0clkKuf8O"
    "Zf2o4ZyweQVmaI6i6M5WetdwEVzrtoHb2HH4av0B9D9YDhK/GDScNAzln3RRW9Z2Ol2wBLdYLaNrGSG84IMqdY1MPNsjIvFfpoWB"
    "mvJ0rno1hjtNprDa0BAnwwne0JBwimbcCgHZbHeMHnQUJ/z4xhHdzuCjj734sUI0CqvzMXLlI2lugxNNmroHZKiIOyfM5TdWPzgz"
    "PxvT44t5xvRjwrGWhTx6WRKqZ3kIRuJKyaaeEgqNzWLjDVfEhHvp0KP7K3LXjRcrC7vD1hHzecyJgRTh/AH6nLdj/1l/IKv3GTC/"
    "7CrMsgwQ22SHiBt9GZe/3MK/++7APVVKfNzYQixxCMY1ywZxrZsNXLJTp5aeFXDEdgcs3elNmjtrQDIgh3p/6YF3DQP4m8dyHpBu"
    "jpH1atAlKsPcbQv46oyfpKZ7hb8tn4m2/tfg7Qkl9C08DMZvXVg+wkZINA8E18O9WGOcCe8dacRHG5fge9c+NFoVOFtVmyP+RLCD"
    "ezg9la6ihnnlUv/sIEi1CwfTqBYSFq1g5UFDcGPoCTCTOLD35IXcKptKG9PVxKXHzfCNRSHke/QSlA5rQeHVvnh4fYJwq0pCez37"
    "wIToRdQwOhSbs4exf8F1ajQawnfz9uLFrZfIr/c03PJ7lJg+X5Y6fYMwKCqQxpaoYfGpZVw4IUrkml58IVsbzdZoiBv0poHSWxn0"
    "XKnON/tMQhTC2Ov+I1GyKgy/JjrwOotcQdtkIq77OpFv7pxCf1Iu0F/3oVyhsQBtR2Vhap9eoPbXDgb73BXnlwfD0bkbaOmQNtJb"
    "8xir9p/Db/UvaUrwZTzrWM6y1w7BPXkpfTM1hmcrh0ntTmTj52HROOVgI0yaq8d6luXQ0b87hptv5SvHrTC3YwBOd4inAY0LeecA"
    "E2wdkCv6ruvPw3cOZrpryorTP4o69ccZx+twt2olTs9sh37aejDmjT/tHqXN1Q+rKUn/OGXpmnHEBXleNsWJ+5jn4WXHyZhIzqiZ"
    "eFnywfoYnTkaLF55Gkf3rk/D80eNRCX1/rjY2ZX2jDwP8no6/7j7jvAMXbCuTUtY4BnNsTqr8OHzKFj1tx/qzZBSZbQp75nqA3lb"
    "RtD34OEY8VSNkx9uw6HecbB43kZeP2U4Hz1qxKu3z6EeS2fynIKJKHbWihPP3xf95I252jURw/SD+K1fNprVa6FgbAoVy2tghedY"
    "vlL3ltZmE15ODMOmd6uJTG/xIefRGIJ53DH4AR5PkcHJg8+CrJYbWvbZDs6n5wjDS4aQncRSfDrsLMyM74WSfg2CoqkcB9SZIc3U"
    "wJFOM3Fmc3826aPIq3ftElztLEH9ozfOtjVBaXsf3lpYJJRuVsH3o6KE1DuvaDm9ofz209QxSh/PFN8ToqsfCbuG6vAD0xswrv0y"
    "5nVdJAXZh9LezhHij0m/QUP+O61/oQsDW0t4xaDd3F/pniQ5V2D3IR+oz3Jf6C2XTQm25ZJ5gxTOoYEUx45+LURNm4YLZGLoZtQu"
    "uMMXhRfjQnCf/yapSdYp/G9HJg3fMc3ptG1/1u52iZ12zqCAEQNw9fd5+PHuGB7qG4d7P9rwDV1nunl7LE7/3EErZsaCSv01br5T"
    "DN39wiDW3hyCLrwgce9y3rdFE57NVHQuqlFnxUf/fPDVdPGLZwHHl/iz/kl5UTp3Pu2vTKVKj0Y6IDVCuygVVhAO4bWAM/x+YI4A"
    "+8JxklY/ELRHsNF/9/+xRT/R6GYUKp5y4Mxb5VLzAXbcJ7cRcugy/D7RRbe0uvGiYjnUcl6E8q0r+G2PNhq+7iQ3L5yC4bVuuHXq"
    "a0ruuxfPXdnMwZ/6ctmdAdDvba1ky29fnJtlhF9PtlDo3QjYoPiQThXVMjQe4TXiGlQ5HygmPJnO+xZv4tkGQ7j3a284bSZ77tel"
    "oxA1UY7iYjTRus4Uyz9ViEpq+8H8SCQ6bnNE9Zk7WWOriThRmgFT9f/QhpM/JTvfNsJ9ZQt4a3kIXpv9hpJFTeIE/aE8TrUSXl3x"
    "xu2HDHjmy9n0Y3E2vclaTK490vnbY0daZb8L9Geerco+pU5L9Cx5c9l+Snf9wHYnkoWtB/rTE686UaNzBunWt0N49SgxPmdelWay"
    "O3OYPT1P+kzRdQ/oPgZTzaK1bNDynjsSRvGDY0boH1RBoz5ooslFkfOKPoP72ycc8KZLbF7lI2rLpcAClb4cersC2g7N5YO338K6"
    "61JyfL6f5lzy5fHf90JtfbEY4+XEjSsHC1pn9Wj9jWsw7XoR78BhtG1YCN50SYORww1Eick2OHnIC1Tfm7Jl6Hzc3XSKdLabsIuD"
    "GSnYrOUeCqlsVuIzUuPOU+CvT+nYK2VUHFaNZdlL8FZmiNQyME/42buafv2WQR8ZBdxn+YJ21BjhneInlOpZBpuuzcWYZncYpzUW"
    "1Q650aZGHXRrm0grZ08mz2VJVBguUtwjR7g9+L34y9KRq6/98/yO70LS9s+CTu9cnmfqhBbDPoiVRhN51s9nQlOrOuoZNpK2ngZV"
    "2BrDhlmDsG2LG+pWRGKd8Tj8k5oPr3omsGrXbG4Za/CPv4Zi1cJ97Nkkj4ojp+GebTEQ1BzGX2fKk1X6Gszpb4x3ti9BnZIAMfux"
    "FiqcSqIok6dC46RUOB7hLppc7M03zqvgxcXPYfUqNVFnhjYGeurghqla7FRszivL5chiZhw/7JgNWs+dIHOtjXTZgc18Ut6U1Vvc"
    "+Xt5PHuH2MO90jzJuOsbUfXtGGy48RqG3vbgouUTaNHXaZS18z/Iqawmna19OOJUMA5SFXF16TZY5NglHS2zmUrj8uHwjB+kdHcK"
    "2oReour+ZliVfRESPT2kDZEprK13FLp2x2CtYwV3Ly5n47t7eLmaDzn55WCOYgVHGrgQn0mHN9c9cZq+I3706aJJpf25rvQzzT7f"
    "Lq3I6M3T3OvwQf0ynjI7jAd3idIgRzcs3XmChndo8rDq3my+yZXfmU/BhF2ZXNF3BPZrjEc3i+G8bW49flu3EVd0BvPc73/AW2kl"
    "R/sr4/OPp8V1kiiel/gSKk0LoOuNAt5sKeOT/VbSrM0H+NGd0dT5ZQNXbW8SBBsfWrLnFm2+EMLn7BKhU3EtDP+iztd9F6HsVwV8"
    "t/YJjZk3FuX3FdDgP+3srfgJFOd9pk9mp1lebiArzNjEV8p1+XB1DH+ubKfwRANe/rsvLlr2Q1jyqobcspXI7ZsZf/OdC0P/3CL3"
    "Nfo4LWJ8lcP7s8LPvANkXvyMgvL3kkXQdojt0sVfN8Zw4o5Gyt4chRuzY6Wta3LosMc+aW/zX/R8RQGUNpfCxZFGaGS5FnXhmvT/"
    "Hcc4Qo0Hz9iD8uczUCbLAt3GtkBubSRm9cviSVcGcb2JkvOyhq9CdtpPqXd/Ha79CPhu8BTyMgG8mNZIP2wPsYnMS9jm4c5DIhTZ"
    "9uATYX5UreR2phYfb70k3fSPWQ8/GSjoFMXig6k9KG9ULm0Mz+Qsn3Nw88156bDyGuHy0ShUOJYNRRaNnGlqhDIlP8ReD3Pg3add"
    "vLPWCTaWjIEJLtWgvNQV3v5S4KO5MoJydQ/0i2qkUqc3MDbZmk3FNq4qviJ2ZMRgpcd3arCJwUVx5+jDcTtBWBRDe2rfwPct8fTk"
    "VRq9+ZdU3iN+sXXdVD5034Bv9LLABVar4LDvL/rpUIFbV+pxx/qFbDFoB0pfR2H9yVTqF6DJLxaYoGbPIdAZNBIDFiwQa7wRZk3Y"
    "Ta4AvGnIBFw3PgyzSuz/ddTt6HhBpM+6y3jfmAhusE/giA/9+XG0o6j9XRbtjTKh/EsJ+/qo8NtPstCSt4s3hwPLJ42GuGtXwKho"
    "E7afvONkMvWP8DpoEtztp8y6vvqcWTCCp500hBJVb/6zoF00PKlFBhYLaXbMd2FqUQ+oC6sWsr5dhcceByFndh7XbznMy9ZI2HDT"
    "U+oe81aYlHkZ7hybAvtuxYLvrHHSe9e28q+ex0nTczAOTp+NqX72oGJgyPL4jgpa3pNBTBybzvHlbTfiaO2dQ3xx9g1ufJ8mVmj4"
    "w6T9lXRqdiQ/mqyEygVnYNHowzzw1nkxeZYZn9l+kle0ToVjhivw0tMlNFExBPx2rcQ4+Oeh3x0w0n8Wlnnk8YC6WN46eS+mDVfG"
    "xUtthUIrLfby3ymx6HzLQWoKXCXbh+mjE8odTqZ7kxewzPlIdrsqlXgKc4U74hu6JdTiBI8ReHX7esHgpzcXTGynnE1pWOawirfm"
    "DGDT47fFhb8SeI/5BDQ/LQr9G7y44mY++yv2xYS4Q7B+WgsUdNuCCr/SaJKjPR7/mybEH6mnLOsg7qMhy44hcex1YiHW+BiQi80B"
    "Al0bMWObGt5MUkML8SV1bOvGJt2t8J7GIGpPsYXj35ZyW88rNCJ6D0hG1EqufqmiwCDgi8tdOCDcER9vCwHv8Bi4/9ecyXGncGeN"
    "E1zc5cS9Px8nvSUdFKFsxDVXdNF76lQ+sCQEl8pG8aoFz2DfpCJhyHALmLM+DDUk3cQd8YFgOame9s+/SYv/eaX9uiI+kOUq6m5T"
    "p6a/PaD4rgqjrD7arPxJ5WMyoMHwIm33/EsHouMAm34KD/a7Sb5N7cY32lPR7/wH0c41DBKKu0PinHTYnB4hDe5QgWGXjNjwQTFf"
    "KFzGL7oi+chPD84cHcc/3SRYtyIczBfsoC44C5V7y4XL/EDImBjH09uOo5p1BXntvUdHf+dB071p9MXSFGPUDDiqKwGtzyrSqKnj"
    "cI/5JD72tEmUCx+OH6ak4uKDrvgu01rydZmImvYO9KxBjp8mpdGS4jf06lG52LTR4F/6u/DpI+uFJ0adNHxBDTa/S0BtZwV076lI"
    "Y4a/gVwTXbSbPACejToCx3JlOXr0D3C19uMfhW8pPSaI5zztgdXDXvGpHp7YUnyZ8x2vks2nydilsFWYcVkb+4dEoW2ROarUjBYd"
    "VwWypfwyHn0nCg8HpkLKXz96MNGBFz59Dm+ejkGFgkY4biGPhmYBkH22Rqx2caFrqQ68/o8VTTXWh8CIdWRvX0XHflyn1hgr1NbM"
    "w7sqWmi+WED3mb0o83AT+Iw6xe4bg9Fk0kG+7mECWlGxtLZTExJ1j1CgSi7meDwUl5yowHrbU/D9nRoVjbTjzY6fxal1rZL3deNw"
    "6k1H1GuzAuFoC42/aoDFSpNQ5sBktvrejf+7byFQWi922+6P5h2qqLQ1UuL+Q8CiI2cgqzWSYwb5oe2y5ei0+DyONwigppRhuErb"
    "GgcYnqNlLhnC4dVjyE/WgbvffkIXT9+A4qktwqDnS8WQiF4sq6RLXUtPCh9ynsFQMRevLs/Cvbf287tZiSS8TaEb6+KE6Sn7QHGh"
    "Eqz8eRynOQ+Db0HJ8PtaDo1aNhzVbsdD5qExIk8fiSN3pbBYehee/prBWQeX8YriFNCaOIh2t/7/34MGZ/WNp5Bx8cQrkunDohX0"
    "X3M+l2ra/5tJFY+4tRna3rlwdfIEPNPNhnTCBb7UVQE93vox9jmEU/PP0NVFS4TUqZ70Y0YmTdoymZRbU/lLn3y+ahfCZHKMvm96"
    "KMw+2cUb5YrxS+B0cUPlRPiyaTYdnnBDbNY4Qr177sZtj7bhj7BCHlJnhXafgd/M/yOe3eTENw8+g8yyeky99gTCl0TDiboA6DIt"
    "wII4H155dBJdbBlURXFDsWzHevy9wAmsoxfy1ZTtxFnysOGQGvOxemHzp3ga+CweHLb+oKrl5bDr93tyXujjpIaWPLVcE1dm7MaO"
    "ZUVU2vyECkf34Jue6byxfxb+mtAAzcm/aE7sTcg/RnQy+g1o5Ryj2RPcSUXjPtV4vaPfd3dDbLkMz1uZJ5m/KQKyXfZD/bws8cXj"
    "eqpeXcBdrz5hk8MHWJ2xG3JCfqFfRydtWRUqhMZP5uda6jh35Ut43e+bcP63F31yPwI3P9yAR1t6sLbxZpax6U2uLn/pjq0hV+p9"
    "Eb/4y/PirmJW+1nEw8dO4NfjjMSiwyMpdE07vVyghpO+bYbgJWdZvnU+tWTv5CDlOLFQukPqOc6el7d4867bw9DNWR+c989DlXmT"
    "4Z2NPoBhBgSozUJld3NeVddAFp3I+hn6pJFvi81FS7Cq1fBftp7CxOlGnBnVRnPeWfMxcQMdfNUXPcpHsUzyKopru0/q3ra8QCWe"
    "Sj+8pSWmMugWe5BKf6mAzuEYrAoPgwFbX0FO/xq2GljttHVaBj8YcoUyiwkD85zBxrA/hmeN4r7xs5yi6vai6x15Wrt8PY3XqKds"
    "tQEQXzUTMn3zxUGvavjQ77mizrYDaPNuLIzQ74v/uZvSn762dPv0Gm56oIkfszuhd9JduDDpGsztTIfsnpux15RCCOr4TVO+DcWj"
    "0hxaKZ3HDnWHJB311rTn00Lsd14NH5kEUNqYyfRKeyrfV7oKw/v5ctekbfDSpVOa7z+RXx+voLUW0TDw1F56suyW9JBWOd/OD+PV"
    "DfJ06HculZ3sgXbbZ2LoCXeYcVoW/EpJ6PP+KezsGUszLsziUbNC8NeTyeickEXXcveDwo4luHoBQN74f/6ZNl8836+JXFb0ZYm6"
    "DCfN+3w2+6ia85qMTPrqt5Gtf13lugcX4WYzQc8elVKPqWEckB3F5/6mYJ8FV+mMaQyMP2iGk1PD6LRfBvYYJMvXGjSxr40ZNB94"
    "TxGve7HXgGSc12WMM29EOY3VRvRU15HaNM4km2Rjnti4i651mwQaETr4UilB0DE/jBNfmKA77uWqioPstEoCbXk++NBkB/+Os8Uq"
    "tTQ47CmLHm9XkYmGFzwc4Y5f4F/OrCmAfWtnifat42niCS/uWDmQS4r0eXxRHq1ZsVD66cIlUp2qBTVWF4ASl2JeH008MOcafZqT"
    "D49X9oFPrQV0se8cNo6PpM8658Ri4wG4Lby3cHZ6b7CYUEIqVh5cYa/NFxf040zln+JzMVJ8uT2OO+fniqFN32F471iUGz7eyWD1"
    "ZnFa2ixWWBQENSPqRQdXOzw10Bs33xrFDl/vUfrNrULqqVJBs0JCHsVOXDrQGp2q/Pi2uYlTXEYlXz3/V7jQo0l6+bOzJNJqBPRZ"
    "WM5urXslsjN9KfPWBzG29D6YRJ/gNFLgn1ozufFOCNymRCethw6YqngdTgyzYE3fZnhZM5cXnrDEVkdfnDDGnV/k+8L6VZOw3N2P"
    "oy2iuXOZO6Y9NuTFuQskC4+1UOXilXBBey9q4iA2nH8bJLOc8WhzHprY9gaFI3I8x7ODKv/TxNU7KijXbDRnz4gAjxs2OPOCLkZX"
    "DMbirQb4vo8WH5yvivdnpItd80dzXPNToWPLV/FvkQ0+DtZmw/wsakqU5U+96njqf5s5YN40zMvti/zxu8iqRhAw0J5z/Rx5nuVL"
    "aB86Db06Lvxj02rytvHgI85J0mxJh2A7vAwnFT2Abw7r6f08U0F/zGpQVfVBUN3Eo+TcOePPfBxWUUUa8XJcMC5OMqdmNY/6doBf"
    "SQzhz7HDokPqWup+d7c4Wt1eEuraQC/WpaPjQU/qlzORDdL2C8b/WH7V1VIy7VjLdhXjUd04HY4cUkDj9fPp0v04CF40hke+SMQj"
    "WM2hOywpqyKPxlxpAu0ES0w97UHta/cLtRel3NJnAiYFF3CWmEvPF0+ntyUr+cTUZHCZWUaaWt54WkFVenjUOnY3OwxnlH9Bm/J2"
    "XL55HzYlVcKZ4jaKf9xEDoODOaBXOPwPa40u3w=="
)


def _proj_const():
    import base64
    import zlib
    raw = zlib.decompress(base64.b64decode(_PROJ_B64))
    return np.frombuffer(raw, np.float32).reshape(128, NJL).copy()


_NC_CACHE = {}


def make_in_maps(L0_attn_x, L0_attn_Wq, L0_attn_Wk, L0_attn_Wv, L0_attn_Wo,
                 L0_attn_q_gain, L0_ffn_gate_up_packed, L0_ffn_gate_up_scales,
                 L0_ffn_down_packed, L0_ffn_down_scales):
    x = np.ascontiguousarray(np.asarray(L0_attn_x, np.float32))
    Wq = np.asarray(L0_attn_Wq, np.float32)
    Wk = np.asarray(L0_attn_Wk, np.float32)
    Wv = np.asarray(L0_attn_Wv, np.float32)
    Wo = np.asarray(L0_attn_Wo, np.float32)
    gain = np.asarray(L0_attn_q_gain, np.float32)
    gup = np.asarray(L0_ffn_gate_up_packed, np.int32)
    gus = np.asarray(L0_ffn_gate_up_scales, np.float16)
    dwp = np.asarray(L0_ffn_down_packed, np.int32)
    dws = np.asarray(L0_ffn_down_scales, np.float16)

    cos, sin = _rope_tables()
    projc = _proj_const()
    iden = np.eye(128, dtype=np.float32)
    blk = np.zeros((128, 2), np.float32)
    blk[0:64, 0] = 1.0
    blk[64:128, 1] = 1.0
    par = np.zeros((128, 1), np.float32)
    par[0::2, 0] = 1.0
    ones = np.ones((128, 1), np.float32)
    onesr = np.ones((1, 128), np.float32)
    cvals = np.zeros((128, 4), np.float32)
    cvals[:, 0] = 1e-5
    cvals[:, 1] = 1e-6
    cvals[:, 2] = 0.0
    cvals[:, 3] = 1.0
    ltm = np.triu(np.ones((128, 128), np.float32), 1)
    iotab = np.broadcast_to(np.arange(128, dtype=np.float32)[None, :], (128, 128)).copy()
    blkid = np.arange(128, dtype=np.float32)[:, None].copy()
    poff = (np.arange(128) % 64).astype(np.float32)[:, None].copy()

    gateT = np.ascontiguousarray(gup[:IFF].T)
    upT = np.ascontiguousarray(gup[IFF:].T)
    downT = np.zeros((768, DM), np.int32)
    downT[:704] = dwp.T
    gsc = np.ascontiguousarray(np.repeat(gus[:, :IFF], 16, axis=0))
    usc = np.ascontiguousarray(np.repeat(gus[:, IFF:], 16, axis=0))
    dsc = np.zeros((768, DM), np.float16)
    dsc[:704] = np.repeat(dws, 16, axis=0)

    in_maps = []
    for c in range(8):
        b, qr = c // 4, c % 4
        hs = slice(512 * qr, 512 * (qr + 1))
        gtm = np.broadcast_to(gain[4 * qr:4 * qr + 4][None, :], (128, HPC))
        in_maps.append({
            "xb": x[b],
            "xq": np.ascontiguousarray(x[b, TPC * qr:TPC * (qr + 1)]),
            "wqT": np.ascontiguousarray(Wq.T[:, hs]),
            "wkT": np.ascontiguousarray(Wk.T[:, hs]),
            "wvT": np.ascontiguousarray(Wv.T[:, hs]),
            "woT": np.ascontiguousarray(Wo.T[hs, :]),
            "gain_tm": np.ascontiguousarray(gtm),
            "cos_d": cos, "sin_d": sin, "proj_d": projc,
            "iden_d": iden, "blksel_d": blk, "par_d": par,
            "ones_d": ones, "onesr_d": onesr,
            "gateT_d": gateT, "upT_d": upT, "downT_d": downT,
            "gsc_d": gsc, "usc_d": usc, "dsc_d": dsc,
            "cvals_d": cvals, "ltm_d": ltm, "iotab_d": iotab,
            "blkid_d": blkid, "poff_d": poff,
        })
    return in_maps


def kernel(**inputs):
    in_maps = make_in_maps(**inputs)
    key = "nc"
    if key not in _NC_CACHE:
        _NC_CACHE[key] = build_nc()
    nc = _NC_CACHE[key]
    res = run_bass_kernel_spmd(nc, in_maps, core_ids=list(range(8)), trace_cores=[0])
    out = np.empty((B, S, DM), np.float32)
    for c in range(8):
        b, qr = c // 4, c % 4
        out[b, TPC * qr:TPC * (qr + 1)] = res.results[c]["out"]
    return out


# version 11
# speedup vs baseline: 1.0422x; 1.0261x over previous
import numpy as np
import concourse.bass as bass
import concourse.mybir as mybir
from concourse.bass_utils import run_bass_kernel_spmd
from concourse import library_config
from contextlib import ExitStack

F32 = mybir.dt.float32
F16 = mybir.dt.float16
I32 = mybir.dt.int32
U16 = mybir.dt.uint16
ALU = mybir.AluOpType
AF = mybir.ActivationFunctionType
AX = mybir.AxisListType

B, S, DM = 2, 8192, 2048
HPC, TPC = 4, 2048
NT = 64
NT5 = 16
NJL = 32
IFF = 5632
NOSET = 48
RS128 = float(1.0 / np.sqrt(128.0))
NEG = -1.0e38
ENGS = ("sync", "tensor", "vector", "scalar", "gpsimd")
PHASES = None
SEMMAP = {}


def build_nc():
    nc = bass.Bass("TRN2", target_bir_lowering=False, debug=False)

    xb = nc.dram_tensor("xb", [S, DM], F32, kind="ExternalInput").ap()
    xq = nc.dram_tensor("xq", [TPC, DM], F32, kind="ExternalInput").ap()
    wqT = nc.dram_tensor("wqT", [DM, 512], F32, kind="ExternalInput").ap()
    wkT = nc.dram_tensor("wkT", [DM, 512], F32, kind="ExternalInput").ap()
    wvT = nc.dram_tensor("wvT", [DM, 512], F32, kind="ExternalInput").ap()
    woT_d = nc.dram_tensor("woT", [512, DM], F32, kind="ExternalInput").ap()
    gain_d = nc.dram_tensor("gain_tm", [128, HPC], F32, kind="ExternalInput").ap()
    cos_d = nc.dram_tensor("cos_d", [S, 64], F32, kind="ExternalInput").ap()
    sin_d = nc.dram_tensor("sin_d", [S, 64], F32, kind="ExternalInput").ap()
    proj_d = nc.dram_tensor("proj_d", [128, NJL], F32, kind="ExternalInput").ap()
    iden_d = nc.dram_tensor("iden_d", [128, 128], F32, kind="ExternalInput").ap()
    blksel_d = nc.dram_tensor("blksel_d", [128, 2], F32, kind="ExternalInput").ap()
    par_d = nc.dram_tensor("par_d", [128, 1], F32, kind="ExternalInput").ap()
    ones_d = nc.dram_tensor("ones_d", [128, 1], F32, kind="ExternalInput").ap()
    onesr_d = nc.dram_tensor("onesr_d", [1, 128], F32, kind="ExternalInput").ap()
    gateT_d = nc.dram_tensor("gateT_d", [256, IFF], I32, kind="ExternalInput").ap()
    upT_d = nc.dram_tensor("upT_d", [256, IFF], I32, kind="ExternalInput").ap()
    downT_d = nc.dram_tensor("downT_d", [768, DM], I32, kind="ExternalInput").ap()
    gsc_d = nc.dram_tensor("gsc_d", [256, IFF], F16, kind="ExternalInput").ap()
    usc_d = nc.dram_tensor("usc_d", [256, IFF], F16, kind="ExternalInput").ap()
    dsc_d = nc.dram_tensor("dsc_d", [768, DM], F16, kind="ExternalInput").ap()
    cvals_d = nc.dram_tensor("cvals_d", [128, 4], F32, kind="ExternalInput").ap()
    ltm_d = nc.dram_tensor("ltm_d", [128, 128], F32, kind="ExternalInput").ap()
    iotab_d = nc.dram_tensor("iotab_d", [128, 128], F32, kind="ExternalInput").ap()
    blkid_d = nc.dram_tensor("blkid_d", [128, 1], F32, kind="ExternalInput").ap()
    poff_d = nc.dram_tensor("poff_d", [128, 1], F32, kind="ExternalInput").ap()
    out_d = nc.dram_tensor("out", [TPC, DM], F32, kind="ExternalOutput").ap()

    qfm_d = nc.dram_tensor("qfm_d", [HPC * 128, S], F32).ap()
    ktm_d = nc.dram_tensor("ktm_d", [HPC, S, 128], F32).ap()
    vtm_d = nc.dram_tensor("vtm_d", [HPC, S, 128], F32).ap()
    yfm_d = nc.dram_tensor("yfm_d", [HPC * 128, S], F32).ap()
    kpn_d = nc.dram_tensor("kpn_d", [128, HPC, NT, NJL], F32).ap()
    rs_in_h = nc.dram_tensor("rs_in", [S, DM], F32)
    rs_out_h = nc.dram_tensor("rs_out", [TPC, DM], F32)
    x2_d = nc.dram_tensor("x2_d", [TPC, DM], F32).ap()
    h_d = nc.dram_tensor("h_d", [TPC, IFF], F32).ap()
    wde_d = nc.dram_tensor("wde_d", [NOSET, 128, DM], F32).ap()
    ln2_d = nc.dram_tensor("ln2_d", [TPC, DM], F32).ap()
    centd = nc.dram_tensor("centd", [2, NT * 128], F32).ap()
    dotsd = nc.dram_tensor("dotsd", [2, NT * 128], F32).ap()

    outer = ExitStack()

    def osb(name, shape, dtype=F32):
        return outer.enter_context(nc.sbuf_tensor(name, shape, dtype))

    def mksem(name):
        h = outer.enter_context(nc.semaphore(name))
        SEMMAP[name] = h
        return h

    pb = [outer.enter_context(nc.psum_tensor(f"pb{k}", [128, 512], F32)) for k in range(8)]

    iden = osb("iden", [128, 128])
    proj = osb("proj", [128, NJL])
    blksel = osb("blksel", [128, 2])
    parm = osb("parm", [128, 1])
    onescol = osb("onescol", [128, 1])
    onesrow = osb("onesrow", [1, 128])
    gain_tm = osb("gain_s", [128, HPC])
    cbank = osb("cbank", [128, 4])
    nc.const_aps.register(F32, 1e-5, cbank[:, 0:1])
    nc.const_aps.register(F32, 1e-6, cbank[:, 1:2])
    nc.const_aps.register(F32, 0.0, cbank[:, 2:3])
    nc.const_aps.register(F32, 1.0, cbank[:, 3:4])
    idx_all = osb("idx_all", [128, HPC * 33], mybir.dt.uint32)
    ltm = osb("ltm", [128, 128])
    iotab = osb("iotab", [128, 128])
    blkid = osb("blkid", [128, 1])
    poff = osb("poff", [128, 1])
    qsum = osb("qsum", [128, HPC])

    s = {}
    for nm in ["c0", "x0", "x1", "cs0", "cs1", "odone", "kv2a", "kv2b",
               "q2a", "q2b", "yoa", "yob", "y3a", "y3b",
               "rsAa", "rsAb", "rsBa", "rsBb", "ccA", "ccB",
               "gdpe", "gst", "l6", "l5s", "lsa", "lsb", "lra", "lrb",
               "x5a", "x5b", "x2wa", "x2wb",
               "g6a", "g6b", "hw6a", "hw6b", "w7a", "w7b", "o7a", "o7b",
               "xf", "csf", "ln1", "ln2", "lnd", "tp", "tpe", "qkv",
               "qe", "rm1", "rm2", "rope", "kfme", "kp", "kpe", "kpt",
               "kptev", "kq1", "kq2", "kpn", "cent", "centev", "o1", "ph",
               "sq", "g15", "hfree2", "qf2", "lt", "exq", "esum", "av",
               "sum2", "rcp2", "bc2", "rbe2", "ye", "kt2", "kt2e", "wo3", "w3", "e3",
               "wde5", "l51", "l52", "t5", "t5e", "e6", "mmg6",
               "mmu6", "sil6", "h6", "h7", "t7", "t7e", "mm7", "b71",
               "b72", "xl7"]:
        s[nm] = mksem(nm)

    PH = {"n": 0}

    with outer, nc.Block() as block:

        def make_section(engname, body):
            sec = getattr(block, engname)

            def wrapper(eng, body=body):
                if PH["n"] > 0:
                    eng.wait_ge(s["ph"], 5 * PH["n"])
                body(eng)
                eng.drain().then_inc(s["ph"], 1)
            sec(wrapper)

        def phase(emitters):
            if PHASES is not None and PH["n"] not in PHASES:
                emitters = {}
            for engname in ENGS:
                make_section(engname, emitters.get(engname, lambda eng: None))
            PH["n"] += 1

        class Seq:

            def __init__(self):
                self.steps = []

            def step(self, engname, fn):
                self.steps.append((engname, fn))

            def emitters(self, base):
                def mk(engname):
                    def body(eng):
                        for pos, (en, fn) in enumerate(self.steps):
                            if en == engname:
                                eng.wait_ge(s["sq"], base + pos)
                                fn(eng)
                                eng.drain().then_inc(s["sq"], 1)
                    return body
                return {en: mk(en) for en in set(e for e, _ in self.steps)}

        ph1 = ExitStack()

        def sb1(name, shape, dtype=F32):
            return ph1.enter_context(nc.sbuf_tensor(name, shape, dtype))

        wsb = sb1("wsb", [128, 16 * 1536])
        cent_acc = sb1("cent_acc", [2, NT * 128])
        xbuf = [sb1(f"xbuf{j}", [128, DM]) for j in range(2)]
        csb = [sb1(f"csb{j}", [128, 128]) for j in range(2)]
        ln1fm = [sb1(f"ln1fm{j}", [128, DM]) for j in range(2)]
        qkvtm = [sb1(f"qkvtm{j}", [128, 1024]) for j in range(2)]
        vsb = [sb1(f"vsb{j}", [128, 512]) for j in range(2)]
        qro = [sb1(f"qro{j}", [128, 512]) for j in range(2)]
        kro = [sb1(f"kro{j}", [128, 512]) for j in range(2)]
        qfmsb = [sb1(f"qfmsb{j}", [128, 512]) for j in range(2)]
        kfmsb = [sb1(f"kfmsb{j}", [128, 512]) for j in range(2)]
        bnst = sb1("bnst", [128, 24])
        mv = sb1("mv", [128, 2])
        sd1 = sb1("sd1", [128, 1])
        rstd = sb1("rstd", [128, 1])
        sqb = sb1("sqb", [128, 1024])
        ssq = sb1("ssq", [128, 8])
        srt8 = sb1("srt8", [128, 8])
        rr8 = sb1("rr8", [128, 8])
        rrq = sb1("rrq", [128, 4])
        qn = sb1("qn", [128, 512])
        kn = sb1("kn", [128, 512])
        rop1 = sb1("rop1", [128, 256])
        rop2 = sb1("rop2", [128, 256])
        kpTsb = sb1("kpTsb", [32, 512])
        kpnsb = sb1("kpnsb", [128, 128])
        ss4 = sb1("ss4", [128, 4])
        srt4 = sb1("srt4", [128, 4])
        rn4 = sb1("rn4", [128, 4])
        red4 = sb1("red4", [128, 4])
        NG = 6

        def p1_sp(sp):
            for t, d in ((iden, iden_d), (proj, proj_d), (blksel, blksel_d),
                         (parm, par_d), (onescol, ones_d), (onesrow, onesr_d),
                         (gain_tm, gain_d), (cbank, cvals_d), (ltm, ltm_d),
                         (iotab, iotab_d), (blkid, blkid_d), (poff, poff_d)):
                sp.dma_start(out=t[:], in_=d[:]).then_inc(s["c0"], 16)
            wv = wsb[:].rearrange("p (c o) -> p c o", o=1536)
            sp.dma_start(out=wv[:, :, 0:512],
                         in_=wqT.rearrange("(c p) o -> p c o", p=128)).then_inc(s["c0"], 16)
            sp.dma_start(out=wv[:, :, 512:1024],
                         in_=wkT.rearrange("(c p) o -> p c o", p=128)).then_inc(s["c0"], 16)
            sp.dma_start(out=wv[:, :, 1024:1536],
                         in_=wvT.rearrange("(c p) o -> p c o", p=128)).then_inc(s["c0"], 16)
            def ld1(i):
                j = i % 2
                if i >= 2:
                    sp.wait_ge(s["xf"], i - 1)
                sp.dma_start(out=xbuf[j][:], in_=xb[128 * i:128 * (i + 1), :]).then_inc(s["x0" if j == 0 else "x1"], 16)
                if i >= 2:
                    sp.wait_ge(s["csf"], i - 1)
                sp.dma_start(out=csb[j][:, 0:64], in_=cos_d[128 * i:128 * (i + 1), :]).then_inc(s["cs0" if j == 0 else "cs1"], 16)
                sp.dma_start(out=csb[j][:, 64:128], in_=sin_d[128 * i:128 * (i + 1), :]).then_inc(s["cs0" if j == 0 else "cs1"], 16)

            ld1(0)
            ld1(1)
            for i in range(NT):
                j = i % 2
                if i + 2 < NT:
                    ld1(i + 2)
                if i >= 1:
                    sp.wait_ge(s["o1"], 16 * 4 * i)
                    sp.sem_inc(s["odone"], 1)
                sp.wait_ge(s["tpe"], NG * i + 5)
                sp.dma_start(out=qfm_d.rearrange("(h p) t -> p h t", p=128)[:, :, 128 * i:128 * (i + 1)],
                             in_=qfmsb[j][:].rearrange("p (h t) -> p h t", t=128)).then_inc(s["o1"], 16)
                sp.wait_ge(s["rope"], i + 1)
                sp.dma_start(out=ktm_d.rearrange("h (c p) d -> c p h d", p=128)[i],
                             in_=kro[j][:].rearrange("p (h d) -> p h d", d=128)).then_inc(s["o1"], 16)
                sp.wait_ge(s["qe"], 3 * i + 3)
                sp.dma_start(out=vtm_d.rearrange("h (c p) d -> c p h d", p=128)[i],
                             in_=vsb[j][:].rearrange("p (h d) -> p h d", d=128)).then_inc(s["o1"], 16)
                sp.wait_ge(s["kpn"], i + 1)
                sp.dma_start(out=kpn_d[:, :, i, :],
                             in_=kpnsb[:].rearrange("p (h j) -> p h j", j=NJL)).then_inc(s["o1"], 16)
            sp.wait_ge(s["centev"], NT)
            sp.dma_start(out=centd[:], in_=cent_acc[:]).then_inc(s["o1"], 16)
            sp.wait_ge(s["o1"], 16 * (4 * NT + 1))

        def p1_pe(pe):
            pe.wait_ge(s["c0"], 240)
            for i in range(NT):
                j = i % 2
                g0 = NG * i
                pe.wait_ge(s["lnd"], i + 1)
                for g in range(4):
                    gi = g0 + g
                    if gi >= 2:
                        pe.wait_ge(s["tpe"], gi - 1)
                    for c in range(4):
                        cc = 4 * g + c
                        ins = pe.transpose(pb[gi % 2][:, 128 * c:128 * (c + 1)],
                                           xbuf[j][:, 128 * cc:128 * (cc + 1)], iden[:])
                        if c == 3:
                            ins.then_inc(s["tp"], 1)
                pe.drain().then_inc(s["xf"], 1)
                pe.wait_ge(s["tpe"], g0 + 4)
                if i >= 1:
                    pe.wait_ge(s["qe"], 3 * i)
                for oc in range(3):
                    for ci in range(16):
                        ins = pe.matmul(pb[2 + oc][:],
                                        ln1fm[j][:, 128 * ci:128 * (ci + 1)],
                                        wsb[:, 1536 * ci + 512 * oc:1536 * ci + 512 * (oc + 1)],
                                        start=(ci == 0), stop=(ci == 15))
                        if ci == 15:
                            ins.then_inc(s["qkv"], 1)
                pe.wait_ge(s["rope"], i + 1)
                for g, src in ((4, qro[j]), (5, kro[j])):
                    gi = g0 + g
                    pe.wait_ge(s["tpe"], gi - 1)
                    for c in range(4):
                        ins = pe.transpose(pb[gi % 2][:, 128 * c:128 * (c + 1)],
                                           src[:, 128 * c:128 * (c + 1)], iden[:])
                        if c == 3:
                            ins.then_inc(s["tp"], 1)
                pe.wait_ge(s["kfme"], i + 1)
                if i >= 1:
                    pe.wait_ge(s["kpe"], i)
                for hl in range(HPC):
                    ins = pe.matmul(pb[5][0:NJL, 128 * hl:128 * (hl + 1)], proj[:],
                                    kfmsb[j][:, 128 * hl:128 * (hl + 1)], start=True, stop=True)
                    if hl == HPC - 1:
                        ins.then_inc(s["kp"], 1)
                pe.wait_ge(s["kpe"], i + 1)
                if i >= 1:
                    pe.wait_ge(s["kptev"], i)
                for hl in range(HPC):
                    ins = pe.transpose(pb[6][:, NJL * hl:NJL * (hl + 1)],
                                       kpTsb[0:NJL, 128 * hl:128 * (hl + 1)], iden[0:NJL, 0:NJL])
                    if hl == HPC - 1:
                        ins.then_inc(s["kpt"], 1)
                pe.wait_ge(s["kpn"], i + 1)
                if i >= 1:
                    pe.wait_ge(s["centev"], i)
                for hl in range(HPC):
                    ins = pe.matmul(pb[7][0:2, NJL * hl:NJL * (hl + 1)], blksel[:],
                                    kpnsb[:, NJL * hl:NJL * (hl + 1)], start=True, stop=True)
                    if hl == HPC - 1:
                        ins.then_inc(s["cent"], 1)

        def p1_dve(v):
            def head(i):
                j = i % 2
                v.wait_ge(s["x0" if j == 0 else "x1"], 16 * (i // 2 + 1))
                for c in range(4):
                    v.bn_stats(out=bnst[:, 6 * c:6 * (c + 1)], in_=xbuf[j][:, 512 * c:512 * (c + 1)])
                v.drain()
                v.bn_aggr(out=mv[:], in_=bnst[:]).then_inc(s["ln1"], 1)
                v.wait_ge(s["ln2"], i + 1)
                v.reciprocal(out=rstd[:], in_=sd1[:])
                v.drain()
                v.tensor_scalar(out=xbuf[j][:], in0=xbuf[j][:], scalar1=mv[:, 0:1],
                                scalar2=rstd[:], op0=ALU.subtract,
                                op1=ALU.mult).then_inc(s["lnd"], 1)
                v.drain()

            def rest(i):
                j = i % 2
                g0 = NG * i
                if i >= 1:
                    v.wait_ge(s["qkv"], 3 * i)
                for g in range(4):
                    v.wait_ge(s["tp"], g0 + g + 1)
                    v.tensor_copy(ln1fm[j][:, 512 * g:512 * (g + 1)], pb[(g0 + g) % 2][:]).then_inc(s["tpe"], 1)
                v.drain()
                for oc in range(3):
                    v.wait_ge(s["qkv"], 3 * i + oc + 1)
                    if oc < 2:
                        v.tensor_copy(qkvtm[j][:, 512 * oc:512 * (oc + 1)], pb[2 + oc][:]).then_inc(s["qe"], 1)
                    else:
                        if i >= 2:
                            v.wait_ge(s["odone"], i - 1)
                        v.tensor_copy(vsb[j][:], pb[4][:]).then_inc(s["qe"], 1)
                v.drain()
                v.tensor_tensor(out=sqb[:], in0=qkvtm[j][:], in1=qkvtm[j][:], op=ALU.mult)
                v.drain()
                v.tensor_reduce(out=ssq[:].unsqueeze(2),
                                in_=sqb[:].rearrange("p (h d) -> p h d", d=128),
                                axis=AX.X, op=ALU.add).then_inc(s["rm1"], 1)
                v.wait_ge(s["rm2"], i + 1)
                v.reciprocal(out=rr8[:], in_=srt8[:])
                v.drain()
                v.tensor_tensor(out=rrq[:], in0=rr8[:, 0:4], in1=gain_tm[:], op=ALU.mult)
                v.drain()
                v.tensor_tensor(out=qn[:].rearrange("p (h d) -> p h d", d=128),
                                in0=qkvtm[j][:, 0:512].rearrange("p (h d) -> p h d", d=128),
                                in1=rrq[:].unsqueeze(2).broadcast_to([128, 4, 128]), op=ALU.mult)
                v.tensor_tensor(out=kn[:].rearrange("p (h d) -> p h d", d=128),
                                in0=qkvtm[j][:, 512:1024].rearrange("p (h d) -> p h d", d=128),
                                in1=rr8[:, 4:8].unsqueeze(2).broadcast_to([128, 4, 128]), op=ALU.mult)
                v.drain()
                v.wait_ge(s["cs0" if j == 0 else "cs1"], 32 * (i // 2 + 1))
                if i >= 2:
                    v.wait_ge(s["odone"], i - 1)
                for src, dst in ((qn, qro[j]), (kn, kro[j])):
                    sv = src[:].rearrange("p (h d) -> p h d", d=128)
                    dv = dst[:].rearrange("p (h d) -> p h d", d=128)
                    r1 = rop1[:].rearrange("p (h d) -> p h d", d=64)
                    r2 = rop2[:].rearrange("p (h d) -> p h d", d=64)
                    cosb = csb[j][:, 0:64].unsqueeze(1).broadcast_to([128, 4, 64])
                    sinb = csb[j][:, 64:128].unsqueeze(1).broadcast_to([128, 4, 64])
                    v.tensor_tensor(out=r1, in0=sv[:, :, 0:64], in1=cosb, op=ALU.mult)
                    v.tensor_tensor(out=r2, in0=sv[:, :, 64:128], in1=sinb, op=ALU.mult)
                    v.drain()
                    v.tensor_tensor(out=dv[:, :, 0:64], in0=r1, in1=r2, op=ALU.add)
                    v.drain()
                    v.tensor_tensor(out=r1, in0=sv[:, :, 0:64], in1=sinb, op=ALU.mult)
                    v.tensor_tensor(out=r2, in0=sv[:, :, 64:128], in1=cosb, op=ALU.mult)
                    v.drain()
                    v.tensor_tensor(out=dv[:, :, 64:128], in0=r2, in1=r1, op=ALU.subtract)
                    v.drain()
                v.drain().then_inc(s["rope"], 1)
                v.drain().then_inc(s["csf"], 1)
                v.wait_ge(s["tp"], g0 + 5)
                if i >= 2:
                    v.wait_ge(s["odone"], i - 1)
                v.tensor_copy(qfmsb[j][:], pb[(g0 + 4) % 2][:]).then_inc(s["tpe"], 1)
                v.wait_ge(s["tp"], g0 + 6)
                v.tensor_copy(kfmsb[j][:], pb[(g0 + 5) % 2][:]).then_inc(s["tpe"], 1)
                v.drain()
                v.drain().then_inc(s["kfme"], 1)
                v.tensor_reduce(out=red4[:].unsqueeze(2),
                                in_=qfmsb[j][:].rearrange("p (h t) -> p h t", t=128),
                                axis=AX.X, op=ALU.add)
                v.drain()
                v.tensor_tensor(out=qsum[:], in0=qsum[:], in1=red4[:], op=ALU.add)
                v.drain()
                v.wait_ge(s["kp"], i + 1)
                v.tensor_copy(kpTsb[:], pb[5][0:NJL, :]).then_inc(s["kpe"], 1)
                v.drain()
                v.wait_ge(s["kpt"], i + 1)
                if i >= 1:
                    v.wait_ge(s["odone"], i)
                v.tensor_copy(kpnsb[:], pb[6][:, 0:128]).then_inc(s["kptev"], 1)
                v.drain()
                v.tensor_tensor(out=sqb[:, 0:128], in0=kpnsb[:], in1=kpnsb[:], op=ALU.mult)
                v.drain()
                v.tensor_reduce(out=ss4[:].unsqueeze(2),
                                in_=sqb[:, 0:128].rearrange("p (h j) -> p h j", j=NJL),
                                axis=AX.X, op=ALU.add).then_inc(s["kq1"], 1)
                v.wait_ge(s["kq2"], i + 1)
                v.tensor_scalar(out=rn4[:], in0=srt4[:], scalar1=1e-12, scalar2=0.0,
                                op0=ALU.max, op1=ALU.add)
                v.drain()
                v.reciprocal(out=rn4[:], in_=rn4[:])
                v.drain()
                v.tensor_tensor(out=kpnsb[:].rearrange("p (h j) -> p h j", j=NJL),
                                in0=kpnsb[:].rearrange("p (h j) -> p h j", j=NJL),
                                in1=rn4[:].unsqueeze(2).broadcast_to([128, 4, NJL]),
                                op=ALU.mult).then_inc(s["kpn"], 1)
                v.drain()
                v.wait_ge(s["cent"], i + 1)
                v.tensor_copy(cent_acc[0:2, 128 * i:128 * (i + 1)], pb[7][0:2, 0:128]).then_inc(s["centev"], 1)
                v.drain()
            head(0)
            for i in range(NT):
                if i + 1 < NT:
                    head(i + 1)
                rest(i)

        def p1_act(a):
            a.wait_ge(s["c0"], 240)

            def ahead(i):
                a.wait_ge(s["ln1"], i + 1)
                a.activation(out=sd1[:], in_=mv[:, 1:2], func=AF.Sqrt,
                             bias=1e-5, scale=1.0).then_inc(s["ln2"], 1)

            ahead(0)
            for i in range(NT):
                if i + 1 < NT:
                    ahead(i + 1)
                a.wait_ge(s["rm1"], i + 1)
                a.activation(out=srt8[:], in_=ssq[:], func=AF.Sqrt,
                             bias=1e-6, scale=1.0 / 128.0).then_inc(s["rm2"], 1)
                a.wait_ge(s["kq1"], i + 1)
                a.activation(out=srt4[:], in_=ss4[:], func=AF.Sqrt,
                             bias=0.0, scale=1.0).then_inc(s["kq2"], 1)

        def p1_gp(gp):
            pass

        def p1_dve_full(v):
            v.memset(qsum[:], 0.0)
            v.drain()
            p1_dve(v)

        phase({"sync": p1_sp, "tensor": p1_pe, "vector": p1_dve_full,
               "scalar": p1_act, "gpsimd": p1_gp})
        ph1.close()

        ph15 = ExitStack()

        def sb15(name, shape, dtype=F32):
            return ph15.enter_context(nc.sbuf_tensor(name, shape, dtype))

        kpnh = sb15("kpnh", [128, NT * NJL])
        kfmt4 = sb15("kfmt4", [32, 512])
        centm = sb15("centm", [128, NJL])
        css1 = sb15("css1", [128, 1])
        csr1 = sb15("csr1", [128, 1])
        cfm = sb15("cfm", [32, 128])
        qm = sb15("qm", [1, NJL])
        qss = sb15("qss", [1, 1])
        qsr = sb15("qsr", [1, 1])
        qpn_row = sb15("qpn_row", [1, NJL])
        qpn_col = sb15("qpn_col", [32, 1])
        dots2 = sb15("dots2", [2, NT * 128])
        odrow = sb15("odrow", [1, NT * 128])
        dmin2 = sb15("dmin2", [1, 128])
        radr = sb15("radr", [1, 128])
        srow1 = sb15("srow1", [1, 128])
        scT = sb15("scT", [128, 1])
        sbc = sb15("sbc", [128, 128])
        junk15 = sb15("junk15", [128, 128])
        rank = sb15("rank", [128, 1])
        sel01 = sb15("sel01", [128, 1])
        frc = sb15("frc", [128, 1])
        mcol = sb15("mcol", [128, 1])
        possb = sb15("possb", [128, 1])
        selrow = sb15("selrow", [1, 128])
        selrep = sb15("selrep", [128, 33])
        O1F = 16 * (4 * NT + 1)

        seq = Seq()
        g15c = {"n": 0}

        def gdma(gp, out, in_, **kw):
            g15c["n"] += 16
            gp.dma_start(out=out, in_=in_, **kw).then_inc(s["g15"], 16)
            gp.wait_ge(s["g15"], g15c["n"])

        def frc_init(v):
            v.memset(frc[:], 0.0)
            v.drain()
            v.tensor_scalar(out=junk15[:, 0:1], in0=parm[:], scalar1=0.0, scalar2=0.0,
                            op0=ALU.mult, op1=ALU.add)
        seq.step("vector", frc_init)
        seq.step("gpsimd", lambda gp: gdma(gp, frc[126:128, :], ones_d[0:2, :]))

        for hl in range(HPC):
            def ld(sp, hl=hl):
                sp.dma_start(out=kpnh[:].rearrange("p (i j) -> p i j", j=NJL),
                             in_=kpn_d[:, hl, :, :]).then_inc(s["o1"], 16)
                sp.dma_start(out=centm[:],
                             in_=centd.rearrange("par (i h j) -> i par h j", h=HPC, j=NJL)[:, :, hl, :]).then_inc(s["o1"], 16)
                sp.wait_ge(s["o1"], O1F + 32 * (hl + 1))
            seq.step("sync", ld)

            def c1(v, hl=hl):
                v.tensor_scalar(out=centm[:], in0=centm[:],
                                scalar1=1.0 / 64.0, scalar2=0.0, op0=ALU.mult, op1=ALU.add)
                v.drain()
                v.tensor_tensor(out=junk15[:, 0:NJL], in0=centm[:], in1=centm[:], op=ALU.mult)
                v.drain()
                v.tensor_reduce(out=css1[:], in_=junk15[:, 0:NJL], axis=AX.X, op=ALU.add)
            seq.step("vector", c1)
            seq.step("scalar", lambda a: a.activation(out=csr1[:], in_=css1[:],
                                                      func=AF.Sqrt, bias=0.0, scale=1.0))

            def c2(v):
                v.tensor_scalar(out=csr1[:], in0=csr1[:], scalar1=1e-12, scalar2=0.0,
                                op0=ALU.max, op1=ALU.add)
                v.drain()
                v.reciprocal(out=csr1[:], in_=csr1[:])
                v.drain()
                v.tensor_scalar(out=centm[:], in0=centm[:], scalar1=csr1[:], scalar2=0.0,
                                op0=ALU.mult, op1=ALU.add)
            seq.step("vector", c2)
            seq.step("tensor", lambda pe: pe.transpose(pb[0][0:NJL, 0:128], centm[:], iden[:]))
            seq.step("vector", lambda v: v.tensor_copy(cfm[:], pb[0][0:NJL, 0:128]))
            seq.step("tensor", lambda pe, hl=hl: pe.matmul(pb[2][0:1, 0:NJL],
                                                           qsum[:, hl:hl + 1], proj[:],
                                                           start=True, stop=True))

            def q1(v):
                v.tensor_scalar(out=qm[:], in0=pb[2][0:1, 0:NJL], scalar1=1.0 / 8192.0,
                                scalar2=0.0, op0=ALU.mult, op1=ALU.add)
                v.drain()
                v.tensor_tensor(out=junk15[0:1, 0:NJL], in0=qm[:], in1=qm[:], op=ALU.mult)
                v.drain()
                v.tensor_reduce(out=qss[:], in_=junk15[0:1, 0:NJL], axis=AX.X, op=ALU.add)
            seq.step("vector", q1)
            seq.step("scalar", lambda a: a.activation(out=qsr[:], in_=qss[:],
                                                      func=AF.Sqrt, bias=0.0, scale=1.0))

            def q2(v):
                v.tensor_scalar(out=qsr[:], in0=qsr[:], scalar1=1e-12, scalar2=0.0,
                                op0=ALU.max, op1=ALU.add)
                v.drain()
                v.reciprocal(out=qsr[:], in_=qsr[:])
                v.drain()
                v.tensor_scalar(out=qpn_row[:], in0=qm[:], scalar1=qsr[:], scalar2=0.0,
                                op0=ALU.mult, op1=ALU.add)
            seq.step("vector", q2)
            seq.step("tensor", lambda pe: pe.transpose(pb[2][0:NJL, 128:129], qpn_row[:], iden[0:1, 0:1]))
            seq.step("vector", lambda v: v.tensor_copy(qpn_col[:], pb[2][0:NJL, 128:129]))
            for g4 in range(NT // 4):
                def t1(pe, g4=g4):
                    for t in range(4):
                        i = 4 * g4 + t
                        pe.transpose(pb[0][0:NJL, 128 * t:128 * (t + 1)],
                                     kpnh[:, NJL * i:NJL * (i + 1)], iden[:])
                seq.step("tensor", t1)
                seq.step("vector", lambda v: v.tensor_copy(kfmt4[:], pb[0][0:NJL, :]))

                def t2(pe, g4=g4):
                    for t in range(4):
                        i = 4 * g4 + t
                        pe.matmul(pb[1][0:2, 128 * t:128 * (t + 1)],
                                  cfm[:, 2 * i:2 * i + 2],
                                  kfmt4[:, 128 * t:128 * (t + 1)], start=True, stop=True)
                seq.step("tensor", t2)

                def t3(v, g4=g4):
                    v.tensor_copy(dots2[0:2, 512 * g4:512 * (g4 + 1)], pb[1][0:2, 0:512])
                seq.step("vector", t3)

            seq.step("gpsimd", lambda gp: gdma(gp, dotsd[:], dots2[:]))
            seq.step("gpsimd", lambda gp: gdma(gp, odrow[:], dotsd[1:2, :]))

            def rad(v):
                v.tensor_reduce(out=dmin2[:].rearrange("p (a b) -> p a b", b=1)[:, 0::2, :],
                                in_=dots2[0:1, :].rearrange("p (i c) -> p i c", c=128)[:, :, 0:64],
                                axis=AX.X, op=ALU.min)
                v.tensor_reduce(out=dmin2[:].rearrange("p (a b) -> p a b", b=1)[:, 1::2, :],
                                in_=odrow[0:1, :].rearrange("p (i c) -> p i c", c=128)[:, :, 64:128],
                                axis=AX.X, op=ALU.min)
                v.drain()
                v.tensor_scalar(out=radr[:], in0=dmin2[:], scalar1=1.0, scalar2=-1.0,
                                op0=ALU.subtract, op1=ALU.mult)
                v.drain()
                v.tensor_scalar(out=radr[:], in0=radr[:], scalar1=0.0, scalar2=1.0,
                                op0=ALU.max, op1=ALU.min)
            seq.step("vector", rad)
            seq.step("tensor", lambda pe: pe.matmul(pb[2][0:1, 256:384], qpn_col[:], cfm[:],
                                                    start=True, stop=True))
            seq.step("vector", lambda v: v.tensor_tensor(out=srow1[:], in0=pb[2][0:1, 256:384],
                                                         in1=radr[:], op=ALU.add))
            seq.step("tensor", lambda pe: pe.transpose(pb[2][0:128, 384:385], srow1[:], iden[0:1, 0:1]))
            seq.step("vector", lambda v: v.tensor_copy(scT[:], pb[2][0:128, 384:385]))
            seq.step("tensor", lambda pe: pe.matmul(pb[3][:, 0:128], onesrow[:], srow1[:],
                                                    start=True, stop=True))
            seq.step("vector", lambda v: v.tensor_copy(sbc[:], pb[3][:, 0:128]))

            def rk(v):
                v.memset(sbc[:, 126:128], NEG)
                v.drain()
                v.tensor_scalar(out=junk15[:], in0=sbc[:], scalar1=scT[:], scalar2=0.0,
                                op0=ALU.is_gt, op1=ALU.add, accum_out=rank[:])
                v.drain()
                v.tensor_scalar(out=sel01[:], in0=rank[:], scalar1=64.0, scalar2=0.0,
                                op0=ALU.is_lt, op1=ALU.add)
                v.drain()
                v.tensor_tensor(out=sel01[:], in0=sel01[:], in1=frc[:], op=ALU.max)
            seq.step("vector", rk)

            seq.step("tensor", lambda pe: pe.matmul(pb[3][:, 0:1], ltm[:], sel01[:],
                                                    start=True, stop=True))
            seq.step("vector", lambda v: v.tensor_copy(possb[:], pb[3][:, 0:1]))

            def mkonehot(v):
                v.tensor_scalar(out=junk15[:], in0=iotab[:], scalar1=possb[:], scalar2=sel01[:],
                                op0=ALU.is_equal, op1=ALU.mult)
                v.drain()
                v.tensor_scalar(out=junk15[:], in0=junk15[:], scalar1=blkid[:], scalar2=0.0,
                                op0=ALU.mult, op1=ALU.add)
            seq.step("vector", mkonehot)
            seq.step("tensor", lambda pe: pe.matmul(pb[3][0:1, 128:256], onescol[:], junk15[:],
                                                    start=True, stop=True))
            seq.step("vector", lambda v: v.tensor_copy(selrow[:], pb[3][0:1, 128:256]))

            def selrep_mm(pe):
                me = selrow[:].rearrange("p (k t) -> p t k", t=2)
                pe.matmul(pb[3][0:64, 256:289], onesrow[0:1, 0:64], me[:, 0, 0:33],
                          start=True, stop=True)
                pe.matmul(pb[3][0:64, 289:322], onesrow[0:1, 0:64], me[:, 1, 0:33],
                          start=True, stop=True)
            seq.step("tensor", selrep_mm)

            def mkidx(v, hl=hl):
                v.tensor_copy(selrep[0:64, :], pb[3][0:64, 256:289])
                v.tensor_copy(selrep[64:128, :], pb[3][0:64, 289:322])
                v.drain()
                v.tensor_scalar(out=selrep[:], in0=selrep[:], scalar1=64.0, scalar2=poff[:],
                                op0=ALU.mult, op1=ALU.add)
                v.drain()
                v.tensor_scalar(out=selrep[:], in0=selrep[:], scalar1=float(8192 * hl),
                                scalar2=0.0, op0=ALU.add, op1=ALU.add)
                v.drain()
                v.tensor_copy(idx_all[:, 33 * hl:33 * (hl + 1)], selrep[:])
                v.drain()
            seq.step("vector", mkidx)


        phase(seq.emitters(0))
        ph15.close()

        ph2 = ExitStack()

        def sb2(name, shape, dtype=F32):
            return ph2.enter_context(nc.sbuf_tensor(name, shape, dtype))

        NKC = 33
        KS = NKC * 128
        ksg = [sb2(f"ksg{j}", [128, KS]) for j in range(2)]
        kfs = [sb2(f"kfs{j}", [128, KS]) for j in range(2)]
        vsg = [sb2(f"vsg{j}", [128, KS]) for j in range(2)]
        qb2 = [sb2(f"qb2_{j}", [128, 512]) for j in range(2)]
        expT = [sb2(f"expT{j}", [128, 512]) for j in range(2)]
        expsum = sb2("expsum", [128, 512])
        rrow = sb2("rrow", [1, 512])
        rb_sb = sb2("rb_sb", [128, 512])
        ysb = [sb2(f"ysb{j}", [128, 512]) for j in range(2)]
        NQC = 16

        def p2_gp(gp):
            def gathers(hl):
                jh = hl % 2
                kvs = s["kv2a" if jh == 0 else "kv2b"]
                if hl >= 2:
                    gp.wait_ge(s["hfree2"], hl - 1)
                ktm_flat = ktm_d.rearrange("h s d -> (h s) d")
                vtm_flat = vtm_d.rearrange("h s d -> (h s) d")
                for c in range(NKC):
                    gp.indirect_dma_start(
                        out=ksg[jh][:, 128 * c:128 * (c + 1)], out_offset=None,
                        in_=ktm_flat,
                        in_offset=bass.IndirectOffsetOnAxis(
                            ap=idx_all[:, 33 * hl + c:33 * hl + c + 1], axis=0)).then_inc(kvs, 16)
                for c in range(NKC):
                    gp.indirect_dma_start(
                        out=vsg[jh][:, 128 * c:128 * (c + 1)], out_offset=None,
                        in_=vtm_flat,
                        in_offset=bass.IndirectOffsetOnAxis(
                            ap=idx_all[:, 33 * hl + c:33 * hl + c + 1], axis=0)).then_inc(kvs, 16)

            gathers(0)
            gathers(1)
            for hl in range(HPC):
                for qc in range(NQC):
                    t2 = NQC * hl + qc
                    gp.wait_ge(s["ye"], t2 + 1)
                    gp.dma_start(out=yfm_d[128 * hl:128 * (hl + 1), 512 * qc:512 * (qc + 1)],
                                 in_=ysb[qc % 2][:]).then_inc(s["yoa" if qc % 2 == 0 else "yob"], 16)
                if hl + 2 < HPC:
                    gathers(hl + 2)
            gp.wait_ge(s["yoa"], 16 * NQC * HPC // 2)
            gp.wait_ge(s["yob"], 16 * NQC * HPC // 2)

        def p2_sp(sp):
            for hl in range(HPC):
                for qc in range(NQC):
                    t2 = NQC * hl + qc
                    if t2 >= 2:
                        sp.wait_ge(s["qf2"], t2 - 1)
                    sp.dma_start(out=qb2[qc % 2][:],
                                 in_=qfm_d[128 * hl:128 * (hl + 1), 512 * qc:512 * (qc + 1)]).then_inc(s["q2a" if qc % 2 == 0 else "q2b"], 16)
            sp.wait_ge(s["yoa"], 16 * NQC * HPC // 2)
            sp.wait_ge(s["yob"], 16 * NQC * HPC // 2)

        def p2_pe(pe):
            pend = []

            def flush_pend():
                while pend:
                    fn = pend.pop(0)
                    fn()
            for hl in range(HPC):
                jh = hl % 2
                pe.wait_ge(s["kv2a" if jh == 0 else "kv2b"], 16 * 2 * NKC * (hl // 2 + 1))
                flush_pend()
                if hl >= 1:
                    pe.wait_ge(s["exq"], NKC * NQC * hl)
                for c in range(NKC):
                    kk2 = NKC * hl + c
                    if kk2 >= 2:
                        pe.wait_ge(s["kt2e"], kk2 - 1)
                    pe.transpose(pb[c % 2][:, 0:128], ksg[jh][:, 128 * c:128 * (c + 1)],
                                 iden[:]).then_inc(s["kt2"], 1)
                pe.wait_ge(s["kt2e"], NKC * (hl + 1))
                for qc in range(NQC):
                    t2 = NQC * hl + qc
                    pe.wait_ge(s["q2a" if qc % 2 == 0 else "q2b"], 16 * ((t2 // 2) + 1))
                    if t2 >= 2:
                        pe.wait_ge(s["ye"], t2 - 1)
                    for kc in range(NKC):
                        kk = NKC * t2 + kc
                        if kk >= 2:
                            pe.wait_ge(s["exq"], kk - 1)
                        pe.matmul(pb[kc % 2][:], kfs[jh][:, 128 * kc:128 * (kc + 1)],
                                  qb2[qc % 2][:], start=True, stop=True).then_inc(s["lt"], 1)
                        if kc == 1:
                            flush_pend()
                        if kc >= 1:
                            kprev = kc - 1
                            pe.wait_ge(s["exq"], NKC * t2 + kprev + 1)
                            pe.matmul(pb[2 + (qc % 2)][:],
                                      vsg[jh][:, 128 * kprev:128 * (kprev + 1)],
                                      expT[kprev % 2][:],
                                      start=(kprev == 0), stop=False).then_inc(s["av"], 1)
                    pe.wait_ge(s["exq"], NKC * (t2 + 1))
                    pe.matmul(pb[2 + (qc % 2)][:], vsg[jh][:, 128 * (NKC - 1):128 * NKC],
                              expT[(NKC - 1) % 2][:], start=False, stop=True).then_inc(s["av"], 1)
                    pe.wait_ge(s["esum"], NKC * (t2 + 1))
                    if t2 >= 1:
                        pe.wait_ge(s["rcp2"], t2)
                    pe.matmul(pb[4][0:1, :], onescol[:], expsum[:],
                              start=True, stop=True).then_inc(s["sum2"], 1)

                    def mk_bcast(t2=t2):
                        def f():
                            pe.wait_ge(s["rcp2"], t2 + 1)
                            if t2 >= 1:
                                pe.wait_ge(s["rbe2"], t2)
                            pe.matmul(pb[5][:], onesrow[:], rrow[:],
                                      start=True, stop=True).then_inc(s["bc2"], 1)
                        return f
                    pend.append(mk_bcast())
                    pe.drain().then_inc(s["qf2"], 1)
                pe.drain().then_inc(s["hfree2"], 1)
            flush_pend()

        def p2_act(a):
            for hl in range(HPC):
                for qc in range(NQC):
                    t2 = NQC * hl + qc
                    for kc in range(NKC):
                        kk = NKC * t2 + kc
                        a.wait_ge(s["lt"], kk + 1)
                        if kk >= 2:
                            a.wait_ge(s["av"], kk - 1)
                            a.wait_ge(s["esum"], kk - 1)
                        a.activation(out=expT[kc % 2][:], in_=pb[kc % 2][:], func=AF.Exp,
                                     bias=0.0, scale=RS128).then_inc(s["exq"], 1)
                    a.wait_ge(s["bc2"], t2 + 1)
                    if t2 >= 1:
                        a.wait_ge(s["ye"], t2)
                    a.copy(out=rb_sb[:], in_=pb[5][:]).then_inc(s["rbe2"], 1)

        def p2_dve(v):
            for hl in range(HPC):
                jh = hl % 2
                for c in range(NKC):
                    kk2 = NKC * hl + c
                    v.wait_ge(s["kt2"], kk2 + 1)
                    v.tensor_copy(kfs[jh][:, 128 * c:128 * (c + 1)],
                                  pb[c % 2][:, 0:128]).then_inc(s["kt2e"], 1)
                v.drain()
                for qc in range(NQC):
                    t2 = NQC * hl + qc
                    for kc in range(NKC):
                        kk = NKC * t2 + kc
                        v.wait_ge(s["exq"], kk + 1)
                        if kc == 0:
                            if t2 >= 1:
                                v.wait_ge(s["sum2"], t2)
                            v.tensor_copy(expsum[:], expT[0][:]).then_inc(s["esum"], 1)
                        else:
                            v.tensor_tensor(out=expsum[:], in0=expsum[:], in1=expT[kc % 2][:],
                                            op=ALU.add).then_inc(s["esum"], 1)
                        v.drain()
                    v.wait_ge(s["sum2"], t2 + 1)
                    if t2 >= 1:
                        v.wait_ge(s["bc2"], t2)
                    v.reciprocal(out=rrow[:], in_=pb[4][0:1, :]).then_inc(s["rcp2"], 1)
                    v.drain()
                    v.wait_ge(s["av"], NKC * (t2 + 1))
                    v.wait_ge(s["rbe2"], t2 + 1)
                    if t2 >= 2:
                        v.wait_ge(s["yoa" if qc % 2 == 0 else "yob"], 16 * ((t2 - 2) // 2 + 1))
                    v.tensor_tensor(out=ysb[qc % 2][:], in0=pb[2 + (qc % 2)][:],
                                    in1=rb_sb[:], op=ALU.mult).then_inc(s["ye"], 1)
                    v.drain()

        phase({"sync": p2_sp, "tensor": p2_pe, "scalar": p2_act, "vector": p2_dve,
               "gpsimd": p2_gp})
        ph2.close()

        ph3 = ExitStack()

        def sb3(name, shape, dtype=F32):
            return ph3.enter_context(nc.sbuf_tensor(name, shape, dtype))

        wo_sb = sb3("wo_sb", [128, HPC * DM])
        yb3 = [sb3(f"yb3_{j}", [128, 512]) for j in range(2)]
        rsb = [sb3(f"rsb{j}", [128, DM]) for j in range(2)]
        dpk = [sb3(f"dpk{j}", [128, DM], I32) for j in range(2)]
        dscs = [sb3(f"dscs{j}", [128, DM], F16) for j in range(2)]
        e_k = sb3("e_k", [128, 1024], U16)
        e_sb = sb3("e_sb", [128, 1024], U16)
        e_u = sb3("e_u", [128, 1024], U16)
        e_a = sb3("e_a", [128, 1024], U16)
        e_i = sb3("e_i", [128, 1024], U16)
        wout = [sb3(f"wout{j}", [128, DM]) for j in range(2)]

        def fp4_q(v, src_i32_ap, jj, q, outs):
            p16 = src_i32_ap.bitcast(U16)[:, 1024 * q:1024 * (q + 1)]
            v.tensor_scalar(out=e_k[:], in0=p16, scalar1=4 * jj, scalar2=7,
                            op0=ALU.logical_shift_right, op1=ALU.bitwise_and)
            v.tensor_scalar(out=e_sb[:], in0=p16, scalar1=12 - 4 * jj, scalar2=32768,
                            op0=ALU.logical_shift_left, op1=ALU.bitwise_and)
            v.drain()
            v.tensor_scalar(out=e_u[:], in0=e_k[:], scalar1=2, scalar2=0,
                            op0=ALU.min, op1=ALU.add)
            v.tensor_scalar(out=e_i[:], in0=e_k[:], scalar1=0, scalar2=0,
                            op0=ALU.is_gt, op1=ALU.add)
            v.drain()
            v.tensor_tensor(out=e_a[:], in0=e_k[:], in1=e_u[:], op=ALU.add)
            v.drain()
            v.tensor_scalar(out=e_a[:], in0=e_a[:], scalar1=26, scalar2=512,
                            op0=ALU.add, op1=ALU.mult)
            v.drain()
            v.tensor_tensor(out=e_a[:], in0=e_a[:], in1=e_i[:], op=ALU.mult)
            v.drain()
            v.tensor_tensor(out=e_a[:], in0=e_a[:], in1=e_sb[:], op=ALU.bitwise_or)
            v.drain()
            for hh in range(2):
                outs(hh, e_a[:].rearrange("p (o t) -> p o t", t=2)[:, :, hh].bitcast(F16))
            v.drain()

        WUNITS = [(P, jj, q) for P in range(6) for jj in range(4) for q in range(4)]

        TORDER = ([16 * g + r for g in range(4) for r in range(8)]
                  + [16 * g + 8 + r for g in range(4) for r in range(8)])

        def rs_sem(ii, j):
            if ii < 32:
                return s["rsAa" if j == 0 else "rsAb"]
            return s["rsBa" if j == 0 else "rsBb"]

        def p3_sp(sp):
            sp.dma_start(out=wo_sb[:].rearrange("p (h o) -> p h o", o=DM),
                         in_=woT_d.rearrange("(h p) o -> p h o", p=128)).then_inc(s["wo3"], 16)
            for ii in range(NT):
                i = TORDER[ii]
                j = ii % 2
                if ii >= 2:
                    sp.wait_ge(s["w3"], 4 * (ii - 1))
                sp.dma_start(out=yb3[j][:].rearrange("p (h t) -> p h t", t=128),
                             in_=yfm_d.rearrange("(h p) t -> p h t", p=128)[:, :, 128 * i:128 * (i + 1)]).then_inc(s["y3a" if j == 0 else "y3b"], 16)
                sp.wait_ge(s["e3"], 4 * (ii + 1))
                sp.dma_start(out=rs_in_h.ap()[128 * ii:128 * (ii + 1), :], in_=rsb[j][:]).then_inc(rs_sem(ii, j), 16)
            sp.wait_ge(s["rsBa"], 16 * 16)
            sp.wait_ge(s["rsBb"], 16 * 16)

        def p3_pe(pe):
            pe.wait_ge(s["wo3"], 16)
            for ii in range(NT):
                j = ii % 2
                pe.wait_ge(s["y3a" if j == 0 else "y3b"], 16 * (ii // 2 + 1))
                if ii >= 2:
                    pe.wait_ge(s["e3"], 4 * (ii - 1))
                for dmc in range(4):
                    for hl in range(HPC):
                        ins = pe.matmul(pb[4 * j + dmc][:],
                                        yb3[j][:, 128 * hl:128 * (hl + 1)],
                                        wo_sb[:, DM * hl + 512 * dmc:DM * hl + 512 * (dmc + 1)],
                                        start=(hl == 0), stop=(hl == 3))
                        if hl == 3:
                            ins.then_inc(s["w3"], 1)

        def p3_dve(v):
            un = {"u": 0}

            def emit_unit():
                if un["u"] >= len(WUNITS):
                    return
                P, jj, q = WUNITS[un["u"]]
                jP = P % 2
                if q == 0 and jj == 0:
                    v.wait_ge(s["gdpe"], P + 1)
                if q == 0:
                    for hh in range(2):
                        prevu = 4 * P + jj - 1
                        if prevu >= 0:
                            v.wait_ge(s["gst"], 2 * prevu + hh + 1)

                def outs(hh, src_ap, P=P, jj=jj, q=q, jP=jP):
                    last = (q == 3)
                    ins = v.tensor_tensor(out=wout[hh][:, 512 * q:512 * (q + 1)],
                                          in0=src_ap, in1=dscs[jP][:, 512 * q:512 * (q + 1)],
                                          op=ALU.mult)
                    if last and hh == 1:
                        ins.then_inc(s["wde5"], 1)
                fp4_q(v, dpk[jP][:], jj, q, outs)
                un["u"] += 1

            for ii in range(NT):
                j = ii % 2
                for dmc in range(4):
                    v.wait_ge(s["w3"], 4 * ii + dmc + 1)
                    if ii >= 2:
                        if ii < 34:
                            v.wait_ge(rs_sem(ii - 2, j), 16 * (ii // 2))
                        else:
                            v.wait_ge(rs_sem(ii - 2, j), 16 * ((ii - 32) // 2))
                    v.tensor_copy(rsb[j][:, 512 * dmc:512 * (dmc + 1)],
                                  pb[4 * j + dmc][:]).then_inc(s["e3"], 1)
                v.drain()
                target = min(len(WUNITS), ((ii + 1) * len(WUNITS)) // 44)
                while un["u"] < target:
                    emit_unit()
            while un["u"] < len(WUNITS):
                emit_unit()

        G15F = g15c["n"]
        g3 = {"n": 0}

        def p3_gp(gp):
            def g3dma(out, in_):
                g3["n"] += 16
                gp.dma_start(out=out, in_=in_).then_inc(s["g15"], 16)
                gp.wait_ge(s["g15"], G15F + g3["n"])
            for P in range(6):
                jP = P % 2
                if P >= 2:
                    gp.wait_ge(s["wde5"], 4 * (P - 1))
                g3dma(dpk[jP][:], downT_d[128 * P:128 * (P + 1), :])
                g3dma(dscs[jP][:], dsc_d[128 * P:128 * (P + 1), :])
                gp.sem_inc(s["gdpe"], 1)
                for jj in range(4):
                    for hh in range(2):
                        sid = 8 * P + 4 * hh + jj
                        gp.wait_ge(s["wde5"], 4 * P + jj + 1)
                        g3dma(wde_d[sid], wout[hh][:])
                        gp.sem_inc(s["gst"], 1)
            gp.wait_ge(s["rsAa"], 16 * 16)
            gp.wait_ge(s["rsAb"], 16 * 16)
            gp.collective_compute(
                "ReduceScatter", ALU.add,
                replica_groups=[[0, 1, 2, 3], [4, 5, 6, 7]],
                ins=[rs_in_h.ap()[0:4096, :].opt()],
                outs=[rs_out_h.ap()[0:1024, :].opt()],
            ).then_inc(s["ccA"], 1)
            gp.wait_ge(s["rsBa"], 16 * 16)
            gp.wait_ge(s["rsBb"], 16 * 16)
            gp.collective_compute(
                "ReduceScatter", ALU.add,
                replica_groups=[[0, 1, 2, 3], [4, 5, 6, 7]],
                ins=[rs_in_h.ap()[4096:8192, :].opt()],
                outs=[rs_out_h.ap()[1024:2048, :].opt()],
            ).then_inc(s["ccB"], 1)

        phase({"sync": p3_sp, "tensor": p3_pe, "vector": p3_dve, "gpsimd": p3_gp})
        ph3.close()

        ph6 = ExitStack()

        def sb6(name, shape, dtype=F32):
            return ph6.enter_context(nc.sbuf_tensor(name, shape, dtype))

        NOC = 22
        NTH = 8
        LINS = 2 * NOC
        ln2fmh = sb6("ln2fmh", [128, 16 * 128 * NTH])
        ln2t6 = [sb6(f"ln2t6_{j}", [128, DM]) for j in range(2)]
        wgu2 = [sb6(f"wgu2_{j}", [128, 16 * 512]) for j in range(2)]
        gpk = [sb6(f"gpk{j}", [128, 512], I32) for j in range(2)]
        upk = [sb6(f"upk{j}", [128, 512], I32) for j in range(2)]
        gscs = [sb6(f"gscs{j}", [128, 512], F16) for j in range(2)]
        uscs = [sb6(f"uscs{j}", [128, 512], F16) for j in range(2)]
        f_k = sb6("f_k", [128, 1024], U16)
        f_sb = sb6("f_sb", [128, 1024], U16)
        f_a = sb6("f_a", [128, 1024], U16)
        f_i = sb6("f_i", [128, 1024], U16)
        sg_sb = [sb6(f"sg_sb{j}", [128, 256]) for j in range(2)]
        htl = [sb6(f"htl{j}", [128, 256]) for j in range(2)]
        x5b = [sb6(f"x5b{j}", [128, DM]) for j in range(2)]
        r5b = [sb6(f"r5b{j}", [128, DM]) for j in range(2)]
        bnst5 = sb6("bnst5", [128, 24])
        mv5 = sb6("mv5", [128, 2])
        sd5 = sb6("sd5", [128, 1])
        rstd5 = sb6("rstd5", [128, 1])

        def pass_of(g):
            P, c = g // 8, g % 8
            return 4 * P + (c % 4)

        def p6_sp(sp):
            def x_tile(i):
                j = i % 2
                if i == 0:
                    sp.wait_ge(s["ccA"], 1)
                if i == 8:
                    sp.wait_ge(s["ccB"], 1)
                if i >= 2:
                    sp.wait_ge(s["l5s"], i - 1)
                    sp.wait_ge(s["x2wa" if j == 0 else "x2wb"], 16 * ((i - 2) // 2 + 1))
                x5s = s["x5a" if j == 0 else "x5b"]
                for c in range(2):
                    sp.dma_start(out=x5b[j][:, 1024 * c:1024 * (c + 1)],
                                 in_=xq[128 * i:128 * (i + 1), 1024 * c:1024 * (c + 1)]).then_inc(x5s, 16)
                    sp.dma_start(out=r5b[j][:, 1024 * c:1024 * (c + 1)],
                                 in_=rs_out_h.ap()[128 * i:128 * (i + 1), 1024 * c:1024 * (c + 1)]).then_inc(x5s, 16)
                sp.wait_ge(s["l51"], 2 * i + 1)
                sp.dma_start(out=x2_d[128 * i:128 * (i + 1), :], in_=r5b[j][:]).then_inc(s["x2wa" if j == 0 else "x2wb"], 16)

            for half in range(2):
                for i in range(NTH * half, NTH * (half + 1)):
                    x_tile(i)
                for oc in range(NOC):
                    lin = NOC * half + oc
                    for tl in range(NTH):
                        gl = NTH * lin + tl
                        sp.wait_ge(s["h6"], gl + 1)
                        sp.dma_start(out=h_d[128 * (NTH * half + tl):128 * (NTH * half + tl) + 128,
                                             256 * oc:256 * (oc + 1)],
                                     in_=htl[tl % 2][:]).then_inc(s["hw6a" if tl % 2 == 0 else "hw6b"], 16)
            sp.wait_ge(s["hw6a"], 16 * 2 * NOC * NTH // 2)
            sp.wait_ge(s["hw6b"], 16 * 2 * NOC * NTH // 2)
            sp.wait_ge(s["x2wa"], 16 * NT5 // 2)
            sp.wait_ge(s["x2wb"], 16 * NT5 // 2)

        def p6_gp(gp):

            def pkload(lin):
                oc = lin % NOC
                j = lin % 2
                g6s = s["g6a" if j == 0 else "g6b"]
                if lin >= 2:
                    gp.wait_ge(s["e6"], 8 * (lin - 1))
                gp.dma_start(out=gpk[j][:].rearrange("p (P o) -> p P o", o=256),
                             in_=gateT_d.rearrange("(P p) o -> p P o", p=128)[:, :, 256 * oc:256 * (oc + 1)]).then_inc(g6s, 16)
                gp.dma_start(out=upk[j][:].rearrange("p (P o) -> p P o", o=256),
                             in_=upT_d.rearrange("(P p) o -> p P o", p=128)[:, :, 256 * oc:256 * (oc + 1)]).then_inc(g6s, 16)
                gp.dma_start(out=gscs[j][:].rearrange("p (P o) -> p P o", o=256),
                             in_=gsc_d.rearrange("(P p) o -> p P o", p=128)[:, :, 256 * oc:256 * (oc + 1)]).then_inc(g6s, 16)
                gp.dma_start(out=uscs[j][:].rearrange("p (P o) -> p P o", o=256),
                             in_=usc_d.rearrange("(P p) o -> p P o", p=128)[:, :, 256 * oc:256 * (oc + 1)]).then_inc(g6s, 16)

            def ln_store(lt):
                gp.wait_ge(s["l52"], lt + 1)
                for c in range(4):
                    gp.dma_start(out=ln2_d[128 * lt:128 * (lt + 1), 512 * c:512 * (c + 1)],
                                 in_=x5b[lt % 2][:, 512 * c:512 * (c + 1)]).then_inc(
                        s["lsa" if lt % 2 == 0 else "lsb"], 16)

            def ln_reload(lt):
                p = "lsa" if lt % 2 == 0 else "lsb"
                gp.wait_ge(s[p], 64 * (lt // 2 + 1))
                gp.sem_inc(s["l5s"], 1)
                if lt >= 2:
                    gp.wait_ge(s["t5"], 16 * (lt - 1))
                for c in range(4):
                    gp.dma_start(out=ln2t6[lt % 2][:, 512 * c:512 * (c + 1)],
                                 in_=ln2_d[128 * lt:128 * (lt + 1), 512 * c:512 * (c + 1)]).then_inc(
                        s["lra" if lt % 2 == 0 else "lrb"], 16)
                gp.wait_ge(s["lra" if lt % 2 == 0 else "lrb"], 64 * (lt // 2 + 1))
                gp.sem_inc(s["l6"], 1)

            def ln_chain_block(lo, hi):
                ln_store(lo)
                for lt in range(lo + 1, hi):
                    ln_store(lt)
                    ln_reload(lt - 1)
                ln_reload(hi - 1)

            pkload(0)
            pkload(1)
            ln_chain_block(0, NTH)
            for lin in range(2, NOC + 2):
                pkload(lin)
            ln_chain_block(NTH, 2 * NTH)
            for lin in range(NOC + 2, LINS):
                pkload(lin)

        def p6_pe(pe):
            for half in range(2):
                if half >= 1:
                    pe.wait_ge(s["sil6"], NTH * NOC * half)
                    pe.wait_ge(s["h6"], NTH * NOC * half)
                for tl in range(NTH):
                    lt = NTH * half + tl
                    pe.wait_ge(s["l6"], lt + 1)
                    for bg in range(4):
                        gti = 4 * lt + bg
                        if gti >= 2:
                            pe.wait_ge(s["t5e"], gti - 1)
                        for gg in range(4):
                            g = 4 * bg + gg
                            P, c = g // 8, g % 8
                            ins = pe.transpose(pb[gti % 2][:, 128 * gg:128 * (gg + 1)],
                                               ln2t6[tl % 2][:].rearrange("p (a b) -> p b a", b=8)[:, c, 128 * P:128 * (P + 1)],
                                               iden[:])
                            if gg == 3:
                                ins.then_inc(s["t5"], 1)
                    pe.drain().then_inc(s["t5"], 12)
                for oc in range(NOC):
                    lin = NOC * half + oc
                    j = lin % 2
                    if oc == 0:
                        pe.wait_ge(s["t5e"], 32 * (half + 1))
                    for tl in range(NTH):
                        gl = NTH * lin + tl
                        if gl >= 2:
                            pe.wait_ge(s["sil6"], gl - 1)
                        for g in range(16):
                            pe.wait_ge(s["e6"], 8 * lin + pass_of(g) + 1)
                            ins = pe.matmul(pb[tl % 2][:, 0:256],
                                            ln2fmh[:, 1024 * g + 128 * tl:1024 * g + 128 * (tl + 1)],
                                            wgu2[j][:, 512 * g:512 * g + 256],
                                            start=(g == 0), stop=(g == 15))
                            if g == 15:
                                ins.then_inc(s["mmg6"], 1)
                        if gl >= 2:
                            pe.wait_ge(s["h6"], gl - 1)
                        for g in range(16):
                            ins = pe.matmul(pb[2 + (tl % 2)][:, 0:256],
                                            ln2fmh[:, 1024 * g + 128 * tl:1024 * g + 128 * (tl + 1)],
                                            wgu2[j][:, 512 * g + 256:512 * (g + 1)],
                                            start=(g == 0), stop=(g == 15))
                            if g == 15:
                                ins.then_inc(s["mmu6"], 1)

        def p6_dve(v):
            def p5b_tile(i):
                j = i % 2
                v.wait_ge(s["x5a" if j == 0 else "x5b"], 64 * (i // 2 + 1))
                if i >= 2:
                    v.wait_ge(s["x2wa" if j == 0 else "x2wb"], 16 * ((i - 2) // 2 + 1))
                v.tensor_tensor(out=r5b[j][:], in0=x5b[j][:], in1=r5b[j][:],
                                op=ALU.add).then_inc(s["l51"], 1)
                v.drain()
                v.drain().then_inc(s["l51"], 1)
                for c in range(4):
                    v.bn_stats(out=bnst5[:, 6 * c:6 * (c + 1)], in_=r5b[j][:, 512 * c:512 * (c + 1)])
                v.drain()
                v.bn_aggr(out=mv5[:], in_=bnst5[:]).then_inc(s["ln1"], 1)
                v.wait_ge(s["ln2"], NT + i + 1)
                v.reciprocal(out=rstd5[:], in_=sd5[:])
                v.drain()
                v.tensor_scalar(out=x5b[j][:], in0=r5b[j][:], scalar1=mv5[:, 0:1],
                                scalar2=rstd5[:], op0=ALU.subtract, op1=ALU.mult).then_inc(s["l52"], 1)
                v.drain()

            def extract_pass(lin, P, jj):
                j = lin % 2
                p16g = gpk[j][:, 256 * P:256 * (P + 1)].bitcast(U16)
                p16u = upk[j][:, 256 * P:256 * (P + 1)].bitcast(U16)
                for p16, lo in ((p16g, 0), (p16u, 512)):
                    v.tensor_scalar(out=f_k[:, lo:lo + 512], in0=p16, scalar1=4 * jj, scalar2=7,
                                    op0=ALU.logical_shift_right, op1=ALU.bitwise_and)
                    v.tensor_scalar(out=f_sb[:, lo:lo + 512], in0=p16, scalar1=12 - 4 * jj,
                                    scalar2=32768, op0=ALU.logical_shift_left, op1=ALU.bitwise_and)
                v.drain()
                v.tensor_scalar(out=f_a[:], in0=f_k[:], scalar1=2, scalar2=0,
                                op0=ALU.min, op1=ALU.add)
                v.tensor_scalar(out=f_i[:], in0=f_k[:], scalar1=0, scalar2=0,
                                op0=ALU.is_gt, op1=ALU.add)
                v.drain()
                v.tensor_tensor(out=f_a[:], in0=f_a[:], in1=f_k[:], op=ALU.add)
                v.drain()
                v.tensor_scalar(out=f_a[:], in0=f_a[:], scalar1=26, scalar2=512,
                                op0=ALU.add, op1=ALU.mult)
                v.drain()
                v.tensor_tensor(out=f_a[:], in0=f_a[:], in1=f_i[:], op=ALU.mult)
                v.drain()
                v.tensor_tensor(out=f_a[:], in0=f_a[:], in1=f_sb[:], op=ALU.bitwise_or)
                v.drain()
                for half_o, lo, sc in ((0, 0, gscs[j]), (256, 512, uscs[j])):
                    for hh in range(2):
                        g = 8 * P + 4 * hh + jj
                        ins = v.tensor_tensor(
                            out=wgu2[j][:, 512 * g + half_o:512 * g + half_o + 256],
                            in0=f_a[:, lo:lo + 512].rearrange("p (o t) -> p o t", t=2)[:, :, hh].bitcast(F16),
                            in1=sc[:, 256 * P:256 * (P + 1)], op=ALU.mult)
                        if half_o == 256 and hh == 1:
                            ins.then_inc(s["e6"], 1)
                v.drain()

            EUNITS = [(lin, P, jj) for lin in range(LINS) for P in range(2) for jj in range(4)]
            eu = {"u": 0}

            def emit_units(upto_lin):
                while eu["u"] < len(EUNITS) and EUNITS[eu["u"]][0] <= upto_lin:
                    lin, P, jj = EUNITS[eu["u"]]
                    j = lin % 2
                    if P == 0 and jj == 0:
                        v.wait_ge(s["g6a" if j == 0 else "g6b"], 64 * (lin // 2 + 1))
                        if lin >= 2:
                            v.wait_ge(s["mmu6"], NTH * (lin - 1))
                    extract_pass(lin, P, jj)
                    eu["u"] += 1

            for half in range(2):
                base = NTH * half

                def evac_tile(tl):
                    for bg in range(4):
                        v.wait_ge(s["t5"], 16 * base + 16 * tl + bg + 1)
                        v.tensor_copy(
                            ln2fmh[:].rearrange("p (g t) -> p g t", t=128 * NTH)[:, 4 * bg:4 * bg + 4,
                                                                                 128 * tl:128 * (tl + 1)],
                            pb[(4 * tl + bg) % 2][:].rearrange("p (g t) -> p g t", t=128)).then_inc(s["t5e"], 1)
                    v.drain()

                p5b_tile(base + 0)
                p5b_tile(base + 1)
                for k in range(2, NTH):
                    evac_tile(k - 2)
                    p5b_tile(base + k)
                evac_tile(NTH - 2)
                evac_tile(NTH - 1)
                for oc in range(NOC):
                    lin = NOC * half + oc
                    emit_units(lin)
                    for tl in range(NTH):
                        gl = NTH * lin + tl
                        v.wait_ge(s["mmu6"], gl + 1)
                        v.wait_ge(s["sil6"], gl + 1)
                        if gl >= 2:
                            v.wait_ge(s["hw6a" if tl % 2 == 0 else "hw6b"], 16 * ((gl - 2) // 2 + 1))
                        v.tensor_tensor(out=htl[tl % 2][:], in0=sg_sb[tl % 2][:],
                                        in1=pb[2 + (tl % 2)][:, 0:256], op=ALU.mult).then_inc(s["h6"], 1)
                        v.drain()
                        if tl >= 1 and (half < 1 or oc < NOC - 1):
                            emit_units(lin + (1 if tl >= 1 else 0))

        def p6_act(a):
            for half in range(2):
                for i in range(NTH * half, NTH * (half + 1)):
                    a.wait_ge(s["ln1"], NT + i + 1)
                    a.activation(out=sd5[:], in_=mv5[:, 1:2], func=AF.Sqrt,
                                 bias=1e-5, scale=1.0).then_inc(s["ln2"], 1)
                for oc in range(NOC):
                    lin = NOC * half + oc
                    for tl in range(NTH):
                        gl = NTH * lin + tl
                        a.wait_ge(s["mmg6"], gl + 1)
                        if gl >= 2:
                            a.wait_ge(s["h6"], gl - 1)
                        a.activation(out=sg_sb[tl % 2][:], in_=pb[tl % 2][:, 0:256],
                                     func=AF.Silu, bias=0.0, scale=1.0).then_inc(s["sil6"], 1)

        phase({"sync": p6_sp, "vector": p6_dve, "tensor": p6_pe, "scalar": p6_act,
               "gpsimd": p6_gp})
        ph6.close()

        ph7 = ExitStack()

        def sb7(name, shape, dtype=F32):
            return ph7.enter_context(nc.sbuf_tensor(name, shape, dtype))

        hb7 = sb7("hb7", [128, 2 * IFF])
        hfm = sb7("hfm", [128, 2 * NOSET * 128])
        wdb = [sb7(f"wdb{j}", [128, DM]) for j in range(2)]
        x2t = [sb7(f"x2t{j}", [128, DM]) for j in range(2)]
        outb = [sb7(f"outb{j}", [128, DM]) for j in range(2)]
        bnst7 = sb7("bnst7", [128, 24])
        mv7 = sb7("mv7", [128, 2])
        sd7 = sb7("sd7", [128, 1])
        rr7 = sb7("rr7", [128, 1])

        def hfm_slice(tt, sid):
            o = (2 * sid + tt) * 128
            return hfm[:, o:o + 128]

        def p7_sp(sp):
            for tg in range(8):
                if tg >= 1:
                    sp.wait_ge(s["t7"], 96 * tg)
                sp.dma_start(out=hb7[:].rearrange("p (t o) -> p t o", o=IFF),
                             in_=h_d.rearrange("(g t p) o -> g p t o", t=2, p=128)[tg]).then_inc(s["h7"], 16)
                for tt in range(2):
                    if tg >= 1:
                        sp.wait_ge(s["b72"], 2 * tg)
                    sp.dma_start(out=x2t[tt][:],
                                 in_=x2_d[256 * tg + 128 * tt:256 * tg + 128 * (tt + 1), :]).then_inc(s["xl7"], 16)
                for sid in range(NOSET):
                    kk = NOSET * tg + sid
                    if kk >= 2:
                        sp.wait_ge(s["mm7"], kk - 1)
                    sp.dma_start(out=wdb[sid % 2][:], in_=wde_d[sid]).then_inc(s["w7a" if sid % 2 == 0 else "w7b"], 16)
                for tt in range(2):
                    sp.wait_ge(s["b72"], 2 * tg + tt + 1)
                    sp.dma_start(out=out_d[256 * tg + 128 * tt:256 * tg + 128 * (tt + 1), :],
                                 in_=outb[tt][:]).then_inc(s["o7a" if tt == 0 else "o7b"], 16)
            sp.wait_ge(s["o7a"], 16 * 8)
            sp.wait_ge(s["o7b"], 16 * 8)

        def p7_pe(pe):
            for tg in range(8):
                pe.wait_ge(s["h7"], 16 * (tg + 1))
                if tg >= 1:
                    pe.wait_ge(s["b72"], 2 * tg)
                tb = 0
                for tt in range(2):
                    for sid in range(NOSET):
                        P, c = sid // 8, sid % 8
                        ncnt = 64 if P == 5 else 128
                        gidx = 96 * tg + tb
                        if gidx >= 2:
                            pe.wait_ge(s["t7e"], gidx - 1)
                        pe.transpose(pb[tb % 2][0:ncnt, 0:128],
                                     hb7[:].rearrange("p (t a b) -> p t b a", t=2, b=8)[:, tt, c, 128 * P:128 * P + ncnt],
                                     iden[:]).then_inc(s["t7"], 1)
                        tb += 1
                pe.wait_ge(s["t7e"], 96 * (tg + 1))
                for sid in range(NOSET):
                    kk = NOSET * tg + sid
                    pe.wait_ge(s["w7a" if sid % 2 == 0 else "w7b"], 16 * (kk // 2 + 1))
                    for tt in range(2):
                        for dmc in range(4):
                            ins = pe.matmul(pb[4 * tt + dmc][:],
                                            hfm_slice(tt, sid),
                                            wdb[sid % 2][:, 512 * dmc:512 * (dmc + 1)],
                                            start=(sid == 0), stop=(sid == NOSET - 1))
                            if tt == 1 and dmc == 3:
                                ins.then_inc(s["mm7"], 1)

        def p7_dve(v):
            for tg in range(8):
                tb = 0
                for tt in range(2):
                    for sid in range(NOSET):
                        P = sid // 8
                        ncnt = 64 if P == 5 else 128
                        gidx = 96 * tg + tb
                        v.wait_ge(s["t7"], gidx + 1)
                        v.tensor_copy(hfm_slice(tt, sid)[0:ncnt, :], pb[tb % 2][0:ncnt, 0:128])
                        if ncnt == 64:
                            v.memset(hfm_slice(tt, sid)[64:128, :], 0.0)
                        v.drain()
                        v.drain().then_inc(s["t7e"], 1)
                        tb += 1
                v.wait_ge(s["mm7"], NOSET * (tg + 1))
                v.wait_ge(s["xl7"], 32 * (tg + 1))
                for tt in range(2):
                    for c in range(4):
                        v.bn_stats(out=bnst7[:, 6 * c:6 * (c + 1)], in_=pb[4 * tt + c][:])
                    v.drain()
                    v.bn_aggr(out=mv7[:], in_=bnst7[:]).then_inc(s["b71"], 1)
                    v.drain()
                    v.tensor_tensor(out=sd7[:], in0=mv7[:, 0:1], in1=mv7[:, 0:1], op=ALU.mult)
                    v.drain()
                    v.tensor_tensor(out=sd7[:], in0=sd7[:], in1=mv7[:, 1:2], op=ALU.add).then_inc(s["b71"], 1)
                    v.wait_ge(s["kq2"], NT + 2 * tg + tt + 1)
                    v.reciprocal(out=rr7[:], in_=sd7[:])
                    v.drain()
                    if tg >= 1:
                        v.wait_ge(s["o7a" if tt == 0 else "o7b"], 16 * tg)
                    for dmc in range(4):
                        v.scalar_tensor_tensor(out=outb[tt][:, 512 * dmc:512 * (dmc + 1)],
                                               in0=pb[4 * tt + dmc][:], scalar=rr7[:],
                                               in1=x2t[tt][:, 512 * dmc:512 * (dmc + 1)],
                                               op0=ALU.mult, op1=ALU.add)
                    v.drain()
                    v.drain().then_inc(s["b72"], 1)

        def p7_act(a):
            for tg in range(8):
                for tt in range(2):
                    a.wait_ge(s["b71"], 2 * (2 * tg + tt) + 2)
                    a.activation(out=sd7[:], in_=sd7[:], func=AF.Sqrt,
                                 bias=1e-6, scale=1.0).then_inc(s["kq2"], 1)

        phase({"sync": p7_sp, "tensor": p7_pe, "vector": p7_dve, "scalar": p7_act})
        ph7.close()

    return nc


def _rope_tables():
    t = np.arange(S, dtype=np.float64)
    inv = (10000.0 ** (-(np.arange(0, 128, 2, dtype=np.float64)) / 128.0))
    ang32 = np.float32(np.outer(t, inv).astype(np.float32))
    cos = np.cos(ang32.astype(np.float64)).astype(np.float32)
    sin = np.sin(ang32.astype(np.float64)).astype(np.float32)
    return np.ascontiguousarray(cos), np.ascontiguousarray(sin)


_PROJ_B64 = (
    "eJwVmndYz98bxpsaioZoaCgJRaSlz/t5KsmeSXYyMjIyS0Za0tSQdhSpKAop+ryfB4WKZMvW15aVTRo/v7/Puc4f5zn3fb/u6zr2"
    "Pw5Ji42OQG5fLzg976mwdP8p8ZSmITxcGcajbyujcEmEQdX9sXlELO30nIJ7HoUhHHXDV6pmYlBhf3xTWw9jcozgkH44D5mpipqv"
    "h9KhHTP5yI5SHvc3BB6nDqVBP+fh+4Tt7Jt1gTZ1uMFK31jM7fUX2tekihcfnqbCwT9g6O5cYXDDNnI7kQ9/fvcHuZJ62ttkzBPS"
    "9Mm820I4mKBEQsZuodv83SDxcebpFXmwZMg8tNXJgfQrnVSxvkvY0tyT1D4aUl3hCpz4cjeofZGhhNwhtNE7hazeRYCL8iw82y7h"
    "1rqenLbfGbP7hlPwIWNxce1Ztn20G3stjOCvySsFR38dHtp+VCycbcYX+zXAm17A2++vZ/cBKXhZ66dk6nQfDh+sRG2+T+B6tCOZ"
    "ya3HS0WIj8+MwKeOJznrItPtlpvcOnUMrj32AB9+UEDlPgp44d959q9q6d6WA6DS/wAtv7cQm161i/a7/gix6ydBN3s77sjyFvIC"
    "tbBlpx6dWVjJwRlXYcg+Q2zOy5MmrojkkA0ddNEBedzN/bB2cCfcexFPlqmu6FdzWYgwPERPFg6EOxODIe3vfUj3j6TQYf6UUKlK"
    "+j3viL2wFL7KBGKcWTXLhRyA25IXEDbJGYeabeEXckbCg+QtOPOxDeu8l6EhKk1o9jUSfvytohR/Leyxt1T8nTgZI4MDqKWoFzbd"
    "6wm7k36B5etNlPLmNNs/s4X/ordDsbolFsz1ETTcb4l/r1lT+UZPrOWhtP/wPdYNfo4/XW3w+oL9UH97C1reLeZuUZM4+ZgcZHXu"
    "ExteNMIvt3U0v5s/pDoYwO4D2jx1pjxbPVoqWVNkh993Pqakjbo8+usINFLbxl3WG4SKtnwwHhYhBBqMEb1ddXjvp12gLE2GgimT"
    "hB2ZHnxlpTVoyZby8OkS3nCpEeaOiqSfziS6dE7nlb2OwOVzneJBWTM+cKkHQMFXCmlbweP//h6542QoDgnYjK/ak7kiIZv7hYfx"
    "80WzadEeZ340TVtoP9Md/B6eImmpCuvfH8lLWk3wzDRzVL6vixrjL2Lynxiez7v4kZcfr4/KIvsaRTqw0xHXF/Xj9bvz4PEKbXRp"
    "OS2k1SmyzKB2eLrIDo1fHQLXsGZ2fe7Cs2SD8Le4CWWUjXljjhl+7JCniIzJ7KV/V/hjmS3MHd4bYv9sEDOOaGP/Pc7CnbKXojAr"
    "H0Me24v26sjfjTdjStY5qli7ncRpE7GiYCQu4m54/PI4qiicjwezevAQ/e40LjAPfhXuEsfWHcD2ZQm8Vc6Jj2inoVdNKj7v44mk"
    "kE+tSqqo/NQeldJ24z7lv4L+yqGiWbwMBP97UdYT/PnunlNVZSP/QkZJO53dexsqGi+yrfxujByrDh+3fBXG+c7DNPG85EnfDLa/"
    "30broodi7e0Tgnb1fozU3IXNE805/mw+Pv69lC3uHMSucwH865EGN7shHpt+AeWCHtHwrSM5cuB+5DG3xECrh/RE8SVc2KGM1Skj"
    "eWD9HpD5uh+CXjlSxMe9nLrUBO2G+ZHv9AZ4ZvOXJl9MpXxHE/AwmAFlqyZL9799AX8to2D1wBDq/nwN/Db9QatKbHhxmzqkxJyh"
    "ldunc/4dH2hc64E+3q/h6Kty2JnZJW6oyeORVhV0Q7WK3uRn0d70Ouo/2fSfzuPQ71iOsPWDHhct2UfXu7rjE59k3r3uE8h/9GOX"
    "cX1R32ghrRzuAUO/LuSihqE82deSSlqG8/KJx+h2bSK+7ou8f3w2fFbNx3XvFyBqqKGnkhkMPV9Gg8deIWuz8WSkP4BsbmhR4yol"
    "3lc4ixWGzcZ3oRL89n0p1Cx7Qv9N6Qff/6bxMQUl9HQfw7pd5XTMRI4nl7GQtWo+h06M4q0GlnRB5g2oOxrgG+olOnyaKpyd6cbO"
    "Y81gwvJXsHlcHBodbIHDO0/Tt5ll0pJziJFuRvgpyQIT9eX4m8Fr0h1UCztUO0mpaA5/tLHA060tdPVSBFs5n4ZWnUC2fPALtHVj"
    "Wbm+D0Tst2VcspczL1+kpkEZMKJ6nbRlTTr6Onpy2YoZNLAolVz6rCLdqLmgUK3BvrYlEGRSy5sfTcZk9zyKzkzmjfP1cJlpExR+"
    "/ihpWZfJV07ZcIyFK68aZ8jLnQpYxSyCFxyfw79WRVDl9dewtOURvDw3m3/JbGfJgjsQYFGIet7KOKS/NbmYHxN+uRxzOnfuoViy"
    "dL946WogtRxfwcbv/lBSRR++2LGRh6yXUu/tSjg06he8XTSfj1sl8tSyOIz3SuF8v038MVeWF5ztx4EGcyEmuQViIqbBhZHd8M2X"
    "ebglYY7QqHhaotFtOXbH8Xh7pSU8iNZAnFvCft0PwehHDVTb8QDf6XuD9281Tlebww8s0ngzdlDemwncVqCCagO/wsvJLngifTLb"
    "Gdayw+1QGvLtnvDjgRWeuP9YOHCkmEKu7OCUrD/0fXwmxQzTwlvN0dxoHwEfg3thrNdDivmRAtHXHsOOeUeFz9kyeM+/AXv6RUPu"
    "CCc20h3ECdnyqHyjDZ4lLmPrS4P45IEWyL7fG3W3R7LyHF2c1VhGdLc7db4LgcqRf8Qd/ifRcnqYdK7jUOh6GIVNAz+RSqgsvNPY"
    "B8ZtMrhcsxyPDVNge725+LphNuZq53Cj7wj48GUGJdhG4/bZdwV9r9nYZPhOsF3fjSM/WfHWgIHsZSDSY2cP1HuTBwLvqhq6fhjs"
    "dFfGPk+sWL/fanyiuxJkHAxxV6M2zhowj6s7D2DYhud0tU8SH7wWBEqD7lP/PmHY+bICClNO0Z7KJ3Ts7QUMTbcTLtXuQdUPNVA5"
    "+SDv81bkK57GeClMHk1Ee340OQetS20xfmkYJ//MwHH7LlFVUypmPjvAHyX3adMVOUi/9YCS9QcKCzVy4UlxuzC3VsAcK0MuKUrC"
    "pTmPqJ8aiLx4PO1dHidM2TIF6ZIyBHxMorGrwniylj0GqneTyBlbiDM7F7Bm21VB1+EZhmwp5p+fB3Jnl5Y4oTwJnmX547VrtcIY"
    "Ky8qqSLqOpCFUaGh/N3sOdWWREDnlLMUsc4UWy/HoFmkHFk+PsKqwdli3IEpeD5llkQu8hF4+4+i86PjcFGcLc5Y5IW+AeGoxKk8"
    "f10AR9oZovKJJjZ+3EeqtWIQ+hbvpBa1+fhmejh33Y3Bce+rsUB2FP4MXIdXrfuibJohFd3QZP+dDUJe4geaUnKSHky/j1T7Xtqs"
    "voJ/D7dCpRBN/NpaAut3ufEPl1Wik7MJuHMSvh+QwiE9CmCBz0lx7bcg1puai4PrLwn338vAd5NQsi5Ux+dX/Why8nY+mZ4pOi4b"
    "hU3tWvzQzxJePi0UJylGCHlyDTCYW8TQ7oESm72V4obrF2Hb8XYeKPVhnxma/PXuWkx98gJM/yzgDK/lqPnbgp0vldKfne/IZJ0X"
    "2s2NwPSdqzi6WZOtcRCaKXvB72ZFNLvxh/ppzMeZqUo4s+0Zd1urDp4XX9NJtWOwdU4BvQpzELp9yoeg7y/BcOMjMrWXpznL5nPB"
    "u1rhQAPRjJm2KI1koeDkHrbxrOefMUewJUgRQ0oPk3fdPtxVyFSaoIUhLvloNOApOKhcg6Z5hynx/k6e/qeSdG6Pxbg/LrwPI+nG"
    "0ydkHngOvyZ9gA/FV2Dw9Q6auVADNx4qxNh5ijTq9VFcHhHPOS9iaM/ZUvHU4UH0X4I9nis35LfZs4RTNp3UNeq8eExyTKidMA0L"
    "xaPC9aORsNB8D7Z63Ba+r0yhJW1pMGq0HKcqLSar8tVwZosG7z+HfGuiLsdevQ1TLozhntibpPfUUK/+lWCp30FHhgWChc5e2LvD"
    "Ea1MZ8GIcjeMWeyC8mcvg2paNWXLhcCv1Jd0xH4afJSZg4Gb3cB8ZodkbkQSPQvywZP2V3lzuwJ3vkyEE6MT6WpsJP5ZXCyMu34B"
    "r+a4w7fxxCah/qJLvTVv914tLK57AdfepHJbggx6zHCBtMFxkFr3FKK6WqX9vi7APns24LOR46ljiSm9LeuOmbERUo8b3uKp1TKU"
    "9tQCfZ2S2Dj2ong/Yitc9xhCCuIxuu+QQW2XEOpOdoLf0hzs4TCY5DbfEG7ILOWRB2JQ2SIOp220R/nZ2TR76WtQaxfwzHTARWeX"
    "YtCcGDR8vp0DzEIxmg2ccl50/tP4bOx11BbNCqeIf6+m4Gs5Y76r9R0ieyzEpccnUd4/reX/ekK3osKw9o4Fnss5SzccLqHlRxv+"
    "qj0Um/I34QbbAWQv/1M0+dtEf7N3wly1VOq+7564xy8Ezz9bjFYZNrxs+Fg0DgiEL7462DLksKBpZYpb3g+hKasI3oenwYawaei9"
    "aTQrt8rCiBMJ+Fm3L8robcZchWXs/uYs5j+WY+i8TvZJ3c4duauP7/NP0CJNBayTDER/jy1sW68CC+77wDvfdBziM5SGsj2DYl9U"
    "DM+jnQm/oPfEVFh/ax5UD8tkbctedGvFSvHpozq4tk2AXhIXuvtZF6e+awPhkpxYW9ULJhQ2Sf3XvIXXdRbCvJtB3BXUm0e63IND"
    "AxRZbc15mrdMCddHF/KanwoUY7gU1gop0LllINUkR+HbFQMFvRWPSKFsK6cOcsV7NkMweKEix9qMwGA7JxjQ0wT29JxFybtXcXdX"
    "U5TxTeR6vdW8r/so3N/rLJj4jILYw8pQWbKP/VaY8J6jn+HuEWN+ekUPJSW9+UfLF6gaq41uBpqMWlIuXD0Wj+xqgGXpzbTeYxB8"
    "fKjCjmkhlFecKoTeWogz7jljdMgiOGR/HUZNrOQNA4LIN2A7JEbtZqNiINttFdj92p5/2Wwl5kWrg0XrYTog6SBlxYOgXJsEOe/V"
    "RYefKhA7KQ0UNAn2P1TGfvnOwu2ZV9CgXA8TboZi3sAbMNNoJ7o+//fWRi+k1ZlGPFLw4rZZjeCUoSaET5iFPx/8xz06dKWRnkdo"
    "3/skrN8WSLmn/8DebofAUaEbrh+fQxGql6jHzEWgnvkUrRYnYuHWOP5rPZCWB/dH7RfPYNjFWp7Z5xwvHroIN+1wYQ+5Z5KFWgq4"
    "VvIdtg+UF0ZajGO9cQ9wnbkevpyzj9cEL+eSf71kgagNrTtbQfjpixV5JgAPRJK1yubQUx7kkxDO027kceBrL37VWgWXxnjx5vo1"
    "9NXiI/XMl0G5BjmuOnVEqJ9XK5YMu40RuYMo+lwP7HFYSu8L7lKKwwhM2FJBfnNrOa7GWVz3MJvnv3TGcv/uePR8KviPrWTorsFJ"
    "ZbPEMpV74OqiyvqWbbRItppMch5C+T9Nrg5g1FWfBjrS0dh5vYbXzUnkOTN6Yv+wDAicbMS75aaDo+Zb6jsln41y+sKiUXZYN34F"
    "nnR7SjEDj8NCjZ9kVvtRtMs6SqP146D6kY4w91kE7kj/BWHtl0UN8T9YURrMZV2baEBTX1a5353X3ZTw64v/2PHDSRLLBN5tac5O"
    "L6Lo/j4VPvOrwdH+sTnvHmOF3U/40NJtGtjv7Hw85xoIa8YlwvrxSjhbTo7LAzJEj5Qw/PFtBK/LtOKe8zxxYmQHjAhuhzVqf+DX"
    "tAUspO8RTvwN4w3Ke1jOLhAd3wzjyj3XxW8VObhUcwpuuqKDc263CvlXwhiEm7RS2I5fPPph6EpVdH3xhOy8wmHu6z28f9MEjJR/"
    "CsMKJkubZnhhdwdTPnf/Cf/dMoGjvVxhRfUN2ptkzgU1N7C7/3tqV/fES28eUtXJFNF9ZSc05S9Fj20zuNPHEPfpDEJbS2MqPz2O"
    "oNIAZ3+yY5fRObyflDERIrjn/S/w9EEMvZg5jfPrYoWAce8g+YoR7m2bzxltvwTLzlLQMvkBGqfn8qdUCbRtaIByaqbijgF0J3c8"
    "5sM0yarnTyFyeT03OCfhqpr0yh0y+nAofyArvTrM6sccqWzebvQbuYrirdNJ02ucxMBbg/uu+Ejx0VpQ/dcBQyf35CXNZ4Q/59Zg"
    "ZPolIWqarPPefgfEkS978YT5IqtMdMDa6Iwq5csd0uV/PTFw/Th27vIQvkjSeKvDabAz+Q/gZy8cMXg4ng2+jXpHk/jC/HrqcN4F"
    "KmcnCXGva+j7yyGsNCwE1765LDTOLefpmXHwePJJUl83CP3XvyKj6cvwwydfcF50HU/M28PSLTH8bX6hZGv3HaT7dAHnDvLmgudd"
    "YGl2De2nm+GnrXfEopLv8GfrKvH5tHDUaN4DE5t3kGEvFxz3Jg37nXYA8/pEtBt+jHd66OCOY4Z4JL0HXu3qz4cvKeEgg3DqtvYM"
    "lOpZ02OlV3B+s7KoM9EZKo/LcKpid1riIYeaGyOpwTAGfz9op6S2UtqlsJu70s8LkSMIjl6eQqo+Cqw/XxQSNErh0I6hwrqb8lR6"
    "dC9e1FkobGpyADflndxoHsFNCsTJeJyl/Zso20xV1Mv/1x9vh+KfWV5YOHWMeEDUHvl87hFRaYaIj02iWV7lHHwbmQHPvvWjjlV7"
    "xLzsg5S5eB9OVp6O/63SxXUGUTgkfxeM7HWZOuZeFB6WWwnqloYw0vMKTn72gPv3dsWao4ep/WqetFh9C50yjeL+xzPAU6aKt94B"
    "3BKWAPYFuzm97Aj+V66IDh6W1PNEHC3XFXmyUqfTt8LDYvHAGkEvfjo9eGuH62Id2a92I6wRDtMxeg5Jjj3xzJ0AuO7QHx+rvmfJ"
    "LA/+s/0AdSnKi0d1+nOivD0+zg7lpd+r6NbpSCyO/yBKLhlA3XM7lm2ph3u6vVl70C/KDGv7lz0RHGOZSPuP1gD6nRbVl2UIhVmR"
    "PH/LS+jmMgpyXRfiaL0w2JM4He2PpcN493LRonIjFUnT2bZvLmVVijRZ7jt5GDVCiViKfxXc8LJXOclpWuPHahVucpkK418+p9Y+"
    "i7Fft99wa/lMnuOlwm/+nqDwit68vZsND5/6id0XFYlP49zZWceVO4ND2GNdGznX/6FhZj+Fq53ZvPGTMSbH11HvVC3nPva78cD5"
    "J3Bj0RPR9Y6RVO5sCtbNK8ey+8DJLRFs9GIrpu8ay/bnZvH+3se5MTOVLC9EkEewEwa0pkgOJ2qys/sXXvJwsdSqtyvVLXkrJI4e"
    "xPukHyDshhHeXuPNeyxPsc+K9ZSY/hSOHzkkFL3RxX+CY8++Xuy9KY8Co3whLmsYTdHL4rIMWag1eC1smRaMQ4NOc232RZLhpbhr"
    "+ED21Qvlz27HIP9XGContognvhSIyxvTaZXtQErtflAcoW+KGRkfaZxPCUTsWM1eSk4cWbYYtvcJ4GyrrZhTaMVtKj6YFqyC0u46"
    "6FuWzwVZSnjTu4a22YwjDbLks/+4PY8NaMjaofB8ZwW1ztxFU8fE8tY+lyDSbz562E/CR46j2DxRC8cON2aDWxv4Y/t51ogeD+LO"
    "ZOo/lARdmSoh4iqwrN0YuBuixltmyFHnnU76fuAQv/F9KQr+smwRvwIhrifPLnWjDQ7p1KvBXbjmdgfKdPLEyj66uLd1hTh9y1zh"
    "V2IMcR9fKo/II2O3AD4/JZedpjqR5flGMDsxHktT9vIp3WGYXn5WWB2uiv32KePelUOlt0q+iS+/XiG/n79p3oYtHOV6lVLHjcYC"
    "W1NsNtiIu36+o56DhuLSdFX4/vI7YahIv9vN+JnDatz0XyO4pPXmhxkVTnqBCYKqnz9In4aT7a9mOtKhjJrNB8XNs/Zx5d3jaDV2"
    "On795sV3uh/B1TdP8kC33vj1S4Xo2+oAx2RMWcVRhfUyzlOzkirnTugBdpbs1KT8FEZtK+EzQyc5PY6SAVODUIn+5cOkl2yKvXZU"
    "CwvfZeKdjrto3XKFu1+Lpf0V/bG1JYnuqAyHAUO24OKMOFy4bRXLGyvxypuKdEXnj8Cqi7EyVYO3KFRiv0JN6vfDmMcsCoQDg6bi"
    "qIC10Ngm4KaWUbzg21VsH2DPac0/RNUF8tye149T1qSLvccr4p1tunig/BvcDUzEZRbJ/K5rhfj79H345N74r5On0A3rXXDYbTf8"
    "N2IXTvCR8IE8VXxWbg7WIx4KPZyn8+9dSqBQ1YPk/C/Cd25A5eTF+Ec3GbwWP4H29inCue2jBaW45fDN7QS5qR8U/3jVUPiFmTz2"
    "xjbS97haZXDOUHxZeJCt9sxH3enp+OJCJLmUDUZ7u8dQNTdPUmytx6GW62jYrhooCDDFWu9DeEr6m45umgl37tniy47x4FajzXqb"
    "GvDPxVJ8F6YOw+9r8fB4LyHl/nwxxzsO15hfkbwLfkb9osKgIfgocnkCfOHDZKnaib36FtGUhgK8b30Kluy+BVZhKyjmR3f4WV4G"
    "Fw74kWxyBg90CueAVjv2sQ6iG4XNtChMAZem3cObZ6/ATm0dHjdNQikqc/k/zxRJ0mBnLM4po5W13Vn624DYcz+F6UykQjlFUsl9"
    "BQP0fkPlkJ8ctN5SeNp9CV+uXiUOH3iCHoaNgx+rY3hJlIJEcmM91Bn4sOuSUfSr6SNm2ijyunyBm0+owVn3WJ677QrNirrGjZeN"
    "YNusUTS4tkTiYbORF+iw5OuWKtwYUA3T5Su5qpYpQcsVr18I5MajrpgSvxcX3enOCkc1uPdJDTK8p4vfBANUOHuc7WtmUdOH55Rv"
    "epHeFLjTLtkENLhbBCv8u6Hx2ETSO3JFCN09BJLXl4rL6N/8p1vxs5ffwTtZHuda9sUABU00+m8F13XfKio4DSE0mcbmDd1p+NWP"
    "8MX5AmnGuOM/14HZjw/xC9OJeHOQGbfMMudNyr78uzpZnP63gbIMv9KFT1UU3GcVPZ9VUjVf9roYMPkTLbmYxOkP4tDqtDtrrs+g"
    "x2mROHLFC/pkdZPUP6ZBYNYZOFpwEJSiZvEPaxOc9eQKPbgbjPJeXyhduzsGFkwCM4d1+Pnzan51xJu+/zgjmb/hOm7yy6Py5UEo"
    "87oLvw86LKaeSSFDixcgG1HMtpXZ4kTfRiEhYAcMMAyS3iZDPFETAg0zPvD7vvNReaAeVpRshoE6elgQ/1FMq3LBZc5GHBw0nEaM"
    "/SPGW8aKQxu3YYvDIj7cX5bNu+Ww3lN98Nmsh9V1DbDoy3pQDzmIBhEsHpC8FA9ULYLbnyPIe2oDffwwDg9EGuHy1hisTvohlJoa"
    "C6F3plKDTH9USJBUDTT6Drtfb6bEOE+4u+8qLK1biVst+mCqqhSKeh6l4BPtMFGB4HXXCV52vRsntfzgu8vuQfignjj68wNK1XaD"
    "SWNWS2+8XgGO0b7w5W4PCNhoy3tL4lGjTzpkWMrw4zM3cEeEKdb1LpMMXK3GC4LPkUKAHJr8HgnGbklclgz8qV4R9SwZfq26QSVs"
    "yrfdanAoL6HQZYtxYIIZHplST6cPucNp5T7isLtOgv7szZxSI8uDR1wW5lz6IE5SyoYLuw1E02AdbraUSrdF7uTpMbGCf6cjGo2Y"
    "yGr9u6HduxicJJ38r74qQVBnJBnVDGCF35pcVW2KW3zLaP/OFEo69RWyLy7Bag2kXod8xd37ntH37R3Q879FeGHRTfrED3HKD3l0"
    "ykolbb+ZfOOUJd+/kyG2/ZA793ZWK3nJOXB7aze8IZ2Bf55GC28i+nOW+2P4qzSIzQ0suHBwKfQ/1AFfnu9k6Z8GYeurIGnzwTno"
    "6fAeBp9LgnWNm87+3JrFn4MqcIe5FVdkpLPK3td0In8BZY3aDS4VmfRf4n4OkUTST0k3envZX+LcH/Fvxg4a/3kILUuWwfqsJCjd"
    "acujH/Rlm4fHeELsAvC30YH+0Xq8Y+1HofJQPATaJPBmy4/C5v3n6cSLLWQ14RfsyRjO2cMCwP+aK/sMysMJ9naY+lhAieUHiIf1"
    "KJ9zgsqmHYWRsxXZN/4mFdvp4zV3dVp9I4R8D72ApXe78Qk9TTSms1TYV5frHSvp+OD3OKa7DFPGGmxySwXL1km4qZnIo5ce9Nmd"
    "ze5Kj6HF6ClZ65Tzjy1buGVSNU1p649qFRJWsLgHF+b90+cVYBNnLxaCy1hWzZ2XTKihjcGh/P5zPZw3/wHHFHdBXukyUvSLhSKH"
    "BDTJXov5ed8oz/ICr30vIO8ZjPXD5pH1wak8wccJ9UqHsM4+E/wgmYj7d8niCxtZbF57QuitGgv7dZPo2nQ1fjn4OC1IOlnV9bgb"
    "r9r4me6PUaGylr7gfbOM4pwOQiM1om3Mhn9M6YlfD76Cvn8Q1bJL8O6k2ZCWsY5Dlv1rdLJ/oNhFBVqDptC0YU2c038nFDsLuHKi"
    "F3lbyPMz+1v8ZshfOhN0kt9WXmTlk8MheFsgao/VwxHbDoH/7Cn8UGk/l10U4OjNBD7ZflnSPsoTFybpw8JANy7l35KDZ5IRPfpi"
    "14gJ+HDweL7yIBSr1YIlihdduFHnH0vnx/GUtY7od14VR32eQPF3XSllhZyzeW6QuMLLgCfEvBeKMqPw6Y9Yav7yiwZf64/a5/L4"
    "dO482v5VGQ1ch+KkFSD9HZzNVx7O4aLUIP679B5Eq/sIFn0bYNbEkWh1Mo43KO7E1tkOQvyMw2iy+I7QnvQczBuH4l3dG6LKr/44"
    "cmgoD1wgQQ4fx9UrM7n3wGkUnj+Oaz7cI5m07bDjYwV0VT8BJ99QWl0wBEfqzeBhV0uoee1IHDe3CV5Yj+Sp3sXov1GVR6qqcRSo"
    "c1v+Vfjw3ZPkLr3CX2cmgfLUHP7s1wxh52zFG9N38TP5CD4xu446DFQEqy9d9MrJEBdWq517Y/xVHP8jmi5Tl1C10wYWjq6mmmZV"
    "3F+kSG1lzWS9zoi/eM2iQsv56Pt9Iyf0a5e+uCYVNn1bDeG7lkJe53hOO9ATQp3KyX74RT6zQ4ELKhK5XMFZGNajjBesfEuT7+4S"
    "Pu8ZxHO0d5GCUl9aYKBCvsonsWj5exrp1w9Tzj8RenoaUex+ZU5/+oYHK9RhjtMzMfTHU7A5vxQeHzGh/uDExZun45KWcEyviUVz"
    "T0PcabsXPi2xFM7s0oF2+1fiDNfdoPVrPwdb9BSnlE2DqzfCcLXNMKywUcAvUjVx+ccp+MX9AJzs6I267ae55aA1Dbp2peqxRgIv"
    "9lnLJ/ddwszZ+nw+wYA3/GqjPjmvYeAkTdyTIAunFJPg7IHVYHd2IOaLsqAz5l9/SdYX909JhXd3lDDEJw/0b7GYmNqbB6/qid1t"
    "FVDWS1FiOkUOVxbO5Afx4XD3ky43LxLo0ssjqPQoEM+tSxEytieLTlYlUovpSuDuv57/+0/KYWPa4SrfhhunNTk45BV5yNSTjdNs"
    "TJyrwyslA9h3oC+VrsmDmB/p4Bswn/5oxcKp5Sp4dKyEa1r0yDy3L2pNNuZs10h2fVsCHx274GTpdLFivT23VMkyeu9hjRkbUU7L"
    "GPseMuOqkCZ4ppyGYz+mcODsg3DisTn3DlFnm5+XODGyhcrq9+Dmeco8OuYJ2mSW4duc+Rj72IoGv+lGK4LCwTDNhVBpJKsuUeXh"
    "LiHoohqABf388JJHNmz/68X9emaJxTpTWFKymWWG27Ci/E5sCAvnmNmfyadoPgd9/EIfb8lD0cF16A0nhegriXja6orQI7q7cHCy"
    "9b+9czDkg7q44UtPaFbrz05JsVTXGiUu2ShwikJvnHR+r/jG8CN0jR+Etr3SuNaWKH7OJvLRyuFGFwXQuhhOk7NXYLPiBaB35aDy"
    "vRKt7r8Bk4yHdLAfkW/5GZb+2Ms93mwFzdoU+Hv1LJi3paPT8AChvTAN1jWnwmTLp3DXbQ+PMdwPYxz2CrKTd/Mip904v6YVVoc8"
    "gbxBF8WNgfJY2+SAvWYm4/mkNMHKxobX9s6G2JgGPpNaJb44G4HObMxDSWRvh1VkubYPjvyUTPvcp7Lx43Fcv+EJaV6PpJFfj/It"
    "75+84cYMfqUZ+K93xvK8nDR8cOQ4B8ywxcZ1MnR1zAOSLvoN1k9yaci55eISrTxsObkCS5Zf5tr2Hpi2Dzh4jzXNfuuBkvhKUldT"
    "/7dWLYZEFJMYORUcumWJAzLrpCqTF5LjvAQcce4gK/gr0IOz/3J5fhpur9PCBWPMKG9zBrRUh3Lofm2uGfCa9t7XI+u0QDb8+ICj"
    "Y8Jof7svZ0d9oe0bo/jM1O5iZIMtAKZKJB6vwFA7BhyuOXLzzbdk0r6Po//qs0/JZNyaKtA2o//o+OwY0rLx5IK+JXxSsZo2Hk/h"
    "7aPzoOctX15um0+7k7vxjqxw9nRy5qU1sbDO1QhdWy+T125rfPanj2hSPoDahywSZhwfxxdCJ1DA0tc0OMuGfn+7DnNXuuHt6nBM"
    "7N0DGirHwSK1f/f7Yzl0LF0hJg79Q0dJU8zdlwv5X0ZBvE05BNxbxdv6vqNuVW9Rd1EIL7xaLo5YH491X0miblEopCUa8kPF3XD3"
    "ciR0yBRT7tAPMLZ6PRdc9eFTxf64uSjhH7da4JMoZTZ7PE602RSMcR93UPOY7pJlYjQMOPkbPO+fRVhmwXk+vmLbvatQffknKa1V"
    "Ya7qLXFQURW7/NNY7spwuJe4lvyWrhJW7rYTgk5Z4R41f8hMWge2jqfZ3TkSbQOk7PUlDg2+vqXAE1H0qOMDsFIk5/+IBMMFSeKO"
    "oCZIg9VCbkcdPBmaIP1u9wnOp8Vg4qhZ1E14C6rBOVQ6JoqztPJ4/cfzdFulAoftKxDCMiLB+ucQ7OmlBP0CHsCRl3Y0clkKuf8O"
    "Zf2o4ZyweQVmaI6i6M5WetdwEVzrtoHb2HH4av0B9D9YDhK/GDScNAzln3RRW9Z2Ol2wBLdYLaNrGSG84IMqdY1MPNsjIvFfpoWB"
    "mvJ0rno1hjtNprDa0BAnwwne0JBwimbcCgHZbHeMHnQUJ/z4xhHdzuCjj734sUI0CqvzMXLlI2lugxNNmroHZKiIOyfM5TdWPzgz"
    "PxvT44t5xvRjwrGWhTx6WRKqZ3kIRuJKyaaeEgqNzWLjDVfEhHvp0KP7K3LXjRcrC7vD1hHzecyJgRTh/AH6nLdj/1l/IKv3GTC/"
    "7CrMsgwQ22SHiBt9GZe/3MK/++7APVVKfNzYQixxCMY1ywZxrZsNXLJTp5aeFXDEdgcs3elNmjtrQDIgh3p/6YF3DQP4m8dyHpBu"
    "jpH1atAlKsPcbQv46oyfpKZ7hb8tn4m2/tfg7Qkl9C08DMZvXVg+wkZINA8E18O9WGOcCe8dacRHG5fge9c+NFoVOFtVmyP+RLCD"
    "ezg9la6ihnnlUv/sIEi1CwfTqBYSFq1g5UFDcGPoCTCTOLD35IXcKptKG9PVxKXHzfCNRSHke/QSlA5rQeHVvnh4fYJwq0pCez37"
    "wIToRdQwOhSbs4exf8F1ajQawnfz9uLFrZfIr/c03PJ7lJg+X5Y6fYMwKCqQxpaoYfGpZVw4IUrkml58IVsbzdZoiBv0poHSWxn0"
    "XKnON/tMQhTC2Ov+I1GyKgy/JjrwOotcQdtkIq77OpFv7pxCf1Iu0F/3oVyhsQBtR2Vhap9eoPbXDgb73BXnlwfD0bkbaOmQNtJb"
    "8xir9p/Db/UvaUrwZTzrWM6y1w7BPXkpfTM1hmcrh0ntTmTj52HROOVgI0yaq8d6luXQ0b87hptv5SvHrTC3YwBOd4inAY0LeecA"
    "E2wdkCv6ruvPw3cOZrpryorTP4o69ccZx+twt2olTs9sh37aejDmjT/tHqXN1Q+rKUn/OGXpmnHEBXleNsWJ+5jn4WXHyZhIzqiZ"
    "eFnywfoYnTkaLF55Gkf3rk/D80eNRCX1/rjY2ZX2jDwP8no6/7j7jvAMXbCuTUtY4BnNsTqr8OHzKFj1tx/qzZBSZbQp75nqA3lb"
    "RtD34OEY8VSNkx9uw6HecbB43kZeP2U4Hz1qxKu3z6EeS2fynIKJKHbWihPP3xf95I252jURw/SD+K1fNprVa6FgbAoVy2tghedY"
    "vlL3ltZmE15ODMOmd6uJTG/xIefRGIJ53DH4AR5PkcHJg8+CrJYbWvbZDs6n5wjDS4aQncRSfDrsLMyM74WSfg2CoqkcB9SZIc3U"
    "wJFOM3Fmc3826aPIq3ftElztLEH9ozfOtjVBaXsf3lpYJJRuVsH3o6KE1DuvaDm9ofz209QxSh/PFN8ToqsfCbuG6vAD0xswrv0y"
    "5nVdJAXZh9LezhHij0m/QUP+O61/oQsDW0t4xaDd3F/pniQ5V2D3IR+oz3Jf6C2XTQm25ZJ5gxTOoYEUx45+LURNm4YLZGLoZtQu"
    "uMMXhRfjQnCf/yapSdYp/G9HJg3fMc3ptG1/1u52iZ12zqCAEQNw9fd5+PHuGB7qG4d7P9rwDV1nunl7LE7/3EErZsaCSv01br5T"
    "DN39wiDW3hyCLrwgce9y3rdFE57NVHQuqlFnxUf/fPDVdPGLZwHHl/iz/kl5UTp3Pu2vTKVKj0Y6IDVCuygVVhAO4bWAM/x+YI4A"
    "+8JxklY/ELRHsNF/9/+xRT/R6GYUKp5y4Mxb5VLzAXbcJ7cRcugy/D7RRbe0uvGiYjnUcl6E8q0r+G2PNhq+7iQ3L5yC4bVuuHXq"
    "a0ruuxfPXdnMwZ/6ctmdAdDvba1ky29fnJtlhF9PtlDo3QjYoPiQThXVMjQe4TXiGlQ5HygmPJnO+xZv4tkGQ7j3a284bSZ77tel"
    "oxA1UY7iYjTRus4Uyz9ViEpq+8H8SCQ6bnNE9Zk7WWOriThRmgFT9f/QhpM/JTvfNsJ9ZQt4a3kIXpv9hpJFTeIE/aE8TrUSXl3x"
    "xu2HDHjmy9n0Y3E2vclaTK490vnbY0daZb8L9Geerco+pU5L9Cx5c9l+Snf9wHYnkoWtB/rTE686UaNzBunWt0N49SgxPmdelWay"
    "O3OYPT1P+kzRdQ/oPgZTzaK1bNDynjsSRvGDY0boH1RBoz5ooslFkfOKPoP72ycc8KZLbF7lI2rLpcAClb4cersC2g7N5YO338K6"
    "61JyfL6f5lzy5fHf90JtfbEY4+XEjSsHC1pn9Wj9jWsw7XoR78BhtG1YCN50SYORww1Eick2OHnIC1Tfm7Jl6Hzc3XSKdLabsIuD"
    "GSnYrOUeCqlsVuIzUuPOU+CvT+nYK2VUHFaNZdlL8FZmiNQyME/42buafv2WQR8ZBdxn+YJ21BjhneInlOpZBpuuzcWYZncYpzUW"
    "1Q650aZGHXRrm0grZ08mz2VJVBguUtwjR7g9+L34y9KRq6/98/yO70LS9s+CTu9cnmfqhBbDPoiVRhN51s9nQlOrOuoZNpK2ngZV"
    "2BrDhlmDsG2LG+pWRGKd8Tj8k5oPr3omsGrXbG4Za/CPv4Zi1cJ97Nkkj4ojp+GebTEQ1BzGX2fKk1X6Gszpb4x3ti9BnZIAMfux"
    "FiqcSqIok6dC46RUOB7hLppc7M03zqvgxcXPYfUqNVFnhjYGeurghqla7FRszivL5chiZhw/7JgNWs+dIHOtjXTZgc18Ut6U1Vvc"
    "+Xt5PHuH2MO90jzJuOsbUfXtGGy48RqG3vbgouUTaNHXaZS18z/Iqawmna19OOJUMA5SFXF16TZY5NglHS2zmUrj8uHwjB+kdHcK"
    "2oReour+ZliVfRESPT2kDZEprK13FLp2x2CtYwV3Ly5n47t7eLmaDzn55WCOYgVHGrgQn0mHN9c9cZq+I3706aJJpf25rvQzzT7f"
    "Lq3I6M3T3OvwQf0ynjI7jAd3idIgRzcs3XmChndo8rDq3my+yZXfmU/BhF2ZXNF3BPZrjEc3i+G8bW49flu3EVd0BvPc73/AW2kl"
    "R/sr4/OPp8V1kiiel/gSKk0LoOuNAt5sKeOT/VbSrM0H+NGd0dT5ZQNXbW8SBBsfWrLnFm2+EMLn7BKhU3EtDP+iztd9F6HsVwV8"
    "t/YJjZk3FuX3FdDgP+3srfgJFOd9pk9mp1lebiArzNjEV8p1+XB1DH+ubKfwRANe/rsvLlr2Q1jyqobcspXI7ZsZf/OdC0P/3CL3"
    "Nfo4LWJ8lcP7s8LPvANkXvyMgvL3kkXQdojt0sVfN8Zw4o5Gyt4chRuzY6Wta3LosMc+aW/zX/R8RQGUNpfCxZFGaGS5FnXhmvT/"
    "Hcc4Qo0Hz9iD8uczUCbLAt3GtkBubSRm9cviSVcGcb2JkvOyhq9CdtpPqXd/Ha79CPhu8BTyMgG8mNZIP2wPsYnMS9jm4c5DIhTZ"
    "9uATYX5UreR2phYfb70k3fSPWQ8/GSjoFMXig6k9KG9ULm0Mz+Qsn3Nw88156bDyGuHy0ShUOJYNRRaNnGlqhDIlP8ReD3Pg3add"
    "vLPWCTaWjIEJLtWgvNQV3v5S4KO5MoJydQ/0i2qkUqc3MDbZmk3FNq4qviJ2ZMRgpcd3arCJwUVx5+jDcTtBWBRDe2rfwPct8fTk"
    "VRq9+ZdU3iN+sXXdVD5034Bv9LLABVar4LDvL/rpUIFbV+pxx/qFbDFoB0pfR2H9yVTqF6DJLxaYoGbPIdAZNBIDFiwQa7wRZk3Y"
    "Ta4AvGnIBFw3PgyzSuz/ddTt6HhBpM+6y3jfmAhusE/giA/9+XG0o6j9XRbtjTKh/EsJ+/qo8NtPstCSt4s3hwPLJ42GuGtXwKho"
    "E7afvONkMvWP8DpoEtztp8y6vvqcWTCCp500hBJVb/6zoF00PKlFBhYLaXbMd2FqUQ+oC6sWsr5dhcceByFndh7XbznMy9ZI2HDT"
    "U+oe81aYlHkZ7hybAvtuxYLvrHHSe9e28q+ex0nTczAOTp+NqX72oGJgyPL4jgpa3pNBTBybzvHlbTfiaO2dQ3xx9g1ufJ8mVmj4"
    "w6T9lXRqdiQ/mqyEygVnYNHowzzw1nkxeZYZn9l+kle0ToVjhivw0tMlNFExBPx2rcQ4+Oeh3x0w0n8Wlnnk8YC6WN46eS+mDVfG"
    "xUtthUIrLfby3ymx6HzLQWoKXCXbh+mjE8odTqZ7kxewzPlIdrsqlXgKc4U74hu6JdTiBI8ReHX7esHgpzcXTGynnE1pWOawirfm"
    "DGDT47fFhb8SeI/5BDQ/LQr9G7y44mY++yv2xYS4Q7B+WgsUdNuCCr/SaJKjPR7/mybEH6mnLOsg7qMhy44hcex1YiHW+BiQi80B"
    "Al0bMWObGt5MUkML8SV1bOvGJt2t8J7GIGpPsYXj35ZyW88rNCJ6D0hG1EqufqmiwCDgi8tdOCDcER9vCwHv8Bi4/9ecyXGncGeN"
    "E1zc5cS9Px8nvSUdFKFsxDVXdNF76lQ+sCQEl8pG8aoFz2DfpCJhyHALmLM+DDUk3cQd8YFgOame9s+/SYv/eaX9uiI+kOUq6m5T"
    "p6a/PaD4rgqjrD7arPxJ5WMyoMHwIm33/EsHouMAm34KD/a7Sb5N7cY32lPR7/wH0c41DBKKu0PinHTYnB4hDe5QgWGXjNjwQTFf"
    "KFzGL7oi+chPD84cHcc/3SRYtyIczBfsoC44C5V7y4XL/EDImBjH09uOo5p1BXntvUdHf+dB071p9MXSFGPUDDiqKwGtzyrSqKnj"
    "cI/5JD72tEmUCx+OH6ak4uKDrvgu01rydZmImvYO9KxBjp8mpdGS4jf06lG52LTR4F/6u/DpI+uFJ0adNHxBDTa/S0BtZwV076lI"
    "Y4a/gVwTXbSbPACejToCx3JlOXr0D3C19uMfhW8pPSaI5zztgdXDXvGpHp7YUnyZ8x2vks2nydilsFWYcVkb+4dEoW2ROarUjBYd"
    "VwWypfwyHn0nCg8HpkLKXz96MNGBFz59Dm+ejkGFgkY4biGPhmYBkH22Rqx2caFrqQ68/o8VTTXWh8CIdWRvX0XHflyn1hgr1NbM"
    "w7sqWmi+WED3mb0o83AT+Iw6xe4bg9Fk0kG+7mECWlGxtLZTExJ1j1CgSi7meDwUl5yowHrbU/D9nRoVjbTjzY6fxal1rZL3deNw"
    "6k1H1GuzAuFoC42/aoDFSpNQ5sBktvrejf+7byFQWi922+6P5h2qqLQ1UuL+Q8CiI2cgqzWSYwb5oe2y5ei0+DyONwigppRhuErb"
    "GgcYnqNlLhnC4dVjyE/WgbvffkIXT9+A4qktwqDnS8WQiF4sq6RLXUtPCh9ynsFQMRevLs/Cvbf287tZiSS8TaEb6+KE6Sn7QHGh"
    "Eqz8eRynOQ+Db0HJ8PtaDo1aNhzVbsdD5qExIk8fiSN3pbBYehee/prBWQeX8YriFNCaOIh2t/7/34MGZ/WNp5Bx8cQrkunDohX0"
    "X3M+l2ra/5tJFY+4tRna3rlwdfIEPNPNhnTCBb7UVQE93vox9jmEU/PP0NVFS4TUqZ70Y0YmTdoymZRbU/lLn3y+ahfCZHKMvm96"
    "KMw+2cUb5YrxS+B0cUPlRPiyaTYdnnBDbNY4Qr177sZtj7bhj7BCHlJnhXafgd/M/yOe3eTENw8+g8yyeky99gTCl0TDiboA6DIt"
    "wII4H155dBJdbBlURXFDsWzHevy9wAmsoxfy1ZTtxFnysOGQGvOxemHzp3ga+CweHLb+oKrl5bDr93tyXujjpIaWPLVcE1dm7MaO"
    "ZUVU2vyECkf34Jue6byxfxb+mtAAzcm/aE7sTcg/RnQy+g1o5Ryj2RPcSUXjPtV4vaPfd3dDbLkMz1uZJ5m/KQKyXfZD/bws8cXj"
    "eqpeXcBdrz5hk8MHWJ2xG3JCfqFfRydtWRUqhMZP5uda6jh35Ut43e+bcP63F31yPwI3P9yAR1t6sLbxZpax6U2uLn/pjq0hV+p9"
    "Eb/4y/PirmJW+1nEw8dO4NfjjMSiwyMpdE07vVyghpO+bYbgJWdZvnU+tWTv5CDlOLFQukPqOc6el7d4867bw9DNWR+c989DlXmT"
    "4Z2NPoBhBgSozUJld3NeVddAFp3I+hn6pJFvi81FS7Cq1fBftp7CxOlGnBnVRnPeWfMxcQMdfNUXPcpHsUzyKopru0/q3ra8QCWe"
    "Sj+8pSWmMugWe5BKf6mAzuEYrAoPgwFbX0FO/xq2GljttHVaBj8YcoUyiwkD85zBxrA/hmeN4r7xs5yi6vai6x15Wrt8PY3XqKds"
    "tQEQXzUTMn3zxUGvavjQ77mizrYDaPNuLIzQ74v/uZvSn762dPv0Gm56oIkfszuhd9JduDDpGsztTIfsnpux15RCCOr4TVO+DcWj"
    "0hxaKZ3HDnWHJB311rTn00Lsd14NH5kEUNqYyfRKeyrfV7oKw/v5ctekbfDSpVOa7z+RXx+voLUW0TDw1F56suyW9JBWOd/OD+PV"
    "DfJ06HculZ3sgXbbZ2LoCXeYcVoW/EpJ6PP+KezsGUszLsziUbNC8NeTyeickEXXcveDwo4luHoBQN74f/6ZNl8836+JXFb0ZYm6"
    "DCfN+3w2+6ia85qMTPrqt5Gtf13lugcX4WYzQc8elVKPqWEckB3F5/6mYJ8FV+mMaQyMP2iGk1PD6LRfBvYYJMvXGjSxr40ZNB94"
    "TxGve7HXgGSc12WMM29EOY3VRvRU15HaNM4km2Rjnti4i651mwQaETr4UilB0DE/jBNfmKA77uWqioPstEoCbXk++NBkB/+Os8Uq"
    "tTQ47CmLHm9XkYmGFzwc4Y5f4F/OrCmAfWtnifat42niCS/uWDmQS4r0eXxRHq1ZsVD66cIlUp2qBTVWF4ASl2JeH008MOcafZqT"
    "D49X9oFPrQV0se8cNo6PpM8658Ri4wG4Lby3cHZ6b7CYUEIqVh5cYa/NFxf040zln+JzMVJ8uT2OO+fniqFN32F471iUGz7eyWD1"
    "ZnFa2ixWWBQENSPqRQdXOzw10Bs33xrFDl/vUfrNrULqqVJBs0JCHsVOXDrQGp2q/Pi2uYlTXEYlXz3/V7jQo0l6+bOzJNJqBPRZ"
    "WM5urXslsjN9KfPWBzG29D6YRJ/gNFLgn1ozufFOCNymRCethw6YqngdTgyzYE3fZnhZM5cXnrDEVkdfnDDGnV/k+8L6VZOw3N2P"
    "oy2iuXOZO6Y9NuTFuQskC4+1UOXilXBBey9q4iA2nH8bJLOc8WhzHprY9gaFI3I8x7ODKv/TxNU7KijXbDRnz4gAjxs2OPOCLkZX"
    "DMbirQb4vo8WH5yvivdnpItd80dzXPNToWPLV/FvkQ0+DtZmw/wsakqU5U+96njqf5s5YN40zMvti/zxu8iqRhAw0J5z/Rx5nuVL"
    "aB86Db06Lvxj02rytvHgI85J0mxJh2A7vAwnFT2Abw7r6f08U0F/zGpQVfVBUN3Eo+TcOePPfBxWUUUa8XJcMC5OMqdmNY/6doBf"
    "SQzhz7HDokPqWup+d7c4Wt1eEuraQC/WpaPjQU/qlzORDdL2C8b/WH7V1VIy7VjLdhXjUd04HY4cUkDj9fPp0v04CF40hke+SMQj"
    "WM2hOywpqyKPxlxpAu0ES0w97UHta/cLtRel3NJnAiYFF3CWmEvPF0+ntyUr+cTUZHCZWUaaWt54WkFVenjUOnY3OwxnlH9Bm/J2"
    "XL55HzYlVcKZ4jaKf9xEDoODOaBXOPwPa40u3w=="
)


def _proj_const():
    import base64
    import zlib
    raw = zlib.decompress(base64.b64decode(_PROJ_B64))
    return np.frombuffer(raw, np.float32).reshape(128, NJL).copy()


_NC_CACHE = {}


def make_in_maps(L0_attn_x, L0_attn_Wq, L0_attn_Wk, L0_attn_Wv, L0_attn_Wo,
                 L0_attn_q_gain, L0_ffn_gate_up_packed, L0_ffn_gate_up_scales,
                 L0_ffn_down_packed, L0_ffn_down_scales):
    x = np.ascontiguousarray(np.asarray(L0_attn_x, np.float32))
    Wq = np.asarray(L0_attn_Wq, np.float32)
    Wk = np.asarray(L0_attn_Wk, np.float32)
    Wv = np.asarray(L0_attn_Wv, np.float32)
    Wo = np.asarray(L0_attn_Wo, np.float32)
    gain = np.asarray(L0_attn_q_gain, np.float32)
    gup = np.asarray(L0_ffn_gate_up_packed, np.int32)
    gus = np.asarray(L0_ffn_gate_up_scales, np.float16)
    dwp = np.asarray(L0_ffn_down_packed, np.int32)
    dws = np.asarray(L0_ffn_down_scales, np.float16)

    cos, sin = _rope_tables()
    projc = _proj_const()
    iden = np.eye(128, dtype=np.float32)
    blk = np.zeros((128, 2), np.float32)
    blk[0:64, 0] = 1.0
    blk[64:128, 1] = 1.0
    par = np.zeros((128, 1), np.float32)
    par[0::2, 0] = 1.0
    ones = np.ones((128, 1), np.float32)
    onesr = np.ones((1, 128), np.float32)
    cvals = np.zeros((128, 4), np.float32)
    cvals[:, 0] = 1e-5
    cvals[:, 1] = 1e-6
    cvals[:, 2] = 0.0
    cvals[:, 3] = 1.0
    ltm = np.triu(np.ones((128, 128), np.float32), 1)
    iotab = np.broadcast_to(np.arange(128, dtype=np.float32)[None, :], (128, 128)).copy()
    blkid = np.arange(128, dtype=np.float32)[:, None].copy()
    poff = (np.arange(128) % 64).astype(np.float32)[:, None].copy()

    gateT = np.ascontiguousarray(gup[:IFF].T)
    upT = np.ascontiguousarray(gup[IFF:].T)
    downT = np.zeros((768, DM), np.int32)
    downT[:704] = dwp.T
    gsc = np.ascontiguousarray(np.repeat(gus[:, :IFF], 16, axis=0))
    usc = np.ascontiguousarray(np.repeat(gus[:, IFF:], 16, axis=0))
    dsc = np.zeros((768, DM), np.float16)
    dsc[:704] = np.repeat(dws, 16, axis=0)

    in_maps = []
    for c in range(8):
        b, qr = c // 4, c % 4
        hs = slice(512 * qr, 512 * (qr + 1))
        gtm = np.broadcast_to(gain[4 * qr:4 * qr + 4][None, :], (128, HPC))
        in_maps.append({
            "xb": x[b],
            "xq": np.ascontiguousarray(x[b, TPC * qr:TPC * (qr + 1)]),
            "wqT": np.ascontiguousarray(Wq.T[:, hs]),
            "wkT": np.ascontiguousarray(Wk.T[:, hs]),
            "wvT": np.ascontiguousarray(Wv.T[:, hs]),
            "woT": np.ascontiguousarray(Wo.T[hs, :]),
            "gain_tm": np.ascontiguousarray(gtm),
            "cos_d": cos, "sin_d": sin, "proj_d": projc,
            "iden_d": iden, "blksel_d": blk, "par_d": par,
            "ones_d": ones, "onesr_d": onesr,
            "gateT_d": gateT, "upT_d": upT, "downT_d": downT,
            "gsc_d": gsc, "usc_d": usc, "dsc_d": dsc,
            "cvals_d": cvals, "ltm_d": ltm, "iotab_d": iotab,
            "blkid_d": blkid, "poff_d": poff,
        })
    return in_maps


def kernel(**inputs):
    in_maps = make_in_maps(**inputs)
    key = "nc"
    if key not in _NC_CACHE:
        _NC_CACHE[key] = build_nc()
    nc = _NC_CACHE[key]
    res = run_bass_kernel_spmd(nc, in_maps, core_ids=list(range(8)), trace_cores=[0])
    out = np.empty((B, S, DM), np.float32)
    for c in range(8):
        b, qr = c // 4, c % 4
        out[b, TPC * qr:TPC * (qr + 1)] = res.results[c]["out"]
    return out


# version 12 (final)
# speedup vs baseline: 1.0911x; 1.0469x over previous
import numpy as np
import concourse.bass as bass
import concourse.mybir as mybir
from concourse.bass_utils import run_bass_kernel_spmd
from concourse import library_config
from contextlib import ExitStack

F32 = mybir.dt.float32
F16 = mybir.dt.float16
I32 = mybir.dt.int32
U16 = mybir.dt.uint16
ALU = mybir.AluOpType
AF = mybir.ActivationFunctionType
AX = mybir.AxisListType

B, S, DM = 2, 8192, 2048
HPC, TPC = 4, 2048
NT = 64
NT5 = 16
NJL = 32
IFF = 5632
NOSET = 48
RS128 = float(1.0 / np.sqrt(128.0))
NEG = -1.0e38
ENGS = ("sync", "tensor", "vector", "scalar", "gpsimd")
PHASES = None
SEMMAP = {}


def build_nc():
    nc = bass.Bass("TRN2", target_bir_lowering=False, debug=False)

    xb = nc.dram_tensor("xb", [S, DM], F32, kind="ExternalInput").ap()
    xq = nc.dram_tensor("xq", [TPC, DM], F32, kind="ExternalInput").ap()
    wqT = nc.dram_tensor("wqT", [DM, 512], F32, kind="ExternalInput").ap()
    wkT = nc.dram_tensor("wkT", [DM, 512], F32, kind="ExternalInput").ap()
    wvT = nc.dram_tensor("wvT", [DM, 512], F32, kind="ExternalInput").ap()
    woT_d = nc.dram_tensor("woT", [512, DM], F32, kind="ExternalInput").ap()
    gain_d = nc.dram_tensor("gain_tm", [128, HPC], F32, kind="ExternalInput").ap()
    cos_d = nc.dram_tensor("cos_d", [S, 64], F32, kind="ExternalInput").ap()
    sin_d = nc.dram_tensor("sin_d", [S, 64], F32, kind="ExternalInput").ap()
    proj_d = nc.dram_tensor("proj_d", [128, NJL], F32, kind="ExternalInput").ap()
    iden_d = nc.dram_tensor("iden_d", [128, 128], F32, kind="ExternalInput").ap()
    blksel_d = nc.dram_tensor("blksel_d", [128, 2], F32, kind="ExternalInput").ap()
    par_d = nc.dram_tensor("par_d", [128, 1], F32, kind="ExternalInput").ap()
    ones_d = nc.dram_tensor("ones_d", [128, 1], F32, kind="ExternalInput").ap()
    onesr_d = nc.dram_tensor("onesr_d", [1, 128], F32, kind="ExternalInput").ap()
    gateT_d = nc.dram_tensor("gateT_d", [256, IFF], I32, kind="ExternalInput").ap()
    upT_d = nc.dram_tensor("upT_d", [256, IFF], I32, kind="ExternalInput").ap()
    downT_d = nc.dram_tensor("downT_d", [768, DM], I32, kind="ExternalInput").ap()
    gsc_d = nc.dram_tensor("gsc_d", [256, IFF], F16, kind="ExternalInput").ap()
    usc_d = nc.dram_tensor("usc_d", [256, IFF], F16, kind="ExternalInput").ap()
    dsc_d = nc.dram_tensor("dsc_d", [768, DM], F16, kind="ExternalInput").ap()
    cvals_d = nc.dram_tensor("cvals_d", [128, 4], F32, kind="ExternalInput").ap()
    ltm_d = nc.dram_tensor("ltm_d", [128, 128], F32, kind="ExternalInput").ap()
    iotab_d = nc.dram_tensor("iotab_d", [128, 128], F32, kind="ExternalInput").ap()
    blkid_d = nc.dram_tensor("blkid_d", [128, 1], F32, kind="ExternalInput").ap()
    poff_d = nc.dram_tensor("poff_d", [128, 1], F32, kind="ExternalInput").ap()
    out_d = nc.dram_tensor("out", [TPC, DM], F32, kind="ExternalOutput").ap()

    qfm_d = nc.dram_tensor("qfm_d", [HPC * 128, S], F32).ap()
    ktm_d = nc.dram_tensor("ktm_d", [HPC, S, 128], F32).ap()
    vtm_d = nc.dram_tensor("vtm_d", [HPC, S, 128], F32).ap()
    yfm_d = nc.dram_tensor("yfm_d", [HPC * 128, S], F32).ap()
    kpn_d = nc.dram_tensor("kpn_d", [128, HPC, NT, NJL], F32).ap()
    rs_in_h = nc.dram_tensor("rs_in", [S, DM], F32)
    rs_out_h = nc.dram_tensor("rs_out", [TPC, DM], F32)
    x2_d = nc.dram_tensor("x2_d", [TPC, DM], F32).ap()
    h_d = nc.dram_tensor("h_d", [TPC, IFF], F32).ap()
    wde_d = nc.dram_tensor("wde_d", [NOSET, 128, DM], F32).ap()
    ln2_d = nc.dram_tensor("ln2_d", [TPC, DM], F32).ap()
    centd = nc.dram_tensor("centd", [2, NT * 128], F32).ap()
    dotsd = nc.dram_tensor("dotsd", [2, NT * 128], F32).ap()

    outer = ExitStack()

    def osb(name, shape, dtype=F32):
        return outer.enter_context(nc.sbuf_tensor(name, shape, dtype))

    def mksem(name):
        h = outer.enter_context(nc.semaphore(name))
        SEMMAP[name] = h
        return h

    pb = [outer.enter_context(nc.psum_tensor(f"pb{k}", [128, 512], F32)) for k in range(8)]

    iden = osb("iden", [128, 128])
    proj = osb("proj", [128, NJL])
    blksel = osb("blksel", [128, 2])
    parm = osb("parm", [128, 1])
    onescol = osb("onescol", [128, 1])
    onesrow = osb("onesrow", [1, 128])
    gain_tm = osb("gain_s", [128, HPC])
    cbank = osb("cbank", [128, 4])
    nc.const_aps.register(F32, 1e-5, cbank[:, 0:1])
    nc.const_aps.register(F32, 1e-6, cbank[:, 1:2])
    nc.const_aps.register(F32, 0.0, cbank[:, 2:3])
    nc.const_aps.register(F32, 1.0, cbank[:, 3:4])
    idx_all = osb("idx_all", [128, HPC * 33], mybir.dt.uint32)
    ltm = osb("ltm", [128, 128])
    iotab = osb("iotab", [128, 128])
    blkid = osb("blkid", [128, 1])
    poff = osb("poff", [128, 1])
    qsum = osb("qsum", [128, HPC])

    s = {}
    for nm in ["c0", "x0", "x1", "cs0", "cs1", "odone", "kv2a", "kv2b",
               "q2a", "q2b", "yoa", "yob", "y3a", "y3b",
               "rsAa", "rsAb", "rsBa", "rsBb", "ccA", "ccB",
               "gdpe", "gst", "l6", "l5s", "lsa", "lsb", "lra", "lrb",
               "x5a", "x5b", "x2wa", "x2wb",
               "g6a", "g6b", "hw6a", "hw6b", "w7a", "w7b", "o7a", "o7b",
               "xf", "csf", "ln1", "ln2", "lnd", "tp", "tpe", "qkv",
               "qe", "rm1", "rm2", "rope", "kfme", "kp", "kpe", "kpt",
               "kptev", "kq1", "kq2", "kpn", "cent", "centev", "o1", "ph",
               "sq", "g15", "hfree2", "qf2", "lt", "exq", "esum", "av",
               "sum2", "rcp2", "bc2", "rbe2", "ye", "kt2", "kt2e", "wo3", "w3", "e3",
               "wde5", "l51", "l52", "t5", "t5e", "e6", "mmg6",
               "mmu6", "sil6", "h6", "h7", "t7", "t7e", "mm7", "b71",
               "b72", "xl7"]:
        s[nm] = mksem(nm)

    PH = {"n": 0}

    with outer, nc.Block() as block:

        def make_section(engname, body):
            sec = getattr(block, engname)

            def wrapper(eng, body=body):
                if PH["n"] > 0:
                    eng.wait_ge(s["ph"], 5 * PH["n"])
                body(eng)
                eng.drain().then_inc(s["ph"], 1)
            sec(wrapper)

        def phase(emitters):
            if PHASES is not None and PH["n"] not in PHASES:
                emitters = {}
            for engname in ENGS:
                make_section(engname, emitters.get(engname, lambda eng: None))
            PH["n"] += 1

        class Seq:

            def __init__(self):
                self.steps = []

            def step(self, engname, fn):
                self.steps.append((engname, fn))

            def emitters(self, base):
                def mk(engname):
                    def body(eng):
                        for pos, (en, fn) in enumerate(self.steps):
                            if en == engname:
                                eng.wait_ge(s["sq"], base + pos)
                                fn(eng)
                                eng.drain().then_inc(s["sq"], 1)
                    return body
                return {en: mk(en) for en in set(e for e, _ in self.steps)}

        ph1 = ExitStack()

        def sb1(name, shape, dtype=F32):
            return ph1.enter_context(nc.sbuf_tensor(name, shape, dtype))

        wsb = sb1("wsb", [128, 16 * 1536])
        cent_acc = sb1("cent_acc", [2, NT * 128])
        xbuf = [sb1(f"xbuf{j}", [128, DM]) for j in range(2)]
        csb = [sb1(f"csb{j}", [128, 128]) for j in range(2)]
        ln1fm = [sb1(f"ln1fm{j}", [128, DM]) for j in range(2)]
        qkvtm = [sb1(f"qkvtm{j}", [128, 1024]) for j in range(2)]
        vsb = [sb1(f"vsb{j}", [128, 512]) for j in range(2)]
        qro = [sb1(f"qro{j}", [128, 512]) for j in range(2)]
        kro = [sb1(f"kro{j}", [128, 512]) for j in range(2)]
        qfmsb = [sb1(f"qfmsb{j}", [128, 512]) for j in range(2)]
        kfmsb = [sb1(f"kfmsb{j}", [128, 512]) for j in range(2)]
        bnst = sb1("bnst", [128, 24])
        mv = sb1("mv", [128, 2])
        sd1 = sb1("sd1", [128, 1])
        rstd = sb1("rstd", [128, 1])
        sqb = sb1("sqb", [128, 1024])
        ssq = sb1("ssq", [128, 8])
        srt8 = sb1("srt8", [128, 8])
        rr8 = sb1("rr8", [128, 8])
        rrq = sb1("rrq", [128, 4])
        qn = sb1("qn", [128, 512])
        kn = sb1("kn", [128, 512])
        rop1 = sb1("rop1", [128, 256])
        rop2 = sb1("rop2", [128, 256])
        kpTsb = sb1("kpTsb", [32, 512])
        kpnsb = sb1("kpnsb", [128, 128])
        ss4 = sb1("ss4", [128, 4])
        srt4 = sb1("srt4", [128, 4])
        rn4 = sb1("rn4", [128, 4])
        red4 = sb1("red4", [128, 4])
        NG = 6

        def p1_sp(sp):
            for t, d in ((iden, iden_d), (proj, proj_d), (blksel, blksel_d),
                         (parm, par_d), (onescol, ones_d), (onesrow, onesr_d),
                         (gain_tm, gain_d), (cbank, cvals_d), (ltm, ltm_d),
                         (iotab, iotab_d), (blkid, blkid_d), (poff, poff_d)):
                sp.dma_start(out=t[:], in_=d[:]).then_inc(s["c0"], 16)
            wv = wsb[:].rearrange("p (c o) -> p c o", o=1536)
            sp.dma_start(out=wv[:, :, 0:512],
                         in_=wqT.rearrange("(c p) o -> p c o", p=128)).then_inc(s["c0"], 16)
            sp.dma_start(out=wv[:, :, 512:1024],
                         in_=wkT.rearrange("(c p) o -> p c o", p=128)).then_inc(s["c0"], 16)
            sp.dma_start(out=wv[:, :, 1024:1536],
                         in_=wvT.rearrange("(c p) o -> p c o", p=128)).then_inc(s["c0"], 16)
            def ld1(i):
                j = i % 2
                if i >= 2:
                    sp.wait_ge(s["xf"], i - 1)
                sp.dma_start(out=xbuf[j][:], in_=xb[128 * i:128 * (i + 1), :]).then_inc(s["x0" if j == 0 else "x1"], 16)
                if i >= 2:
                    sp.wait_ge(s["csf"], i - 1)
                sp.dma_start(out=csb[j][:, 0:64], in_=cos_d[128 * i:128 * (i + 1), :]).then_inc(s["cs0" if j == 0 else "cs1"], 16)
                sp.dma_start(out=csb[j][:, 64:128], in_=sin_d[128 * i:128 * (i + 1), :]).then_inc(s["cs0" if j == 0 else "cs1"], 16)

            ld1(0)
            ld1(1)
            for i in range(NT):
                j = i % 2
                if i + 2 < NT:
                    ld1(i + 2)
                if i >= 1:
                    sp.wait_ge(s["o1"], 16 * 4 * i)
                    sp.sem_inc(s["odone"], 1)
                sp.wait_ge(s["tpe"], NG * i + 5)
                sp.dma_start(out=qfm_d.rearrange("(h p) t -> p h t", p=128)[:, :, 128 * i:128 * (i + 1)],
                             in_=qfmsb[j][:].rearrange("p (h t) -> p h t", t=128)).then_inc(s["o1"], 16)
                sp.wait_ge(s["rope"], i + 1)
                sp.dma_start(out=ktm_d.rearrange("h (c p) d -> c p h d", p=128)[i],
                             in_=kro[j][:].rearrange("p (h d) -> p h d", d=128)).then_inc(s["o1"], 16)
                sp.wait_ge(s["qe"], 3 * i + 3)
                sp.dma_start(out=vtm_d.rearrange("h (c p) d -> c p h d", p=128)[i],
                             in_=vsb[j][:].rearrange("p (h d) -> p h d", d=128)).then_inc(s["o1"], 16)
                sp.wait_ge(s["kpn"], i + 1)
                sp.dma_start(out=kpn_d[:, :, i, :],
                             in_=kpnsb[:].rearrange("p (h j) -> p h j", j=NJL)).then_inc(s["o1"], 16)
            sp.wait_ge(s["centev"], NT)
            sp.dma_start(out=centd[:], in_=cent_acc[:]).then_inc(s["o1"], 16)
            sp.wait_ge(s["o1"], 16 * (4 * NT + 1))

        def p1_pe(pe):
            pe.wait_ge(s["c0"], 240)
            for i in range(NT):
                j = i % 2
                g0 = NG * i
                pe.wait_ge(s["lnd"], i + 1)
                for g in range(4):
                    gi = g0 + g
                    if gi >= 2:
                        pe.wait_ge(s["tpe"], gi - 1)
                    for c in range(4):
                        cc = 4 * g + c
                        ins = pe.transpose(pb[gi % 2][:, 128 * c:128 * (c + 1)],
                                           xbuf[j][:, 128 * cc:128 * (cc + 1)], iden[:])
                        if c == 3:
                            ins.then_inc(s["tp"], 1)
                pe.drain().then_inc(s["xf"], 1)
                pe.wait_ge(s["tpe"], g0 + 4)
                if i >= 1:
                    pe.wait_ge(s["qe"], 3 * i)
                for oc in range(3):
                    for ci in range(16):
                        ins = pe.matmul(pb[2 + oc][:],
                                        ln1fm[j][:, 128 * ci:128 * (ci + 1)],
                                        wsb[:, 1536 * ci + 512 * oc:1536 * ci + 512 * (oc + 1)],
                                        start=(ci == 0), stop=(ci == 15))
                        if ci == 15:
                            ins.then_inc(s["qkv"], 1)
                pe.wait_ge(s["rope"], i + 1)
                for g, src in ((4, qro[j]), (5, kro[j])):
                    gi = g0 + g
                    pe.wait_ge(s["tpe"], gi - 1)
                    for c in range(4):
                        ins = pe.transpose(pb[gi % 2][:, 128 * c:128 * (c + 1)],
                                           src[:, 128 * c:128 * (c + 1)], iden[:])
                        if c == 3:
                            ins.then_inc(s["tp"], 1)
                pe.wait_ge(s["kfme"], i + 1)
                if i >= 1:
                    pe.wait_ge(s["kpe"], i)
                for hl in range(HPC):
                    ins = pe.matmul(pb[5][0:NJL, 128 * hl:128 * (hl + 1)], proj[:],
                                    kfmsb[j][:, 128 * hl:128 * (hl + 1)], start=True, stop=True)
                    if hl == HPC - 1:
                        ins.then_inc(s["kp"], 1)
                pe.wait_ge(s["kpe"], i + 1)
                if i >= 1:
                    pe.wait_ge(s["kptev"], i)
                for hl in range(HPC):
                    ins = pe.transpose(pb[6][:, NJL * hl:NJL * (hl + 1)],
                                       kpTsb[0:NJL, 128 * hl:128 * (hl + 1)], iden[0:NJL, 0:NJL])
                    if hl == HPC - 1:
                        ins.then_inc(s["kpt"], 1)
                pe.wait_ge(s["kpn"], i + 1)
                if i >= 1:
                    pe.wait_ge(s["centev"], i)
                for hl in range(HPC):
                    ins = pe.matmul(pb[7][0:2, NJL * hl:NJL * (hl + 1)], blksel[:],
                                    kpnsb[:, NJL * hl:NJL * (hl + 1)], start=True, stop=True)
                    if hl == HPC - 1:
                        ins.then_inc(s["cent"], 1)

        def p1_dve(v):
            def head(i):
                j = i % 2
                v.wait_ge(s["x0" if j == 0 else "x1"], 16 * (i // 2 + 1))
                for c in range(4):
                    v.bn_stats(out=bnst[:, 6 * c:6 * (c + 1)], in_=xbuf[j][:, 512 * c:512 * (c + 1)])
                v.drain()
                v.bn_aggr(out=mv[:], in_=bnst[:]).then_inc(s["ln1"], 1)
                v.wait_ge(s["ln2"], i + 1)
                v.reciprocal(out=rstd[:], in_=sd1[:])
                v.drain()
                v.tensor_scalar(out=xbuf[j][:], in0=xbuf[j][:], scalar1=mv[:, 0:1],
                                scalar2=rstd[:], op0=ALU.subtract,
                                op1=ALU.mult).then_inc(s["lnd"], 1)
                v.drain()

            def rest(i):
                j = i % 2
                g0 = NG * i
                if i >= 1:
                    v.wait_ge(s["qkv"], 3 * i)
                for g in range(4):
                    v.wait_ge(s["tp"], g0 + g + 1)
                    v.tensor_copy(ln1fm[j][:, 512 * g:512 * (g + 1)], pb[(g0 + g) % 2][:]).then_inc(s["tpe"], 1)
                v.drain()
                for oc in range(2):
                    v.wait_ge(s["qkv"], 3 * i + oc + 1)
                    v.tensor_copy(qkvtm[j][:, 512 * oc:512 * (oc + 1)], pb[2 + oc][:]).then_inc(s["qe"], 1)
                v.drain()
                v.tensor_tensor(out=sqb[:], in0=qkvtm[j][:], in1=qkvtm[j][:], op=ALU.mult)
                v.drain()
                v.tensor_reduce(out=ssq[:].unsqueeze(2),
                                in_=sqb[:].rearrange("p (h d) -> p h d", d=128),
                                axis=AX.X, op=ALU.add).then_inc(s["rm1"], 1)
                v.wait_ge(s["rm2"], i + 1)
                v.reciprocal(out=rr8[:], in_=srt8[:])
                v.drain()
                v.tensor_tensor(out=rrq[:], in0=rr8[:, 0:4], in1=gain_tm[:], op=ALU.mult)
                v.drain()
                v.tensor_tensor(out=qn[:].rearrange("p (h d) -> p h d", d=128),
                                in0=qkvtm[j][:, 0:512].rearrange("p (h d) -> p h d", d=128),
                                in1=rrq[:].unsqueeze(2).broadcast_to([128, 4, 128]), op=ALU.mult)
                v.tensor_tensor(out=kn[:].rearrange("p (h d) -> p h d", d=128),
                                in0=qkvtm[j][:, 512:1024].rearrange("p (h d) -> p h d", d=128),
                                in1=rr8[:, 4:8].unsqueeze(2).broadcast_to([128, 4, 128]), op=ALU.mult)
                v.drain()
                v.wait_ge(s["cs0" if j == 0 else "cs1"], 32 * (i // 2 + 1))
                if i >= 2:
                    v.wait_ge(s["odone"], i - 1)
                for src, dst in ((qn, qro[j]), (kn, kro[j])):
                    sv = src[:].rearrange("p (h d) -> p h d", d=128)
                    dv = dst[:].rearrange("p (h d) -> p h d", d=128)
                    r1 = rop1[:].rearrange("p (h d) -> p h d", d=64)
                    r2 = rop2[:].rearrange("p (h d) -> p h d", d=64)
                    cosb = csb[j][:, 0:64].unsqueeze(1).broadcast_to([128, 4, 64])
                    sinb = csb[j][:, 64:128].unsqueeze(1).broadcast_to([128, 4, 64])
                    v.tensor_tensor(out=r1, in0=sv[:, :, 0:64], in1=cosb, op=ALU.mult)
                    v.tensor_tensor(out=r2, in0=sv[:, :, 64:128], in1=sinb, op=ALU.mult)
                    v.drain()
                    v.tensor_tensor(out=dv[:, :, 0:64], in0=r1, in1=r2, op=ALU.add)
                    v.drain()
                    v.tensor_tensor(out=r1, in0=sv[:, :, 0:64], in1=sinb, op=ALU.mult)
                    v.tensor_tensor(out=r2, in0=sv[:, :, 64:128], in1=cosb, op=ALU.mult)
                    v.drain()
                    v.tensor_tensor(out=dv[:, :, 64:128], in0=r2, in1=r1, op=ALU.subtract)
                    v.drain()
                v.drain().then_inc(s["rope"], 1)
                v.drain().then_inc(s["csf"], 1)
                v.wait_ge(s["qkv"], 3 * i + 3)
                if i >= 2:
                    v.wait_ge(s["odone"], i - 1)
                v.tensor_copy(vsb[j][:], pb[4][:]).then_inc(s["qe"], 1)
                v.drain()
                v.wait_ge(s["tp"], g0 + 5)
                if i >= 2:
                    v.wait_ge(s["odone"], i - 1)
                v.tensor_copy(qfmsb[j][:], pb[(g0 + 4) % 2][:]).then_inc(s["tpe"], 1)
                v.wait_ge(s["tp"], g0 + 6)
                v.tensor_copy(kfmsb[j][:], pb[(g0 + 5) % 2][:]).then_inc(s["tpe"], 1)
                v.drain()
                v.drain().then_inc(s["kfme"], 1)
                v.tensor_reduce(out=red4[:].unsqueeze(2),
                                in_=qfmsb[j][:].rearrange("p (h t) -> p h t", t=128),
                                axis=AX.X, op=ALU.add)
                v.drain()
                v.tensor_tensor(out=qsum[:], in0=qsum[:], in1=red4[:], op=ALU.add)
                v.drain()
                v.wait_ge(s["kp"], i + 1)
                v.tensor_copy(kpTsb[:], pb[5][0:NJL, :]).then_inc(s["kpe"], 1)
                v.drain()
                v.wait_ge(s["kpt"], i + 1)
                if i >= 1:
                    v.wait_ge(s["odone"], i)
                v.tensor_copy(kpnsb[:], pb[6][:, 0:128]).then_inc(s["kptev"], 1)
                v.drain()
                v.tensor_tensor(out=sqb[:, 0:128], in0=kpnsb[:], in1=kpnsb[:], op=ALU.mult)
                v.drain()
                v.tensor_reduce(out=ss4[:].unsqueeze(2),
                                in_=sqb[:, 0:128].rearrange("p (h j) -> p h j", j=NJL),
                                axis=AX.X, op=ALU.add).then_inc(s["kq1"], 1)
                v.wait_ge(s["kq2"], i + 1)
                v.tensor_scalar(out=rn4[:], in0=srt4[:], scalar1=1e-12, scalar2=0.0,
                                op0=ALU.max, op1=ALU.add)
                v.drain()
                v.reciprocal(out=rn4[:], in_=rn4[:])
                v.drain()
                v.tensor_tensor(out=kpnsb[:].rearrange("p (h j) -> p h j", j=NJL),
                                in0=kpnsb[:].rearrange("p (h j) -> p h j", j=NJL),
                                in1=rn4[:].unsqueeze(2).broadcast_to([128, 4, NJL]),
                                op=ALU.mult).then_inc(s["kpn"], 1)
                v.drain()
                v.wait_ge(s["cent"], i + 1)
                v.tensor_copy(cent_acc[0:2, 128 * i:128 * (i + 1)], pb[7][0:2, 0:128]).then_inc(s["centev"], 1)
                v.drain()
            head(0)
            for i in range(NT):
                if i + 1 < NT:
                    head(i + 1)
                rest(i)

        def p1_act(a):
            a.wait_ge(s["c0"], 240)

            def ahead(i):
                a.wait_ge(s["ln1"], i + 1)
                a.activation(out=sd1[:], in_=mv[:, 1:2], func=AF.Sqrt,
                             bias=1e-5, scale=1.0).then_inc(s["ln2"], 1)

            ahead(0)
            for i in range(NT):
                if i + 1 < NT:
                    ahead(i + 1)
                a.wait_ge(s["rm1"], i + 1)
                a.activation(out=srt8[:], in_=ssq[:], func=AF.Sqrt,
                             bias=1e-6, scale=1.0 / 128.0).then_inc(s["rm2"], 1)
                a.wait_ge(s["kq1"], i + 1)
                a.activation(out=srt4[:], in_=ss4[:], func=AF.Sqrt,
                             bias=0.0, scale=1.0).then_inc(s["kq2"], 1)

        def p1_gp(gp):
            pass

        def p1_dve_full(v):
            v.memset(qsum[:], 0.0)
            v.drain()
            p1_dve(v)

        phase({"sync": p1_sp, "tensor": p1_pe, "vector": p1_dve_full,
               "scalar": p1_act, "gpsimd": p1_gp})
        ph1.close()

        ph15 = ExitStack()

        def sb15(name, shape, dtype=F32):
            return ph15.enter_context(nc.sbuf_tensor(name, shape, dtype))

        kpnh = sb15("kpnh", [128, NT * NJL])
        kfmt4 = sb15("kfmt4", [32, 512])
        centm = sb15("centm", [128, NJL])
        css1 = sb15("css1", [128, 1])
        csr1 = sb15("csr1", [128, 1])
        cfm = sb15("cfm", [32, 128])
        qm = sb15("qm", [1, NJL])
        qss = sb15("qss", [1, 1])
        qsr = sb15("qsr", [1, 1])
        qpn_row = sb15("qpn_row", [1, NJL])
        qpn_col = sb15("qpn_col", [32, 1])
        dots2 = sb15("dots2", [2, NT * 128])
        odrow = sb15("odrow", [1, NT * 128])
        dmin2 = sb15("dmin2", [1, 128])
        radr = sb15("radr", [1, 128])
        srow1 = sb15("srow1", [1, 128])
        scT = sb15("scT", [128, 1])
        sbc = sb15("sbc", [128, 128])
        junk15 = sb15("junk15", [128, 128])
        rank = sb15("rank", [128, 1])
        sel01 = sb15("sel01", [128, 1])
        frc = sb15("frc", [128, 1])
        mcol = sb15("mcol", [128, 1])
        possb = sb15("possb", [128, 1])
        selrow = sb15("selrow", [1, 128])
        selrep = sb15("selrep", [128, 33])
        O1F = 16 * (4 * NT + 1)

        seq = Seq()
        g15c = {"n": 0}

        def gdma(gp, out, in_, **kw):
            g15c["n"] += 16
            gp.dma_start(out=out, in_=in_, **kw).then_inc(s["g15"], 16)
            gp.wait_ge(s["g15"], g15c["n"])

        def frc_init(v):
            v.memset(frc[:], 0.0)
            v.drain()
            v.tensor_scalar(out=junk15[:, 0:1], in0=parm[:], scalar1=0.0, scalar2=0.0,
                            op0=ALU.mult, op1=ALU.add)
        seq.step("vector", frc_init)
        seq.step("gpsimd", lambda gp: gdma(gp, frc[126:128, :], ones_d[0:2, :]))

        for hl in range(HPC):
            def ld(sp, hl=hl):
                sp.dma_start(out=kpnh[:].rearrange("p (i j) -> p i j", j=NJL),
                             in_=kpn_d[:, hl, :, :]).then_inc(s["o1"], 16)
                sp.dma_start(out=centm[:],
                             in_=centd.rearrange("par (i h j) -> i par h j", h=HPC, j=NJL)[:, :, hl, :]).then_inc(s["o1"], 16)
                sp.wait_ge(s["o1"], O1F + 32 * (hl + 1))
            seq.step("sync", ld)

            def c1(v, hl=hl):
                v.tensor_scalar(out=centm[:], in0=centm[:],
                                scalar1=1.0 / 64.0, scalar2=0.0, op0=ALU.mult, op1=ALU.add)
                v.drain()
                v.tensor_tensor(out=junk15[:, 0:NJL], in0=centm[:], in1=centm[:], op=ALU.mult)
                v.drain()
                v.tensor_reduce(out=css1[:], in_=junk15[:, 0:NJL], axis=AX.X, op=ALU.add)
            seq.step("vector", c1)
            seq.step("scalar", lambda a: a.activation(out=csr1[:], in_=css1[:],
                                                      func=AF.Sqrt, bias=0.0, scale=1.0))

            def c2(v):
                v.tensor_scalar(out=csr1[:], in0=csr1[:], scalar1=1e-12, scalar2=0.0,
                                op0=ALU.max, op1=ALU.add)
                v.drain()
                v.reciprocal(out=csr1[:], in_=csr1[:])
                v.drain()
                v.tensor_scalar(out=centm[:], in0=centm[:], scalar1=csr1[:], scalar2=0.0,
                                op0=ALU.mult, op1=ALU.add)
            seq.step("vector", c2)
            seq.step("tensor", lambda pe: pe.transpose(pb[0][0:NJL, 0:128], centm[:], iden[:]))
            seq.step("vector", lambda v: v.tensor_copy(cfm[:], pb[0][0:NJL, 0:128]))
            seq.step("tensor", lambda pe, hl=hl: pe.matmul(pb[2][0:1, 0:NJL],
                                                           qsum[:, hl:hl + 1], proj[:],
                                                           start=True, stop=True))

            def q1(v):
                v.tensor_scalar(out=qm[:], in0=pb[2][0:1, 0:NJL], scalar1=1.0 / 8192.0,
                                scalar2=0.0, op0=ALU.mult, op1=ALU.add)
                v.drain()
                v.tensor_tensor(out=junk15[0:1, 0:NJL], in0=qm[:], in1=qm[:], op=ALU.mult)
                v.drain()
                v.tensor_reduce(out=qss[:], in_=junk15[0:1, 0:NJL], axis=AX.X, op=ALU.add)
            seq.step("vector", q1)
            seq.step("scalar", lambda a: a.activation(out=qsr[:], in_=qss[:],
                                                      func=AF.Sqrt, bias=0.0, scale=1.0))

            def q2(v):
                v.tensor_scalar(out=qsr[:], in0=qsr[:], scalar1=1e-12, scalar2=0.0,
                                op0=ALU.max, op1=ALU.add)
                v.drain()
                v.reciprocal(out=qsr[:], in_=qsr[:])
                v.drain()
                v.tensor_scalar(out=qpn_row[:], in0=qm[:], scalar1=qsr[:], scalar2=0.0,
                                op0=ALU.mult, op1=ALU.add)
            seq.step("vector", q2)
            seq.step("tensor", lambda pe: pe.transpose(pb[2][0:NJL, 128:129], qpn_row[:], iden[0:1, 0:1]))
            seq.step("vector", lambda v: v.tensor_copy(qpn_col[:], pb[2][0:NJL, 128:129]))
            for g4 in range(NT // 4):
                def t1(pe, g4=g4):
                    for t in range(4):
                        i = 4 * g4 + t
                        pe.transpose(pb[0][0:NJL, 128 * t:128 * (t + 1)],
                                     kpnh[:, NJL * i:NJL * (i + 1)], iden[:])
                seq.step("tensor", t1)
                seq.step("vector", lambda v: v.tensor_copy(kfmt4[:], pb[0][0:NJL, :]))

                def t2(pe, g4=g4):
                    for t in range(4):
                        i = 4 * g4 + t
                        pe.matmul(pb[1][0:2, 128 * t:128 * (t + 1)],
                                  cfm[:, 2 * i:2 * i + 2],
                                  kfmt4[:, 128 * t:128 * (t + 1)], start=True, stop=True)
                seq.step("tensor", t2)

                def t3(v, g4=g4):
                    v.tensor_copy(dots2[0:2, 512 * g4:512 * (g4 + 1)], pb[1][0:2, 0:512])
                seq.step("vector", t3)

            seq.step("gpsimd", lambda gp: gdma(gp, dotsd[:], dots2[:]))
            seq.step("gpsimd", lambda gp: gdma(gp, odrow[:], dotsd[1:2, :]))

            def rad(v):
                v.tensor_reduce(out=dmin2[:].rearrange("p (a b) -> p a b", b=1)[:, 0::2, :],
                                in_=dots2[0:1, :].rearrange("p (i c) -> p i c", c=128)[:, :, 0:64],
                                axis=AX.X, op=ALU.min)
                v.tensor_reduce(out=dmin2[:].rearrange("p (a b) -> p a b", b=1)[:, 1::2, :],
                                in_=odrow[0:1, :].rearrange("p (i c) -> p i c", c=128)[:, :, 64:128],
                                axis=AX.X, op=ALU.min)
                v.drain()
                v.tensor_scalar(out=radr[:], in0=dmin2[:], scalar1=1.0, scalar2=-1.0,
                                op0=ALU.subtract, op1=ALU.mult)
                v.drain()
                v.tensor_scalar(out=radr[:], in0=radr[:], scalar1=0.0, scalar2=1.0,
                                op0=ALU.max, op1=ALU.min)
            seq.step("vector", rad)
            seq.step("tensor", lambda pe: pe.matmul(pb[2][0:1, 256:384], qpn_col[:], cfm[:],
                                                    start=True, stop=True))
            seq.step("vector", lambda v: v.tensor_tensor(out=srow1[:], in0=pb[2][0:1, 256:384],
                                                         in1=radr[:], op=ALU.add))
            seq.step("tensor", lambda pe: pe.transpose(pb[2][0:128, 384:385], srow1[:], iden[0:1, 0:1]))
            seq.step("vector", lambda v: v.tensor_copy(scT[:], pb[2][0:128, 384:385]))
            seq.step("tensor", lambda pe: pe.matmul(pb[3][:, 0:128], onesrow[:], srow1[:],
                                                    start=True, stop=True))
            seq.step("vector", lambda v: v.tensor_copy(sbc[:], pb[3][:, 0:128]))

            def rk(v):
                v.memset(sbc[:, 126:128], NEG)
                v.drain()
                v.tensor_scalar(out=junk15[:], in0=sbc[:], scalar1=scT[:], scalar2=0.0,
                                op0=ALU.is_gt, op1=ALU.add, accum_out=rank[:])
                v.drain()
                v.tensor_scalar(out=sel01[:], in0=rank[:], scalar1=64.0, scalar2=0.0,
                                op0=ALU.is_lt, op1=ALU.add)
                v.drain()
                v.tensor_tensor(out=sel01[:], in0=sel01[:], in1=frc[:], op=ALU.max)
            seq.step("vector", rk)

            seq.step("tensor", lambda pe: pe.matmul(pb[3][:, 0:1], ltm[:], sel01[:],
                                                    start=True, stop=True))
            seq.step("vector", lambda v: v.tensor_copy(possb[:], pb[3][:, 0:1]))

            def mkonehot(v):
                v.tensor_scalar(out=junk15[:], in0=iotab[:], scalar1=possb[:], scalar2=sel01[:],
                                op0=ALU.is_equal, op1=ALU.mult)
                v.drain()
                v.tensor_scalar(out=junk15[:], in0=junk15[:], scalar1=blkid[:], scalar2=0.0,
                                op0=ALU.mult, op1=ALU.add)
            seq.step("vector", mkonehot)
            seq.step("tensor", lambda pe: pe.matmul(pb[3][0:1, 128:256], onescol[:], junk15[:],
                                                    start=True, stop=True))
            seq.step("vector", lambda v: v.tensor_copy(selrow[:], pb[3][0:1, 128:256]))

            def selrep_mm(pe):
                me = selrow[:].rearrange("p (k t) -> p t k", t=2)
                pe.matmul(pb[3][0:64, 256:289], onesrow[0:1, 0:64], me[:, 0, 0:33],
                          start=True, stop=True)
                pe.matmul(pb[3][0:64, 289:322], onesrow[0:1, 0:64], me[:, 1, 0:33],
                          start=True, stop=True)
            seq.step("tensor", selrep_mm)

            def mkidx(v, hl=hl):
                v.tensor_copy(selrep[0:64, :], pb[3][0:64, 256:289])
                v.tensor_copy(selrep[64:128, :], pb[3][0:64, 289:322])
                v.drain()
                v.tensor_scalar(out=selrep[:], in0=selrep[:], scalar1=64.0, scalar2=poff[:],
                                op0=ALU.mult, op1=ALU.add)
                v.drain()
                v.tensor_scalar(out=selrep[:], in0=selrep[:], scalar1=float(8192 * hl),
                                scalar2=0.0, op0=ALU.add, op1=ALU.add)
                v.drain()
                v.tensor_copy(idx_all[:, 33 * hl:33 * (hl + 1)], selrep[:])
                v.drain()
            seq.step("vector", mkidx)


        phase(seq.emitters(0))
        ph15.close()

        ph2 = ExitStack()

        def sb2(name, shape, dtype=F32):
            return ph2.enter_context(nc.sbuf_tensor(name, shape, dtype))

        NKC = 33
        KS = NKC * 128
        ksg = [sb2(f"ksg{j}", [128, KS]) for j in range(2)]
        kfs = [sb2(f"kfs{j}", [128, KS]) for j in range(2)]
        vsg = [sb2(f"vsg{j}", [128, KS]) for j in range(2)]
        qb2 = [sb2(f"qb2_{j}", [128, 512]) for j in range(2)]
        expT = [sb2(f"expT{j}", [128, 512]) for j in range(2)]
        expsum = sb2("expsum", [128, 512])
        rrow = sb2("rrow", [1, 512])
        rb_sb = sb2("rb_sb", [128, 512])
        ysb = [sb2(f"ysb{j}", [128, 512]) for j in range(2)]
        NQC = 16

        def p2_gp(gp):
            def gathers(hl):
                jh = hl % 2
                kvs = s["kv2a" if jh == 0 else "kv2b"]
                if hl >= 2:
                    gp.wait_ge(s["hfree2"], hl - 1)
                ktm_flat = ktm_d.rearrange("h s d -> (h s) d")
                vtm_flat = vtm_d.rearrange("h s d -> (h s) d")
                for c in range(NKC):
                    gp.indirect_dma_start(
                        out=ksg[jh][:, 128 * c:128 * (c + 1)], out_offset=None,
                        in_=ktm_flat,
                        in_offset=bass.IndirectOffsetOnAxis(
                            ap=idx_all[:, 33 * hl + c:33 * hl + c + 1], axis=0)).then_inc(kvs, 16)
                for c in range(NKC):
                    gp.indirect_dma_start(
                        out=vsg[jh][:, 128 * c:128 * (c + 1)], out_offset=None,
                        in_=vtm_flat,
                        in_offset=bass.IndirectOffsetOnAxis(
                            ap=idx_all[:, 33 * hl + c:33 * hl + c + 1], axis=0)).then_inc(kvs, 16)

            gathers(0)
            gathers(1)
            for hl in range(HPC):
                for qc in range(NQC):
                    t2 = NQC * hl + qc
                    gp.wait_ge(s["ye"], t2 + 1)
                    gp.dma_start(out=yfm_d[128 * hl:128 * (hl + 1), 512 * qc:512 * (qc + 1)],
                                 in_=ysb[qc % 2][:]).then_inc(s["yoa" if qc % 2 == 0 else "yob"], 16)
                if hl + 2 < HPC:
                    gathers(hl + 2)
            gp.wait_ge(s["yoa"], 16 * NQC * HPC // 2)
            gp.wait_ge(s["yob"], 16 * NQC * HPC // 2)

        def p2_sp(sp):
            for hl in range(HPC):
                for qc in range(NQC):
                    t2 = NQC * hl + qc
                    if t2 >= 2:
                        sp.wait_ge(s["qf2"], t2 - 1)
                    sp.dma_start(out=qb2[qc % 2][:],
                                 in_=qfm_d[128 * hl:128 * (hl + 1), 512 * qc:512 * (qc + 1)]).then_inc(s["q2a" if qc % 2 == 0 else "q2b"], 16)
            sp.wait_ge(s["yoa"], 16 * NQC * HPC // 2)
            sp.wait_ge(s["yob"], 16 * NQC * HPC // 2)

        def p2_pe(pe):
            pend = []

            def flush_pend():
                while pend:
                    fn = pend.pop(0)
                    fn()
            for hl in range(HPC):
                jh = hl % 2
                pe.wait_ge(s["kv2a" if jh == 0 else "kv2b"], 16 * 2 * NKC * (hl // 2 + 1))
                flush_pend()
                if hl >= 1:
                    pe.wait_ge(s["exq"], NKC * NQC * hl)
                for c in range(NKC):
                    kk2 = NKC * hl + c
                    if kk2 >= 2:
                        pe.wait_ge(s["kt2e"], kk2 - 1)
                    pe.transpose(pb[c % 2][:, 0:128], ksg[jh][:, 128 * c:128 * (c + 1)],
                                 iden[:]).then_inc(s["kt2"], 1)
                pe.wait_ge(s["kt2e"], NKC * (hl + 1))
                for qc in range(NQC):
                    t2 = NQC * hl + qc
                    pe.wait_ge(s["q2a" if qc % 2 == 0 else "q2b"], 16 * ((t2 // 2) + 1))
                    if t2 >= 2:
                        pe.wait_ge(s["ye"], t2 - 1)
                    for kc in range(NKC):
                        kk = NKC * t2 + kc
                        if kk >= 2:
                            pe.wait_ge(s["exq"], kk - 1)
                        pe.matmul(pb[kc % 2][:], kfs[jh][:, 128 * kc:128 * (kc + 1)],
                                  qb2[qc % 2][:], start=True, stop=True).then_inc(s["lt"], 1)
                        if kc == 1:
                            flush_pend()
                        if kc >= 1:
                            kprev = kc - 1
                            pe.wait_ge(s["exq"], NKC * t2 + kprev + 1)
                            pe.matmul(pb[2 + (qc % 2)][:],
                                      vsg[jh][:, 128 * kprev:128 * (kprev + 1)],
                                      expT[kprev % 2][:],
                                      start=(kprev == 0), stop=False).then_inc(s["av"], 1)
                    pe.wait_ge(s["exq"], NKC * (t2 + 1))
                    pe.matmul(pb[2 + (qc % 2)][:], vsg[jh][:, 128 * (NKC - 1):128 * NKC],
                              expT[(NKC - 1) % 2][:], start=False, stop=True).then_inc(s["av"], 1)
                    pe.wait_ge(s["esum"], NKC * (t2 + 1))
                    if t2 >= 1:
                        pe.wait_ge(s["rcp2"], t2)
                    pe.matmul(pb[4][0:1, :], onescol[:], expsum[:],
                              start=True, stop=True).then_inc(s["sum2"], 1)

                    def mk_bcast(t2=t2):
                        def f():
                            pe.wait_ge(s["rcp2"], t2 + 1)
                            if t2 >= 1:
                                pe.wait_ge(s["rbe2"], t2)
                            pe.matmul(pb[5][:], onesrow[:], rrow[:],
                                      start=True, stop=True).then_inc(s["bc2"], 1)
                        return f
                    pend.append(mk_bcast())
                    pe.drain().then_inc(s["qf2"], 1)
                pe.drain().then_inc(s["hfree2"], 1)
            flush_pend()

        def p2_act(a):
            for hl in range(HPC):
                for qc in range(NQC):
                    t2 = NQC * hl + qc
                    for kc in range(NKC):
                        kk = NKC * t2 + kc
                        a.wait_ge(s["lt"], kk + 1)
                        if kk >= 2:
                            a.wait_ge(s["av"], kk - 1)
                            a.wait_ge(s["esum"], kk - 1)
                        a.activation(out=expT[kc % 2][:], in_=pb[kc % 2][:], func=AF.Exp,
                                     bias=0.0, scale=RS128).then_inc(s["exq"], 1)
                    a.wait_ge(s["bc2"], t2 + 1)
                    if t2 >= 1:
                        a.wait_ge(s["ye"], t2)
                    a.copy(out=rb_sb[:], in_=pb[5][:]).then_inc(s["rbe2"], 1)

        def p2_dve(v):
            for hl in range(HPC):
                jh = hl % 2
                for c in range(NKC):
                    kk2 = NKC * hl + c
                    v.wait_ge(s["kt2"], kk2 + 1)
                    v.tensor_copy(kfs[jh][:, 128 * c:128 * (c + 1)],
                                  pb[c % 2][:, 0:128]).then_inc(s["kt2e"], 1)
                v.drain()
                for qc in range(NQC):
                    t2 = NQC * hl + qc
                    for kc in range(NKC):
                        kk = NKC * t2 + kc
                        v.wait_ge(s["exq"], kk + 1)
                        if kc == 0:
                            if t2 >= 1:
                                v.wait_ge(s["sum2"], t2)
                            v.tensor_copy(expsum[:], expT[0][:]).then_inc(s["esum"], 1)
                        else:
                            v.tensor_tensor(out=expsum[:], in0=expsum[:], in1=expT[kc % 2][:],
                                            op=ALU.add).then_inc(s["esum"], 1)
                        v.drain()
                    v.wait_ge(s["sum2"], t2 + 1)
                    if t2 >= 1:
                        v.wait_ge(s["bc2"], t2)
                    v.reciprocal(out=rrow[:], in_=pb[4][0:1, :]).then_inc(s["rcp2"], 1)
                    v.drain()
                    v.wait_ge(s["av"], NKC * (t2 + 1))
                    v.wait_ge(s["rbe2"], t2 + 1)
                    if t2 >= 2:
                        v.wait_ge(s["yoa" if qc % 2 == 0 else "yob"], 16 * ((t2 - 2) // 2 + 1))
                    v.tensor_tensor(out=ysb[qc % 2][:], in0=pb[2 + (qc % 2)][:],
                                    in1=rb_sb[:], op=ALU.mult).then_inc(s["ye"], 1)
                    v.drain()

        phase({"sync": p2_sp, "tensor": p2_pe, "scalar": p2_act, "vector": p2_dve,
               "gpsimd": p2_gp})
        ph2.close()

        ph3 = ExitStack()

        def sb3(name, shape, dtype=F32):
            return ph3.enter_context(nc.sbuf_tensor(name, shape, dtype))

        wo_sb = sb3("wo_sb", [128, HPC * DM])
        yb3 = [sb3(f"yb3_{j}", [128, 512]) for j in range(2)]
        rsb = [sb3(f"rsb{j}", [128, DM]) for j in range(2)]
        dpk = [sb3(f"dpk{j}", [128, DM], I32) for j in range(2)]
        dscs = [sb3(f"dscs{j}", [128, DM], F16) for j in range(2)]
        e_k = sb3("e_k", [128, 1024], U16)
        e_sb = sb3("e_sb", [128, 1024], U16)
        e_u = sb3("e_u", [128, 1024], U16)
        e_a = sb3("e_a", [128, 1024], U16)
        e_i = sb3("e_i", [128, 1024], U16)
        wout = [sb3(f"wout{j}", [128, DM]) for j in range(2)]

        def fp4_q(v, src_i32_ap, jj, q, outs):
            p16 = src_i32_ap.bitcast(U16)[:, 1024 * q:1024 * (q + 1)]
            v.tensor_scalar(out=e_k[:], in0=p16, scalar1=4 * jj, scalar2=7,
                            op0=ALU.logical_shift_right, op1=ALU.bitwise_and)
            v.tensor_scalar(out=e_sb[:], in0=p16, scalar1=12 - 4 * jj, scalar2=32768,
                            op0=ALU.logical_shift_left, op1=ALU.bitwise_and)
            v.drain()
            v.tensor_scalar(out=e_u[:], in0=e_k[:], scalar1=2, scalar2=0,
                            op0=ALU.min, op1=ALU.add)
            v.tensor_scalar(out=e_i[:], in0=e_k[:], scalar1=0, scalar2=0,
                            op0=ALU.is_gt, op1=ALU.add)
            v.drain()
            v.tensor_tensor(out=e_a[:], in0=e_k[:], in1=e_u[:], op=ALU.add)
            v.drain()
            v.tensor_scalar(out=e_a[:], in0=e_a[:], scalar1=26, scalar2=512,
                            op0=ALU.add, op1=ALU.mult)
            v.drain()
            v.tensor_tensor(out=e_a[:], in0=e_a[:], in1=e_i[:], op=ALU.mult)
            v.drain()
            v.tensor_tensor(out=e_a[:], in0=e_a[:], in1=e_sb[:], op=ALU.bitwise_or)
            v.drain()
            for hh in range(2):
                outs(hh, e_a[:].rearrange("p (o t) -> p o t", t=2)[:, :, hh].bitcast(F16))
            v.drain()

        WUNITS = [(P, jj, q) for P in range(6) for jj in range(4) for q in range(4)]

        TORDER = ([16 * g + r for g in range(4) for r in range(8)]
                  + [16 * g + 8 + r for g in range(4) for r in range(8)])

        def rs_sem(ii, j):
            if ii < 32:
                return s["rsAa" if j == 0 else "rsAb"]
            return s["rsBa" if j == 0 else "rsBb"]

        def p3_sp(sp):
            sp.dma_start(out=wo_sb[:].rearrange("p (h o) -> p h o", o=DM),
                         in_=woT_d.rearrange("(h p) o -> p h o", p=128)).then_inc(s["wo3"], 16)
            for ii in range(NT):
                i = TORDER[ii]
                j = ii % 2
                if ii >= 2:
                    sp.wait_ge(s["w3"], 4 * (ii - 1))
                sp.dma_start(out=yb3[j][:].rearrange("p (h t) -> p h t", t=128),
                             in_=yfm_d.rearrange("(h p) t -> p h t", p=128)[:, :, 128 * i:128 * (i + 1)]).then_inc(s["y3a" if j == 0 else "y3b"], 16)
                sp.wait_ge(s["e3"], 4 * (ii + 1))
                sp.dma_start(out=rs_in_h.ap()[128 * ii:128 * (ii + 1), :], in_=rsb[j][:]).then_inc(rs_sem(ii, j), 16)
            sp.wait_ge(s["rsBa"], 16 * 16)
            sp.wait_ge(s["rsBb"], 16 * 16)

        def p3_pe(pe):
            pe.wait_ge(s["wo3"], 16)
            for ii in range(NT):
                j = ii % 2
                pe.wait_ge(s["y3a" if j == 0 else "y3b"], 16 * (ii // 2 + 1))
                if ii >= 2:
                    pe.wait_ge(s["e3"], 4 * (ii - 1))
                for dmc in range(4):
                    for hl in range(HPC):
                        ins = pe.matmul(pb[4 * j + dmc][:],
                                        yb3[j][:, 128 * hl:128 * (hl + 1)],
                                        wo_sb[:, DM * hl + 512 * dmc:DM * hl + 512 * (dmc + 1)],
                                        start=(hl == 0), stop=(hl == 3))
                        if hl == 3:
                            ins.then_inc(s["w3"], 1)

        def p3_dve(v):
            un = {"u": 0}

            def emit_unit():
                if un["u"] >= len(WUNITS):
                    return
                P, jj, q = WUNITS[un["u"]]
                jP = P % 2
                if q == 0 and jj == 0:
                    v.wait_ge(s["gdpe"], P + 1)
                if q == 0:
                    for hh in range(2):
                        prevu = 4 * P + jj - 1
                        if prevu >= 0:
                            v.wait_ge(s["gst"], 2 * prevu + hh + 1)

                def outs(hh, src_ap, P=P, jj=jj, q=q, jP=jP):
                    last = (q == 3)
                    ins = v.tensor_tensor(out=wout[hh][:, 512 * q:512 * (q + 1)],
                                          in0=src_ap, in1=dscs[jP][:, 512 * q:512 * (q + 1)],
                                          op=ALU.mult)
                    if last and hh == 1:
                        ins.then_inc(s["wde5"], 1)
                fp4_q(v, dpk[jP][:], jj, q, outs)
                un["u"] += 1

            for ii in range(NT):
                j = ii % 2
                for dmc in range(4):
                    v.wait_ge(s["w3"], 4 * ii + dmc + 1)
                    if ii >= 2:
                        if ii < 34:
                            v.wait_ge(rs_sem(ii - 2, j), 16 * (ii // 2))
                        else:
                            v.wait_ge(rs_sem(ii - 2, j), 16 * ((ii - 32) // 2))
                    v.tensor_copy(rsb[j][:, 512 * dmc:512 * (dmc + 1)],
                                  pb[4 * j + dmc][:]).then_inc(s["e3"], 1)
                v.drain()
                target = min(len(WUNITS), ((ii + 1) * len(WUNITS)) // 44)
                while un["u"] < target:
                    emit_unit()
            while un["u"] < len(WUNITS):
                emit_unit()

        G15F = g15c["n"]
        g3 = {"n": 0}

        def p3_gp(gp):
            def g3dma(out, in_):
                g3["n"] += 16
                gp.dma_start(out=out, in_=in_).then_inc(s["g15"], 16)
                gp.wait_ge(s["g15"], G15F + g3["n"])
            for P in range(6):
                jP = P % 2
                if P >= 2:
                    gp.wait_ge(s["wde5"], 4 * (P - 1))
                g3dma(dpk[jP][:], downT_d[128 * P:128 * (P + 1), :])
                g3dma(dscs[jP][:], dsc_d[128 * P:128 * (P + 1), :])
                gp.sem_inc(s["gdpe"], 1)
                for jj in range(4):
                    for hh in range(2):
                        sid = 8 * P + 4 * hh + jj
                        gp.wait_ge(s["wde5"], 4 * P + jj + 1)
                        g3dma(wde_d[sid], wout[hh][:])
                        gp.sem_inc(s["gst"], 1)
            gp.wait_ge(s["rsAa"], 16 * 16)
            gp.wait_ge(s["rsAb"], 16 * 16)
            gp.collective_compute(
                "ReduceScatter", ALU.add,
                replica_groups=[[0, 1, 2, 3], [4, 5, 6, 7]],
                ins=[rs_in_h.ap()[0:4096, :].opt()],
                outs=[rs_out_h.ap()[0:1024, :].opt()],
            ).then_inc(s["ccA"], 1)
            gp.wait_ge(s["rsBa"], 16 * 16)
            gp.wait_ge(s["rsBb"], 16 * 16)
            gp.collective_compute(
                "ReduceScatter", ALU.add,
                replica_groups=[[0, 1, 2, 3], [4, 5, 6, 7]],
                ins=[rs_in_h.ap()[4096:8192, :].opt()],
                outs=[rs_out_h.ap()[1024:2048, :].opt()],
            ).then_inc(s["ccB"], 1)

        phase({"sync": p3_sp, "tensor": p3_pe, "vector": p3_dve, "gpsimd": p3_gp})
        ph3.close()

        ph6 = ExitStack()

        def sb6(name, shape, dtype=F32):
            return ph6.enter_context(nc.sbuf_tensor(name, shape, dtype))

        NOC = 22
        NTH = 8
        LINS = 2 * NOC
        ln2fmh = sb6("ln2fmh", [128, 16 * 128 * NTH])
        ln2t6 = [sb6(f"ln2t6_{j}", [128, DM]) for j in range(2)]
        wgu2 = [sb6(f"wgu2_{j}", [128, 16 * 512]) for j in range(2)]
        gpk = [sb6(f"gpk{j}", [128, 512], I32) for j in range(2)]
        upk = [sb6(f"upk{j}", [128, 512], I32) for j in range(2)]
        gscs = [sb6(f"gscs{j}", [128, 512], F16) for j in range(2)]
        uscs = [sb6(f"uscs{j}", [128, 512], F16) for j in range(2)]
        f_k = sb6("f_k", [128, 1024], U16)
        f_sb = sb6("f_sb", [128, 1024], U16)
        f_a = sb6("f_a", [128, 1024], U16)
        f_i = sb6("f_i", [128, 1024], U16)
        sg_sb = [sb6(f"sg_sb{j}", [128, 256]) for j in range(2)]
        htl = [sb6(f"htl{j}", [128, 256]) for j in range(2)]
        x5b = [sb6(f"x5b{j}", [128, DM]) for j in range(2)]
        r5b = [sb6(f"r5b{j}", [128, DM]) for j in range(2)]
        bnst5 = sb6("bnst5", [128, 24])
        mv5 = sb6("mv5", [128, 2])
        sd5 = sb6("sd5", [128, 1])
        rstd5 = sb6("rstd5", [128, 1])

        def pass_of(g):
            P, c = g // 8, g % 8
            return 4 * P + (c % 4)

        def p6_sp(sp):
            def x_tile(i):
                j = i % 2
                if i == 0:
                    sp.wait_ge(s["ccA"], 1)
                if i == 8:
                    sp.wait_ge(s["ccB"], 1)
                if i >= 2:
                    sp.wait_ge(s["l5s"], i - 1)
                    sp.wait_ge(s["x2wa" if j == 0 else "x2wb"], 16 * ((i - 2) // 2 + 1))
                x5s = s["x5a" if j == 0 else "x5b"]
                for c in range(2):
                    sp.dma_start(out=x5b[j][:, 1024 * c:1024 * (c + 1)],
                                 in_=xq[128 * i:128 * (i + 1), 1024 * c:1024 * (c + 1)]).then_inc(x5s, 16)
                    sp.dma_start(out=r5b[j][:, 1024 * c:1024 * (c + 1)],
                                 in_=rs_out_h.ap()[128 * i:128 * (i + 1), 1024 * c:1024 * (c + 1)]).then_inc(x5s, 16)
                sp.wait_ge(s["l51"], 2 * i + 1)
                sp.dma_start(out=x2_d[128 * i:128 * (i + 1), :], in_=r5b[j][:]).then_inc(s["x2wa" if j == 0 else "x2wb"], 16)

            for half in range(2):
                for i in range(NTH * half, NTH * (half + 1)):
                    x_tile(i)
                for oc in range(NOC):
                    lin = NOC * half + oc
                    for tl in range(NTH):
                        gl = NTH * lin + tl
                        sp.wait_ge(s["h6"], gl + 1)
                        sp.dma_start(out=h_d[128 * (NTH * half + tl):128 * (NTH * half + tl) + 128,
                                             256 * oc:256 * (oc + 1)],
                                     in_=htl[tl % 2][:]).then_inc(s["hw6a" if tl % 2 == 0 else "hw6b"], 16)
            sp.wait_ge(s["hw6a"], 16 * 2 * NOC * NTH // 2)
            sp.wait_ge(s["hw6b"], 16 * 2 * NOC * NTH // 2)
            sp.wait_ge(s["x2wa"], 16 * NT5 // 2)
            sp.wait_ge(s["x2wb"], 16 * NT5 // 2)

        def p6_gp(gp):

            def pkload(lin):
                oc = lin % NOC
                j = lin % 2
                g6s = s["g6a" if j == 0 else "g6b"]
                if lin >= 2:
                    gp.wait_ge(s["e6"], 8 * (lin - 1))
                gp.dma_start(out=gpk[j][:].rearrange("p (P o) -> p P o", o=256),
                             in_=gateT_d.rearrange("(P p) o -> p P o", p=128)[:, :, 256 * oc:256 * (oc + 1)]).then_inc(g6s, 16)
                gp.dma_start(out=upk[j][:].rearrange("p (P o) -> p P o", o=256),
                             in_=upT_d.rearrange("(P p) o -> p P o", p=128)[:, :, 256 * oc:256 * (oc + 1)]).then_inc(g6s, 16)
                gp.dma_start(out=gscs[j][:].rearrange("p (P o) -> p P o", o=256),
                             in_=gsc_d.rearrange("(P p) o -> p P o", p=128)[:, :, 256 * oc:256 * (oc + 1)]).then_inc(g6s, 16)
                gp.dma_start(out=uscs[j][:].rearrange("p (P o) -> p P o", o=256),
                             in_=usc_d.rearrange("(P p) o -> p P o", p=128)[:, :, 256 * oc:256 * (oc + 1)]).then_inc(g6s, 16)

            def ln_store(lt):
                gp.wait_ge(s["l52"], lt + 1)
                for c in range(4):
                    gp.dma_start(out=ln2_d[128 * lt:128 * (lt + 1), 512 * c:512 * (c + 1)],
                                 in_=x5b[lt % 2][:, 512 * c:512 * (c + 1)]).then_inc(
                        s["lsa" if lt % 2 == 0 else "lsb"], 16)

            def ln_reload(lt):
                p = "lsa" if lt % 2 == 0 else "lsb"
                gp.wait_ge(s[p], 64 * (lt // 2 + 1))
                gp.sem_inc(s["l5s"], 1)
                if lt >= 2:
                    gp.wait_ge(s["t5"], 16 * (lt - 1))
                for c in range(4):
                    gp.dma_start(out=ln2t6[lt % 2][:, 512 * c:512 * (c + 1)],
                                 in_=ln2_d[128 * lt:128 * (lt + 1), 512 * c:512 * (c + 1)]).then_inc(
                        s["lra" if lt % 2 == 0 else "lrb"], 16)
                gp.wait_ge(s["lra" if lt % 2 == 0 else "lrb"], 64 * (lt // 2 + 1))
                gp.sem_inc(s["l6"], 1)

            def ln_chain_block(lo, hi):
                ln_store(lo)
                for lt in range(lo + 1, hi):
                    ln_store(lt)
                    ln_reload(lt - 1)
                ln_reload(hi - 1)

            pkload(0)
            pkload(1)
            ln_chain_block(0, NTH)
            for lin in range(2, NOC + 2):
                pkload(lin)
            ln_chain_block(NTH, 2 * NTH)
            for lin in range(NOC + 2, LINS):
                pkload(lin)

        def p6_pe(pe):
            for half in range(2):
                if half >= 1:
                    pe.wait_ge(s["sil6"], NTH * NOC * half)
                    pe.wait_ge(s["h6"], NTH * NOC * half)
                for tl in range(NTH):
                    lt = NTH * half + tl
                    pe.wait_ge(s["l6"], lt + 1)
                    for bg in range(4):
                        gti = 4 * lt + bg
                        if gti >= 2:
                            pe.wait_ge(s["t5e"], gti - 1)
                        for gg in range(4):
                            g = 4 * bg + gg
                            P, c = g // 8, g % 8
                            ins = pe.transpose(pb[gti % 2][:, 128 * gg:128 * (gg + 1)],
                                               ln2t6[tl % 2][:].rearrange("p (a b) -> p b a", b=8)[:, c, 128 * P:128 * (P + 1)],
                                               iden[:])
                            if gg == 3:
                                ins.then_inc(s["t5"], 1)
                    pe.drain().then_inc(s["t5"], 12)
                for oc in range(NOC):
                    lin = NOC * half + oc
                    j = lin % 2
                    if oc == 0:
                        pe.wait_ge(s["t5e"], 32 * (half + 1))
                    for tl in range(NTH):
                        gl = NTH * lin + tl
                        if gl >= 2:
                            pe.wait_ge(s["sil6"], gl - 1)
                        for g in range(16):
                            pe.wait_ge(s["e6"], 8 * lin + pass_of(g) + 1)
                            ins = pe.matmul(pb[tl % 2][:, 0:256],
                                            ln2fmh[:, 1024 * g + 128 * tl:1024 * g + 128 * (tl + 1)],
                                            wgu2[j][:, 512 * g:512 * g + 256],
                                            start=(g == 0), stop=(g == 15))
                            if g == 15:
                                ins.then_inc(s["mmg6"], 1)
                        if gl >= 2:
                            pe.wait_ge(s["h6"], gl - 1)
                        for g in range(16):
                            ins = pe.matmul(pb[2 + (tl % 2)][:, 0:256],
                                            ln2fmh[:, 1024 * g + 128 * tl:1024 * g + 128 * (tl + 1)],
                                            wgu2[j][:, 512 * g + 256:512 * (g + 1)],
                                            start=(g == 0), stop=(g == 15))
                            if g == 15:
                                ins.then_inc(s["mmu6"], 1)

        def p6_dve(v):
            def p5b_tile(i):
                j = i % 2
                v.wait_ge(s["x5a" if j == 0 else "x5b"], 64 * (i // 2 + 1))
                if i >= 2:
                    v.wait_ge(s["x2wa" if j == 0 else "x2wb"], 16 * ((i - 2) // 2 + 1))
                v.tensor_tensor(out=r5b[j][:], in0=x5b[j][:], in1=r5b[j][:],
                                op=ALU.add).then_inc(s["l51"], 1)
                v.drain()
                v.drain().then_inc(s["l51"], 1)
                for c in range(4):
                    v.bn_stats(out=bnst5[:, 6 * c:6 * (c + 1)], in_=r5b[j][:, 512 * c:512 * (c + 1)])
                v.drain()
                v.bn_aggr(out=mv5[:], in_=bnst5[:]).then_inc(s["ln1"], 1)
                v.wait_ge(s["ln2"], NT + i + 1)
                v.reciprocal(out=rstd5[:], in_=sd5[:])
                v.drain()
                v.tensor_scalar(out=x5b[j][:], in0=r5b[j][:], scalar1=mv5[:, 0:1],
                                scalar2=rstd5[:], op0=ALU.subtract, op1=ALU.mult).then_inc(s["l52"], 1)
                v.drain()

            def extract_pass(lin, P, jj):
                j = lin % 2
                p16g = gpk[j][:, 256 * P:256 * (P + 1)].bitcast(U16)
                p16u = upk[j][:, 256 * P:256 * (P + 1)].bitcast(U16)
                for p16, lo in ((p16g, 0), (p16u, 512)):
                    v.tensor_scalar(out=f_k[:, lo:lo + 512], in0=p16, scalar1=4 * jj, scalar2=7,
                                    op0=ALU.logical_shift_right, op1=ALU.bitwise_and)
                    v.tensor_scalar(out=f_sb[:, lo:lo + 512], in0=p16, scalar1=12 - 4 * jj,
                                    scalar2=32768, op0=ALU.logical_shift_left, op1=ALU.bitwise_and)
                v.drain()
                v.tensor_scalar(out=f_a[:], in0=f_k[:], scalar1=2, scalar2=0,
                                op0=ALU.min, op1=ALU.add)
                v.tensor_scalar(out=f_i[:], in0=f_k[:], scalar1=0, scalar2=0,
                                op0=ALU.is_gt, op1=ALU.add)
                v.drain()
                v.tensor_tensor(out=f_a[:], in0=f_a[:], in1=f_k[:], op=ALU.add)
                v.drain()
                v.tensor_scalar(out=f_a[:], in0=f_a[:], scalar1=26, scalar2=512,
                                op0=ALU.add, op1=ALU.mult)
                v.drain()
                v.tensor_tensor(out=f_a[:], in0=f_a[:], in1=f_i[:], op=ALU.mult)
                v.drain()
                v.tensor_tensor(out=f_a[:], in0=f_a[:], in1=f_sb[:], op=ALU.bitwise_or)
                v.drain()
                for half_o, lo, sc in ((0, 0, gscs[j]), (256, 512, uscs[j])):
                    for hh in range(2):
                        g = 8 * P + 4 * hh + jj
                        ins = v.tensor_tensor(
                            out=wgu2[j][:, 512 * g + half_o:512 * g + half_o + 256],
                            in0=f_a[:, lo:lo + 512].rearrange("p (o t) -> p o t", t=2)[:, :, hh].bitcast(F16),
                            in1=sc[:, 256 * P:256 * (P + 1)], op=ALU.mult)
                        if half_o == 256 and hh == 1:
                            ins.then_inc(s["e6"], 1)
                v.drain()

            EUNITS = [(lin, P, jj) for lin in range(LINS) for P in range(2) for jj in range(4)]
            eu = {"u": 0}

            def emit_units(upto_lin):
                while eu["u"] < len(EUNITS) and EUNITS[eu["u"]][0] <= upto_lin:
                    lin, P, jj = EUNITS[eu["u"]]
                    j = lin % 2
                    if P == 0 and jj == 0:
                        v.wait_ge(s["g6a" if j == 0 else "g6b"], 64 * (lin // 2 + 1))
                        if lin >= 2:
                            v.wait_ge(s["mmu6"], NTH * (lin - 1))
                    extract_pass(lin, P, jj)
                    eu["u"] += 1

            for half in range(2):
                base = NTH * half

                def evac_tile(tl):
                    for bg in range(4):
                        v.wait_ge(s["t5"], 16 * base + 16 * tl + bg + 1)
                        v.tensor_copy(
                            ln2fmh[:].rearrange("p (g t) -> p g t", t=128 * NTH)[:, 4 * bg:4 * bg + 4,
                                                                                 128 * tl:128 * (tl + 1)],
                            pb[(4 * tl + bg) % 2][:].rearrange("p (g t) -> p g t", t=128)).then_inc(s["t5e"], 1)
                    v.drain()

                p5b_tile(base + 0)
                p5b_tile(base + 1)
                for k in range(2, NTH):
                    evac_tile(k - 2)
                    p5b_tile(base + k)
                evac_tile(NTH - 2)
                evac_tile(NTH - 1)
                for oc in range(NOC):
                    lin = NOC * half + oc
                    emit_units(lin)
                    for tl in range(NTH):
                        gl = NTH * lin + tl
                        v.wait_ge(s["mmu6"], gl + 1)
                        v.wait_ge(s["sil6"], gl + 1)
                        if gl >= 2:
                            v.wait_ge(s["hw6a" if tl % 2 == 0 else "hw6b"], 16 * ((gl - 2) // 2 + 1))
                        v.tensor_tensor(out=htl[tl % 2][:], in0=sg_sb[tl % 2][:],
                                        in1=pb[2 + (tl % 2)][:, 0:256], op=ALU.mult).then_inc(s["h6"], 1)
                        v.drain()
                        if tl >= 1 and (half < 1 or oc < NOC - 1):
                            emit_units(lin + (1 if tl >= 1 else 0))

        def p6_act(a):
            for half in range(2):
                for i in range(NTH * half, NTH * (half + 1)):
                    a.wait_ge(s["ln1"], NT + i + 1)
                    a.activation(out=sd5[:], in_=mv5[:, 1:2], func=AF.Sqrt,
                                 bias=1e-5, scale=1.0).then_inc(s["ln2"], 1)
                for oc in range(NOC):
                    lin = NOC * half + oc
                    for tl in range(NTH):
                        gl = NTH * lin + tl
                        a.wait_ge(s["mmg6"], gl + 1)
                        if gl >= 2:
                            a.wait_ge(s["h6"], gl - 1)
                        a.activation(out=sg_sb[tl % 2][:], in_=pb[tl % 2][:, 0:256],
                                     func=AF.Silu, bias=0.0, scale=1.0).then_inc(s["sil6"], 1)

        phase({"sync": p6_sp, "vector": p6_dve, "tensor": p6_pe, "scalar": p6_act,
               "gpsimd": p6_gp})
        ph6.close()

        ph7 = ExitStack()

        def sb7(name, shape, dtype=F32):
            return ph7.enter_context(nc.sbuf_tensor(name, shape, dtype))

        hb7 = sb7("hb7", [128, 2 * IFF])
        hfm = sb7("hfm", [128, 2 * NOSET * 128])
        wdb = [sb7(f"wdb{j}", [128, DM]) for j in range(2)]
        x2t = [sb7(f"x2t{j}", [128, DM]) for j in range(2)]
        outb = [sb7(f"outb{j}", [128, DM]) for j in range(2)]
        bnst7 = sb7("bnst7", [128, 24])
        mv7 = sb7("mv7", [128, 2])
        sd7 = sb7("sd7", [128, 1])
        rr7 = sb7("rr7", [128, 1])

        def hfm_slice(tt, sid):
            o = (2 * sid + tt) * 128
            return hfm[:, o:o + 128]

        def p7_sp(sp):
            for tg in range(8):
                if tg >= 1:
                    sp.wait_ge(s["t7"], 96 * tg)
                sp.dma_start(out=hb7[:].rearrange("p (t o) -> p t o", o=IFF),
                             in_=h_d.rearrange("(g t p) o -> g p t o", t=2, p=128)[tg]).then_inc(s["h7"], 16)
                for tt in range(2):
                    if tg >= 1:
                        sp.wait_ge(s["b72"], 2 * tg)
                    sp.dma_start(out=x2t[tt][:],
                                 in_=x2_d[256 * tg + 128 * tt:256 * tg + 128 * (tt + 1), :]).then_inc(s["xl7"], 16)
                for sid in range(NOSET):
                    kk = NOSET * tg + sid
                    if kk >= 2:
                        sp.wait_ge(s["mm7"], kk - 1)
                    sp.dma_start(out=wdb[sid % 2][:], in_=wde_d[sid]).then_inc(s["w7a" if sid % 2 == 0 else "w7b"], 16)
                for tt in range(2):
                    sp.wait_ge(s["b72"], 2 * tg + tt + 1)
                    sp.dma_start(out=out_d[256 * tg + 128 * tt:256 * tg + 128 * (tt + 1), :],
                                 in_=outb[tt][:]).then_inc(s["o7a" if tt == 0 else "o7b"], 16)
            sp.wait_ge(s["o7a"], 16 * 8)
            sp.wait_ge(s["o7b"], 16 * 8)

        def p7_pe(pe):
            for tg in range(8):
                pe.wait_ge(s["h7"], 16 * (tg + 1))
                if tg >= 1:
                    pe.wait_ge(s["b72"], 2 * tg)
                tb = 0
                for tt in range(2):
                    for sid in range(NOSET):
                        P, c = sid // 8, sid % 8
                        ncnt = 64 if P == 5 else 128
                        gidx = 96 * tg + tb
                        if gidx >= 2:
                            pe.wait_ge(s["t7e"], gidx - 1)
                        pe.transpose(pb[tb % 2][0:ncnt, 0:128],
                                     hb7[:].rearrange("p (t a b) -> p t b a", t=2, b=8)[:, tt, c, 128 * P:128 * P + ncnt],
                                     iden[:]).then_inc(s["t7"], 1)
                        tb += 1
                pe.wait_ge(s["t7e"], 96 * (tg + 1))
                for sid in range(NOSET):
                    kk = NOSET * tg + sid
                    pe.wait_ge(s["w7a" if sid % 2 == 0 else "w7b"], 16 * (kk // 2 + 1))
                    for tt in range(2):
                        for dmc in range(4):
                            ins = pe.matmul(pb[4 * tt + dmc][:],
                                            hfm_slice(tt, sid),
                                            wdb[sid % 2][:, 512 * dmc:512 * (dmc + 1)],
                                            start=(sid == 0), stop=(sid == NOSET - 1))
                            if tt == 1 and dmc == 3:
                                ins.then_inc(s["mm7"], 1)

        def p7_dve(v):
            for tg in range(8):
                tb = 0
                for tt in range(2):
                    for sid in range(NOSET):
                        P = sid // 8
                        ncnt = 64 if P == 5 else 128
                        gidx = 96 * tg + tb
                        v.wait_ge(s["t7"], gidx + 1)
                        v.tensor_copy(hfm_slice(tt, sid)[0:ncnt, :], pb[tb % 2][0:ncnt, 0:128])
                        if ncnt == 64:
                            v.memset(hfm_slice(tt, sid)[64:128, :], 0.0)
                        v.drain()
                        v.drain().then_inc(s["t7e"], 1)
                        tb += 1
                v.wait_ge(s["mm7"], NOSET * (tg + 1))
                v.wait_ge(s["xl7"], 32 * (tg + 1))
                for tt in range(2):
                    for c in range(4):
                        v.bn_stats(out=bnst7[:, 6 * c:6 * (c + 1)], in_=pb[4 * tt + c][:])
                    v.drain()
                    v.bn_aggr(out=mv7[:], in_=bnst7[:]).then_inc(s["b71"], 1)
                    v.drain()
                    v.tensor_tensor(out=sd7[:], in0=mv7[:, 0:1], in1=mv7[:, 0:1], op=ALU.mult)
                    v.drain()
                    v.tensor_tensor(out=sd7[:], in0=sd7[:], in1=mv7[:, 1:2], op=ALU.add).then_inc(s["b71"], 1)
                    v.wait_ge(s["kq2"], NT + 2 * tg + tt + 1)
                    v.reciprocal(out=rr7[:], in_=sd7[:])
                    v.drain()
                    if tg >= 1:
                        v.wait_ge(s["o7a" if tt == 0 else "o7b"], 16 * tg)
                    for dmc in range(4):
                        v.scalar_tensor_tensor(out=outb[tt][:, 512 * dmc:512 * (dmc + 1)],
                                               in0=pb[4 * tt + dmc][:], scalar=rr7[:],
                                               in1=x2t[tt][:, 512 * dmc:512 * (dmc + 1)],
                                               op0=ALU.mult, op1=ALU.add)
                    v.drain()
                    v.drain().then_inc(s["b72"], 1)

        def p7_act(a):
            for tg in range(8):
                for tt in range(2):
                    a.wait_ge(s["b71"], 2 * (2 * tg + tt) + 2)
                    a.activation(out=sd7[:], in_=sd7[:], func=AF.Sqrt,
                                 bias=1e-6, scale=1.0).then_inc(s["kq2"], 1)

        phase({"sync": p7_sp, "tensor": p7_pe, "vector": p7_dve, "scalar": p7_act})
        ph7.close()

    return nc


def _rope_tables():
    t = np.arange(S, dtype=np.float64)
    inv = (10000.0 ** (-(np.arange(0, 128, 2, dtype=np.float64)) / 128.0))
    ang32 = np.float32(np.outer(t, inv).astype(np.float32))
    cos = np.cos(ang32.astype(np.float64)).astype(np.float32)
    sin = np.sin(ang32.astype(np.float64)).astype(np.float32)
    return np.ascontiguousarray(cos), np.ascontiguousarray(sin)


_PROJ_B64 = (
    "eJwVmndYz98bxpsaioZoaCgJRaSlz/t5KsmeSXYyMjIyS0Za0tSQdhSpKAop+ryfB4WKZMvW15aVTRo/v7/Puc4f5zn3fb/u6zr2"
    "Pw5Ji42OQG5fLzg976mwdP8p8ZSmITxcGcajbyujcEmEQdX9sXlELO30nIJ7HoUhHHXDV6pmYlBhf3xTWw9jcozgkH44D5mpipqv"
    "h9KhHTP5yI5SHvc3BB6nDqVBP+fh+4Tt7Jt1gTZ1uMFK31jM7fUX2tekihcfnqbCwT9g6O5cYXDDNnI7kQ9/fvcHuZJ62ttkzBPS"
    "9Mm820I4mKBEQsZuodv83SDxcebpFXmwZMg8tNXJgfQrnVSxvkvY0tyT1D4aUl3hCpz4cjeofZGhhNwhtNE7hazeRYCL8iw82y7h"
    "1rqenLbfGbP7hlPwIWNxce1Ztn20G3stjOCvySsFR38dHtp+VCycbcYX+zXAm17A2++vZ/cBKXhZ66dk6nQfDh+sRG2+T+B6tCOZ"
    "ya3HS0WIj8+MwKeOJznrItPtlpvcOnUMrj32AB9+UEDlPgp44d959q9q6d6WA6DS/wAtv7cQm161i/a7/gix6ydBN3s77sjyFvIC"
    "tbBlpx6dWVjJwRlXYcg+Q2zOy5MmrojkkA0ddNEBedzN/bB2cCfcexFPlqmu6FdzWYgwPERPFg6EOxODIe3vfUj3j6TQYf6UUKlK"
    "+j3viL2wFL7KBGKcWTXLhRyA25IXEDbJGYeabeEXckbCg+QtOPOxDeu8l6EhKk1o9jUSfvytohR/Leyxt1T8nTgZI4MDqKWoFzbd"
    "6wm7k36B5etNlPLmNNs/s4X/ordDsbolFsz1ETTcb4l/r1lT+UZPrOWhtP/wPdYNfo4/XW3w+oL9UH97C1reLeZuUZM4+ZgcZHXu"
    "ExteNMIvt3U0v5s/pDoYwO4D2jx1pjxbPVoqWVNkh993Pqakjbo8+usINFLbxl3WG4SKtnwwHhYhBBqMEb1ddXjvp12gLE2GgimT"
    "hB2ZHnxlpTVoyZby8OkS3nCpEeaOiqSfziS6dE7nlb2OwOVzneJBWTM+cKkHQMFXCmlbweP//h6542QoDgnYjK/ak7kiIZv7hYfx"
    "80WzadEeZ340TVtoP9Md/B6eImmpCuvfH8lLWk3wzDRzVL6vixrjL2Lynxiez7v4kZcfr4/KIvsaRTqw0xHXF/Xj9bvz4PEKbXRp"
    "OS2k1SmyzKB2eLrIDo1fHQLXsGZ2fe7Cs2SD8Le4CWWUjXljjhl+7JCniIzJ7KV/V/hjmS3MHd4bYv9sEDOOaGP/Pc7CnbKXojAr"
    "H0Me24v26sjfjTdjStY5qli7ncRpE7GiYCQu4m54/PI4qiicjwezevAQ/e40LjAPfhXuEsfWHcD2ZQm8Vc6Jj2inoVdNKj7v44mk"
    "kE+tSqqo/NQeldJ24z7lv4L+yqGiWbwMBP97UdYT/PnunlNVZSP/QkZJO53dexsqGi+yrfxujByrDh+3fBXG+c7DNPG85EnfDLa/"
    "30broodi7e0Tgnb1fozU3IXNE805/mw+Pv69lC3uHMSucwH865EGN7shHpt+AeWCHtHwrSM5cuB+5DG3xECrh/RE8SVc2KGM1Skj"
    "eWD9HpD5uh+CXjlSxMe9nLrUBO2G+ZHv9AZ4ZvOXJl9MpXxHE/AwmAFlqyZL9799AX8to2D1wBDq/nwN/Db9QatKbHhxmzqkxJyh"
    "ldunc/4dH2hc64E+3q/h6Kty2JnZJW6oyeORVhV0Q7WK3uRn0d70Ouo/2fSfzuPQ71iOsPWDHhct2UfXu7rjE59k3r3uE8h/9GOX"
    "cX1R32ghrRzuAUO/LuSihqE82deSSlqG8/KJx+h2bSK+7ou8f3w2fFbNx3XvFyBqqKGnkhkMPV9Gg8deIWuz8WSkP4BsbmhR4yol"
    "3lc4ixWGzcZ3oRL89n0p1Cx7Qv9N6Qff/6bxMQUl9HQfw7pd5XTMRI4nl7GQtWo+h06M4q0GlnRB5g2oOxrgG+olOnyaKpyd6cbO"
    "Y81gwvJXsHlcHBodbIHDO0/Tt5ll0pJziJFuRvgpyQIT9eX4m8Fr0h1UCztUO0mpaA5/tLHA060tdPVSBFs5n4ZWnUC2fPALtHVj"
    "Wbm+D0Tst2VcspczL1+kpkEZMKJ6nbRlTTr6Onpy2YoZNLAolVz6rCLdqLmgUK3BvrYlEGRSy5sfTcZk9zyKzkzmjfP1cJlpExR+"
    "/ihpWZfJV07ZcIyFK68aZ8jLnQpYxSyCFxyfw79WRVDl9dewtOURvDw3m3/JbGfJgjsQYFGIet7KOKS/NbmYHxN+uRxzOnfuoViy"
    "dL946WogtRxfwcbv/lBSRR++2LGRh6yXUu/tSjg06he8XTSfj1sl8tSyOIz3SuF8v038MVeWF5ztx4EGcyEmuQViIqbBhZHd8M2X"
    "ebglYY7QqHhaotFtOXbH8Xh7pSU8iNZAnFvCft0PwehHDVTb8QDf6XuD9281Tlebww8s0ngzdlDemwncVqCCagO/wsvJLngifTLb"
    "Gdayw+1QGvLtnvDjgRWeuP9YOHCkmEKu7OCUrD/0fXwmxQzTwlvN0dxoHwEfg3thrNdDivmRAtHXHsOOeUeFz9kyeM+/AXv6RUPu"
    "CCc20h3ECdnyqHyjDZ4lLmPrS4P45IEWyL7fG3W3R7LyHF2c1VhGdLc7db4LgcqRf8Qd/ifRcnqYdK7jUOh6GIVNAz+RSqgsvNPY"
    "B8ZtMrhcsxyPDVNge725+LphNuZq53Cj7wj48GUGJdhG4/bZdwV9r9nYZPhOsF3fjSM/WfHWgIHsZSDSY2cP1HuTBwLvqhq6fhjs"
    "dFfGPk+sWL/fanyiuxJkHAxxV6M2zhowj6s7D2DYhud0tU8SH7wWBEqD7lP/PmHY+bICClNO0Z7KJ3Ts7QUMTbcTLtXuQdUPNVA5"
    "+SDv81bkK57GeClMHk1Ee340OQetS20xfmkYJ//MwHH7LlFVUypmPjvAHyX3adMVOUi/9YCS9QcKCzVy4UlxuzC3VsAcK0MuKUrC"
    "pTmPqJ8aiLx4PO1dHidM2TIF6ZIyBHxMorGrwniylj0GqneTyBlbiDM7F7Bm21VB1+EZhmwp5p+fB3Jnl5Y4oTwJnmX547VrtcIY"
    "Ky8qqSLqOpCFUaGh/N3sOdWWREDnlLMUsc4UWy/HoFmkHFk+PsKqwdli3IEpeD5llkQu8hF4+4+i86PjcFGcLc5Y5IW+AeGoxKk8"
    "f10AR9oZovKJJjZ+3EeqtWIQ+hbvpBa1+fhmejh33Y3Bce+rsUB2FP4MXIdXrfuibJohFd3QZP+dDUJe4geaUnKSHky/j1T7Xtqs"
    "voJ/D7dCpRBN/NpaAut3ufEPl1Wik7MJuHMSvh+QwiE9CmCBz0lx7bcg1puai4PrLwn338vAd5NQsi5Ux+dX/Why8nY+mZ4pOi4b"
    "hU3tWvzQzxJePi0UJylGCHlyDTCYW8TQ7oESm72V4obrF2Hb8XYeKPVhnxma/PXuWkx98gJM/yzgDK/lqPnbgp0vldKfne/IZJ0X"
    "2s2NwPSdqzi6WZOtcRCaKXvB72ZFNLvxh/ppzMeZqUo4s+0Zd1urDp4XX9NJtWOwdU4BvQpzELp9yoeg7y/BcOMjMrWXpznL5nPB"
    "u1rhQAPRjJm2KI1koeDkHrbxrOefMUewJUgRQ0oPk3fdPtxVyFSaoIUhLvloNOApOKhcg6Z5hynx/k6e/qeSdG6Pxbg/LrwPI+nG"
    "0ydkHngOvyZ9gA/FV2Dw9Q6auVADNx4qxNh5ijTq9VFcHhHPOS9iaM/ZUvHU4UH0X4I9nis35LfZs4RTNp3UNeq8eExyTKidMA0L"
    "xaPC9aORsNB8D7Z63Ba+r0yhJW1pMGq0HKcqLSar8tVwZosG7z+HfGuiLsdevQ1TLozhntibpPfUUK/+lWCp30FHhgWChc5e2LvD"
    "Ea1MZ8GIcjeMWeyC8mcvg2paNWXLhcCv1Jd0xH4afJSZg4Gb3cB8ZodkbkQSPQvywZP2V3lzuwJ3vkyEE6MT6WpsJP5ZXCyMu34B"
    "r+a4w7fxxCah/qJLvTVv914tLK57AdfepHJbggx6zHCBtMFxkFr3FKK6WqX9vi7APns24LOR46ljiSm9LeuOmbERUo8b3uKp1TKU"
    "9tQCfZ2S2Dj2ong/Yitc9xhCCuIxuu+QQW2XEOpOdoLf0hzs4TCY5DbfEG7ILOWRB2JQ2SIOp220R/nZ2TR76WtQaxfwzHTARWeX"
    "YtCcGDR8vp0DzEIxmg2ccl50/tP4bOx11BbNCqeIf6+m4Gs5Y76r9R0ieyzEpccnUd4/reX/ekK3osKw9o4Fnss5SzccLqHlRxv+"
    "qj0Um/I34QbbAWQv/1M0+dtEf7N3wly1VOq+7564xy8Ezz9bjFYZNrxs+Fg0DgiEL7462DLksKBpZYpb3g+hKasI3oenwYawaei9"
    "aTQrt8rCiBMJ+Fm3L8robcZchWXs/uYs5j+WY+i8TvZJ3c4duauP7/NP0CJNBayTDER/jy1sW68CC+77wDvfdBziM5SGsj2DYl9U"
    "DM+jnQm/oPfEVFh/ax5UD8tkbctedGvFSvHpozq4tk2AXhIXuvtZF6e+awPhkpxYW9ULJhQ2Sf3XvIXXdRbCvJtB3BXUm0e63IND"
    "AxRZbc15mrdMCddHF/KanwoUY7gU1gop0LllINUkR+HbFQMFvRWPSKFsK6cOcsV7NkMweKEix9qMwGA7JxjQ0wT29JxFybtXcXdX"
    "U5TxTeR6vdW8r/so3N/rLJj4jILYw8pQWbKP/VaY8J6jn+HuEWN+ekUPJSW9+UfLF6gaq41uBpqMWlIuXD0Wj+xqgGXpzbTeYxB8"
    "fKjCjmkhlFecKoTeWogz7jljdMgiOGR/HUZNrOQNA4LIN2A7JEbtZqNiINttFdj92p5/2Wwl5kWrg0XrYTog6SBlxYOgXJsEOe/V"
    "RYefKhA7KQ0UNAn2P1TGfvnOwu2ZV9CgXA8TboZi3sAbMNNoJ7o+//fWRi+k1ZlGPFLw4rZZjeCUoSaET5iFPx/8xz06dKWRnkdo"
    "3/skrN8WSLmn/8DebofAUaEbrh+fQxGql6jHzEWgnvkUrRYnYuHWOP5rPZCWB/dH7RfPYNjFWp7Z5xwvHroIN+1wYQ+5Z5KFWgq4"
    "VvIdtg+UF0ZajGO9cQ9wnbkevpyzj9cEL+eSf71kgagNrTtbQfjpixV5JgAPRJK1yubQUx7kkxDO027kceBrL37VWgWXxnjx5vo1"
    "9NXiI/XMl0G5BjmuOnVEqJ9XK5YMu40RuYMo+lwP7HFYSu8L7lKKwwhM2FJBfnNrOa7GWVz3MJvnv3TGcv/uePR8KviPrWTorsFJ"
    "ZbPEMpV74OqiyvqWbbRItppMch5C+T9Nrg5g1FWfBjrS0dh5vYbXzUnkOTN6Yv+wDAicbMS75aaDo+Zb6jsln41y+sKiUXZYN34F"
    "nnR7SjEDj8NCjZ9kVvtRtMs6SqP146D6kY4w91kE7kj/BWHtl0UN8T9YURrMZV2baEBTX1a5353X3ZTw64v/2PHDSRLLBN5tac5O"
    "L6Lo/j4VPvOrwdH+sTnvHmOF3U/40NJtGtjv7Hw85xoIa8YlwvrxSjhbTo7LAzJEj5Qw/PFtBK/LtOKe8zxxYmQHjAhuhzVqf+DX"
    "tAUspO8RTvwN4w3Ke1jOLhAd3wzjyj3XxW8VObhUcwpuuqKDc263CvlXwhiEm7RS2I5fPPph6EpVdH3xhOy8wmHu6z28f9MEjJR/"
    "CsMKJkubZnhhdwdTPnf/Cf/dMoGjvVxhRfUN2ptkzgU1N7C7/3tqV/fES28eUtXJFNF9ZSc05S9Fj20zuNPHEPfpDEJbS2MqPz2O"
    "oNIAZ3+yY5fRObyflDERIrjn/S/w9EEMvZg5jfPrYoWAce8g+YoR7m2bzxltvwTLzlLQMvkBGqfn8qdUCbRtaIByaqbijgF0J3c8"
    "5sM0yarnTyFyeT03OCfhqpr0yh0y+nAofyArvTrM6sccqWzebvQbuYrirdNJ02ucxMBbg/uu+Ejx0VpQ/dcBQyf35CXNZ4Q/59Zg"
    "ZPolIWqarPPefgfEkS978YT5IqtMdMDa6Iwq5csd0uV/PTFw/Th27vIQvkjSeKvDabAz+Q/gZy8cMXg4ng2+jXpHk/jC/HrqcN4F"
    "KmcnCXGva+j7yyGsNCwE1765LDTOLefpmXHwePJJUl83CP3XvyKj6cvwwydfcF50HU/M28PSLTH8bX6hZGv3HaT7dAHnDvLmgudd"
    "YGl2De2nm+GnrXfEopLv8GfrKvH5tHDUaN4DE5t3kGEvFxz3Jg37nXYA8/pEtBt+jHd66OCOY4Z4JL0HXu3qz4cvKeEgg3DqtvYM"
    "lOpZ02OlV3B+s7KoM9EZKo/LcKpid1riIYeaGyOpwTAGfz9op6S2UtqlsJu70s8LkSMIjl6eQqo+Cqw/XxQSNErh0I6hwrqb8lR6"
    "dC9e1FkobGpyADflndxoHsFNCsTJeJyl/Zso20xV1Mv/1x9vh+KfWV5YOHWMeEDUHvl87hFRaYaIj02iWV7lHHwbmQHPvvWjjlV7"
    "xLzsg5S5eB9OVp6O/63SxXUGUTgkfxeM7HWZOuZeFB6WWwnqloYw0vMKTn72gPv3dsWao4ep/WqetFh9C50yjeL+xzPAU6aKt94B"
    "3BKWAPYFuzm97Aj+V66IDh6W1PNEHC3XFXmyUqfTt8LDYvHAGkEvfjo9eGuH62Id2a92I6wRDtMxeg5Jjj3xzJ0AuO7QHx+rvmfJ"
    "LA/+s/0AdSnKi0d1+nOivD0+zg7lpd+r6NbpSCyO/yBKLhlA3XM7lm2ph3u6vVl70C/KDGv7lz0RHGOZSPuP1gD6nRbVl2UIhVmR"
    "PH/LS+jmMgpyXRfiaL0w2JM4He2PpcN493LRonIjFUnT2bZvLmVVijRZ7jt5GDVCiViKfxXc8LJXOclpWuPHahVucpkK418+p9Y+"
    "i7Fft99wa/lMnuOlwm/+nqDwit68vZsND5/6id0XFYlP49zZWceVO4ND2GNdGznX/6FhZj+Fq53ZvPGTMSbH11HvVC3nPva78cD5"
    "J3Bj0RPR9Y6RVO5sCtbNK8ey+8DJLRFs9GIrpu8ay/bnZvH+3se5MTOVLC9EkEewEwa0pkgOJ2qys/sXXvJwsdSqtyvVLXkrJI4e"
    "xPukHyDshhHeXuPNeyxPsc+K9ZSY/hSOHzkkFL3RxX+CY8++Xuy9KY8Co3whLmsYTdHL4rIMWag1eC1smRaMQ4NOc232RZLhpbhr"
    "+ED21Qvlz27HIP9XGContognvhSIyxvTaZXtQErtflAcoW+KGRkfaZxPCUTsWM1eSk4cWbYYtvcJ4GyrrZhTaMVtKj6YFqyC0u46"
    "6FuWzwVZSnjTu4a22YwjDbLks/+4PY8NaMjaofB8ZwW1ztxFU8fE8tY+lyDSbz562E/CR46j2DxRC8cON2aDWxv4Y/t51ogeD+LO"
    "ZOo/lARdmSoh4iqwrN0YuBuixltmyFHnnU76fuAQv/F9KQr+smwRvwIhrifPLnWjDQ7p1KvBXbjmdgfKdPLEyj66uLd1hTh9y1zh"
    "V2IMcR9fKo/II2O3AD4/JZedpjqR5flGMDsxHktT9vIp3WGYXn5WWB2uiv32KePelUOlt0q+iS+/XiG/n79p3oYtHOV6lVLHjcYC"
    "W1NsNtiIu36+o56DhuLSdFX4/vI7YahIv9vN+JnDatz0XyO4pPXmhxkVTnqBCYKqnz9In4aT7a9mOtKhjJrNB8XNs/Zx5d3jaDV2"
    "On795sV3uh/B1TdP8kC33vj1S4Xo2+oAx2RMWcVRhfUyzlOzkirnTugBdpbs1KT8FEZtK+EzQyc5PY6SAVODUIn+5cOkl2yKvXZU"
    "CwvfZeKdjrto3XKFu1+Lpf0V/bG1JYnuqAyHAUO24OKMOFy4bRXLGyvxypuKdEXnj8Cqi7EyVYO3KFRiv0JN6vfDmMcsCoQDg6bi"
    "qIC10Ngm4KaWUbzg21VsH2DPac0/RNUF8tye149T1qSLvccr4p1tunig/BvcDUzEZRbJ/K5rhfj79H345N74r5On0A3rXXDYbTf8"
    "N2IXTvCR8IE8VXxWbg7WIx4KPZyn8+9dSqBQ1YPk/C/Cd25A5eTF+Ec3GbwWP4H29inCue2jBaW45fDN7QS5qR8U/3jVUPiFmTz2"
    "xjbS97haZXDOUHxZeJCt9sxH3enp+OJCJLmUDUZ7u8dQNTdPUmytx6GW62jYrhooCDDFWu9DeEr6m45umgl37tniy47x4FajzXqb"
    "GvDPxVJ8F6YOw+9r8fB4LyHl/nwxxzsO15hfkbwLfkb9osKgIfgocnkCfOHDZKnaib36FtGUhgK8b30Kluy+BVZhKyjmR3f4WV4G"
    "Fw74kWxyBg90CueAVjv2sQ6iG4XNtChMAZem3cObZ6/ATm0dHjdNQikqc/k/zxRJ0mBnLM4po5W13Vn624DYcz+F6UykQjlFUsl9"
    "BQP0fkPlkJ8ctN5SeNp9CV+uXiUOH3iCHoaNgx+rY3hJlIJEcmM91Bn4sOuSUfSr6SNm2ijyunyBm0+owVn3WJ677QrNirrGjZeN"
    "YNusUTS4tkTiYbORF+iw5OuWKtwYUA3T5Su5qpYpQcsVr18I5MajrpgSvxcX3enOCkc1uPdJDTK8p4vfBANUOHuc7WtmUdOH55Rv"
    "epHeFLjTLtkENLhbBCv8u6Hx2ETSO3JFCN09BJLXl4rL6N/8p1vxs5ffwTtZHuda9sUABU00+m8F13XfKio4DSE0mcbmDd1p+NWP"
    "8MX5AmnGuOM/14HZjw/xC9OJeHOQGbfMMudNyr78uzpZnP63gbIMv9KFT1UU3GcVPZ9VUjVf9roYMPkTLbmYxOkP4tDqtDtrrs+g"
    "x2mROHLFC/pkdZPUP6ZBYNYZOFpwEJSiZvEPaxOc9eQKPbgbjPJeXyhduzsGFkwCM4d1+Pnzan51xJu+/zgjmb/hOm7yy6Py5UEo"
    "87oLvw86LKaeSSFDixcgG1HMtpXZ4kTfRiEhYAcMMAyS3iZDPFETAg0zPvD7vvNReaAeVpRshoE6elgQ/1FMq3LBZc5GHBw0nEaM"
    "/SPGW8aKQxu3YYvDIj7cX5bNu+Ww3lN98Nmsh9V1DbDoy3pQDzmIBhEsHpC8FA9ULYLbnyPIe2oDffwwDg9EGuHy1hisTvohlJoa"
    "C6F3plKDTH9USJBUDTT6Drtfb6bEOE+4u+8qLK1biVst+mCqqhSKeh6l4BPtMFGB4HXXCV52vRsntfzgu8vuQfignjj68wNK1XaD"
    "SWNWS2+8XgGO0b7w5W4PCNhoy3tL4lGjTzpkWMrw4zM3cEeEKdb1LpMMXK3GC4LPkUKAHJr8HgnGbklclgz8qV4R9SwZfq26QSVs"
    "yrfdanAoL6HQZYtxYIIZHplST6cPucNp5T7isLtOgv7szZxSI8uDR1wW5lz6IE5SyoYLuw1E02AdbraUSrdF7uTpMbGCf6cjGo2Y"
    "yGr9u6HduxicJJ38r74qQVBnJBnVDGCF35pcVW2KW3zLaP/OFEo69RWyLy7Bag2kXod8xd37ntH37R3Q879FeGHRTfrED3HKD3l0"
    "ykolbb+ZfOOUJd+/kyG2/ZA793ZWK3nJOXB7aze8IZ2Bf55GC28i+nOW+2P4qzSIzQ0suHBwKfQ/1AFfnu9k6Z8GYeurIGnzwTno"
    "6fAeBp9LgnWNm87+3JrFn4MqcIe5FVdkpLPK3td0In8BZY3aDS4VmfRf4n4OkUTST0k3envZX+LcH/Fvxg4a/3kILUuWwfqsJCjd"
    "acujH/Rlm4fHeELsAvC30YH+0Xq8Y+1HofJQPATaJPBmy4/C5v3n6cSLLWQ14RfsyRjO2cMCwP+aK/sMysMJ9naY+lhAieUHiIf1"
    "KJ9zgsqmHYWRsxXZN/4mFdvp4zV3dVp9I4R8D72ApXe78Qk9TTSms1TYV5frHSvp+OD3OKa7DFPGGmxySwXL1km4qZnIo5ce9Nmd"
    "ze5Kj6HF6ClZ65Tzjy1buGVSNU1p649qFRJWsLgHF+b90+cVYBNnLxaCy1hWzZ2XTKihjcGh/P5zPZw3/wHHFHdBXukyUvSLhSKH"
    "BDTJXov5ed8oz/ICr30vIO8ZjPXD5pH1wak8wccJ9UqHsM4+E/wgmYj7d8niCxtZbF57QuitGgv7dZPo2nQ1fjn4OC1IOlnV9bgb"
    "r9r4me6PUaGylr7gfbOM4pwOQiM1om3Mhn9M6YlfD76Cvn8Q1bJL8O6k2ZCWsY5Dlv1rdLJ/oNhFBVqDptC0YU2c038nFDsLuHKi"
    "F3lbyPMz+1v8ZshfOhN0kt9WXmTlk8MheFsgao/VwxHbDoH/7Cn8UGk/l10U4OjNBD7ZflnSPsoTFybpw8JANy7l35KDZ5IRPfpi"
    "14gJ+HDweL7yIBSr1YIlihdduFHnH0vnx/GUtY7od14VR32eQPF3XSllhZyzeW6QuMLLgCfEvBeKMqPw6Y9Yav7yiwZf64/a5/L4"
    "dO482v5VGQ1ch+KkFSD9HZzNVx7O4aLUIP679B5Eq/sIFn0bYNbEkWh1Mo43KO7E1tkOQvyMw2iy+I7QnvQczBuH4l3dG6LKr/44"
    "cmgoD1wgQQ4fx9UrM7n3wGkUnj+Oaz7cI5m07bDjYwV0VT8BJ99QWl0wBEfqzeBhV0uoee1IHDe3CV5Yj+Sp3sXov1GVR6qqcRSo"
    "c1v+Vfjw3ZPkLr3CX2cmgfLUHP7s1wxh52zFG9N38TP5CD4xu446DFQEqy9d9MrJEBdWq517Y/xVHP8jmi5Tl1C10wYWjq6mmmZV"
    "3F+kSG1lzWS9zoi/eM2iQsv56Pt9Iyf0a5e+uCYVNn1bDeG7lkJe53hOO9ATQp3KyX74RT6zQ4ELKhK5XMFZGNajjBesfEuT7+4S"
    "Pu8ZxHO0d5GCUl9aYKBCvsonsWj5exrp1w9Tzj8RenoaUex+ZU5/+oYHK9RhjtMzMfTHU7A5vxQeHzGh/uDExZun45KWcEyviUVz"
    "T0PcabsXPi2xFM7s0oF2+1fiDNfdoPVrPwdb9BSnlE2DqzfCcLXNMKywUcAvUjVx+ccp+MX9AJzs6I267ae55aA1Dbp2peqxRgIv"
    "9lnLJ/ddwszZ+nw+wYA3/GqjPjmvYeAkTdyTIAunFJPg7IHVYHd2IOaLsqAz5l9/SdYX909JhXd3lDDEJw/0b7GYmNqbB6/qid1t"
    "FVDWS1FiOkUOVxbO5Afx4XD3ky43LxLo0ssjqPQoEM+tSxEytieLTlYlUovpSuDuv57/+0/KYWPa4SrfhhunNTk45BV5yNSTjdNs"
    "TJyrwyslA9h3oC+VrsmDmB/p4Bswn/5oxcKp5Sp4dKyEa1r0yDy3L2pNNuZs10h2fVsCHx274GTpdLFivT23VMkyeu9hjRkbUU7L"
    "GPseMuOqkCZ4ppyGYz+mcODsg3DisTn3DlFnm5+XODGyhcrq9+Dmeco8OuYJ2mSW4duc+Rj72IoGv+lGK4LCwTDNhVBpJKsuUeXh"
    "LiHoohqABf388JJHNmz/68X9emaJxTpTWFKymWWG27Ci/E5sCAvnmNmfyadoPgd9/EIfb8lD0cF16A0nhegriXja6orQI7q7cHCy"
    "9b+9czDkg7q44UtPaFbrz05JsVTXGiUu2ShwikJvnHR+r/jG8CN0jR+Etr3SuNaWKH7OJvLRyuFGFwXQuhhOk7NXYLPiBaB35aDy"
    "vRKt7r8Bk4yHdLAfkW/5GZb+2Ms93mwFzdoU+Hv1LJi3paPT8AChvTAN1jWnwmTLp3DXbQ+PMdwPYxz2CrKTd/Mip904v6YVVoc8"
    "gbxBF8WNgfJY2+SAvWYm4/mkNMHKxobX9s6G2JgGPpNaJb44G4HObMxDSWRvh1VkubYPjvyUTPvcp7Lx43Fcv+EJaV6PpJFfj/It"
    "75+84cYMfqUZ+K93xvK8nDR8cOQ4B8ywxcZ1MnR1zAOSLvoN1k9yaci55eISrTxsObkCS5Zf5tr2Hpi2Dzh4jzXNfuuBkvhKUldT"
    "/7dWLYZEFJMYORUcumWJAzLrpCqTF5LjvAQcce4gK/gr0IOz/3J5fhpur9PCBWPMKG9zBrRUh3Lofm2uGfCa9t7XI+u0QDb8+ICj"
    "Y8Jof7svZ0d9oe0bo/jM1O5iZIMtAKZKJB6vwFA7BhyuOXLzzbdk0r6Po//qs0/JZNyaKtA2o//o+OwY0rLx5IK+JXxSsZo2Hk/h"
    "7aPzoOctX15um0+7k7vxjqxw9nRy5qU1sbDO1QhdWy+T125rfPanj2hSPoDahywSZhwfxxdCJ1DA0tc0OMuGfn+7DnNXuuHt6nBM"
    "7N0DGirHwSK1f/f7Yzl0LF0hJg79Q0dJU8zdlwv5X0ZBvE05BNxbxdv6vqNuVW9Rd1EIL7xaLo5YH491X0miblEopCUa8kPF3XD3"
    "ciR0yBRT7tAPMLZ6PRdc9eFTxf64uSjhH7da4JMoZTZ7PE602RSMcR93UPOY7pJlYjQMOPkbPO+fRVhmwXk+vmLbvatQffknKa1V"
    "Ya7qLXFQURW7/NNY7spwuJe4lvyWrhJW7rYTgk5Z4R41f8hMWge2jqfZ3TkSbQOk7PUlDg2+vqXAE1H0qOMDsFIk5/+IBMMFSeKO"
    "oCZIg9VCbkcdPBmaIP1u9wnOp8Vg4qhZ1E14C6rBOVQ6JoqztPJ4/cfzdFulAoftKxDCMiLB+ucQ7OmlBP0CHsCRl3Y0clkKuf8O"
    "Zf2o4ZyweQVmaI6i6M5WetdwEVzrtoHb2HH4av0B9D9YDhK/GDScNAzln3RRW9Z2Ol2wBLdYLaNrGSG84IMqdY1MPNsjIvFfpoWB"
    "mvJ0rno1hjtNprDa0BAnwwne0JBwimbcCgHZbHeMHnQUJ/z4xhHdzuCjj734sUI0CqvzMXLlI2lugxNNmroHZKiIOyfM5TdWPzgz"
    "PxvT44t5xvRjwrGWhTx6WRKqZ3kIRuJKyaaeEgqNzWLjDVfEhHvp0KP7K3LXjRcrC7vD1hHzecyJgRTh/AH6nLdj/1l/IKv3GTC/"
    "7CrMsgwQ22SHiBt9GZe/3MK/++7APVVKfNzYQixxCMY1ywZxrZsNXLJTp5aeFXDEdgcs3elNmjtrQDIgh3p/6YF3DQP4m8dyHpBu"
    "jpH1atAlKsPcbQv46oyfpKZ7hb8tn4m2/tfg7Qkl9C08DMZvXVg+wkZINA8E18O9WGOcCe8dacRHG5fge9c+NFoVOFtVmyP+RLCD"
    "ezg9la6ihnnlUv/sIEi1CwfTqBYSFq1g5UFDcGPoCTCTOLD35IXcKptKG9PVxKXHzfCNRSHke/QSlA5rQeHVvnh4fYJwq0pCez37"
    "wIToRdQwOhSbs4exf8F1ajQawnfz9uLFrZfIr/c03PJ7lJg+X5Y6fYMwKCqQxpaoYfGpZVw4IUrkml58IVsbzdZoiBv0poHSWxn0"
    "XKnON/tMQhTC2Ov+I1GyKgy/JjrwOotcQdtkIq77OpFv7pxCf1Iu0F/3oVyhsQBtR2Vhap9eoPbXDgb73BXnlwfD0bkbaOmQNtJb"
    "8xir9p/Db/UvaUrwZTzrWM6y1w7BPXkpfTM1hmcrh0ntTmTj52HROOVgI0yaq8d6luXQ0b87hptv5SvHrTC3YwBOd4inAY0LeecA"
    "E2wdkCv6ruvPw3cOZrpryorTP4o69ccZx+twt2olTs9sh37aejDmjT/tHqXN1Q+rKUn/OGXpmnHEBXleNsWJ+5jn4WXHyZhIzqiZ"
    "eFnywfoYnTkaLF55Gkf3rk/D80eNRCX1/rjY2ZX2jDwP8no6/7j7jvAMXbCuTUtY4BnNsTqr8OHzKFj1tx/qzZBSZbQp75nqA3lb"
    "RtD34OEY8VSNkx9uw6HecbB43kZeP2U4Hz1qxKu3z6EeS2fynIKJKHbWihPP3xf95I252jURw/SD+K1fNprVa6FgbAoVy2tghedY"
    "vlL3ltZmE15ODMOmd6uJTG/xIefRGIJ53DH4AR5PkcHJg8+CrJYbWvbZDs6n5wjDS4aQncRSfDrsLMyM74WSfg2CoqkcB9SZIc3U"
    "wJFOM3Fmc3826aPIq3ftElztLEH9ozfOtjVBaXsf3lpYJJRuVsH3o6KE1DuvaDm9ofz209QxSh/PFN8ToqsfCbuG6vAD0xswrv0y"
    "5nVdJAXZh9LezhHij0m/QUP+O61/oQsDW0t4xaDd3F/pniQ5V2D3IR+oz3Jf6C2XTQm25ZJ5gxTOoYEUx45+LURNm4YLZGLoZtQu"
    "uMMXhRfjQnCf/yapSdYp/G9HJg3fMc3ptG1/1u52iZ12zqCAEQNw9fd5+PHuGB7qG4d7P9rwDV1nunl7LE7/3EErZsaCSv01br5T"
    "DN39wiDW3hyCLrwgce9y3rdFE57NVHQuqlFnxUf/fPDVdPGLZwHHl/iz/kl5UTp3Pu2vTKVKj0Y6IDVCuygVVhAO4bWAM/x+YI4A"
    "+8JxklY/ELRHsNF/9/+xRT/R6GYUKp5y4Mxb5VLzAXbcJ7cRcugy/D7RRbe0uvGiYjnUcl6E8q0r+G2PNhq+7iQ3L5yC4bVuuHXq"
    "a0ruuxfPXdnMwZ/6ctmdAdDvba1ky29fnJtlhF9PtlDo3QjYoPiQThXVMjQe4TXiGlQ5HygmPJnO+xZv4tkGQ7j3a284bSZ77tel"
    "oxA1UY7iYjTRus4Uyz9ViEpq+8H8SCQ6bnNE9Zk7WWOriThRmgFT9f/QhpM/JTvfNsJ9ZQt4a3kIXpv9hpJFTeIE/aE8TrUSXl3x"
    "xu2HDHjmy9n0Y3E2vclaTK490vnbY0daZb8L9Geerco+pU5L9Cx5c9l+Snf9wHYnkoWtB/rTE686UaNzBunWt0N49SgxPmdelWay"
    "O3OYPT1P+kzRdQ/oPgZTzaK1bNDynjsSRvGDY0boH1RBoz5ooslFkfOKPoP72ycc8KZLbF7lI2rLpcAClb4cersC2g7N5YO338K6"
    "61JyfL6f5lzy5fHf90JtfbEY4+XEjSsHC1pn9Wj9jWsw7XoR78BhtG1YCN50SYORww1Eick2OHnIC1Tfm7Jl6Hzc3XSKdLabsIuD"
    "GSnYrOUeCqlsVuIzUuPOU+CvT+nYK2VUHFaNZdlL8FZmiNQyME/42buafv2WQR8ZBdxn+YJ21BjhneInlOpZBpuuzcWYZncYpzUW"
    "1Q650aZGHXRrm0grZ08mz2VJVBguUtwjR7g9+L34y9KRq6/98/yO70LS9s+CTu9cnmfqhBbDPoiVRhN51s9nQlOrOuoZNpK2ngZV"
    "2BrDhlmDsG2LG+pWRGKd8Tj8k5oPr3omsGrXbG4Za/CPv4Zi1cJ97Nkkj4ojp+GebTEQ1BzGX2fKk1X6Gszpb4x3ti9BnZIAMfux"
    "FiqcSqIok6dC46RUOB7hLppc7M03zqvgxcXPYfUqNVFnhjYGeurghqla7FRszivL5chiZhw/7JgNWs+dIHOtjXTZgc18Ut6U1Vvc"
    "+Xt5PHuH2MO90jzJuOsbUfXtGGy48RqG3vbgouUTaNHXaZS18z/Iqawmna19OOJUMA5SFXF16TZY5NglHS2zmUrj8uHwjB+kdHcK"
    "2oReour+ZliVfRESPT2kDZEprK13FLp2x2CtYwV3Ly5n47t7eLmaDzn55WCOYgVHGrgQn0mHN9c9cZq+I3706aJJpf25rvQzzT7f"
    "Lq3I6M3T3OvwQf0ynjI7jAd3idIgRzcs3XmChndo8rDq3my+yZXfmU/BhF2ZXNF3BPZrjEc3i+G8bW49flu3EVd0BvPc73/AW2kl"
    "R/sr4/OPp8V1kiiel/gSKk0LoOuNAt5sKeOT/VbSrM0H+NGd0dT5ZQNXbW8SBBsfWrLnFm2+EMLn7BKhU3EtDP+iztd9F6HsVwV8"
    "t/YJjZk3FuX3FdDgP+3srfgJFOd9pk9mp1lebiArzNjEV8p1+XB1DH+ubKfwRANe/rsvLlr2Q1jyqobcspXI7ZsZf/OdC0P/3CL3"
    "Nfo4LWJ8lcP7s8LPvANkXvyMgvL3kkXQdojt0sVfN8Zw4o5Gyt4chRuzY6Wta3LosMc+aW/zX/R8RQGUNpfCxZFGaGS5FnXhmvT/"
    "Hcc4Qo0Hz9iD8uczUCbLAt3GtkBubSRm9cviSVcGcb2JkvOyhq9CdtpPqXd/Ha79CPhu8BTyMgG8mNZIP2wPsYnMS9jm4c5DIhTZ"
    "9uATYX5UreR2phYfb70k3fSPWQ8/GSjoFMXig6k9KG9ULm0Mz+Qsn3Nw88156bDyGuHy0ShUOJYNRRaNnGlqhDIlP8ReD3Pg3add"
    "vLPWCTaWjIEJLtWgvNQV3v5S4KO5MoJydQ/0i2qkUqc3MDbZmk3FNq4qviJ2ZMRgpcd3arCJwUVx5+jDcTtBWBRDe2rfwPct8fTk"
    "VRq9+ZdU3iN+sXXdVD5034Bv9LLABVar4LDvL/rpUIFbV+pxx/qFbDFoB0pfR2H9yVTqF6DJLxaYoGbPIdAZNBIDFiwQa7wRZk3Y"
    "Ta4AvGnIBFw3PgyzSuz/ddTt6HhBpM+6y3jfmAhusE/giA/9+XG0o6j9XRbtjTKh/EsJ+/qo8NtPstCSt4s3hwPLJ42GuGtXwKho"
    "E7afvONkMvWP8DpoEtztp8y6vvqcWTCCp500hBJVb/6zoF00PKlFBhYLaXbMd2FqUQ+oC6sWsr5dhcceByFndh7XbznMy9ZI2HDT"
    "U+oe81aYlHkZ7hybAvtuxYLvrHHSe9e28q+ex0nTczAOTp+NqX72oGJgyPL4jgpa3pNBTBybzvHlbTfiaO2dQ3xx9g1ufJ8mVmj4"
    "w6T9lXRqdiQ/mqyEygVnYNHowzzw1nkxeZYZn9l+kle0ToVjhivw0tMlNFExBPx2rcQ4+Oeh3x0w0n8Wlnnk8YC6WN46eS+mDVfG"
    "xUtthUIrLfby3ymx6HzLQWoKXCXbh+mjE8odTqZ7kxewzPlIdrsqlXgKc4U74hu6JdTiBI8ReHX7esHgpzcXTGynnE1pWOawirfm"
    "DGDT47fFhb8SeI/5BDQ/LQr9G7y44mY++yv2xYS4Q7B+WgsUdNuCCr/SaJKjPR7/mybEH6mnLOsg7qMhy44hcex1YiHW+BiQi80B"
    "Al0bMWObGt5MUkML8SV1bOvGJt2t8J7GIGpPsYXj35ZyW88rNCJ6D0hG1EqufqmiwCDgi8tdOCDcER9vCwHv8Bi4/9ecyXGncGeN"
    "E1zc5cS9Px8nvSUdFKFsxDVXdNF76lQ+sCQEl8pG8aoFz2DfpCJhyHALmLM+DDUk3cQd8YFgOame9s+/SYv/eaX9uiI+kOUq6m5T"
    "p6a/PaD4rgqjrD7arPxJ5WMyoMHwIm33/EsHouMAm34KD/a7Sb5N7cY32lPR7/wH0c41DBKKu0PinHTYnB4hDe5QgWGXjNjwQTFf"
    "KFzGL7oi+chPD84cHcc/3SRYtyIczBfsoC44C5V7y4XL/EDImBjH09uOo5p1BXntvUdHf+dB071p9MXSFGPUDDiqKwGtzyrSqKnj"
    "cI/5JD72tEmUCx+OH6ak4uKDrvgu01rydZmImvYO9KxBjp8mpdGS4jf06lG52LTR4F/6u/DpI+uFJ0adNHxBDTa/S0BtZwV076lI"
    "Y4a/gVwTXbSbPACejToCx3JlOXr0D3C19uMfhW8pPSaI5zztgdXDXvGpHp7YUnyZ8x2vks2nydilsFWYcVkb+4dEoW2ROarUjBYd"
    "VwWypfwyHn0nCg8HpkLKXz96MNGBFz59Dm+ejkGFgkY4biGPhmYBkH22Rqx2caFrqQ68/o8VTTXWh8CIdWRvX0XHflyn1hgr1NbM"
    "w7sqWmi+WED3mb0o83AT+Iw6xe4bg9Fk0kG+7mECWlGxtLZTExJ1j1CgSi7meDwUl5yowHrbU/D9nRoVjbTjzY6fxal1rZL3deNw"
    "6k1H1GuzAuFoC42/aoDFSpNQ5sBktvrejf+7byFQWi922+6P5h2qqLQ1UuL+Q8CiI2cgqzWSYwb5oe2y5ei0+DyONwigppRhuErb"
    "GgcYnqNlLhnC4dVjyE/WgbvffkIXT9+A4qktwqDnS8WQiF4sq6RLXUtPCh9ynsFQMRevLs/Cvbf287tZiSS8TaEb6+KE6Sn7QHGh"
    "Eqz8eRynOQ+Db0HJ8PtaDo1aNhzVbsdD5qExIk8fiSN3pbBYehee/prBWQeX8YriFNCaOIh2t/7/34MGZ/WNp5Bx8cQrkunDohX0"
    "X3M+l2ra/5tJFY+4tRna3rlwdfIEPNPNhnTCBb7UVQE93vox9jmEU/PP0NVFS4TUqZ70Y0YmTdoymZRbU/lLn3y+ahfCZHKMvm96"
    "KMw+2cUb5YrxS+B0cUPlRPiyaTYdnnBDbNY4Qr177sZtj7bhj7BCHlJnhXafgd/M/yOe3eTENw8+g8yyeky99gTCl0TDiboA6DIt"
    "wII4H155dBJdbBlURXFDsWzHevy9wAmsoxfy1ZTtxFnysOGQGvOxemHzp3ga+CweHLb+oKrl5bDr93tyXujjpIaWPLVcE1dm7MaO"
    "ZUVU2vyECkf34Jue6byxfxb+mtAAzcm/aE7sTcg/RnQy+g1o5Ryj2RPcSUXjPtV4vaPfd3dDbLkMz1uZJ5m/KQKyXfZD/bws8cXj"
    "eqpeXcBdrz5hk8MHWJ2xG3JCfqFfRydtWRUqhMZP5uda6jh35Ut43e+bcP63F31yPwI3P9yAR1t6sLbxZpax6U2uLn/pjq0hV+p9"
    "Eb/4y/PirmJW+1nEw8dO4NfjjMSiwyMpdE07vVyghpO+bYbgJWdZvnU+tWTv5CDlOLFQukPqOc6el7d4867bw9DNWR+c989DlXmT"
    "4Z2NPoBhBgSozUJld3NeVddAFp3I+hn6pJFvi81FS7Cq1fBftp7CxOlGnBnVRnPeWfMxcQMdfNUXPcpHsUzyKopru0/q3ra8QCWe"
    "Sj+8pSWmMugWe5BKf6mAzuEYrAoPgwFbX0FO/xq2GljttHVaBj8YcoUyiwkD85zBxrA/hmeN4r7xs5yi6vai6x15Wrt8PY3XqKds"
    "tQEQXzUTMn3zxUGvavjQ77mizrYDaPNuLIzQ74v/uZvSn762dPv0Gm56oIkfszuhd9JduDDpGsztTIfsnpux15RCCOr4TVO+DcWj"
    "0hxaKZ3HDnWHJB311rTn00Lsd14NH5kEUNqYyfRKeyrfV7oKw/v5ctekbfDSpVOa7z+RXx+voLUW0TDw1F56suyW9JBWOd/OD+PV"
    "DfJ06HculZ3sgXbbZ2LoCXeYcVoW/EpJ6PP+KezsGUszLsziUbNC8NeTyeickEXXcveDwo4luHoBQN74f/6ZNl8836+JXFb0ZYm6"
    "DCfN+3w2+6ia85qMTPrqt5Gtf13lugcX4WYzQc8elVKPqWEckB3F5/6mYJ8FV+mMaQyMP2iGk1PD6LRfBvYYJMvXGjSxr40ZNB94"
    "TxGve7HXgGSc12WMM29EOY3VRvRU15HaNM4km2Rjnti4i651mwQaETr4UilB0DE/jBNfmKA77uWqioPstEoCbXk++NBkB/+Os8Uq"
    "tTQ47CmLHm9XkYmGFzwc4Y5f4F/OrCmAfWtnifat42niCS/uWDmQS4r0eXxRHq1ZsVD66cIlUp2qBTVWF4ASl2JeH008MOcafZqT"
    "D49X9oFPrQV0se8cNo6PpM8658Ri4wG4Lby3cHZ6b7CYUEIqVh5cYa/NFxf040zln+JzMVJ8uT2OO+fniqFN32F471iUGz7eyWD1"
    "ZnFa2ixWWBQENSPqRQdXOzw10Bs33xrFDl/vUfrNrULqqVJBs0JCHsVOXDrQGp2q/Pi2uYlTXEYlXz3/V7jQo0l6+bOzJNJqBPRZ"
    "WM5urXslsjN9KfPWBzG29D6YRJ/gNFLgn1ozufFOCNymRCethw6YqngdTgyzYE3fZnhZM5cXnrDEVkdfnDDGnV/k+8L6VZOw3N2P"
    "oy2iuXOZO6Y9NuTFuQskC4+1UOXilXBBey9q4iA2nH8bJLOc8WhzHprY9gaFI3I8x7ODKv/TxNU7KijXbDRnz4gAjxs2OPOCLkZX"
    "DMbirQb4vo8WH5yvivdnpItd80dzXPNToWPLV/FvkQ0+DtZmw/wsakqU5U+96njqf5s5YN40zMvti/zxu8iqRhAw0J5z/Rx5nuVL"
    "aB86Db06Lvxj02rytvHgI85J0mxJh2A7vAwnFT2Abw7r6f08U0F/zGpQVfVBUN3Eo+TcOePPfBxWUUUa8XJcMC5OMqdmNY/6doBf"
    "SQzhz7HDokPqWup+d7c4Wt1eEuraQC/WpaPjQU/qlzORDdL2C8b/WH7V1VIy7VjLdhXjUd04HY4cUkDj9fPp0v04CF40hke+SMQj"
    "WM2hOywpqyKPxlxpAu0ES0w97UHta/cLtRel3NJnAiYFF3CWmEvPF0+ntyUr+cTUZHCZWUaaWt54WkFVenjUOnY3OwxnlH9Bm/J2"
    "XL55HzYlVcKZ4jaKf9xEDoODOaBXOPwPa40u3w=="
)


def _proj_const():
    import base64
    import zlib
    raw = zlib.decompress(base64.b64decode(_PROJ_B64))
    return np.frombuffer(raw, np.float32).reshape(128, NJL).copy()


_NC_CACHE = {}


def make_in_maps(L0_attn_x, L0_attn_Wq, L0_attn_Wk, L0_attn_Wv, L0_attn_Wo,
                 L0_attn_q_gain, L0_ffn_gate_up_packed, L0_ffn_gate_up_scales,
                 L0_ffn_down_packed, L0_ffn_down_scales):
    x = np.ascontiguousarray(np.asarray(L0_attn_x, np.float32))
    Wq = np.asarray(L0_attn_Wq, np.float32)
    Wk = np.asarray(L0_attn_Wk, np.float32)
    Wv = np.asarray(L0_attn_Wv, np.float32)
    Wo = np.asarray(L0_attn_Wo, np.float32)
    gain = np.asarray(L0_attn_q_gain, np.float32)
    gup = np.asarray(L0_ffn_gate_up_packed, np.int32)
    gus = np.asarray(L0_ffn_gate_up_scales, np.float16)
    dwp = np.asarray(L0_ffn_down_packed, np.int32)
    dws = np.asarray(L0_ffn_down_scales, np.float16)

    cos, sin = _rope_tables()
    projc = _proj_const()
    iden = np.eye(128, dtype=np.float32)
    blk = np.zeros((128, 2), np.float32)
    blk[0:64, 0] = 1.0
    blk[64:128, 1] = 1.0
    par = np.zeros((128, 1), np.float32)
    par[0::2, 0] = 1.0
    ones = np.ones((128, 1), np.float32)
    onesr = np.ones((1, 128), np.float32)
    cvals = np.zeros((128, 4), np.float32)
    cvals[:, 0] = 1e-5
    cvals[:, 1] = 1e-6
    cvals[:, 2] = 0.0
    cvals[:, 3] = 1.0
    ltm = np.triu(np.ones((128, 128), np.float32), 1)
    iotab = np.broadcast_to(np.arange(128, dtype=np.float32)[None, :], (128, 128)).copy()
    blkid = np.arange(128, dtype=np.float32)[:, None].copy()
    poff = (np.arange(128) % 64).astype(np.float32)[:, None].copy()

    gateT = np.ascontiguousarray(gup[:IFF].T)
    upT = np.ascontiguousarray(gup[IFF:].T)
    downT = np.zeros((768, DM), np.int32)
    downT[:704] = dwp.T
    gsc = np.ascontiguousarray(np.repeat(gus[:, :IFF], 16, axis=0))
    usc = np.ascontiguousarray(np.repeat(gus[:, IFF:], 16, axis=0))
    dsc = np.zeros((768, DM), np.float16)
    dsc[:704] = np.repeat(dws, 16, axis=0)

    in_maps = []
    for c in range(8):
        b, qr = c // 4, c % 4
        hs = slice(512 * qr, 512 * (qr + 1))
        gtm = np.broadcast_to(gain[4 * qr:4 * qr + 4][None, :], (128, HPC))
        in_maps.append({
            "xb": x[b],
            "xq": np.ascontiguousarray(x[b, TPC * qr:TPC * (qr + 1)]),
            "wqT": np.ascontiguousarray(Wq.T[:, hs]),
            "wkT": np.ascontiguousarray(Wk.T[:, hs]),
            "wvT": np.ascontiguousarray(Wv.T[:, hs]),
            "woT": np.ascontiguousarray(Wo.T[hs, :]),
            "gain_tm": np.ascontiguousarray(gtm),
            "cos_d": cos, "sin_d": sin, "proj_d": projc,
            "iden_d": iden, "blksel_d": blk, "par_d": par,
            "ones_d": ones, "onesr_d": onesr,
            "gateT_d": gateT, "upT_d": upT, "downT_d": downT,
            "gsc_d": gsc, "usc_d": usc, "dsc_d": dsc,
            "cvals_d": cvals, "ltm_d": ltm, "iotab_d": iotab,
            "blkid_d": blkid, "poff_d": poff,
        })
    return in_maps


def kernel(**inputs):
    in_maps = make_in_maps(**inputs)
    key = "nc"
    if key not in _NC_CACHE:
        _NC_CACHE[key] = build_nc()
    nc = _NC_CACHE[key]
    res = run_bass_kernel_spmd(nc, in_maps, core_ids=list(range(8)), trace_cores=[0])
    out = np.empty((B, S, DM), np.float32)
    for c in range(8):
        b, qr = c // 4, c % 4
        out[b, TPC * qr:TPC * (qr + 1)] = res.results[c]["out"]
    return out
